# version 04
# baseline (speedup 1.0000x reference)
import contextlib
import numpy as np
import ml_dtypes
import concourse.bass as bass
import concourse.mybir as mybir
from concourse.bass_utils import run_bass_kernel_spmd

F32 = mybir.dt.float32
BF16 = mybir.dt.bfloat16
I32 = mybir.dt.int32
U32 = mybir.dt.uint32
AF = mybir.ActivationFunctionType
ALU = mybir.AluOpType
AX = mybir.AxisListType
BF16NP = ml_dtypes.bfloat16

NCORE = 8
B, T, D = 2, 8192, 1024
NTOK = B * T
PT = NTOK // NCORE
NCH = D // 128


class Tok:
    __slots__ = ("sem", "val")
    def __init__(self, sem, val):
        self.sem, self.val = sem, val

class Buf:
    __slots__ = ("name", "w", "readers")
    def __init__(self, name=""):
        self.name = name
        self.w = None
        self.readers = []

ENGS = ("pe", "dve", "act", "pool", "sp")

class Em:
    def __init__(self, nc):
        self.nc = nc
        self.es = contextlib.ExitStack()
        self.sems = {}
        self.cnt = {}
        for s in ("s_pe", "s_dve", "s_act", "s_pool", "s_sp"):
            self.sems[s] = self.es.enter_context(nc.semaphore(s))
            self.cnt[s] = 0
        self._batch = {}
        self.q = {e: [] for e in ENGS}
        self.waited = {e: {} for e in ENGS}
        self._sb_bytes = 0

    def sb(self, name, shape, dt):
        h = self.es.enter_context(self.nc.sbuf_tensor(name, list(shape), dt))
        self._sb_bytes += int(np.prod(shape[1:])) * mybir.dt.size(dt) * 0 + \
            int(np.prod(shape)) // shape[0] * mybir.dt.size(dt)
        return h

    def ps(self, name, shape, dt=F32):
        return self.es.enter_context(self.nc.psum_tensor(name, list(shape), dt))

    def sem(self, name):
        self.sems[name] = self.es.enter_context(self.nc.semaphore(name))
        self.cnt[name] = 0
        return self.sems[name]

    def _engine_sem(self, eng):
        return {"pe": "s_pe", "dve": "s_dve", "act": "s_act",
                "pool": "s_pool", "sp": "s_sp"}[eng]

    def dma_slot(self, slot):
        s = "d_" + slot
        if s not in self.sems:
            self.sems[s] = self.es.enter_context(self.nc.semaphore(s))
            self.cnt[s] = 0
            self._batch[s] = []
        return s

    def emit_dma(self, eng, slot, fn, reads=(), writes=(), deps=()):
        s = self.dma_slot(slot)
        tok = self._emit(eng, fn, reads, writes, deps, dma_sem=s)
        self._batch[s].extend(writes)
        self._batch[s].extend(("R", b) for b in reads)
        return tok

    def close_batch(self, slot):
        s = self.dma_slot(slot)
        fence = Tok(s, self.cnt[s])
        for item in self._batch[s]:
            if isinstance(item, tuple):
                b = item[1]
                for i, t in enumerate(b.readers):
                    if t.sem == s and t.val < fence.val:
                        b.readers[i] = fence
            else:
                item.w = fence
        self._batch[s] = []
        return fence

    def emit(self, eng, fn, reads=(), writes=(), deps=()):
        return self._emit(eng, fn, reads, writes, deps, None)

    def _emit(self, eng, fn, reads=(), writes=(), deps=(), dma_sem=None):
        want = []
        for b in reads:
            if b.w is not None:
                want.append(b.w)
        for b in writes:
            if b.w is not None:
                want.append(b.w)
            want.extend(b.readers)
        want.extend(deps)
        need = {}
        for t in want:
            if t is None:
                continue
            if need.get(t.sem, -1) < t.val:
                need[t.sem] = t.val
        if dma_sem is None:
            semn = self._engine_sem(eng)
            self.cnt[semn] += 1
            tok = Tok(semn, self.cnt[semn])
            inc = (self.sems[semn], 1)
        else:
            semn = dma_sem
            self.cnt[semn] += 16
            tok = Tok(semn, self.cnt[semn])
            inc = (self.sems[semn], 16)
        waits = []
        wtab = self.waited[eng]
        for s, v in sorted(need.items()):
            if wtab.get(s, 0) < v:
                waits.append((self.sems[s], v))
                wtab[s] = v
        self.q[eng].append((fn, waits, inc))
        for b in reads:
            b.readers.append(tok)
        for b in writes:
            b.w = tok
            b.readers = []
        return tok

    def barrier_tok(self, eng):
        semn = self._engine_sem(eng)
        return Tok(semn, self.cnt[semn])

    def finalize(self, final_deps):
        nc = self.nc
        need = {}
        for t in final_deps:
            if need.get(t.sem, -1) < t.val:
                need[t.sem] = t.val
        qmap = {"pe": "tensor", "dve": "vector", "act": "scalar",
                "pool": "gpsimd", "sp": "sync"}
        with nc.Block() as block:
            for eng in ENGS:
                items = self.q[eng]
                if not items and eng != "sp":
                    continue
                def mk(eng_items, is_sp):
                    def emit_all(engobj):
                        for fn, waits, inc in eng_items:
                            for sem, v in waits:
                                engobj.wait_ge(sem, v)
                            ins = fn(engobj)
                            assert ins is not None, "emit fn must return last instruction"
                            ins.then_inc(*inc)
                        if is_sp:
                            for s, v in sorted(need.items()):
                                engobj.wait_ge(self.sems[s], v)
                    return emit_all
                getattr(block, qmap[eng])(mk(items, eng == "sp"))
        self.es.close()


def ceil_div(a, b):
    return (a + b - 1) // b

def mm_tiles(n, step=512):
    return [(j, min(step, n - j)) for j in range(0, n, step)]

class Chunked:
    def __init__(self, em, name, nch, ntok, dt):
        self.h = [em.sb(f"{name}{i}", [128, ntok], dt) for i in range(nch)]
        self.b = [Buf(f"{name}{i}") for i in range(nch)]
        self.nch = nch
        self.ntok = ntok


def prep_wT(w):
    return np.ascontiguousarray(np.asarray(w).T.astype(np.float32))

def prep_w(w):
    return np.ascontiguousarray(np.asarray(w).astype(np.float32))


def build_layernorm(em, x, out, psum_pool, misc, ntok, eps=1e-5, inv_n=1.0/1024.0, out_col0=0,
                    xoff=None, gain=None, lbias=None, x_col0=0):
    nc = em.nc
    if xoff is None:
        xoff = [0] * x.nch
    onesc, onesc_b = misc["ones_col"]
    onesr, onesr_b = misc["ones_row"]
    sq, sq_b = misc["sq"]
    r_s1, r_s1_b = misc["r_s1"]
    r_s2, r_s2_b = misc["r_s2"]
    r_a, r_a_b = misc["r_a"]
    r_b, r_b_b = misc["r_b"]
    ps_s1, ps_s1_b = psum_pool.take1()
    ps_s2, ps_s2_b = psum_pool.take1()
    for t0, tn in mm_tiles(ntok):
        def mk_s1(t0=t0, tn=tn):
            def f(e):
                ins = None
                for ci in range(x.nch):
                    o = x_col0 + xoff[ci] + t0
                    ins = e.matmul(ps_s1[0:1, t0:t0+tn], onesc[:, :], x.h[ci][:, o:o+tn],
                                   start=(ci == 0), stop=(ci == x.nch - 1))
                return ins
            return f
        tmm = em.emit("pe", mk_s1(), reads=[onesc_b] + x.b, writes=[ps_s1_b])
        em.emit("act", (lambda t0=t0, tn=tn: lambda e: e.activation(
            r_s1[0:1, t0:t0+tn], ps_s1[0:1, t0:t0+tn], AF.Identity, scale=inv_n))(),
            reads=[ps_s1_b], writes=[r_s1_b])
        for ci in range(x.nch):
            em.emit("act", (lambda ci=ci, t0=t0, tn=tn: lambda e: e.square(
                sq[:, 0:tn], x.h[ci][:, x_col0+xoff[ci]+t0:x_col0+xoff[ci]+t0+tn]))(),
                reads=[x.b[ci]], writes=[sq_b])
            em.emit("pe", (lambda ci=ci, t0=t0, tn=tn: lambda e: e.matmul(
                ps_s2[0:1, t0:t0+tn], onesc[:, :], sq[:, 0:tn],
                start=(ci == 0), stop=(ci == x.nch - 1)))(),
                reads=[onesc_b, sq_b], writes=[ps_s2_b])
        em.emit("act", (lambda t0=t0, tn=tn: lambda e: e.activation(
            r_s2[0:1, t0:t0+tn], ps_s2[0:1, t0:t0+tn], AF.Identity, scale=inv_n))(),
            reads=[ps_s2_b], writes=[r_s2_b])
    em.emit("act", lambda e: e.square(r_a[0:1, 0:ntok], r_s1[0:1, 0:ntok]),
            reads=[r_s1_b], writes=[r_a_b])
    em.emit("dve", lambda e: e.tensor_sub(r_s2[0:1, 0:ntok], r_s2[0:1, 0:ntok], r_a[0:1, 0:ntok]),
            reads=[r_a_b, r_s2_b], writes=[r_s2_b])
    epst, epst_b = misc["eps"]
    em.emit("act", lambda e: e.activation(r_s2[0:1, 0:ntok], r_s2[0:1, 0:ntok], AF.Ln, bias=epst[0:1, 0:1]),
            reads=[r_s2_b, epst_b], writes=[r_s2_b])
    em.emit("act", lambda e: e.activation(r_a[0:1, 0:ntok], r_s2[0:1, 0:ntok], AF.Exp, scale=-0.5),
            reads=[r_s2_b], writes=[r_a_b])
    em.emit("dve", lambda e: e.tensor_mul(r_b[0:1, 0:ntok], r_s1[0:1, 0:ntok], r_a[0:1, 0:ntok]),
            reads=[r_s1_b, r_a_b], writes=[r_b_b])
    em.emit("dve", lambda e: e.tensor_scalar_mul(r_b[0:1, 0:ntok], r_b[0:1, 0:ntok], -1.0),
            reads=[r_b_b], writes=[r_b_b])
    for t0, tn in mm_tiles(ntok):
        ps_a, ps_a_b = psum_pool.take1()
        ps_b, ps_b_b = psum_pool.take1()
        em.emit("pe", (lambda t0=t0, tn=tn, ps_a=ps_a: lambda e: e.matmul(
            ps_a[:, 0:tn], onesr[:, :], r_a[0:1, t0:t0+tn], start=True, stop=True))(),
            reads=[onesr_b, r_a_b], writes=[ps_a_b])
        em.emit("pe", (lambda t0=t0, tn=tn, ps_b=ps_b: lambda e: e.matmul(
            ps_b[:, 0:tn], onesr[:, :], r_b[0:1, t0:t0+tn], start=True, stop=True))(),
            reads=[onesr_b, r_b_b], writes=[ps_b_b])
        for ci in range(x.nch):
            em.emit("dve", (lambda ci=ci, t0=t0, tn=tn, ps_a=ps_a: lambda e: e.tensor_mul(
                sq[:, 0:tn], x.h[ci][:, x_col0+xoff[ci]+t0:x_col0+xoff[ci]+t0+tn], ps_a[:, 0:tn]))(),
                reads=[x.b[ci], ps_a_b], writes=[sq_b])
            if gain is None:
                em.emit("dve", (lambda ci=ci, t0=t0, tn=tn, ps_b=ps_b: lambda e: e.tensor_add(
                    out.h[ci][:, out_col0+t0:out_col0+t0+tn], sq[:, 0:tn], ps_b[:, 0:tn]))(),
                    reads=[sq_b, ps_b_b], writes=[out.b[ci]])
            else:
                gt, gt_b = gain
                lt, lt_b = lbias
                em.emit("dve", (lambda ci=ci, t0=t0, tn=tn, ps_b=ps_b: lambda e: e.tensor_add(
                    sq[:, 0:tn], sq[:, 0:tn], ps_b[:, 0:tn]))(),
                    reads=[sq_b, ps_b_b], writes=[sq_b])
                em.emit("dve", (lambda ci=ci, t0=t0, tn=tn: lambda e: e.scalar_tensor_tensor(
                    out.h[ci][:, out_col0+t0:out_col0+t0+tn], sq[:, 0:tn], gt[:, ci:ci+1],
                    lt[:, ci:ci+1].broadcast_to([128, tn]), ALU.mult, ALU.add))(),
                    reads=[sq_b, gt_b, lt_b], writes=[out.b[ci]])


class PsumPool:
    def __init__(self, em, n=8, width=512, name="pp"):
        self.t = [em.ps(f"{name}{i}", [128, width]) for i in range(n)]
        self.b = [Buf(f"{name}{i}") for i in range(n)]
        self.i = 0
        self.n = n
    def take1(self):
        j = self.i % self.n
        self.i += 1
        return self.t[j], self.b[j]


def build_project(em, xin, w_hbm, K, M, ntok, psum_pool, wbufs, epilogue,
                  mgroup_chunks=4, max_j_per_dma=8, xoff=None):
    NJ = K // 128
    nmch = M // 128
    if xoff is None:
        xoff = [0] * NJ
    for g0 in range(0, nmch, mgroup_chunks):
        gch = min(mgroup_chunks, nmch - g0)
        mg_w = gch * 128
        wslot = "wA" if (g0 // mgroup_chunks) % 2 == 0 else "wB"
        wb, wb_b = wbufs[(g0 // mgroup_chunks) % 2]
        m0 = g0 * 128
        for j0 in range(0, NJ, max_j_per_dma):
            jn = min(max_j_per_dma, NJ - j0)
            def mkload(j0=j0, jn=jn, m0=m0, mg_w=mg_w, wb=wb):
                def f(e):
                    src = w_hbm[j0*128:(j0+jn)*128, m0:m0+mg_w].rearrange(
                        "(j p) m -> p j m", p=128)
                    dst = wb[:, j0*mg_w:(j0+jn)*mg_w].rearrange(
                        "p (j m) -> p j m", m=mg_w)
                    return e.dma_start(out=dst, in_=src)
                return f
            em.emit_dma("pool", wslot, mkload(), writes=[wb_b])
        em.close_batch(wslot)
        for i in range(gch):
            mi = g0 + i
            for t0, tn in mm_tiles(ntok):
                ps, ps_b = psum_pool.take1()
                def mkmm(i=i, t0=t0, tn=tn, mg_w=mg_w, wb=wb, ps=ps):
                    def f(e):
                        ins = None
                        for j in range(NJ):
                            o = xoff[j] + t0
                            ins = e.matmul(ps[:, 0:tn],
                                           wb[:, j*mg_w + i*128: j*mg_w + (i+1)*128],
                                           xin.h[j][:, o:o+tn],
                                           start=(j == 0), stop=(j == NJ - 1))
                        return ins
                    return f
                em.emit("pe", mkmm(), reads=[wb_b] + xin.b, writes=[ps_b])
                epilogue(mi, ps, ps_b, t0, tn)


def build_l0ffn(ntok_core=PT, tt=512):
    nc = bass.Bass("TRN2", target_bir_lowering=False)
    xs = nc.dram_tensor("xs", [D, ntok_core], F32, kind="ExternalInput").ap()
    w1T = nc.dram_tensor("w1T", [1024, 4096], F32, kind="ExternalInput").ap()
    w2T = nc.dram_tensor("w2T", [1024, 4096], F32, kind="ExternalInput").ap()
    h1T = nc.dram_tensor("h1T", [4096, 4096], F32, kind="ExternalInput").ap()
    h2T = nc.dram_tensor("h2T", [4096, 4096], F32, kind="ExternalInput").ap()
    w3T = nc.dram_tensor("w3T", [4096, 1024], F32, kind="ExternalInput").ap()
    s1h = nc.dram_tensor("s1", [128, 32], F32, kind="ExternalInput").ap()
    s2h = nc.dram_tensor("s2", [128, 32], F32, kind="ExternalInput").ap()
    sh1h = nc.dram_tensor("sh1", [128, 32], F32, kind="ExternalInput").ap()
    sh2h = nc.dram_tensor("sh2", [128, 32], F32, kind="ExternalInput").ap()
    s3h = nc.dram_tensor("s3", [128, 8], F32, kind="ExternalInput").ap()
    xo = nc.dram_tensor("xo", [D, ntok_core], F32, kind="ExternalOutput").ap()

    em = Em(nc)
    x = Chunked(em, "x", NCH, tt, F32)
    xn = Chunked(em, "xn", NCH, tt, BF16)
    hA = Chunked(em, "hA", 32, tt, BF16)
    hB = Chunked(em, "hB", 32, tt, BF16)
    xout = em.sb("xout", [128, tt], F32)
    xout_b = Buf("xout")
    scs = {}
    for nm, hh, w in (("s1", s1h, 32), ("s2", s2h, 32), ("sh1", sh1h, 32),
                      ("sh2", sh2h, 32), ("s3", s3h, 8)):
        t = em.sb("sb_" + nm, [128, w], F32)
        b = Buf(nm)
        em.emit_dma("sp", "cst", (lambda t=t, hh=hh: lambda e: e.dma_start(out=t[:, :], in_=hh))(),
                writes=[b])
        scs[nm] = (t, b)
    em.close_batch("cst")
    misc = {}
    for nm, shape, val in (("ones_col", [128, 1], 1.0), ("ones_row", [1, 128], 1.0),
                           ("eps", [1, 1], 1e-5)):
        t = em.sb(nm, shape, F32)
        b = Buf(nm)
        em.emit("dve", (lambda t=t, val=val: lambda e: e.memset(t[:, :], val))(), writes=[b])
        misc[nm] = (t, b)
    for nm, shape in (("sq", [128, 512]), ("r_s1", [1, tt]), ("r_s2", [1, tt]),
                      ("r_a", [1, tt]), ("r_b", [1, tt])):
        t = em.sb(nm, shape, F32)
        misc[nm] = (t, Buf(nm))
    wb1 = em.sb("wbufA", [128, 8192], BF16)
    wb2 = em.sb("wbufB", [128, 8192], BF16)
    wbufs = [(wb1, Buf("wbufA")), (wb2, Buf("wbufB"))]
    pp = PsumPool(em, n=8, width=512)
    out_toks = []

    for t0c in range(0, ntok_core, tt):
        for ci in range(NCH):
            em.emit_dma("sp", "xin", (lambda ci=ci, t0c=t0c: lambda e: e.dma_start(
                out=x.h[ci][:, :], in_=xs[ci*128:(ci+1)*128, t0c:t0c+tt]))(),
                writes=[x.b[ci]])
        em.close_batch("xin")
        build_layernorm(em, x, xn, pp, misc, tt)
        def ep_a(mi, ps, ps_b, t0, tn):
            em.emit("act", (lambda mi=mi, ps=ps, t0=t0, tn=tn: lambda e: e.activation(
                hA.h[mi][:, t0:t0+tn], ps[:, 0:tn], AF.Silu, scale=scs["s1"][0][:, mi:mi+1]))(),
                reads=[ps_b, scs["s1"][1]], writes=[hA.b[mi]])
        build_project(em, xn, w1T, 1024, 4096, tt, pp, wbufs, ep_a, mgroup_chunks=4)
        def ep_b(mi, ps, ps_b, t0, tn):
            em.emit("dve", (lambda mi=mi, ps=ps, t0=t0, tn=tn: lambda e: e.scalar_tensor_tensor(
                hA.h[mi][:, t0:t0+tn], ps[:, 0:tn], scs["s2"][0][:, mi:mi+1],
                hA.h[mi][:, t0:t0+tn], ALU.mult, ALU.mult))(),
                reads=[ps_b, scs["s2"][1], hA.b[mi]], writes=[hA.b[mi]])
        build_project(em, xn, w2T, 1024, 4096, tt, pp, wbufs, ep_b, mgroup_chunks=4)
        def ep_h1(mi, ps, ps_b, t0, tn):
            em.emit("act", (lambda mi=mi, ps=ps, t0=t0, tn=tn: lambda e: e.activation(
                hB.h[mi][:, t0:t0+tn], ps[:, 0:tn], AF.Gelu, scale=scs["sh1"][0][:, mi:mi+1]))(),
                reads=[ps_b, scs["sh1"][1]], writes=[hB.b[mi]])
        build_project(em, hA, h1T, 4096, 4096, tt, pp, wbufs, ep_h1, mgroup_chunks=2)
        def ep_h2(mi, ps, ps_b, t0, tn):
            em.emit("act", (lambda mi=mi, ps=ps, t0=t0, tn=tn: lambda e: e.activation(
                hA.h[mi][:, t0:t0+tn], ps[:, 0:tn], AF.Gelu, scale=scs["sh2"][0][:, mi:mi+1]))(),
                reads=[ps_b, scs["sh2"][1]], writes=[hA.b[mi]])
        build_project(em, hB, h2T, 4096, 4096, tt, pp, wbufs, ep_h2, mgroup_chunks=2)
        def ep_o(mi, ps, ps_b, t0, tn, t0c=t0c):
            em.emit("dve", (lambda mi=mi, ps=ps, t0=t0, tn=tn: lambda e: e.scalar_tensor_tensor(
                xout[:, t0:t0+tn], ps[:, 0:tn], scs["s3"][0][:, mi:mi+1],
                x.h[mi][:, t0:t0+tn], ALU.mult, ALU.add))(),
                reads=[ps_b, scs["s3"][1], x.b[mi]], writes=[xout_b])
            out_toks.append(em.emit_dma("sp", "out", (lambda mi=mi, t0=t0, tn=tn, t0c=t0c: lambda e: e.dma_start(
                out=xo[mi*128:(mi+1)*128, t0c+t0:t0c+t0+tn], in_=xout[:, t0:t0+tn]))(),
                reads=[xout_b]))
            em.close_batch("out")
        build_project(em, hA, w3T, 4096, 1024, tt, pp, wbufs, ep_o, mgroup_chunks=2)
    em.finalize(out_toks)
    return nc

def l0ffn_inputs(xT_shard, ins):
    d = {
        "xs": np.ascontiguousarray(xT_shard, np.float32),
        "w1T": prep_wT(ins["L0_ffn_w1_w"]), "w2T": prep_wT(ins["L0_ffn_w2_w"]),
        "h1T": prep_wT(ins["L0_ffn_h1_w"]), "h2T": prep_wT(ins["L0_ffn_h2_w"]),
        "w3T": prep_wT(ins["L0_ffn_w3_w"]),
    }
    for nm, key, w in (("s1", "L0_ffn_w1_s", 32), ("s2", "L0_ffn_w2_s", 32),
                       ("sh1", "L0_ffn_h1_s", 32), ("sh2", "L0_ffn_h2_s", 32),
                       ("s3", "L0_ffn_w3_s", 8)):
        v = np.asarray(ins[key], np.float32).reshape(w, 128).T
        d[nm] = np.ascontiguousarray(v)
    return d


def rope_apply(em, chunks_h, chunks_b, cosT, cosT_b, sinT, sinT_b, perm, perm_b,
               pp, ntok, strip=None, col0=0):
    p0, pn = (0, 128) if strip is None else strip
    for ci, (h, b) in enumerate(zip(chunks_h, chunks_b)):
        for t0, tn in mm_tiles(ntok):
            c0 = col0 + t0
            ps, ps_b = pp.take1()
            em.emit("pe", (lambda h=h, c0=c0, tn=tn, ps=ps: lambda e: e.matmul(
                ps[:, 0:tn], perm[:, :], h[:, c0:c0+tn], start=True, stop=True))(),
                reads=[perm_b, b], writes=[ps_b])
            sc, sc_b = em.scr
            em.emit("dve", (lambda t0=t0, tn=tn, ps=ps, sc=sc, p0=p0, pn=pn: lambda e: e.tensor_mul(
                sc[p0:pn, 0:tn], ps[p0:pn, 0:tn], sinT[p0:pn, t0:t0+tn]))(),
                reads=[ps_b, sinT_b], writes=[sc_b])
            em.emit("dve", (lambda h=h, c0=c0, t0=t0, tn=tn, sc=sc, p0=p0, pn=pn: lambda e: e.tensor_mul(
                h[p0:pn, c0:c0+tn], h[p0:pn, c0:c0+tn], cosT[p0:pn, t0:t0+tn]))(),
                reads=[b, cosT_b], writes=[b])
            em.emit("dve", (lambda h=h, c0=c0, tn=tn, sc=sc, p0=p0, pn=pn: lambda e: e.tensor_add(
                h[p0:pn, c0:c0+tn], h[p0:pn, c0:c0+tn], sc[p0:pn, 0:tn]))(),
                reads=[b, sc_b], writes=[b])

def pe_transpose(em, dst, dst_b, dst_col, src_ap_fn, src_bufs, ident, ident_b, pt, pt_b):
    em.emit("pe", (lambda: lambda e: e.transpose(pt[:, 0:128], src_ap_fn(), ident[:, :]))(),
            reads=list(src_bufs) + [ident_b], writes=[pt_b])
    em.emit("dve", (lambda dst=dst, dst_col=dst_col: lambda e: e.tensor_copy(
        dst[:, dst_col:dst_col+128], pt[:, 0:128]))(),
        reads=[pt_b], writes=[dst_b])


def build_l0attn(ntok_core=PT):
    n = ntok_core
    nc = bass.Bass("TRN2", target_bir_lowering=False)
    xs = nc.dram_tensor("xs", [D, n], F32, kind="ExternalInput").ap()
    WqT = nc.dram_tensor("WqT", [1024, 1024], F32, kind="ExternalInput").ap()
    WdT = nc.dram_tensor("WdT", [1024, 256], F32, kind="ExternalInput").ap()
    WuT = nc.dram_tensor("WuT", [256, 2048], F32, kind="ExternalInput").ap()
    WoT = nc.dram_tensor("WoT", [1024, 1024], F32, kind="ExternalInput").ap()
    bq = nc.dram_tensor("bq", [128, 8], F32, kind="ExternalInput").ap()
    bd = nc.dram_tensor("bd", [128, 2], F32, kind="ExternalInput").ap()
    bu = nc.dram_tensor("bu", [128, 16], F32, kind="ExternalInput").ap()
    bo = nc.dram_tensor("bo", [128, 8], F32, kind="ExternalInput").ap()
    cosq = nc.dram_tensor("cosq", [128, n], BF16, kind="ExternalInput").ap()
    sinq = nc.dram_tensor("sinq", [128, n], BF16, kind="ExternalInput").ap()
    permh = nc.dram_tensor("perm", [128, 128], BF16, kind="ExternalInput").ap()
    identh = nc.dram_tensor("ident", [128, 128], BF16, kind="ExternalInput").ap()
    xo = nc.dram_tensor("xo", [D, n], F32, kind="ExternalOutput").ap()

    em = Em(nc)
    x = Chunked(em, "x", NCH, 512, F32)
    xn = Chunked(em, "xn", NCH, n, BF16)
    q = Chunked(em, "q", NCH, n, BF16)
    kvl = Chunked(em, "kvl", 2, n, BF16)
    kv = Chunked(em, "kv", 16, n, BF16)
    of = xn
    q_tm = em.sb("q_tm", [128, 1024], BF16); q_tm_b = Buf("q_tm")
    kv_tm = em.sb("kv_tm", [128, 2048], BF16); kv_tm_b = Buf("kv_tm")
    p_scr = em.sb("p_scr", [128, 4096], BF16); p_scr_b = Buf("p_scr")
    s_red = em.sb("s_red", [128, 256], F32); s_red_b = Buf("s_red")
    e_tm = em.sb("e_tm", [128, 256], BF16); e_tm_b = Buf("e_tm")
    z_tm = em.sb("z_tm", [128, 16], F32); z_tm_b = Buf("z_tm")
    o_red = em.sb("o_red", [128, 1024], F32); o_red_b = Buf("o_red")
    o_tm = em.sb("o_tm", [128, 1024], BF16); o_tm_b = Buf("o_tm")
    xout = em.sb("xout", [128, 512], F32); xout_b = Buf("xout")
    scr = em.sb("scr", [128, 512], F32)
    em.scr = (scr, Buf("scr"))
    consts = {}
    for nm, hh, shape, dt in (("bq", bq, [128, 8], F32), ("bd", bd, [128, 2], F32),
                              ("bu", bu, [128, 16], F32), ("bo", bo, [128, 8], F32),
                              ("cosq", cosq, [128, n], BF16), ("sinq", sinq, [128, n], BF16),
                              ("perm", permh, [128, 128], BF16), ("ident", identh, [128, 128], BF16)):
        t = em.sb("c_" + nm, shape, dt)
        b = Buf(nm)
        em.emit_dma("sp", "cst", (lambda t=t, hh=hh: lambda e: e.dma_start(out=t[:, :], in_=hh))(),
                    writes=[b])
        consts[nm] = (t, b)
    misc = {}
    for nm, shape, val in (("ones_col", [128, 1], 1.0), ("ones_row", [1, 128], 1.0),
                           ("eps", [1, 1], 1e-5)):
        t = em.sb(nm, shape, F32)
        b = Buf(nm)
        em.emit("dve", (lambda t=t, val=val: lambda e: e.memset(t[:, :], val))(), writes=[b])
        misc[nm] = (t, b)
    for nm, shape in (("sq", [128, 512]), ("r_s1", [1, 512]), ("r_s2", [1, 512]),
                      ("r_a", [1, 512]), ("r_b", [1, 512])):
        t = em.sb(nm, shape, F32)
        misc[nm] = (t, Buf(nm))
    em.close_batch("cst")
    wb1 = em.sb("wbufA", [128, 2048], BF16)
    wb2 = em.sb("wbufB", [128, 2048], BF16)
    wbufs = [(wb1, Buf("wbufA")), (wb2, Buf("wbufB"))]
    pp = PsumPool(em, n=6, width=512)
    pt1 = em.ps("pt1", [128, 512], BF16)
    pt1_b = Buf("pt1")
    out_toks = []

    for t0c in range(0, n, 512):
        for ci in range(NCH):
            em.emit_dma("sp", "xin", (lambda ci=ci, t0c=t0c: lambda e: e.dma_start(
                out=x.h[ci][:, :], in_=xs[ci*128:(ci+1)*128, t0c:t0c+512]))(), writes=[x.b[ci]])
        em.close_batch("xin")
        build_layernorm(em, x, xn, pp, misc, 512, out_col0=t0c)
    def mk_ep(dst, bias_nm):
        def ep(mi, ps, ps_b, t0, tn):
            em.emit("act", (lambda mi=mi, ps=ps, t0=t0, tn=tn: lambda e: e.activation(
                dst.h[mi][:, t0:t0+tn], ps[:, 0:tn], AF.Identity,
                bias=consts[bias_nm][0][:, mi:mi+1]))(),
                reads=[ps_b, consts[bias_nm][1]], writes=[dst.b[mi]])
        return ep
    build_project(em, xn, WqT, 1024, 1024, n, pp, wbufs, mk_ep(q, "bq"), mgroup_chunks=2)
    build_project(em, xn, WdT, 1024, 256, n, pp, wbufs, mk_ep(kvl, "bd"), mgroup_chunks=2)
    build_project(em, kvl, WuT, 256, 2048, n, pp, wbufs, mk_ep(kv, "bu"), mgroup_chunks=2)
    rope_apply(em, q.h, q.b, consts["cosq"][0], consts["cosq"][1],
               consts["sinq"][0], consts["sinq"][1], consts["perm"][0], consts["perm"][1],
               pp, n)
    rope_apply(em, kv.h, kv.b, consts["cosq"][0], consts["cosq"][1],
               consts["sinq"][0], consts["sinq"][1], consts["perm"][0], consts["perm"][1],
               pp, n, strip=(0, 64))
    ident, ident_b = consts["ident"]
    for tt in range(n // 128):
        c0 = tt * 128
        for ci in range(8):
            pe_transpose(em, q_tm, q_tm_b, ci*128,
                         (lambda ci=ci, c0=c0: q.h[ci][:, c0:c0+128])(ci, c0) if False else
                         (lambda ci=ci, c0=c0: lambda: q.h[ci][:, c0:c0+128])()(),
                         [q.b[ci]], ident, ident_b, pt1, pt1_b) if False else None
        for ci in range(8):
            em.emit("pe", (lambda ci=ci, c0=c0: lambda e: e.transpose(
                pt1[:, 0:128], q.h[ci][:, c0:c0+128], ident[:, :]))(),
                reads=[q.b[ci], ident_b], writes=[pt1_b])
            em.emit("dve", (lambda ci=ci: lambda e: e.tensor_copy(
                q_tm[:, ci*128:(ci+1)*128], pt1[:, 0:128]))(),
                reads=[pt1_b], writes=[q_tm_b])
        for ci in range(16):
            em.emit("pe", (lambda ci=ci, c0=c0: lambda e: e.transpose(
                pt1[:, 0:128], kv.h[ci][:, c0:c0+128], ident[:, :]))(),
                reads=[kv.b[ci], ident_b], writes=[pt1_b])
            em.emit("dve", (lambda ci=ci: lambda e: e.tensor_copy(
                kv_tm[:, ci*128:(ci+1)*128], pt1[:, 0:128]))(),
                reads=[pt1_b], writes=[kv_tm_b])
        for hh in range(4):
            em.emit("dve", (lambda hh=hh: lambda e: e.tensor_mul(
                bass.AP(p_scr, 0, [[4096, 128], [1024, 4], [64, 16], [1, 64]]),
                bass.AP(q_tm, hh*256, [[1024, 128], [64, 4], [0, 16], [1, 64]]),
                bass.AP(kv_tm, 0, [[2048, 128], [0, 4], [128, 16], [1, 64]])))(),
                reads=[q_tm_b, kv_tm_b], writes=[p_scr_b])
            em.emit("dve", (lambda hh=hh: lambda e: e.tensor_reduce(
                s_red[:, hh*64:(hh+1)*64], bass.AP(p_scr, 0, [[4096, 128], [64, 64], [1, 64]]),
                axis=AX.X, op=ALU.add))(),
                reads=[p_scr_b], writes=[s_red_b])
        em.emit("act", (lambda: lambda e: e.activation(
            e_tm[:, :], s_red[:, :], AF.Exp, scale=0.125))(),
            reads=[s_red_b], writes=[e_tm_b])
        em.emit("dve", (lambda: lambda e: e.tensor_reduce(
            z_tm[:, :], bass.AP(e_tm, 0, [[256, 128], [16, 16], [1, 16]]),
            axis=AX.X, op=ALU.add))(),
            reads=[e_tm_b], writes=[z_tm_b])
        em.emit("dve", (lambda: lambda e: e.reciprocal(z_tm[:, :], z_tm[:, :]))(),
            reads=[z_tm_b], writes=[z_tm_b])
        for hh in range(4):
            em.emit("dve", (lambda hh=hh: lambda e: e.tensor_mul(
                bass.AP(p_scr, 0, [[4096, 128], [1024, 4], [16, 64], [1, 16]]),
                bass.AP(e_tm, hh*64, [[256, 128], [16, 4], [0, 64], [1, 16]]),
                bass.AP(kv_tm, 64, [[2048, 128], [0, 4], [1, 64], [128, 16]])))(),
                reads=[e_tm_b, kv_tm_b], writes=[p_scr_b])
            em.emit("dve", (lambda hh=hh: lambda e: e.tensor_reduce(
                o_red[:, hh*256:(hh+1)*256], bass.AP(p_scr, 0, [[4096, 128], [16, 256], [1, 16]]),
                axis=AX.X, op=ALU.add))(),
                reads=[p_scr_b], writes=[o_red_b])
        em.emit("dve", (lambda: lambda e: e.tensor_mul(
            o_tm[:, :], o_red[:, :],
            bass.AP(z_tm, 0, [[16, 128], [1, 16], [0, 64]])))(),
            reads=[o_red_b, z_tm_b], writes=[o_tm_b])
        for ci in range(8):
            em.emit("pe", (lambda ci=ci: lambda e: e.transpose(
                pt1[:, 0:128], o_tm[:, ci*128:(ci+1)*128], ident[:, :]))(),
                reads=[o_tm_b, ident_b], writes=[pt1_b])
            em.emit("dve", (lambda ci=ci, c0=c0: lambda e: e.tensor_copy(
                of.h[ci][:, c0:c0+128], pt1[:, 0:128]))(),
                reads=[pt1_b], writes=[of.b[ci]])
    def ep_o(mi, ps, ps_b, t0, tn):
        em.emit_dma("sp", "xin", (lambda mi=mi, t0=t0, tn=tn: lambda e: e.dma_start(
            out=x.h[mi][:, 0:tn], in_=xs[mi*128:(mi+1)*128, t0:t0+tn]))(),
            writes=[x.b[mi]])
        em.close_batch("xin")
        em.emit("dve", (lambda mi=mi, ps=ps, t0=t0, tn=tn: lambda e: e.scalar_tensor_tensor(
            xout[:, 0:tn], ps[:, 0:tn], consts["bo"][0][:, mi:mi+1],
            x.h[mi][:, 0:tn], ALU.add, ALU.add))(),
            reads=[ps_b, consts["bo"][1], x.b[mi]], writes=[xout_b])
        out_toks.append(em.emit_dma("sp", "out", (lambda mi=mi, t0=t0, tn=tn: lambda e: e.dma_start(
            out=xo[mi*128:(mi+1)*128, t0:t0+tn], in_=xout[:, 0:tn]))(),
            reads=[xout_b]))
        em.close_batch("out")
    build_project(em, of, WoT, 1024, 1024, n, pp, wbufs, ep_o, mgroup_chunks=2)
    em.finalize(out_toks)
    return nc

def rope_tables_l0(pos):
    n = len(pos)
    inv = 1.0 / 10000.0 ** (np.arange(0, 64, 2, dtype=np.float64) / 64.0)
    f = pos[None, :].astype(np.float64) * inv[:, None]
    cos, sin = np.cos(f), np.sin(f)
    cosT = np.zeros((128, n), np.float32)
    sinT = np.zeros((128, n), np.float32)
    for base in (0, 64):
        cosT[base:base+32] = cos; cosT[base+32:base+64] = cos
        sinT[base:base+32] = -sin; sinT[base+32:base+64] = sin
    return cosT, sinT

def swap_perm():
    P = np.zeros((128, 128), np.float32)
    for base in (0, 64):
        for i in range(32):
            P[base + 32 + i, base + i] = 1.0
            P[base + i, base + 32 + i] = 1.0
    return P.astype(BF16NP)

def pack_bias(b, nch):
    return np.ascontiguousarray(np.asarray(b, np.float32).reshape(nch, 128).T)

def l0attn_inputs(xT_shard, pos, ins):
    cosT, sinT = rope_tables_l0(pos)
    return {
        "xs": np.ascontiguousarray(xT_shard, np.float32),
        "WqT": prep_w(ins["L0_attn_Wq"]), "WdT": prep_w(ins["L0_attn_Wdown"]),
        "WuT": prep_w(ins["L0_attn_Wup"]), "WoT": prep_w(ins["L0_attn_Wout"]),
        "bq": pack_bias(ins["L0_attn_bq"], 8), "bd": pack_bias(ins["L0_attn_bdown"], 2),
        "bu": pack_bias(ins["L0_attn_bup"], 16), "bo": pack_bias(ins["L0_attn_bout"], 8),
        "cosq": cosT.astype(BF16NP), "sinq": sinT.astype(BF16NP), "perm": swap_perm(),
        "ident": np.eye(128, dtype=BF16NP),
    }


def setup_misc(em, row_w=512):
    misc = {}
    for nm, shape, val in (("ones_col", [128, 1], 1.0), ("ones_row", [1, 128], 1.0),
                           ("eps", [1, 1], 1e-5)):
        t = em.sb(nm, shape, F32)
        b = Buf(nm)
        em.emit("dve", (lambda t=t, val=val: lambda e: e.memset(t[:, :], val))(), writes=[b])
        misc[nm] = (t, b)
    for nm, shape in (("sq", [128, 512]), ("r_s1", [1, row_w]), ("r_s2", [1, row_w]),
                      ("r_a", [1, row_w]), ("r_b", [1, row_w])):
        t = em.sb(nm, shape, F32)
        misc[nm] = (t, Buf(nm))
    return misc

def load_consts(em, nc, specs):
    out = {}
    for nm, shape, dt in specs:
        hh = nc.dram_tensor(nm, list(shape), dt, kind="ExternalInput").ap()
        t = em.sb("c_" + nm, list(shape), dt)
        b = Buf(nm)
        em.emit_dma("sp", "cst", (lambda t=t, hh=hh: lambda e: e.dma_start(out=t[:, :], in_=hh))(),
                    writes=[b])
        out[nm] = (t, b)
    return out

def load_x_tiled(em, x, xs_ap, n, width=512, slot="xin"):
    for ci in range(x.nch):
        em.emit_dma("sp", slot, (lambda ci=ci: lambda e: e.dma_start(
            out=x.h[ci][:, 0:n], in_=xs_ap[ci*128:(ci+1)*128, 0:n]))(), writes=[x.b[ci]])
    em.close_batch(slot)


def build_qkv3(ntok_core=PT):
    n = ntok_core
    nc = bass.Bass("TRN2", target_bir_lowering=False)
    xs = nc.dram_tensor("xs", [D, n], F32, kind="ExternalInput").ap()
    wcatT = nc.dram_tensor("wcatT", [1024, 3072], F32, kind="ExternalInput").ap()
    qo = nc.dram_tensor("qo", [D, n], BF16, kind="ExternalOutput").ap()
    ko = nc.dram_tensor("ko", [D, n], BF16, kind="ExternalOutput").ap()
    vo = nc.dram_tensor("vo", [D, n], BF16, kind="ExternalOutput").ap()
    em = Em(nc)
    x = Chunked(em, "x", NCH, 512, F32)
    xn = Chunked(em, "xn", NCH, n, BF16)
    ot = em.sb("ot", [128, 512], BF16)
    ot_b = Buf("ot")
    misc = setup_misc(em)
    wb1 = em.sb("wbufA", [128, 4096], BF16)
    wb2 = em.sb("wbufB", [128, 4096], BF16)
    wbufs = [(wb1, Buf("wbufA")), (wb2, Buf("wbufB"))]
    pp = PsumPool(em, n=8, width=512)
    out_toks = []
    for t0c in range(0, n, 512):
        for ci in range(NCH):
            em.emit_dma("sp", "xin", (lambda ci=ci, t0c=t0c: lambda e: e.dma_start(
                out=x.h[ci][:, :], in_=xs[ci*128:(ci+1)*128, t0c:t0c+512]))(), writes=[x.b[ci]])
        em.close_batch("xin")
        build_layernorm(em, x, xn, pp, misc, 512, out_col0=t0c)
    outs = [qo, ko, vo]
    def ep(mi, ps, ps_b, t0, tn):
        dst = outs[mi // 8]
        r0 = (mi % 8) * 128
        em.emit("act", (lambda ps=ps, tn=tn: lambda e: e.activation(
            ot[:, 0:tn], ps[:, 0:tn], AF.Identity))(),
            reads=[ps_b], writes=[ot_b])
        out_toks.append(em.emit_dma("sp", "out", (lambda dst=dst, r0=r0, t0=t0, tn=tn: lambda e: e.dma_start(
            out=dst[r0:r0+128, t0:t0+tn], in_=ot[:, 0:tn]))(), reads=[ot_b]))
        em.close_batch("out")
    build_project(em, xn, wcatT, 1024, 3072, n, pp, wbufs, ep, mgroup_chunks=4)
    em.finalize(out_toks)
    return nc

def qkv3_inputs(xT_shard, wcat):
    return {"xs": np.ascontiguousarray(xT_shard, np.float32),
            "wcatT": np.ascontiguousarray(wcat, np.float32)}


def build_f1attn(do_ln, ntok_core=PT, HALO=1024, debug=False, upto=None):
    n = ntok_core
    W = HALO + n
    nw = n // 512
    nc = bass.Bass("TRN2", target_bir_lowering=False)
    xs = nc.dram_tensor("xe", [D, W], F32, kind="ExternalInput").ap()
    wqkvT = nc.dram_tensor("wqkvT", [1024, 1536], F32, kind="ExternalInput").ap()
    woT = nc.dram_tensor("woT", [512, 1024], F32, kind="ExternalInput").ap()
    xo = nc.dram_tensor("xo", [D, n], F32, kind="ExternalOutput").ap()
    em = Em(nc)
    xe = Chunked(em, "xe", NCH, 512, F32)
    xn = Chunked(em, "xn", NCH, W + 64, BF16)
    qkv = Chunked(em, "qkv", 12, W, BF16)
    import types as _t
    of = _t.SimpleNamespace(h=qkv.h[8:12], b=qkv.b[8:12], nch=4, ntok=W)
    vaug2 = em.sb("vaug2", [128, 1300], BF16); vaug2_b = Buf("vaug2")
    consts = load_consts(em, nc, [
        ("qs", [128, 1], F32), ("ks", [128, 1], F32),
        ("cq", [128, W - 512], BF16), ("sq_", [128, W - 512], BF16),
        ("ck", [128, W - 512], BF16), ("sk", [128, W - 512], BF16),
        ("perm", [128, 128], BF16), ("ident", [128, 128], BF16),
        ("e2", [128, 2], F32), ("e2t", [2, 128], F32),
        ("mA", [128, 4096], BF16), ("mB", [128, 2048], BF16)])
    misc = setup_misc(em)
    em.close_batch("cst")
    rr = em.sb("rr", [2, 512], F32); rr_b = Buf("rr")
    vs = em.sb("vs", [64, 128], BF16); vs_b = Buf("vs")
    NKC = (W - 512) // 128
    vaug = em.sb("vaug", [128, NKC * 65], BF16); vaug_b = Buf("vaug")
    expS = em.sb("expS", [128, 2048], BF16); expS_b = Buf("expS")
    o_un = em.sb("o_un", [64, 512], BF16); o_un_b = Buf("o_un")
    r_row = em.sb("r_row", [1, 512], F32); r_row_b = Buf("r_row")
    xout = em.sb("xout", [128, 512], F32); xout_b = Buf("xout")
    scr = em.sb("scr", [128, 512], BF16)
    em.scr = (scr, Buf("scr"))
    wb1 = em.sb("wbufA", [128, 2048], BF16)
    wb2 = em.sb("wbufB", [128, 2048], BF16)
    wbufs = [(wb1, Buf("wbufA")), (wb2, Buf("wbufB"))]
    pp = PsumPool(em, n=2, width=512)
    ps_tb = em.ps("ps_tb", [128, 256], BF16); ps_tb_b = Buf("ps_tb")
    ps_s = em.ps("ps_s", [128, 2048]); ps_s_b = Buf("ps_s")
    ps_o = em.ps("ps_o", [128, 512]); ps_o_b = Buf("ps_o")
    out_toks = []
    for ci in range(NCH):
        em.emit("dve", (lambda ci=ci: lambda e: e.memset(xn.h[ci][:, 0:1], 0.0))(),
                writes=[xn.b[ci]])
    if do_ln:
        for t0c in range(0, W, 512):
            for ci in range(NCH):
                em.emit_dma("sp", "xin", (lambda ci=ci, t0c=t0c: lambda e: e.dma_start(
                    out=xe.h[ci][:, 0:512], in_=xs[ci*128:(ci+1)*128, t0c:t0c+512]))(),
                    writes=[xe.b[ci]])
            em.close_batch("xin")
            build_layernorm(em, xe, xn, pp, misc, 512, out_col0=t0c + 1, x_col0=0)
    else:
        for ci in range(NCH):
            em.emit_dma("pool", "xin", (lambda ci=ci: lambda e: e.dma_start(
                out=xn.h[ci][:, 1:1+W], in_=xs[ci*128:(ci+1)*128, :]))(),
                writes=[xn.b[ci]])
        em.close_batch("xin")
    if upto == "ln":
        dbg2 = nc.dram_tensor("dbg2", [2 * 128, W], BF16, kind="ExternalOutput").ap()
        toks = []
        for i, ci in enumerate((0, 4)):
            toks.append(em.emit_dma("sp", "dbg", (lambda i=i, ci=ci: lambda e: e.dma_start(
                out=dbg2[i*128:(i+1)*128, :], in_=xn.h[ci][:, 1:1+W]))(), reads=[xn.b[ci]]))
        em.close_batch("dbg")
        for s in list(em.cnt):
            if s.startswith("d_"):
                toks.append(Tok(s, em.cnt[s]))
        em.finalize(toks)
        return nc
    def ep_qkv(mi, ps, ps_b, t0, tn):
        em.emit("act", (lambda mi=mi, ps=ps, t0=t0, tn=tn: lambda e: e.activation(
            qkv.h[mi][:, t0:t0+tn], ps[:, 0:tn], AF.Identity))(),
            reads=[ps_b], writes=[qkv.b[mi]])
    build_project(em, xn, wqkvT, 1024, 1536, W, pp, wbufs, ep_qkv,
                  mgroup_chunks=2, xoff=[1, 1, 1, 1, 0, 0, 0, 0])
    e2, e2_b = consts["e2"]
    e2t, e2t_b = consts["e2t"]
    for ci in range(8):
        scale_nm = "qs" if ci < 4 else "ks"
        for t0_, tn in mm_tiles(W - 512):
            t0 = t0_ + 512
            ps_n, ps_n_b = pp.take1()
            em.emit("act", (lambda ci=ci, t0=t0, tn=tn: lambda e: e.square(
                misc["sq"][0][:, 0:tn], qkv.h[ci][:, t0:t0+tn]))(),
                reads=[qkv.b[ci]], writes=[misc["sq"][1]])
            em.emit("pe", (lambda t0=t0, tn=tn, ps_n=ps_n: lambda e: e.matmul(
                ps_n[0:2, 0:tn], e2[:, :], misc["sq"][0][:, 0:tn], start=True, stop=True))(),
                reads=[e2_b, misc["sq"][1]], writes=[ps_n_b])
            em.emit("dve", (lambda tn=tn, ps_n=ps_n: lambda e: e.tensor_scalar_max(
                rr[0:2, 0:tn], ps_n[0:2, 0:tn], 1e-24))(),
                reads=[ps_n_b], writes=[rr_b])
            em.emit("act", (lambda tn=tn: lambda e: e.activation(
                rr[0:2, 0:tn], rr[0:2, 0:tn], AF.Ln))(),
                reads=[rr_b], writes=[rr_b])
            em.emit("act", (lambda tn=tn: lambda e: e.activation(
                rr[0:2, 0:tn], rr[0:2, 0:tn], AF.Exp, scale=-0.5))(),
                reads=[rr_b], writes=[rr_b])
            ps_r, ps_r_b = pp.take1()
            em.emit("pe", (lambda tn=tn, ps_r=ps_r: lambda e: e.matmul(
                ps_r[:, 0:tn], e2t[:, :], rr[0:2, 0:tn], start=True, stop=True))(),
                reads=[e2t_b, rr_b], writes=[ps_r_b])
            em.emit("dve", (lambda ci=ci, t0=t0, tn=tn, ps_r=ps_r, scale_nm=scale_nm: lambda e: e.scalar_tensor_tensor(
                qkv.h[ci][:, t0:t0+tn], qkv.h[ci][:, t0:t0+tn], consts[scale_nm][0][:, 0:1],
                ps_r[:, 0:tn], ALU.mult, ALU.mult))(),
                reads=[qkv.b[ci], consts[scale_nm][1], ps_r_b], writes=[qkv.b[ci]])
    rope_apply(em, qkv.h[0:4], qkv.b[0:4], consts["cq"][0], consts["cq"][1],
               consts["sq_"][0], consts["sq_"][1], consts["perm"][0], consts["perm"][1],
               pp, W - 512, col0=512)
    rope_apply(em, qkv.h[4:8], qkv.b[4:8], consts["ck"][0], consts["ck"][1],
               consts["sk"][0], consts["sk"][1], consts["perm"][0], consts["perm"][1],
               pp, W - 512, col0=512)
    if debug:
        dbg = nc.dram_tensor("dbg", [3 * 128, W], BF16, kind="ExternalOutput").ap()
        for i, ci in enumerate((0, 4, 8)):
            out_toks.append(em.emit_dma("sp", "dbg", (lambda i=i, ci=ci: lambda e: e.dma_start(
                out=dbg[i*128:(i+1)*128, :], in_=qkv.h[ci][:, 0:W]))(), reads=[qkv.b[ci]]))
        dbg2 = nc.dram_tensor("dbg2", [2 * 128, W], BF16, kind="ExternalOutput").ap()
        for i, ci in enumerate((0, 4)):
            out_toks.append(em.emit_dma("sp", "dbg", (lambda i=i, ci=ci: lambda e: e.dma_start(
                out=dbg2[i*128:(i+1)*128, :], in_=xn.h[ci][:, 1:1+W]))(), reads=[xn.b[ci]]))
        em.close_batch("dbg")
    ident, ident_b = consts["ident"]
    def build_vaug(vdst, vdst_b, vch, base):
        em.emit("dve", (lambda vdst=vdst: lambda e: e.memset(vdst[:, :], 1.0))(), writes=[vdst_b])
        for kc in range(NKC):
            em.emit("dve", (lambda vch=vch, base=base, kc=kc: lambda e: e.tensor_copy(
                vs[0:64, 0:128], qkv.h[vch][base:base+64, 512+kc*128:512+(kc+1)*128]))(),
                reads=[qkv.b[vch]], writes=[vs_b])
            em.emit("pe", (lambda kc=kc: lambda e: e.transpose(
                ps_tb[:, 0:64], vs[0:64, 0:128], ident[0:64, 0:64]))(),
                reads=[vs_b, ident_b], writes=[ps_tb_b])
            em.emit("dve", (lambda vdst=vdst, kc=kc: lambda e: e.tensor_copy(
                vdst[:, kc*65:kc*65+64], ps_tb[:, 0:64]))(),
                reads=[ps_tb_b], writes=[vdst_b])
    for hp in range(4):
        vch = 8 + hp
        build_vaug(vaug, vaug_b, vch, 0)
        build_vaug(vaug2, vaug2_b, vch, 64)
        for h in (2*hp, 2*hp + 1):
            base = (h % 2) * 64
            va, va_b = (vaug, vaug_b) if h % 2 == 0 else (vaug2, vaug2_b)
            qch = h // 2
            kch = 4 + h // 2
            for w in range(nw):
                q0 = HALO + w * 512
                k0 = 512 + w * 512
                for g in range(2):
                    mk, mk_b = consts["mB"] if (w == 0 and g == 0) else consts["mA"]
                    def mk_s(g=g, base=base, qch=qch, kch=kch, q0=q0, k0=k0):
                        def f(e):
                            ins = None
                            for j in range(4):
                                kc = g * 4 + j
                                ins = e.matmul(ps_s[:, j*512:(j+1)*512],
                                               qkv.h[kch][base:base+64, k0+kc*128:k0+(kc+1)*128],
                                               qkv.h[qch][base:base+64, q0:q0+512],
                                               start=True, stop=True)
                            return ins
                        return f
                    em.emit("pe", mk_s(), reads=[qkv.b[kch], qkv.b[qch]], writes=[ps_s_b])
                    em.emit("act", (lambda g=g: lambda e: e.activation(
                        expS[:, 0:2048], ps_s[:, 0:2048], AF.Exp, scale=8.0))(),
                        reads=[ps_s_b], writes=[expS_b])
                    em.emit("dve", (lambda mk=mk, g=g, w=w: lambda e: e.tensor_mul(
                        expS[:, 0:2048], expS[:, 0:2048],
                        mk[:, 0:2048] if (w == 0 and g == 0) else mk[:, g*2048:(g+1)*2048]))(),
                        reads=[expS_b, mk_b], writes=[expS_b])
                    def mk_av(w=w, g=g, va=va):
                        def f(e):
                            ins = None
                            for j in range(4):
                                kc = g * 4 + j
                                blk = w * 4 + kc
                                ins = e.matmul(ps_o[0:65, 0:512],
                                               va[:, blk*65:blk*65+65],
                                               expS[:, j*512:(j+1)*512],
                                               start=(kc == 0), stop=(kc == 7))
                            return ins
                        return f
                    em.emit("pe", mk_av(), reads=[va_b, expS_b], writes=[ps_o_b])
                em.emit("dve", (lambda: lambda e: e.reciprocal(r_row[0:1, 0:512], ps_o[64:65, 0:512]))(),
                    reads=[ps_o_b], writes=[r_row_b])
                em.emit("dve", (lambda: lambda e: e.tensor_copy(o_un[0:64, 0:512], ps_o[0:64, 0:512]))(),
                    reads=[ps_o_b], writes=[o_un_b])
                ps_bc, ps_bc_b = pp.take1()
                em.emit("pe", (lambda ps_bc=ps_bc: lambda e: e.matmul(
                    ps_bc[0:64, 0:512], misc["ones_row"][0][0:1, 0:64], r_row[0:1, 0:512],
                    start=True, stop=True))(),
                    reads=[misc["ones_row"][1], r_row_b], writes=[ps_bc_b])
                em.emit("dve", (lambda hp=hp, w=w, base=base, ps_bc=ps_bc: lambda e: e.tensor_mul(
                    of.h[hp][base:base+64, w*512:(w+1)*512], o_un[0:64, 0:512], ps_bc[0:64, 0:512]))(),
                    reads=[o_un_b, ps_bc_b], writes=[of.b[hp]])
    if upto == "attn":
        dbg3 = nc.dram_tensor("dbg3", [4 * 128, n], BF16, kind="ExternalOutput").ap()
        toks = []
        for ci in range(4):
            toks.append(em.emit_dma("sp", "dbg", (lambda ci=ci: lambda e: e.dma_start(
                out=dbg3[ci*128:(ci+1)*128, :], in_=of.h[ci][:, 0:n]))(), reads=[of.b[ci]]))
        em.close_batch("dbg")
        for s in list(em.cnt):
            if s.startswith("d_"):
                toks.append(Tok(s, em.cnt[s]))
        em.finalize(toks)
        return nc
    def ep_o(mi, ps, ps_b, t0, tn):
        if do_ln:
            em.emit("dve", (lambda mi=mi, ps=ps, t0=t0, tn=tn: lambda e: e.tensor_add(
                xout[:, 0:tn], ps[:, 0:tn], xn.h[mi][:, HALO+1+t0:HALO+1+t0+tn]))(),
                reads=[ps_b, xn.b[mi]], writes=[xout_b])
        else:
            em.emit_dma("sp", "xres", (lambda mi=mi, t0=t0, tn=tn: lambda e: e.dma_start(
                out=xe.h[mi][:, 0:tn], in_=xs[mi*128:(mi+1)*128, HALO+t0:HALO+t0+tn]))(),
                writes=[xe.b[mi]])
            em.close_batch("xres")
            em.emit("dve", (lambda mi=mi, ps=ps, t0=t0, tn=tn: lambda e: e.tensor_add(
                xout[:, 0:tn], ps[:, 0:tn], xe.h[mi][:, 0:tn]))(),
                reads=[ps_b, xe.b[mi]], writes=[xout_b])
        out_toks.append(em.emit_dma("sp", "out", (lambda mi=mi, t0=t0, tn=tn: lambda e: e.dma_start(
            out=xo[mi*128:(mi+1)*128, t0:t0+tn], in_=xout[:, 0:tn]))(), reads=[xout_b]))
    build_project(em, of, woT, 512, 1024, n, pp, wbufs, ep_o, mgroup_chunks=2)
    em.finalize(out_toks)
    return nc


def build_f1ffn(final_add, ntok_core=PT):
    n = ntok_core
    nc = bass.Bass("TRN2", target_bir_lowering=False)
    xs = nc.dram_tensor("xe", [D, n + 1], F32, kind="ExternalInput").ap()
    w1gT = nc.dram_tensor("w1gT", [1024, 2816], F32, kind="ExternalInput").ap()
    w1aT = nc.dram_tensor("w1aT", [1024, 2816], F32, kind="ExternalInput").ap()
    w2T = nc.dram_tensor("w2T", [2816, 1024], F32, kind="ExternalInput").ap()
    x3s = nc.dram_tensor("x3s", [D, n], F32, kind="ExternalInput").ap() if final_add else None
    xo = nc.dram_tensor("xo", [D, n], F32, kind="ExternalOutput").ap()
    em = Em(nc)
    xe = Chunked(em, "xe", NCH, 513, F32)
    xn = Chunked(em, "xn", NCH, n, BF16)
    hg = Chunked(em, "hg", 22, 1024, BF16)
    hh = Chunked(em, "hh", 22, 1024, BF16)
    consts = load_consts(em, nc, [
        ("lg", [128, 8], F32), ("lb", [128, 8], F32),
        ("b1a", [128, 22], F32), ("b1g", [128, 22], F32), ("b2", [128, 8], F32)])
    misc = setup_misc(em)
    em.close_batch("cst")
    xout = em.sb("xout", [128, 512], F32); xout_b = Buf("xout")
    x3t = em.sb("x3t", [128, 512], F32); x3t_b = Buf("x3t")
    wb1 = em.sb("wbufA", [128, 5632], BF16)
    wb2 = em.sb("wbufB", [128, 5632], BF16)
    wbufs = [(wb1, Buf("wbufA")), (wb2, Buf("wbufB"))]
    pp = PsumPool(em, n=8, width=512)
    out_toks = []
    for t0c in range(0, n, 512):
        for ci in range(NCH):
            em.emit_dma("sp", "xin", (lambda ci=ci, t0c=t0c: lambda e: e.dma_start(
                out=xe.h[ci][:, 0:513], in_=xs[ci*128:(ci+1)*128, t0c:t0c+513]))(),
                writes=[xe.b[ci]])
        em.close_batch("xin")
        build_layernorm(em, xe, xn, pp, misc, 512, out_col0=t0c, x_col0=0,
                        xoff=[1, 1, 1, 1, 0, 0, 0, 0],
                        gain=consts["lg"], lbias=consts["lb"])
    for tt in range(0, n, 1024):
        def ep_g(mi, ps, ps_b, t0, tn):
            em.emit("act", (lambda mi=mi, ps=ps, t0=t0, tn=tn: lambda e: e.activation(
                hg.h[mi][:, t0:t0+tn], ps[:, 0:tn], AF.Gelu,
                bias=consts["b1g"][0][:, mi:mi+1]))(),
                reads=[ps_b, consts["b1g"][1]], writes=[hg.b[mi]])
        build_project(em, xn, w1gT, 1024, 2816, 1024, pp, wbufs, ep_g,
                      mgroup_chunks=4, xoff=[tt] * 8)
        def ep_a(mi, ps, ps_b, t0, tn):
            em.emit("dve", (lambda mi=mi, ps=ps, t0=t0, tn=tn: lambda e: e.scalar_tensor_tensor(
                hh.h[mi][:, t0:t0+tn], ps[:, 0:tn], consts["b1a"][0][:, mi:mi+1],
                hg.h[mi][:, t0:t0+tn], ALU.add, ALU.mult))(),
                reads=[ps_b, consts["b1a"][1], hg.b[mi]], writes=[hh.b[mi]])
        build_project(em, xn, w1aT, 1024, 2816, 1024, pp, wbufs, ep_a,
                      mgroup_chunks=4, xoff=[tt] * 8)
        def ep_o(mi, ps, ps_b, t0, tn, tt=tt):
            em.emit_dma("sp", "xres", (lambda mi=mi, t0=t0, tn=tn, tt=tt: lambda e: e.dma_start(
                out=xe.h[mi][:, 0:tn], in_=xs[mi*128:(mi+1)*128, 1+tt+t0:1+tt+t0+tn]))(),
                writes=[xe.b[mi]])
            em.close_batch("xres")
            em.emit("dve", (lambda mi=mi, ps=ps, t0=t0, tn=tn: lambda e: e.scalar_tensor_tensor(
                xout[:, 0:tn], ps[:, 0:tn], consts["b2"][0][:, mi:mi+1],
                xe.h[mi][:, 0:tn], ALU.add, ALU.add))(),
                reads=[ps_b, consts["b2"][1], xe.b[mi]], writes=[xout_b])
            if final_add:
                em.emit_dma("sp", "x3in", (lambda mi=mi, t0=t0, tn=tn, tt=tt: lambda e: e.dma_start(
                    out=x3t[:, 0:tn], in_=x3s[mi*128:(mi+1)*128, tt+t0:tt+t0+tn]))(),
                    writes=[x3t_b])
                em.close_batch("x3in")
                em.emit("dve", (lambda tn=tn: lambda e: e.tensor_add(
                    xout[:, 0:tn], xout[:, 0:tn], x3t[:, 0:tn]))(),
                    reads=[xout_b, x3t_b], writes=[xout_b])
            out_toks.append(em.emit_dma("sp", "out", (lambda mi=mi, t0=t0, tn=tn, tt=tt: lambda e: e.dma_start(
                out=xo[mi*128:(mi+1)*128, tt+t0:tt+t0+tn], in_=xout[:, 0:tn]))(),
                reads=[xout_b]))
            em.close_batch("out")
        build_project(em, hh, w2T, 2816, 1024, 1024, pp, wbufs, ep_o, mgroup_chunks=2)
    em.finalize(out_toks)
    return nc


def xpos_tables(pos):
    DH, WIN = 64, 512
    half = np.arange(0, DH, 2, dtype=np.float64)
    inv = 1.0 / 10000.0 ** (half / DH)
    t = pos.astype(np.float64)
    f = t[None, :] * inv[:, None]
    base = (half + 0.4 * DH) / (1.4 * DH)
    power = (t - 8192 // 2) / float(WIN // 2)
    scale = base[:, None] ** power[None, :]
    cos, sin = np.cos(f), np.sin(f)
    def pack(cv, sv):
        cT = np.zeros((128, len(pos)), np.float32)
        sT = np.zeros((128, len(pos)), np.float32)
        for b0 in (0, 64):
            cT[b0:b0+32] = cv; cT[b0+32:b0+64] = cv
            sT[b0:b0+32] = -sv; sT[b0+32:b0+64] = sv
        return cT.astype(BF16NP), sT.astype(BF16NP)
    cq, sq = pack(cos * scale, sin * scale)
    ck, sk = pack(cos / scale, sin / scale)
    return cq, sq, ck, sk

def f1_masks(batch_start):
    m = np.zeros((128, 8, 512), np.float32)
    for kc in range(8):
        ki = np.arange(128)
        j = kc * 128 + ki
        for qi in range(512):
            d = qi + 512 - j
            m[:, kc, qi] = ((d >= 0) & (d <= 512)).astype(np.float32)
    mA = m.reshape(128, 4096)
    mB = mA.copy()
    if batch_start:
        mB[:, 0:2048] = 0.0
    return mA.astype(BF16NP), mB.astype(BF16NP)

def f1attn_inputs(xe_shard, pos_qkv, wqkv, qs, ks, wo, batch_start):
    cq, sq, ck, sk = xpos_tables(pos_qkv[512:])
    mA, mB = f1_masks(batch_start)
    mB = mB[:, 0:2048].copy()
    return {
        "xe": np.ascontiguousarray(xe_shard, np.float32),
        "wqkvT": prep_w(wqkv), "woT": prep_w(wo),
        "qs": np.ascontiguousarray(np.tile(np.asarray(qs, np.float32), 2)[:, None]),
        "ks": np.ascontiguousarray(np.tile(np.asarray(ks, np.float32), 2)[:, None]),
        "cq": cq, "sq_": sq, "ck": ck, "sk": sk,
        "perm": swap_perm(), "ident": np.eye(128, dtype=BF16NP),
        "e2": np.ascontiguousarray(np.kron(np.eye(2), np.ones((64, 1))).astype(np.float32)),
        "e2t": np.ascontiguousarray(np.kron(np.eye(2), np.ones((1, 64))).astype(np.float32)),
        "mA": mA, "mB": mB,
    }

def f1ffn_inputs(xe_shard, lg, lb, w1, b1, w2, b2, x3_shard=None):
    HID = 2730
    w1 = np.asarray(w1, np.float32)
    w1a = np.zeros((1024, 2816), np.float32); w1a[:, :HID] = w1[:, :HID]
    w1g = np.zeros((1024, 2816), np.float32); w1g[:, :HID] = w1[:, HID:]
    b1 = np.asarray(b1, np.float32)
    b1a = np.zeros(2816, np.float32); b1a[:HID] = b1[:HID]
    b1g = np.zeros(2816, np.float32); b1g[:HID] = b1[HID:]
    w2p = np.zeros((2816, 1024), np.float32); w2p[:HID] = np.asarray(w2, np.float32)
    d = {
        "xe": np.ascontiguousarray(xe_shard, np.float32),
        "w1gT": w1g, "w1aT": w1a, "w2T": w2p,
        "lg": pack_bias(lg, 8), "lb": pack_bias(lb, 8),
        "b1a": pack_bias(b1a, 22), "b1g": pack_bias(b1g, 22),
        "b2": pack_bias(b2, 8),
    }
    if x3_shard is not None:
        d["x3s"] = np.ascontiguousarray(x3_shard, np.float32)
    return d


def build_l2a(ntok_core=PT, HALO=256):
    n = ntok_core
    W = HALO + n
    NKC = W // 128
    nc = bass.Bass("TRN2", target_bir_lowering=False)
    xs = nc.dram_tensor("xe", [D, W], F32, kind="ExternalInput").ap()
    WqT = nc.dram_tensor("WqT", [1024, 1024], F32, kind="ExternalInput").ap()
    WkT = nc.dram_tensor("WkT", [1024, 256], F32, kind="ExternalInput").ap()
    WvT = nc.dram_tensor("WvT", [1024, 256], F32, kind="ExternalInput").ap()
    WoT = nc.dram_tensor("WoT", [1024, 1024], F32, kind="ExternalInput").ap()
    xo = nc.dram_tensor("xo", [D, n], F32, kind="ExternalOutput").ap()
    em = Em(nc)
    xe = Chunked(em, "xe", NCH, 512, F32)
    xn = Chunked(em, "xn", NCH, W, BF16)
    q = Chunked(em, "q", NCH, n, BF16)
    kk = Chunked(em, "kk", 2, W, BF16)
    vv = Chunked(em, "vv", 2, W, BF16)
    of = xn
    of2 = Chunked(em, "of2", NCH, n, BF16)
    consts = load_consts(em, nc, [
        ("cq", [128, n], BF16), ("sq_", [128, n], BF16),
        ("ck", [128, W], BF16), ("sk", [128, W], BF16),
        ("perm", [128, 128], BF16), ("ident", [128, 128], BF16),
        ("e2", [128, 2], F32), ("e2t", [2, 128], F32),
        ("qg", [128, 8], F32), ("pm", [128, 16], F32),
        ("msk", [128, 16 * 384], BF16), ("eps6", [128, 1], F32)])
    misc = setup_misc(em)
    em.close_batch("cst")
    rr = em.sb("rr", [2, 512], F32); rr_b = Buf("rr")
    vs = em.sb("vs", [64, 128], BF16); vs_b = Buf("vs")
    krep = em.sb("krep", [128, W], BF16); krep_b = Buf("krep")
    vaug = em.sb("vaug", [128, NKC * 65], BF16); vaug_b = Buf("vaug")
    expS = em.sb("expS", [128, 384], BF16); expS_b = Buf("expS")
    o_un = em.sb("o_un", [64, 128], BF16); o_un_b = Buf("o_un")
    r_row = em.sb("r_row", [1, 128], F32); r_row_b = Buf("r_row")
    mixt = em.sb("mixt", [128, 512], F32); mixt_b = Buf("mixt")
    mixt2 = em.sb("mixt2", [128, 512], F32); mixt2_b = Buf("mixt2")
    xout = em.sb("xout", [128, 512], F32); xout_b = Buf("xout")
    scr = em.sb("scr", [128, 512], BF16)
    em.scr = (scr, Buf("scr"))
    wb1 = em.sb("wbufA", [128, 2048], BF16)
    wb2 = em.sb("wbufB", [128, 2048], BF16)
    wbufs = [(wb1, Buf("wbufA")), (wb2, Buf("wbufB"))]
    pp = PsumPool(em, n=2, width=512)
    ps_tb = em.ps("ps_tb", [128, 256], BF16); ps_tb_b = Buf("ps_tb")
    ps_s = em.ps("ps_s", [128, 512]); ps_s_b = Buf("ps_s")
    ps_o = em.ps("ps_o", [128, 128]); ps_o_b = Buf("ps_o")
    ps_bc = em.ps("ps_bc", [128, 128]); ps_bc_b = Buf("ps_bc")
    out_toks = []
    for t0c in range(0, W, 512):
        tw = min(512, W - t0c)
        for ci in range(NCH):
            em.emit_dma("sp", "xin", (lambda ci=ci, t0c=t0c, tw=tw: lambda e: e.dma_start(
                out=xe.h[ci][:, 0:tw], in_=xs[ci*128:(ci+1)*128, t0c:t0c+tw]))(),
                writes=[xe.b[ci]])
        em.close_batch("xin")
        build_layernorm(em, xe, xn, pp, misc, tw, out_col0=t0c, x_col0=0)
    def mk_ep(dst, col0=0):
        def ep(mi, ps, ps_b, t0, tn):
            em.emit("act", (lambda mi=mi, ps=ps, t0=t0, tn=tn: lambda e: e.activation(
                dst.h[mi][:, col0+t0:col0+t0+tn], ps[:, 0:tn], AF.Identity))(),
                reads=[ps_b], writes=[dst.b[mi]])
        return ep
    build_project(em, xn, WqT, 1024, 1024, n, pp, wbufs, mk_ep(q), mgroup_chunks=2,
                  xoff=[HALO] * 8)
    build_project(em, xn, WkT, 1024, 256, W, pp, wbufs, mk_ep(kk), mgroup_chunks=2)
    build_project(em, xn, WvT, 1024, 256, W, pp, wbufs, mk_ep(vv), mgroup_chunks=2)
    e2, e2_b = consts["e2"]
    e2t, e2t_b = consts["e2t"]
    def rmsn(ch, b, width):
        for t0, tn in mm_tiles(width):
            ps_n, ps_n_b = pp.take1()
            em.emit("act", (lambda ch=ch, t0=t0, tn=tn: lambda e: e.square(
                misc["sq"][0][:, 0:tn], ch[:, t0:t0+tn]))(),
                reads=[b], writes=[misc["sq"][1]])
            em.emit("pe", (lambda tn=tn, ps_n=ps_n: lambda e: e.matmul(
                ps_n[0:2, 0:tn], e2[:, :], misc["sq"][0][:, 0:tn], start=True, stop=True))(),
                reads=[e2_b, misc["sq"][1]], writes=[ps_n_b])
            em.emit("act", (lambda tn=tn, ps_n=ps_n: lambda e: e.activation(
                rr[0:2, 0:tn], ps_n[0:2, 0:tn], AF.Ln, scale=1.0/64.0,
                bias=consts["eps6"][0][0:2, 0:1]))(),
                reads=[ps_n_b, consts["eps6"][1]], writes=[rr_b])
            em.emit("act", (lambda tn=tn: lambda e: e.activation(
                rr[0:2, 0:tn], rr[0:2, 0:tn], AF.Exp, scale=-0.5))(),
                reads=[rr_b], writes=[rr_b])
            ps_r, ps_r_b = pp.take1()
            em.emit("pe", (lambda tn=tn, ps_r=ps_r: lambda e: e.matmul(
                ps_r[:, 0:tn], e2t[:, :], rr[0:2, 0:tn], start=True, stop=True))(),
                reads=[e2t_b, rr_b], writes=[ps_r_b])
            em.emit("dve", (lambda ch=ch, t0=t0, tn=tn, ps_r=ps_r: lambda e: e.tensor_mul(
                ch[:, t0:t0+tn], ch[:, t0:t0+tn], ps_r[:, 0:tn]))(),
                reads=[b, ps_r_b], writes=[b])
    for ci in range(8):
        rmsn(q.h[ci], q.b[ci], n)
    for ci in range(2):
        rmsn(kk.h[ci], kk.b[ci], W)
    rope_apply(em, q.h, q.b, consts["cq"][0], consts["cq"][1], consts["sq_"][0],
               consts["sq_"][1], consts["perm"][0], consts["perm"][1], pp, n)
    rope_apply(em, kk.h, kk.b, consts["ck"][0], consts["ck"][1], consts["sk"][0],
               consts["sk"][1], consts["perm"][0], consts["perm"][1], pp, W)
    for ci in range(8):
        em.emit("dve", (lambda ci=ci: lambda e: e.tensor_scalar_mul(
            q.h[ci][:, :], q.h[ci][:, :], consts["qg"][0][:, ci:ci+1]))(),
            reads=[q.b[ci], consts["qg"][1]], writes=[q.b[ci]])
    ident, ident_b = consts["ident"]
    for g in range(4):
        kch, kbase = g // 2, (g % 2) * 64
        em.emit("dve", (lambda kch=kch, kbase=kbase: lambda e: e.tensor_copy(
            krep[0:64, 0:W], kk.h[kch][kbase:kbase+64, 0:W]))(),
            reads=[kk.b[kch]], writes=[krep_b])
        em.emit("dve", (lambda kch=kch, kbase=kbase: lambda e: e.tensor_copy(
            krep[64:128, 0:W], kk.h[kch][kbase:kbase+64, 0:W]))(),
            reads=[kk.b[kch]], writes=[krep_b])
        em.emit("dve", (lambda: lambda e: e.memset(vaug[:, :], 1.0))(), writes=[vaug_b])
        for kc in range(NKC):
            em.emit("dve", (lambda kch=kch, kbase=kbase, kc=kc: lambda e: e.tensor_copy(
                vs[0:64, 0:128], vv.h[kch][kbase:kbase+64, kc*128:(kc+1)*128]))(),
                reads=[vv.b[kch]], writes=[vs_b])
            em.emit("pe", (lambda kc=kc: lambda e: e.transpose(
                ps_tb[:, 0:64], vs[0:64, 0:128], ident[0:64, 0:64]))(),
                reads=[vs_b, ident_b], writes=[ps_tb_b])
            em.emit("dve", (lambda kc=kc: lambda e: e.tensor_copy(
                vaug[:, kc*65:kc*65+64], ps_tb[:, 0:64]))(),
                reads=[ps_tb_b], writes=[vaug_b])
        for h in range(4 * g, 4 * g + 4):
            qch, qbase = h // 2, (h % 2) * 64
            for qt in range(n // 128):
                def mk_s(qt=qt, qch=qch, qbase=qbase):
                    def f(e):
                        ins = None
                        for j in range(3):
                            ins = e.matmul(ps_s[:, j*128:(j+1)*128],
                                           krep[qbase:qbase+64, (qt+j)*128:(qt+j+1)*128],
                                           q.h[qch][qbase:qbase+64, qt*128:(qt+1)*128],
                                           start=True, stop=True)
                        return ins
                    return f
                em.emit("pe", mk_s(), reads=[krep_b, q.b[qch]], writes=[ps_s_b])
                em.emit("act", (lambda: lambda e: e.activation(
                    expS[:, 0:384], ps_s[:, 0:384], AF.Exp, scale=0.125))(),
                    reads=[ps_s_b], writes=[expS_b])
                em.emit("dve", (lambda qt=qt: lambda e: e.tensor_mul(
                    expS[:, 0:384], expS[:, 0:384],
                    consts["msk"][0][:, qt*384:(qt+1)*384]))(),
                    reads=[expS_b, consts["msk"][1]], writes=[expS_b])
                def mk_av(qt=qt):
                    def f(e):
                        ins = None
                        for j in range(3):
                            ins = e.matmul(ps_o[0:65, 0:128],
                                           vaug[:, (qt+j)*65:(qt+j)*65+65],
                                           expS[:, j*128:(j+1)*128],
                                           start=(j == 0), stop=(j == 2))
                        return ins
                    return f
                em.emit("pe", mk_av(), reads=[vaug_b, expS_b], writes=[ps_o_b])
                em.emit("dve", (lambda: lambda e: e.reciprocal(
                    r_row[0:1, 0:128], ps_o[64:65, 0:128]))(),
                    reads=[ps_o_b], writes=[r_row_b])
                em.emit("dve", (lambda: lambda e: e.tensor_copy(
                    o_un[0:64, 0:128], ps_o[0:64, 0:128]))(),
                    reads=[ps_o_b], writes=[o_un_b])
                em.emit("pe", (lambda: lambda e: e.matmul(
                    ps_bc[0:64, 0:128], misc["ones_row"][0][0:1, 0:64], r_row[0:1, 0:128],
                    start=True, stop=True))(),
                    reads=[misc["ones_row"][1], r_row_b], writes=[ps_bc_b])
                em.emit("dve", (lambda qch=qch, qbase=qbase, qt=qt: lambda e: e.tensor_mul(
                    of.h[qch][qbase:qbase+64, qt*128:(qt+1)*128], o_un[0:64, 0:128],
                    ps_bc[0:64, 0:128]))(),
                    reads=[o_un_b, ps_bc_b], writes=[of.b[qch]])
    for p in range(8):
        for t0, tn in mm_tiles(n):
            em.emit("dve", (lambda p=p, t0=t0, tn=tn: lambda e: e.tensor_copy(
                mixt[0:64, 0:tn], of.h[p][64:128, t0:t0+tn]))(),
                reads=[of.b[p]], writes=[mixt_b])
            em.emit("dve", (lambda p=p, t0=t0, tn=tn: lambda e: e.tensor_copy(
                mixt[64:128, 0:tn], of.h[p][0:64, t0:t0+tn]))(),
                reads=[of.b[p]], writes=[mixt_b])
            em.emit("dve", (lambda p=p, t0=t0, tn=tn: lambda e: e.tensor_scalar_mul(
                mixt2[0:64, 0:tn], mixt[0:64, 0:tn], consts["pm"][0][0:64, 2*p:2*p+1]))(),
                reads=[mixt_b, consts["pm"][1]], writes=[mixt2_b])
            em.emit("dve", (lambda p=p, t0=t0, tn=tn: lambda e: e.tensor_scalar_mul(
                mixt2[64:128, 0:tn], mixt[64:128, 0:tn], consts["pm"][0][64:128, 2*p:2*p+1]))(),
                reads=[mixt_b, consts["pm"][1]], writes=[mixt2_b])
            em.emit("dve", (lambda p=p, t0=t0, tn=tn: lambda e: e.scalar_tensor_tensor(
                of2.h[p][0:64, t0:t0+tn], of.h[p][0:64, t0:t0+tn],
                consts["pm"][0][0:64, 2*p+1:2*p+2],
                mixt2[0:64, 0:tn], ALU.mult, ALU.add))(),
                reads=[of.b[p], consts["pm"][1], mixt2_b], writes=[of2.b[p]])
            em.emit("dve", (lambda p=p, t0=t0, tn=tn: lambda e: e.scalar_tensor_tensor(
                of2.h[p][64:128, t0:t0+tn], of.h[p][64:128, t0:t0+tn],
                consts["pm"][0][64:128, 2*p+1:2*p+2],
                mixt2[64:128, 0:tn], ALU.mult, ALU.add))(),
                reads=[of.b[p], consts["pm"][1], mixt2_b], writes=[of2.b[p]])
    def ep_o(mi, ps, ps_b, t0, tn):
        em.emit_dma("sp", "xres", (lambda mi=mi, t0=t0, tn=tn: lambda e: e.dma_start(
            out=xe.h[mi][:, 0:tn], in_=xs[mi*128:(mi+1)*128, HALO+t0:HALO+t0+tn]))(),
            writes=[xe.b[mi]])
        em.close_batch("xres")
        em.emit("dve", (lambda mi=mi, ps=ps, t0=t0, tn=tn: lambda e: e.tensor_add(
            xout[:, 0:tn], ps[:, 0:tn], xe.h[mi][:, 0:tn]))(),
            reads=[ps_b, xe.b[mi]], writes=[xout_b])
        out_toks.append(em.emit_dma("sp", "out", (lambda mi=mi, t0=t0, tn=tn: lambda e: e.dma_start(
            out=xo[mi*128:(mi+1)*128, t0:t0+tn], in_=xout[:, 0:tn]))(), reads=[xout_b]))
        em.close_batch("out")
    build_project(em, of2, WoT, 1024, 1024, n, pp, wbufs, ep_o, mgroup_chunks=2)
    em.finalize(out_toks)
    return nc

def rope_tables_l2(pos, width):
    HD = 64
    inv = 1.0 / 10000.0 ** (np.arange(0, HD, 2, dtype=np.float64) / HD)
    f = pos[None, :].astype(np.float64) * inv[:, None]
    cos, sin = np.cos(f), np.sin(f)
    cT = np.zeros((128, width), np.float32)
    sT = np.zeros((128, width), np.float32)
    for b0 in (0, 64):
        cT[b0:b0+32] = cos; cT[b0+32:b0+64] = cos
        sT[b0:b0+32] = sin; sT[b0+32:b0+64] = -sin
    return cT.astype(BF16NP), sT.astype(BF16NP)

def l2a_masks(pos_q, pos_k):
    n = len(pos_q)
    m = np.zeros((128, 16, 3, 128), np.float32)
    for qt in range(16):
        for j in range(3):
            kidx = (qt + j) * 128 + np.arange(128)
            kp = pos_k[kidx]
            qp = pos_q[qt*128:(qt+1)*128]
            dd = qp[None, :] - kp[:, None]
            keep = (dd >= 0) & (dd < 256) & (kp[:, None] >= 0)
            m[:, qt, j, :] = 0.0
            m[:, qt, j, :] = keep.astype(np.float32)[:, 0:128].T if False else keep.astype(np.float32)
    return np.ascontiguousarray(m.reshape(128, 16*384)).astype(BF16NP)

def l2a_inputs(xe_shard, pos_q, pos_k, ins):
    cq, sq = rope_tables_l2(pos_q, len(pos_q))
    ck, sk = rope_tables_l2(np.maximum(pos_k, 0), len(pos_k))
    qg = np.zeros((128, 8), np.float32)
    g = np.asarray(ins["L2_attn_q_gain"], np.float32)
    for ci in range(8):
        qg[0:64, ci] = g[2*ci]
        qg[64:128, ci] = g[2*ci+1]
    pm = np.zeros((128, 16), np.float32)
    pmx = np.asarray(ins["L2_attn_pair_mix"], np.float32)
    for p in range(8):
        pm[0:64, 2*p] = pmx[p, 0, 1]
        pm[0:64, 2*p+1] = pmx[p, 0, 0]
        pm[64:128, 2*p] = pmx[p, 1, 0]
        pm[64:128, 2*p+1] = pmx[p, 1, 1]
    return {
        "xe": np.ascontiguousarray(xe_shard, np.float32),
        "WqT": prep_wT(ins["L2_attn_Wq"]), "WkT": prep_wT(ins["L2_attn_Wk"]),
        "WvT": prep_wT(ins["L2_attn_Wv"]), "WoT": prep_wT(ins["L2_attn_Wo"]),
        "cq": cq, "sq_": sq, "ck": ck, "sk": sk,
        "perm": swap_perm(), "ident": np.eye(128, dtype=BF16NP),
        "e2": np.ascontiguousarray(np.kron(np.eye(2), np.ones((64, 1))).astype(np.float32)),
        "e2t": np.ascontiguousarray(np.kron(np.eye(2), np.ones((1, 64))).astype(np.float32)),
        "qg": qg, "pm": pm,
        "msk": l2a_masks(pos_q, pos_k),
        "eps6": np.full((128, 1), 1e-6, np.float32),
    }

def _mk64(em, name, k, n):
    import types as _t
    h = [em.sb(f"{name}{i}", [64, n], BF16) for i in range(k)]
    b = [Buf(f"{name}{i}") for i in range(k)]
    return _t.SimpleNamespace(h=h, b=b, nch=k, ntok=n)


def build_l2b1(ntok_core=PT):
    n = ntok_core
    nc = bass.Bass("TRN2", target_bir_lowering=False)
    xs = nc.dram_tensor("xs", [D, n], F32, kind="ExternalInput").ap()
    WQT = nc.dram_tensor("WQT", [1024, 1024], F32, kind="ExternalInput").ap()
    WKT = nc.dram_tensor("WKT", [1024, 1024], F32, kind="ExternalInput").ap()
    WiT = nc.dram_tensor("WiT", [1024, 1024], F32, kind="ExternalInput").ap()
    qo = nc.dram_tensor("qo", [D, n], BF16, kind="ExternalOutput").ap()
    ko = nc.dram_tensor("ko", [D, n], BF16, kind="ExternalOutput").ap()
    iqo = nc.dram_tensor("iqo", [D, n], BF16, kind="ExternalOutput").ap()
    iko = nc.dram_tensor("iko", [D, n], BF16, kind="ExternalOutput").ap()
    sso = nc.dram_tensor("sso", [D, 2], F32, kind="ExternalOutput").ap()
    em = Em(nc)
    x = Chunked(em, "x", NCH, 512, F32)
    xn = Chunked(em, "xn", NCH, n, BF16)
    Q = Chunked(em, "Q", NCH, n, BF16)
    K = Chunked(em, "K", NCH, n, BF16)
    iQ = Chunked(em, "iQ", NCH, n, BF16)
    ss = em.sb("ss", [128, 16], F32); ss_b = Buf("ss")
    misc = setup_misc(em)
    wb1 = em.sb("wbufA", [128, 4096], BF16)
    wb2 = em.sb("wbufB", [128, 4096], BF16)
    wbufs = [(wb1, Buf("wbufA")), (wb2, Buf("wbufB"))]
    pp = PsumPool(em, n=8, width=512)
    out_toks = []
    for t0c in range(0, n, 512):
        for ci in range(NCH):
            em.emit_dma("sp", "xin", (lambda ci=ci, t0c=t0c: lambda e: e.dma_start(
                out=x.h[ci][:, :], in_=xs[ci*128:(ci+1)*128, t0c:t0c+512]))(), writes=[x.b[ci]])
        em.close_batch("xin")
        build_layernorm(em, x, xn, pp, misc, 512, out_col0=t0c)
    def mk_ep(dst, out_h, scol):
        def ep(mi, ps, ps_b, t0, tn):
            em.emit("act", (lambda mi=mi, ps=ps, t0=t0, tn=tn: lambda e: e.activation(
                dst.h[mi][:, t0:t0+tn], ps[:, 0:tn], AF.Identity))(),
                reads=[ps_b], writes=[dst.b[mi]])
            out_toks.append(em.emit_dma("sp", "out", (lambda mi=mi, t0=t0, tn=tn: lambda e: e.dma_start(
                out=out_h[mi*128:(mi+1)*128, t0:t0+tn], in_=dst.h[mi][:, t0:t0+tn]))(),
                reads=[dst.b[mi]]))
            em.close_batch("out")
        return ep
    build_project(em, xn, WQT, 1024, 1024, n, pp, wbufs, mk_ep(Q, qo, 0), mgroup_chunks=4)
    build_project(em, xn, WKT, 1024, 1024, n, pp, wbufs, mk_ep(K, ko, 1), mgroup_chunks=4)
    build_project(em, Q, WiT, 1024, 1024, n, pp, wbufs, mk_ep(iQ, iqo, 2), mgroup_chunks=4)
    iK = Q
    build_project(em, K, WiT, 1024, 1024, n, pp, wbufs, mk_ep(iK, iko, 3), mgroup_chunks=4)
    for ci in range(NCH):
        em.emit("dve", (lambda ci=ci: lambda e: e.reduce_sum(
            ss[:, 0:1], K.h[ci][:, 0:n], axis=AX.X))(),
            reads=[K.b[ci]], writes=[ss_b])
        em.emit("dve", (lambda ci=ci: lambda e: e.reduce_sum(
            ss[:, 1:2], iK.h[ci][:, 0:n], axis=AX.X))(),
            reads=[iK.b[ci]], writes=[ss_b])
        out_toks.append(em.emit_dma("sp", "out", (lambda ci=ci: lambda e: e.dma_start(
            out=sso[ci*128:(ci+1)*128, 0:2], in_=ss[:, 0:2]))(), reads=[ss_b]))
        em.close_batch("out")
    em.finalize(out_toks)
    return nc

def build_l2b2(ntok_core=PT):
    n = ntok_core
    nc = bass.Bass("TRN2", target_bir_lowering=False)
    xs = nc.dram_tensor("xs", [D, n], F32, kind="ExternalInput").ap()
    Qh = nc.dram_tensor("Qh", [D, n], BF16, kind="ExternalInput").ap()
    Kh = nc.dram_tensor("Kh", [D, n], BF16, kind="ExternalInput").ap()
    iQh = nc.dram_tensor("iQh", [D, n], BF16, kind="ExternalInput").ap()
    iKh = nc.dram_tensor("iKh", [D, n], BF16, kind="ExternalInput").ap()
    pksh = nc.dram_tensor("pksh", [128, 64], F32, kind="ExternalInput").ap()
    vf = nc.dram_tensor("vf", [1024, 512], F32, kind="ExternalInput").ap()
    wf = nc.dram_tensor("wf", [1024, 512], F32, kind="ExternalInput").ap()
    vi = nc.dram_tensor("vi", [1024, 512], F32, kind="ExternalInput").ap()
    wi = nc.dram_tensor("wi", [1024, 512], F32, kind="ExternalInput").ap()
    uf = nc.dram_tensor("uf", [512, 1024], F32, kind="ExternalInput").ap()
    ui = nc.dram_tensor("ui", [512, 1024], F32, kind="ExternalInput").ap()
    rw1T = nc.dram_tensor("rw1T", [1024, 1024], F32, kind="ExternalInput").ap()
    rw2T = nc.dram_tensor("rw2T", [1024, 128], F32, kind="ExternalInput").ap()
    WOT = nc.dram_tensor("WOT", [1024, 1024], F32, kind="ExternalInput").ap()
    xo = nc.dram_tensor("xo", [D, n], F32, kind="ExternalOutput").ap()
    em = Em(nc)
    x = Chunked(em, "x", NCH, 512, F32)
    QQ = Chunked(em, "QQ", NCH, n, BF16)
    KK = Chunked(em, "KK", NCH, n, BF16)
    prodF = _mk64(em, "prodF", 8, n)
    prodI = _mk64(em, "prodI", 8, n)
    import types as _t
    hR = _t.SimpleNamespace(h=KK.h, b=KK.b, nch=8, ntok=n)
    outp = _t.SimpleNamespace(h=QQ.h, b=QQ.b, nch=8, ntok=n)
    consts = load_consts(em, nc, [
        ("rcol", [128, 1], F32), ("alpha", [128, 1], F32),
        ("rnv", [1, PT], BF16),
        ("bfp", [8, 1024], BF16), ("bip", [8, 1024], BF16),
        ("esel", [8, 512], BF16), ("ones8b", [8, 1], BF16),
    ])
    misc = setup_misc(em)
    em.close_batch("cst")
    yw = em.sb("yw", [64, PT], BF16); yw_b = Buf("yw")
    zz = em.sb("zz", [64, PT], BF16); zz_b = Buf("zz")
    cum = em.sb("cum", [64, PT], F32); cum_b = Buf("cum")
    carry = em.sb("carry", [64, 16], F32); carry_b = Buf("carry")
    scr64 = em.sb("scr64", [64, 512], BF16); scr64_b = Buf("scr64")
    wrow = em.sb("wrow", [128, 2 * PT], BF16); wrow_b = Buf("wrow")
    pksb = em.sb("pksb", [128, 64], BF16); pksb_b = Buf("pksb")
    xout = em.sb("xout", [128, 512], F32); xout_b = Buf("xout")
    wb1 = em.sb("wbufA", [128, 4096], BF16)
    wb2 = em.sb("wbufB", [128, 4096], BF16)
    wbufs = [(wb1, Buf("wbufA")), (wb2, Buf("wbufB"))]
    pp = PsumPool(em, n=6, width=512)
    out_toks = []
    em.emit_dma("pool", "cst2", (lambda: lambda e: e.dma_start(out=pksb[:, :], in_=pksh))(),
                writes=[pksb_b])
    em.close_batch("cst2")
    em.emit("dve", (lambda: lambda e: e.memset(zz[:, :], 0.0))(), writes=[zz_b])

    def load_chunked(dst, src_h, slot):
        for ci in range(NCH):
            em.emit_dma("pool", slot, (lambda ci=ci: lambda e: e.dma_start(
                out=dst.h[ci][:, :], in_=src_h[ci*128:(ci+1)*128, :]))(), writes=[dst.b[ci]])
        em.close_batch(slot)

    def router(srcH, off):
        load_chunked(QQ, srcH, "qin")
        def ep_h(mi, ps, ps_b, t0, tn):
            em.emit("act", (lambda mi=mi, ps=ps, t0=t0, tn=tn: lambda e: e.activation(
                hR.h[mi][:, t0:t0+tn], ps[:, 0:tn], AF.Gelu))(),
                reads=[ps_b], writes=[hR.b[mi]])
        build_project(em, QQ, rw1T, 1024, 1024, n, pp, wbufs, ep_h, mgroup_chunks=4)
        def ep_l(mi, ps, ps_b, t0, tn):
            em.emit("act", (lambda ps=ps, t0=t0, tn=tn: lambda e: e.activation(
                wrow[0:8, off+t0:off+t0+tn], ps[0:8, 0:tn], AF.Exp,
                bias=consts["rcol"][0][0:8, 0:1]))(),
                reads=[ps_b, consts["rcol"][1]], writes=[wrow_b])
        build_project(em, hR, rw2T, 1024, 128, n, pp, wbufs, ep_l, mgroup_chunks=1)
        for t0, tn in mm_tiles(n):
            ps_z, ps_z_b = pp.take1()
            em.emit("pe", (lambda t0=t0, tn=tn, ps_z=ps_z: lambda e: e.matmul(
                ps_z[0:1, 0:tn], consts["ones8b"][0][0:8, 0:1], wrow[0:8, off+t0:off+t0+tn],
                start=True, stop=True))(),
                reads=[consts["ones8b"][1], wrow_b], writes=[ps_z_b])
            em.emit("dve", (lambda tn=tn, ps_z=ps_z: lambda e: e.reciprocal(
                misc["r_a"][0][0:1, 0:tn], ps_z[0:1, 0:tn]))(),
                reads=[ps_z_b], writes=[misc["r_a"][1]])
            em.emit("dve", (lambda t0=t0, tn=tn: lambda e: e.tensor_mul(
                misc["r_b"][0][0:1, 0:tn], misc["r_a"][0][0:1, 0:tn],
                consts["rnv"][0][0:1, t0:t0+tn]))(),
                reads=[misc["r_a"][1], consts["rnv"][1]], writes=[misc["r_b"][1]])
            ps_bc, ps_bc_b = pp.take1()
            em.emit("pe", (lambda tn=tn, ps_bc=ps_bc: lambda e: e.matmul(
                ps_bc[0:8, 0:tn], misc["ones_row"][0][0:1, 0:8], misc["r_b"][0][0:1, 0:tn],
                start=True, stop=True))(),
                reads=[misc["ones_row"][1], misc["r_b"][1]], writes=[ps_bc_b])
            em.emit("dve", (lambda t0=t0, tn=tn, ps_bc=ps_bc: lambda e: e.tensor_mul(
                wrow[0:8, off+t0:off+t0+tn], wrow[0:8, off+t0:off+t0+tn], ps_bc[0:8, 0:tn]))(),
                reads=[wrow_b, ps_bc_b], writes=[wrow_b])
    router(Qh, 0)
    router(iQh, n)
    em.emit("dve", (lambda: lambda e: e.tensor_scalar_mul(
        wrow[0:8, n:2*n], wrow[0:8, n:2*n], consts["alpha"][0][0:8, 0:1]))(),
        reads=[wrow_b, consts["alpha"][1]], writes=[wrow_b])

    def load_slab(dst_wb, dst_b, src, ncols, krows=1024, slot="wslabA"):
        nj = krows // 128
        def f(e):
            sap = src[0:krows, 0:ncols].rearrange("(j p) m -> p j m", p=128)
            dap = dst_wb[:, 0:nj*ncols].rearrange("p (j m) -> p j m", m=ncols)
            return e.dma_start(out=dap, in_=sap)
        em.emit_dma("pool", slot, f, writes=[dst_b])
        em.close_batch(slot)

    def branch(QsrcH, KsrcH, vT, wT, prod, off, pcol):
        load_slab(wb2, wbufs[1][1], wT, 512, slot="wslabB")
        load_slab(wb1, wbufs[0][1], vT, 512, slot="wslabA")
        load_chunked(QQ, QsrcH, "qin")
        load_chunked(KK, KsrcH, "kin")
        for ex in range(8):
            pbase = (ex % 2) * 64
            for t0, tn in mm_tiles(n):
                ps_y, ps_y_b = pp.take1()
                def mk_y(ex=ex, t0=t0, tn=tn, ps_y=ps_y):
                    def f(e):
                        ins = None
                        for j in range(8):
                            ins = e.matmul(ps_y[0:64, 0:tn],
                                           wb2[:, j*512 + ex*64: j*512 + ex*64 + 64],
                                           KK.h[j][:, t0:t0+tn], start=(j == 0), stop=(j == 7))
                        return ins
                    return f
                em.emit("pe", mk_y(), reads=[wbufs[1][1]] + KK.b, writes=[ps_y_b])
                em.emit("act", (lambda t0=t0, tn=tn, ps_y=ps_y: lambda e: e.activation(
                    yw[0:64, t0:t0+tn], ps_y[0:64, 0:tn], AF.Identity))(),
                    reads=[ps_y_b], writes=[yw_b])
            ps_c, ps_c_b = pp.take1()
            def mk_c(ex=ex, ps_c=ps_c, pcol=pcol):
                def f(e):
                    ins = None
                    for j in range(32):
                        ins = e.matmul(ps_c[0:64, 0:1],
                                       wb2[:, (j % 8)*512 + ex*64: (j % 8)*512 + ex*64 + 64],
                                       pksb[:, j*2+pcol:j*2+pcol+1], start=(j == 0), stop=(j == 31))
                    return ins
                return f
            em.emit("pe", mk_c(), reads=[wbufs[1][1], pksb_b], writes=[ps_c_b])
            em.emit("dve", (lambda ex=ex, ps_c=ps_c: lambda e: e.tensor_copy(
                carry[0:64, ex+8*(0 if pcol == 0 else 1):ex+8*(0 if pcol == 0 else 1)+1],
                ps_c[0:64, 0:1]))(),
                reads=[ps_c_b], writes=[carry_b])
            ccol = ex + 8 * (0 if pcol == 0 else 1)
            em.emit("dve", (lambda ccol=ccol: lambda e: e.tensor_tensor_scan(
                cum[0:64, 0:n], yw[0:64, 0:n], zz[0:64, 0:n],
                carry[0:64, ccol:ccol+1], ALU.add, ALU.add))(),
                reads=[yw_b, carry_b, zz_b], writes=[cum_b])
            for t0, tn in mm_tiles(n):
                ps_x, ps_x_b = pp.take1()
                def mk_x(ex=ex, t0=t0, tn=tn, ps_x=ps_x):
                    def f(e):
                        ins = None
                        for j in range(8):
                            ins = e.matmul(ps_x[0:64, 0:tn],
                                           wb1[:, j*512 + ex*64: j*512 + ex*64 + 64],
                                           QQ.h[j][:, t0:t0+tn], start=(j == 0), stop=(j == 7))
                        return ins
                    return f
                em.emit("pe", mk_x(), reads=[wbufs[0][1]] + QQ.b, writes=[ps_x_b])
                em.emit("dve", (lambda t0=t0, tn=tn, ps_x=ps_x: lambda e: e.tensor_mul(
                    scr64[0:64, 0:tn], ps_x[0:64, 0:tn], cum[0:64, t0:t0+tn]))(),
                    reads=[ps_x_b, cum_b], writes=[scr64_b])
                ps_w, ps_w_b = pp.take1()
                em.emit("pe", (lambda ex=ex, t0=t0, tn=tn, ps_w=ps_w, off=off: lambda e: e.matmul(
                    ps_w[0:64, 0:tn], consts["esel"][0][0:8, ex*64:(ex+1)*64],
                    wrow[0:8, off+t0:off+t0+tn], start=True, stop=True))(),
                    reads=[consts["esel"][1], wrow_b], writes=[ps_w_b])
                em.emit("dve", (lambda ex=ex, t0=t0, tn=tn, ps_w=ps_w, prod=prod: lambda e: e.tensor_mul(
                    prod.h[ex][0:64, t0:t0+tn], scr64[0:64, 0:tn], ps_w[0:64, 0:tn]))(),
                    reads=[scr64_b, ps_w_b], writes=[prod.b[ex]])
    import os as _os
    _stop = _os.environ.get("L2STOP", "")
    if _stop == "r":
        toks = [Tok(s, em.cnt[s]) for s in list(em.cnt) if s.startswith("d_")]
        em.finalize(toks + [em.barrier_tok("dve"), em.barrier_tok("pe"), em.barrier_tok("act")])
        return nc
    branch(Qh, Kh, vf, wf, prodF, 0, 0)
    if _stop == "b1":
        toks = [Tok(s, em.cnt[s]) for s in list(em.cnt) if s.startswith("d_")]
        em.finalize(toks + [em.barrier_tok("dve"), em.barrier_tok("pe"), em.barrier_tok("act")])
        return nc
    branch(iQh, iKh, vi, wi, prodI, n, 1)
    if _stop == "b2":
        toks = [Tok(s, em.cnt[s]) for s in list(em.cnt) if s.startswith("d_")]
        em.finalize(toks + [em.barrier_tok("dve"), em.barrier_tok("pe"), em.barrier_tok("act")])
        return nc
    for mi in range(8):
        def mk_lu(mi=mi):
            def f(e):
                sap = uf[0:512, mi*128:(mi+1)*128].rearrange("(a p) m -> p a m", p=64)
                dap = wb1[0:64, 0:1024].rearrange("p (a m) -> p a m", m=128)
                return e.dma_start(out=dap, in_=sap)
            return f
        em.emit_dma("pool", "wslabA", mk_lu(), writes=[wbufs[0][1]])
        em.close_batch("wslabA")
        def mk_li(mi=mi):
            def f(e):
                sap = ui[0:512, mi*128:(mi+1)*128].rearrange("(a p) m -> p a m", p=64)
                dap = wb1[0:64, 1024:2048].rearrange("p (a m) -> p a m", m=128)
                return e.dma_start(out=dap, in_=sap)
            return f
        em.emit_dma("pool", "wslabA", mk_li(), writes=[wbufs[0][1]])
        em.close_batch("wslabA")
        for t0, tn in mm_tiles(n):
            ps, ps_b = pp.take1()
            def mk_u(mi=mi, t0=t0, tn=tn, ps=ps):
                def f(e):
                    ins = None
                    for br in range(2):
                        prod = prodF if br == 0 else prodI
                        for ex in range(8):
                            ins = e.matmul(ps[:, 0:tn],
                                           wb1[0:64, br*1024 + ex*128: br*1024 + (ex+1)*128],
                                           prod.h[ex][0:64, t0:t0+tn],
                                           start=(br == 0 and ex == 0),
                                           stop=(br == 1 and ex == 7))
                    return ins
                return f
            em.emit("pe", mk_u(), reads=[wbufs[0][1], wrow_b]
                    + prodF.b + prodI.b, writes=[ps_b])
            ps2, ps2_b = pp.take1()
            def mk_b(mi=mi, t0=t0, tn=tn, ps2=ps2):
                def f(e):
                    bcol = consts["bfp"][0][0:8, mi*128:(mi+1)*128]
                    e.matmul(ps2[:, 0:tn], bcol, wrow[0:8, t0:t0+tn], start=True, stop=False)
                    bcol2 = consts["bip"][0][0:8, mi*128:(mi+1)*128]
                    return e.matmul(ps2[:, 0:tn], bcol2, wrow[0:8, n+t0:n+t0+tn],
                                    start=False, stop=True)
                return f
            em.emit("pe", mk_b(), reads=[consts["bfp"][1], consts["bip"][1], wrow_b],
                    writes=[ps2_b])
            em.emit("act", (lambda mi=mi, ps=ps, t0=t0, tn=tn: lambda e: e.activation(
                outp.h[mi][:, t0:t0+tn], ps[:, 0:tn], AF.Identity))(),
                reads=[ps_b], writes=[outp.b[mi]])
            em.emit("dve", (lambda mi=mi, ps2=ps2, t0=t0, tn=tn: lambda e: e.tensor_add(
                outp.h[mi][:, t0:t0+tn], outp.h[mi][:, t0:t0+tn], ps2[:, 0:tn]))(),
                reads=[ps2_b, outp.b[mi]], writes=[outp.b[mi]])
    if _stop == "u":
        toks = [Tok(s, em.cnt[s]) for s in list(em.cnt) if s.startswith("d_")]
        em.finalize(toks + [em.barrier_tok("dve"), em.barrier_tok("pe"), em.barrier_tok("act")])
        return nc
    def ep_o(mi, ps, ps_b, t0, tn):
        em.emit_dma("sp", "xres", (lambda mi=mi, t0=t0, tn=tn: lambda e: e.dma_start(
            out=x.h[mi][:, 0:tn], in_=xs[mi*128:(mi+1)*128, t0:t0+tn]))(),
            writes=[x.b[mi]])
        em.close_batch("xres")
        em.emit("dve", (lambda mi=mi, ps=ps, t0=t0, tn=tn: lambda e: e.tensor_add(
            xout[:, 0:tn], ps[:, 0:tn], x.h[mi][:, 0:tn]))(),
            reads=[ps_b, x.b[mi]], writes=[xout_b])
        out_toks.append(em.emit_dma("sp", "out", (lambda mi=mi, t0=t0, tn=tn: lambda e: e.dma_start(
            out=xo[mi*128:(mi+1)*128, t0:t0+tn], in_=xout[:, 0:tn]))(), reads=[xout_b]))
        em.close_batch("out")
    build_project(em, outp, WOT, 1024, 1024, n, pp, wbufs, ep_o, mgroup_chunks=2)
    em.finalize(out_toks)
    return nc

def l2b1_inputs(x5T_shard, ins):
    return {"xs": np.ascontiguousarray(x5T_shard, np.float32),
            "WQT": prep_wT(ins["L2_ffn_W_Q"]), "WKT": prep_wT(ins["L2_ffn_W_K"]),
            "WiT": prep_wT(ins["L2_ffn_W_inv"])}

def l2b2_inputs(x5T_shard, Q, K, iQ, iK, pks_sel, gpos, ins):
    raise NotImplementedError("use l2b2_inputs_v2")

def l2b2_inputs_v2(x5T_shard, Q, K, iQ, iK, prev_sums, gpos, ins):
    pk = np.concatenate([np.asarray(p, np.float32) for p in prev_sums], axis=0)
    pksh = np.zeros((128, 64), np.float32)
    for j in range(32):
        pksh[:, j*2:(j+1)*2] = pk[j*128:(j+1)*128, :]
    al = float(np.asarray(ins["L2_ffn_alpha_bi"]))
    rcol = np.zeros((128, 1), np.float32)
    rcol[0:8, 0] = np.asarray(ins["L2_ffn_router_b2"], np.float32) + np.asarray(ins["L2_ffn_expert_bias"], np.float32)
    esel = np.zeros((8, 512), np.float32)
    for ex in range(8):
        esel[ex, ex*64:(ex+1)*64] = 1.0
    def vpack(a):
        a = np.asarray(a, np.float32)
        return np.ascontiguousarray(a.transpose(1, 0, 2).reshape(1024, 512))
    def upack(a):
        a = np.asarray(a, np.float32)
        return np.ascontiguousarray(a.transpose(0, 2, 1).reshape(512, 1024))
    rw2 = np.zeros((1024, 128), np.float32)
    rw2[:, 0:8] = prep_wT(ins["L2_ffn_router_w2"])
    return {
        "xs": np.ascontiguousarray(x5T_shard, np.float32),
        "Qh": Q, "Kh": K, "iQh": iQ, "iKh": iK,
        "pksh": pksh,
        "vf": vpack(ins["L2_ffn_V_fwd"]), "wf": vpack(ins["L2_ffn_W_fwd"]),
        "vi": vpack(ins["L2_ffn_W_inv_exp"]), "wi": vpack(ins["L2_ffn_V_inv"]),
        "uf": upack(ins["L2_ffn_U_fwd"]), "ui": upack(ins["L2_ffn_U_inv"]),
        "rw1T": prep_wT(ins["L2_ffn_router_w1"]), "rw2T": rw2,
        "WOT": prep_wT(ins["L2_ffn_W_O"]),
        "rcol": rcol, "alpha": np.full((128, 1), al, np.float32),
        "rnv": (1.0 / (np.asarray(gpos, np.float64) + 1.0)).astype(np.float32)[None, :].astype(BF16NP),
        "bfp": np.ascontiguousarray(np.asarray(ins["L2_ffn_b_fwd"], np.float32)).astype(BF16NP),
        "bip": np.ascontiguousarray(np.asarray(ins["L2_ffn_b_inv"], np.float32)).astype(BF16NP),
        "esel": esel.astype(BF16NP), "ones8b": np.ones((8, 1), BF16NP),
    }


def build_l3f(ntok_core=PT):
    n = ntok_core
    TT = 1024
    nc = bass.Bass("TRN2", target_bir_lowering=False)
    xs = nc.dram_tensor("xs", [D, n], F32, kind="ExternalInput").ap()
    rwT = nc.dram_tensor("rwT", [1024, 128], F32, kind="ExternalInput").ap()
    sw1c = nc.dram_tensor("sw1c", [1024, 4096], F32, kind="ExternalInput").ap()
    sw2c = nc.dram_tensor("sw2c", [4096, 1024], F32, kind="ExternalInput").ap()
    w1c = nc.dram_tensor("w1c", [1024, 16384], F32, kind="ExternalInput").ap()
    w2c = nc.dram_tensor("w2c", [16384, 1024], F32, kind="ExternalInput").ap()
    xo = nc.dram_tensor("xo", [D, n], F32, kind="ExternalOutput").ap()
    em = Em(nc)
    x = Chunked(em, "x", NCH, 512, F32)
    xn = Chunked(em, "xn", NCH, n, BF16)
    shc = Chunked(em, "shc", 32, TT, BF16)
    import types as _t
    he = _t.SimpleNamespace(h=shc.h[0:16], b=shc.b[0:16], nch=16, ntok=TT)
    outacc = Chunked(em, "outacc", NCH, TT, F32)
    consts = load_consts(em, nc, [
        ("rcol", [128, 1], F32),
        ("sb1c", [128, 32], F32), ("sb2a", [128, 8], F32), ("sb2b", [128, 8], F32),
        ("b1c", [128, 128], F32),
        ("b2p", [8, 1024], F32),
        ("esel", [8, 1024], F32),
        ("iota8", [128, 8], F32), ("idf", [128, 128], F32)])
    misc = setup_misc(em)
    em.close_batch("cst")
    halfb = em.sb("halfb", [128, 8], F32); halfb_b = Buf("halfb")
    lgt = em.sb("lgt", [8, TT], F32); lgt_b = Buf("lgt")
    lg_tm = em.sb("lg_tm", [128, 8], F32); lg_tm_b = Buf("lg_tm")
    m8 = em.sb("m8", [128, 8], F32); m8_b = Buf("m8")
    idx8 = em.sb("idx8", [128, 8], U32); idx8_b = Buf("idx8")
    idxf = em.sb("idxf", [128, 2], F32); idxf_b = Buf("idxf")
    pp12 = em.sb("pp12", [128, 4], F32); pp12_b = Buf("pp12")
    ctm = em.sb("ctm", [128, 8], F32); ctm_b = Buf("ctm")
    cscr = em.sb("cscr", [128, 8], F32); cscr_b = Buf("cscr")
    comb = em.sb("comb", [8, TT], F32); comb_b = Buf("comb")
    xout = em.sb("xout", [128, 512], F32); xout_b = Buf("xout")
    wb1 = em.sb("wbufA", [128, 4096], BF16)
    wb2 = em.sb("wbufB", [128, 4096], BF16)
    wbufs = [(wb1, Buf("wbufA")), (wb2, Buf("wbufB"))]
    pp = PsumPool(em, n=8, width=512)
    out_toks = []
    em.emit("dve", (lambda: lambda e: e.tensor_add(
        halfb[:, :], consts["sb2a"][0][:, :], consts["sb2b"][0][:, :]))(),
        reads=[consts["sb2a"][1], consts["sb2b"][1]], writes=[halfb_b])
    em.emit("dve", (lambda: lambda e: e.tensor_scalar_mul(halfb[:, :], halfb[:, :], 0.5))(),
        reads=[halfb_b], writes=[halfb_b])
    for t0c in range(0, n, 512):
        for ci in range(NCH):
            em.emit_dma("sp", "xin", (lambda ci=ci, t0c=t0c: lambda e: e.dma_start(
                out=x.h[ci][:, :], in_=xs[ci*128:(ci+1)*128, t0c:t0c+512]))(), writes=[x.b[ci]])
        em.close_batch("xin")
        build_layernorm(em, x, xn, pp, misc, 512, out_col0=t0c)
    for tt in range(0, n, TT):
        xoffs = [tt] * 8
        def ep_sh(mi, ps, ps_b, t0, tn):
            em.emit("act", (lambda mi=mi, ps=ps, t0=t0, tn=tn: lambda e: e.activation(
                shc.h[mi][:, t0:t0+tn], ps[:, 0:tn], AF.Gelu,
                bias=consts["sb1c"][0][:, mi:mi+1]))(),
                reads=[ps_b, consts["sb1c"][1]], writes=[shc.b[mi]])
        build_project(em, xn, sw1c, 1024, 4096, TT, pp, wbufs, ep_sh,
                      mgroup_chunks=4, xoff=xoffs)
        def ep_sh2(mi, ps, ps_b, t0, tn):
            em.emit("act", (lambda mi=mi, ps=ps, t0=t0, tn=tn: lambda e: e.activation(
                outacc.h[mi][:, t0:t0+tn], ps[:, 0:tn], AF.Identity,
                bias=halfb[:, mi:mi+1], scale=0.5))(),
                reads=[ps_b, halfb_b], writes=[outacc.b[mi]])
        build_project(em, shc, sw2c, 4096, 1024, TT, pp, wbufs, ep_sh2, mgroup_chunks=1)
        def ep_r(mi, ps, ps_b, t0, tn):
            em.emit("act", (lambda ps=ps, t0=t0, tn=tn: lambda e: e.activation(
                lgt[0:8, t0:t0+tn], ps[0:8, 0:tn], AF.Identity,
                bias=consts["rcol"][0][0:8, 0:1]))(),
                reads=[ps_b, consts["rcol"][1]], writes=[lgt_b])
        build_project(em, xn, rwT, 1024, 128, TT, pp, wbufs, ep_r,
                      mgroup_chunks=1, xoff=xoffs)
        for q in range(TT // 128):
            ps_t, ps_t_b = pp.take1()
            em.emit("pe", (lambda q=q, ps_t=ps_t: lambda e: e.transpose(
                ps_t[:, 0:8], lgt[0:8, q*128:(q+1)*128], consts["idf"][0][0:8, 0:8]))(),
                reads=[lgt_b, consts["idf"][1]], writes=[ps_t_b])
            em.emit("dve", (lambda ps_t=ps_t: lambda e: e.tensor_copy(
                lg_tm[:, 0:8], ps_t[:, 0:8]))(),
                reads=[ps_t_b], writes=[lg_tm_b])
            em.emit("dve", (lambda: lambda e: e.max(m8[:, 0:8], lg_tm[:, 0:8]))(),
                reads=[lg_tm_b], writes=[m8_b])
            em.emit("dve", (lambda: lambda e: e.max_index(idx8[:, 0:8], m8[:, 0:8], lg_tm[:, 0:8]))(),
                reads=[m8_b, lg_tm_b], writes=[idx8_b])
            em.emit("dve", (lambda: lambda e: e.tensor_copy(idxf[:, 0:2], idx8[:, 0:2]))(),
                reads=[idx8_b], writes=[idxf_b])
            em.emit("dve", (lambda: lambda e: e.tensor_sub(pp12[:, 0:1], m8[:, 1:2], m8[:, 0:1]))(),
                reads=[m8_b], writes=[pp12_b])
            em.emit("act", (lambda: lambda e: e.activation(pp12[:, 1:2], pp12[:, 0:1], AF.Exp))(),
                reads=[pp12_b], writes=[pp12_b])
            em.emit("dve", (lambda: lambda e: e.tensor_scalar_add(pp12[:, 2:3], pp12[:, 1:2], 1.0))(),
                reads=[pp12_b], writes=[pp12_b])
            em.emit("dve", (lambda: lambda e: e.reciprocal(pp12[:, 2:3], pp12[:, 2:3]))(),
                reads=[pp12_b], writes=[pp12_b])
            em.emit("dve", (lambda: lambda e: e.tensor_mul(pp12[:, 3:4], pp12[:, 1:2], pp12[:, 2:3]))(),
                reads=[pp12_b], writes=[pp12_b])
            em.emit("dve", (lambda: lambda e: e.tensor_tensor(
                ctm[:, 0:8], idxf[:, 0:1].broadcast_to([128, 8]), consts["iota8"][0][:, 0:8],
                ALU.is_equal))(),
                reads=[idxf_b, consts["iota8"][1]], writes=[ctm_b])
            em.emit("dve", (lambda: lambda e: e.tensor_scalar_mul(
                ctm[:, 0:8], ctm[:, 0:8], pp12[:, 2:3]))(),
                reads=[ctm_b, pp12_b], writes=[ctm_b])
            em.emit("dve", (lambda: lambda e: e.tensor_tensor(
                cscr[:, 0:8], idxf[:, 1:2].broadcast_to([128, 8]), consts["iota8"][0][:, 0:8],
                ALU.is_equal))(),
                reads=[idxf_b, consts["iota8"][1]], writes=[cscr_b])
            em.emit("dve", (lambda: lambda e: e.scalar_tensor_tensor(
                ctm[:, 0:8], cscr[:, 0:8], pp12[:, 3:4], ctm[:, 0:8], ALU.mult, ALU.add))(),
                reads=[cscr_b, pp12_b, ctm_b], writes=[ctm_b])
            ps_t2, ps_t2_b = pp.take1()
            em.emit("pe", (lambda q=q, ps_t2=ps_t2: lambda e: e.transpose(
                ps_t2[0:8, 0:128], ctm[:, 0:8], consts["idf"][0][:, :]))(),
                reads=[ctm_b, consts["idf"][1]], writes=[ps_t2_b])
            em.emit("dve", (lambda q=q, ps_t2=ps_t2: lambda e: e.tensor_copy(
                comb[0:8, q*128:(q+1)*128], ps_t2[0:8, 0:128]))(),
                reads=[ps_t2_b], writes=[comb_b])
        for ex in range(8):
            def ep_e(mi, ps, ps_b, t0, tn, ex=ex):
                em.emit("act", (lambda mi=mi, ps=ps, t0=t0, tn=tn, ex=ex: lambda e: e.activation(
                    he.h[mi][:, t0:t0+tn], ps[:, 0:tn], AF.Gelu,
                    bias=consts["b1c"][0][:, ex*16+mi:ex*16+mi+1]))(),
                    reads=[ps_b, consts["b1c"][1]], writes=[he.b[mi]])
            build_project(em, xn, w1c[0:1024, ex*2048:(ex+1)*2048], 1024, 2048, TT,
                          pp, wbufs, ep_e, mgroup_chunks=4, xoff=xoffs)
            for t0, tn in mm_tiles(TT):
                ps_cb, ps_cb_b = pp.take1()
                em.emit("pe", (lambda ex=ex, t0=t0, tn=tn, ps_cb=ps_cb: lambda e: e.matmul(
                    ps_cb[:, 0:tn], consts["esel"][0][0:8, ex*128:(ex+1)*128],
                    comb[0:8, t0:t0+tn], start=True, stop=True))(),
                    reads=[consts["esel"][1], comb_b], writes=[ps_cb_b])
                for mi in range(16):
                    em.emit("dve", (lambda mi=mi, t0=t0, tn=tn, ps_cb=ps_cb: lambda e: e.tensor_mul(
                        he.h[mi][:, t0:t0+tn], he.h[mi][:, t0:t0+tn], ps_cb[:, 0:tn]))(),
                        reads=[he.b[mi], ps_cb_b], writes=[he.b[mi]])
            def ep_w2(mi, ps, ps_b, t0, tn, ex=ex):
                em.emit("dve", (lambda mi=mi, ps=ps, t0=t0, tn=tn: lambda e: e.tensor_add(
                    outacc.h[mi][:, t0:t0+tn], outacc.h[mi][:, t0:t0+tn], ps[:, 0:tn]))(),
                    reads=[ps_b, outacc.b[mi]], writes=[outacc.b[mi]])
            build_project(em, he, w2c[ex*2048:(ex+1)*2048, 0:1024], 2048, 1024, TT,
                          pp, wbufs, ep_w2, mgroup_chunks=2)
        for t0, tn in mm_tiles(TT):
            for mi in range(NCH):
                ps_b2, ps_b2_b = pp.take1()
                em.emit("pe", (lambda mi=mi, t0=t0, tn=tn, ps_b2=ps_b2: lambda e: e.matmul(
                    ps_b2[:, 0:tn], consts["b2p"][0][0:8, mi*128:(mi+1)*128],
                    comb[0:8, t0:t0+tn], start=True, stop=True))(),
                    reads=[consts["b2p"][1], comb_b], writes=[ps_b2_b])
                em.emit_dma("sp", "xres", (lambda mi=mi, t0=t0, tn=tn, tt=tt: lambda e: e.dma_start(
                    out=x.h[mi][:, 0:tn], in_=xs[mi*128:(mi+1)*128, tt+t0:tt+t0+tn]))(),
                    writes=[x.b[mi]])
                em.close_batch("xres")
                em.emit("dve", (lambda mi=mi, t0=t0, tn=tn, ps_b2=ps_b2: lambda e: e.tensor_add(
                    xout[:, 0:tn], ps_b2[:, 0:tn], x.h[mi][:, 0:tn]))(),
                    reads=[ps_b2_b, x.b[mi]], writes=[xout_b])
                em.emit("dve", (lambda mi=mi, t0=t0, tn=tn: lambda e: e.tensor_add(
                    xout[:, 0:tn], xout[:, 0:tn], outacc.h[mi][:, t0:t0+tn]))(),
                    reads=[xout_b, outacc.b[mi]], writes=[xout_b])
                out_toks.append(em.emit_dma("sp", "out", (lambda mi=mi, t0=t0, tn=tn, tt=tt: lambda e: e.dma_start(
                    out=xo[mi*128:(mi+1)*128, tt+t0:tt+t0+tn], in_=xout[:, 0:tn]))(), reads=[xout_b]))
                em.close_batch("out")
    em.finalize(out_toks)
    return nc

def l3f_inputs(xT_shard, ins):
    rw = np.zeros((1024, 128), np.float32)
    rw[:, 0:8] = np.asarray(ins["L3_ffn_router_w"], np.float32)
    rcol = np.zeros((128, 1), np.float32)
    rcol[0:8, 0] = np.asarray(ins["L3_ffn_router_b"], np.float32)
    sw1 = np.asarray(ins["L3_ffn_sw1"], np.float32)
    sw1c = np.concatenate([sw1[0], sw1[1]], axis=1)
    sb1 = np.asarray(ins["L3_ffn_sb1"], np.float32)
    sb1c = np.concatenate([sb1[0], sb1[1]])
    sw2 = np.asarray(ins["L3_ffn_sw2"], np.float32)
    sw2c = np.concatenate([sw2[0], sw2[1]], axis=0)
    w1 = np.asarray(ins["L3_ffn_w1"], np.float32)
    w1c = np.concatenate([w1[e] for e in range(8)], axis=1)
    b1 = np.asarray(ins["L3_ffn_b1"], np.float32)
    b1c = np.zeros((128, 128), np.float32)
    for ex in range(8):
        b1c[:, ex*16:(ex+1)*16] = b1[ex].reshape(16, 128).T
    w2 = np.asarray(ins["L3_ffn_w2"], np.float32)
    w2c = np.concatenate([w2[e] for e in range(8)], axis=0)
    esel = np.zeros((8, 1024), np.float32)
    for ex in range(8):
        esel[ex, ex*128:(ex+1)*128] = 1.0
    iota8 = np.tile(np.arange(8, dtype=np.float32)[None, :], (128, 1))
    return {
        "xs": np.ascontiguousarray(xT_shard, np.float32),
        "rwT": rw, "sw1c": sw1c, "sw2c": sw2c, "w1c": np.ascontiguousarray(w1c),
        "w2c": np.ascontiguousarray(w2c),
        "rcol": rcol, "sb1c": pack_bias(sb1c, 32),
        "sb2a": pack_bias(np.asarray(ins["L3_ffn_sb2"], np.float32)[0], 8),
        "sb2b": pack_bias(np.asarray(ins["L3_ffn_sb2"], np.float32)[1], 8),
        "b1c": b1c, "b2p": np.ascontiguousarray(np.asarray(ins["L3_ffn_b2"], np.float32)),
        "esel": esel, "iota8": np.ascontiguousarray(iota8),
        "idf": np.eye(128, dtype=np.float32),
    }


def build_l1attn(ntok_core=PT):
    n = ntok_core
    NQB = n // 128
    nc = bass.Bass("TRN2", target_bir_lowering=False)
    x2s = nc.dram_tensor("x2s", [D, n], F32, kind="ExternalInput").ap()
    qh = nc.dram_tensor("qh", [D, n], BF16, kind="ExternalInput").ap()
    kf = nc.dram_tensor("kf", [D, T], BF16, kind="ExternalInput").ap()
    vaugh = nc.dram_tensor("vaugh", [16 * 128, 64 * 65], BF16, kind="ExternalInput").ap()
    projT = nc.dram_tensor("projT", [1024, 1024], F32, kind="ExternalInput").ap()
    xo = nc.dram_tensor("xo", [D, n], F32, kind="ExternalOutput").ap()
    em = Em(nc)
    x = Chunked(em, "x", NCH, 512, F32)
    q = Chunked(em, "q", NCH, n, BF16)
    of = q
    consts = load_consts(em, nc, [("b16", [16, n], BF16), ("projb", [128, 8], F32)])
    misc = setup_misc(em)
    em.close_batch("cst")
    qaug = em.sb("qaug", [128, n], BF16); qaug_b = Buf("qaug")
    kaug = [em.sb(f"kaug{i}", [128, T], BF16) for i in range(2)]
    kaug_b = [Buf("kaug0"), Buf("kaug1")]
    vab = [em.sb(f"vab{i}", [128, 64 * 65], BF16) for i in range(2)]
    vab_b = [Buf("vab0"), Buf("vab1")]
    qm = em.sb("qm", [128, 16], F32); qm_b = Buf("qm")
    km = em.sb("km", [128, 64], F32); km_b = Buf("km")
    sel = em.sb("sel", [16, 64], F32); sel_b = Buf("sel")
    selm8 = em.sb("selm8", [16, 8], F32); selm8_b = Buf("selm8")
    expS = em.sb("expS", [128, 2048], BF16); expS_b = Buf("expS")
    o_un = em.sb("o_un", [64, 512], BF16); o_un_b = Buf("o_un")
    r_row = em.sb("r_row", [1, 512], F32); r_row_b = Buf("r_row")
    xout = em.sb("xout", [128, 512], F32); xout_b = Buf("xout")
    wb1 = em.sb("wbufA", [128, 2048], BF16)
    wb2 = em.sb("wbufB", [128, 2048], BF16)
    wbufs = [(wb1, Buf("wbufA")), (wb2, Buf("wbufB"))]
    pp = PsumPool(em, n=2, width=512)
    ps_s = em.ps("ps_s", [128, 2048]); ps_s_b = Buf("ps_s")
    ps_o = em.ps("ps_o", [128, 512]); ps_o_b = Buf("ps_o")
    ps_m = em.ps("ps_m", [128, 64]); ps_m_b = Buf("ps_m")
    out_toks = []
    for ci in range(NCH):
        em.emit_dma("pool", "qin", (lambda ci=ci: lambda e: e.dma_start(
            out=q.h[ci][:, :], in_=qh[ci*128:(ci+1)*128, :]))(), writes=[q.b[ci]])
    em.close_batch("qin")
    for ci in range(NCH):
        em.emit("dve", (lambda ci=ci: lambda e: e.tensor_reduce(
            qm[:, 0:NQB], q.h[ci][:, :].rearrange("p (a b) -> p a b", a=NQB, b=128),
            axis=AX.X, op=ALU.add))(),
            reads=[q.b[ci]], writes=[qm_b]) if ci == 0 else None
    for h in range(16):
        hp, base = h // 2, (h % 2) * 64
        kb, kb_i = kaug[h % 2], kaug_b[h % 2]
        va, va_i = vab[h % 2], vab_b[h % 2]
        em.emit_dma("pool", f"kin{h % 2}", (lambda h=h, kb=kb: lambda e: e.dma_start(
            out=kb[0:64, :], in_=kf[h*64:(h+1)*64, :]))(), writes=[kb_i])
        em.close_batch(f"kin{h % 2}")
        em.emit_dma("pool", f"vin{h % 2}", (lambda h=h, va=va: lambda e: e.dma_start(
            out=va[:, :], in_=vaugh[h*128:(h+1)*128, :]))(), writes=[va_i])
        em.close_batch(f"vin{h % 2}")
        em.emit("dve", (lambda hp=hp, base=base: lambda e: e.tensor_reduce(
            qm[0:64, 0:NQB],
            q.h[hp][base:base+64, :].rearrange("p (a b) -> p a b", a=NQB, b=128),
            axis=AX.X, op=ALU.add))(),
            reads=[q.b[hp]], writes=[qm_b])
        em.emit("dve", (lambda kb=kb: lambda e: e.tensor_reduce(
            km[0:64, 0:64], kb[0:64, :].rearrange("p (a b) -> p a b", a=64, b=128),
            axis=AX.X, op=ALU.add))(),
            reads=[kb_i], writes=[km_b])
        em.emit("pe", (lambda: lambda e: e.matmul(
            ps_m[0:NQB, 0:64], qm[0:64, 0:NQB], km[0:64, 0:64], start=True, stop=True))(),
            reads=[qm_b, km_b], writes=[ps_m_b])
        em.emit("dve", (lambda: lambda e: e.tensor_copy(sel[0:NQB, 0:64], ps_m[0:NQB, 0:64]))(),
            reads=[ps_m_b], writes=[sel_b])
        for rnd in range(4):
            em.emit("dve", (lambda: lambda e: e.max(selm8[0:NQB, 0:8], sel[0:NQB, 0:64]))(),
                reads=[sel_b], writes=[selm8_b])
            em.emit("dve", (lambda: lambda e: e.match_replace(
                sel[0:NQB, 0:64], selm8[0:NQB, 0:8], sel[0:NQB, 0:64], -1e30))(),
                reads=[selm8_b, sel_b], writes=[sel_b])
        em.emit("dve", (lambda: lambda e: e.tensor_single_scalar(
            sel[0:NQB, 0:64], sel[0:NQB, 0:64], -1e30, ALU.is_equal))(),
            reads=[sel_b], writes=[sel_b])
        em.emit("dve", (lambda kb=kb: lambda e: e.tensor_scalar(
            kb[64:64+NQB, 0:T].rearrange("p (a b) -> p a b", a=64, b=128),
            sel[0:NQB, 0:64].rearrange("p (a o) -> p a o", o=1).broadcast_to([NQB, 64, 128]),
            1.0, 1e9, ALU.subtract, ALU.mult))(),
            reads=[sel_b], writes=[kb_i])
        em.emit("dve", (lambda hp=hp, base=base: lambda e: e.tensor_copy(
            qaug[0:64, 0:n], q.h[hp][base:base+64, :]))(),
            reads=[q.b[hp]], writes=[qaug_b])
        em.emit("dve", (lambda: lambda e: e.tensor_copy(
            qaug[64:80, 0:n], consts["b16"][0][0:16, :]))(),
            reads=[consts["b16"][1]], writes=[qaug_b])
        for qt in range(n // 512):
            for g in range(16):
                def mk_s(g=g, qt=qt, kb=kb):
                    def f(e):
                        ins = None
                        for j in range(4):
                            kc = g * 4 + j
                            ins = e.matmul(ps_s[:, j*512:(j+1)*512],
                                           kb[0:80, kc*128:(kc+1)*128],
                                           qaug[0:80, qt*512:(qt+1)*512],
                                           start=True, stop=True)
                        return ins
                    return f
                em.emit("pe", mk_s(), reads=[kb_i, qaug_b], writes=[ps_s_b])
                em.emit("act", (lambda: lambda e: e.activation(
                    expS[:, 0:2048], ps_s[:, 0:2048], AF.Exp, scale=0.125))(),
                    reads=[ps_s_b], writes=[expS_b])
                def mk_av(g=g, va=va):
                    def f(e):
                        ins = None
                        for j in range(4):
                            kc = g * 4 + j
                            ins = e.matmul(ps_o[0:65, 0:512],
                                           va[:, kc*65:kc*65+65],
                                           expS[:, j*512:(j+1)*512],
                                           start=(kc == 0), stop=(kc == 63))
                        return ins
                    return f
                em.emit("pe", mk_av(), reads=[va_i, expS_b], writes=[ps_o_b])
            em.emit("dve", (lambda: lambda e: e.reciprocal(r_row[0:1, 0:512], ps_o[64:65, 0:512]))(),
                reads=[ps_o_b], writes=[r_row_b])
            em.emit("dve", (lambda: lambda e: e.tensor_copy(o_un[0:64, 0:512], ps_o[0:64, 0:512]))(),
                reads=[ps_o_b], writes=[o_un_b])
            ps_bc, ps_bc_b = pp.take1()
            em.emit("pe", (lambda ps_bc=ps_bc: lambda e: e.matmul(
                ps_bc[0:64, 0:512], misc["ones_row"][0][0:1, 0:64], r_row[0:1, 0:512],
                start=True, stop=True))(),
                reads=[misc["ones_row"][1], r_row_b], writes=[ps_bc_b])
            em.emit("dve", (lambda hp=hp, base=base, qt=qt, ps_bc=ps_bc: lambda e: e.tensor_mul(
                of.h[hp][base:base+64, qt*512:(qt+1)*512], o_un[0:64, 0:512],
                ps_bc[0:64, 0:512]))(),
                reads=[o_un_b, ps_bc_b], writes=[of.b[hp]])
    import os as _os
    if _os.environ.get("L1DBG"):
        dbg = nc.dram_tensor("dbg", [1024, n], BF16, kind="ExternalOutput").ap()
        toks = []
        for ci in range(8):
            toks.append(em.emit_dma("sp", "dbgs", (lambda ci=ci: lambda e: e.dma_start(
                out=dbg[ci*128:(ci+1)*128, :], in_=of.h[ci][:, 0:n]))(), reads=[of.b[ci]]))
        em.close_batch("dbgs")
        for s in list(em.cnt):
            if s.startswith("d_"):
                toks.append(Tok(s, em.cnt[s]))
        em.finalize(toks)
        return nc
    def ep_o(mi, ps, ps_b, t0, tn):
        em.emit_dma("sp", "xres", (lambda mi=mi, t0=t0, tn=tn: lambda e: e.dma_start(
            out=x.h[mi][:, 0:tn], in_=x2s[mi*128:(mi+1)*128, t0:t0+tn]))(),
            writes=[x.b[mi]])
        em.close_batch("xres")
        em.emit("dve", (lambda mi=mi, ps=ps, t0=t0, tn=tn: lambda e: e.scalar_tensor_tensor(
            xout[:, 0:tn], ps[:, 0:tn], consts["projb"][0][:, mi:mi+1],
            x.h[mi][:, 0:tn], ALU.add, ALU.add))(),
            reads=[ps_b, consts["projb"][1], x.b[mi]], writes=[xout_b])
        out_toks.append(em.emit_dma("sp", "out", (lambda mi=mi, t0=t0, tn=tn: lambda e: e.dma_start(
            out=xo[mi*128:(mi+1)*128, t0:t0+tn], in_=xout[:, 0:tn]))(), reads=[xout_b]))
        em.close_batch("out")
    build_project(em, of, projT, 1024, 1024, n, pp, wbufs, ep_o, mgroup_chunks=2)
    em.finalize(out_toks)
    return nc

def l1attn_inputs(x2T_shard, q_shard, K_full, V_full, ins):
    vaug = np.ones((16 * 128, 64 * 65), BF16NP)
    Vf = np.asarray(V_full)
    for h in range(16):
        for kc in range(64):
            blk = Vf[h*64:(h+1)*64, kc*128:(kc+1)*128]
            vaug[h*128:(h+1)*128, kc*65:kc*65+64] = blk.T
    b16 = np.zeros((16, PT), np.float32)
    for qb in range(16):
        b16[qb, qb*128:(qb+1)*128] = 1.0
    return {
        "x2s": np.ascontiguousarray(x2T_shard, np.float32),
        "qh": np.ascontiguousarray(q_shard),
        "kf": np.ascontiguousarray(K_full),
        "vaugh": vaug, "b16": b16.astype(BF16NP),
        "projT": prep_wT(ins["L1_attn_proj_w"]),
        "projb": pack_bias(ins["L1_attn_proj_b"], 8),
    }


L3_BOUNDS = [16, 32, 48, 64]
L3_OFFS = [0, 16, 48, 96]
L3_TOTAL = 160

def build_l3attn(ntok_core=PT):
    n = ntok_core
    nc = bass.Bass("TRN2", target_bir_lowering=False)
    x6s = nc.dram_tensor("x6s", [D, n], F32, kind="ExternalInput").ap()
    qsl = nc.dram_tensor("qsl", [D, n], BF16, kind="ExternalInput").ap()
    ksl = nc.dram_tensor("ksl", [D, L3_TOTAL * 128], BF16, kind="ExternalInput").ap()
    vaugsl = nc.dram_tensor("vaugsl", [16 * 128, L3_TOTAL * 65], BF16, kind="ExternalInput").ap()
    WoT = nc.dram_tensor("WoT", [1024, 1024], F32, kind="ExternalInput").ap()
    xo = nc.dram_tensor("xo", [D, n], F32, kind="ExternalOutput").ap()
    em = Em(nc)
    x = Chunked(em, "x", NCH, 512, F32)
    q = Chunked(em, "q", NCH, n, BF16)
    of = q
    consts = load_consts(em, nc, [("dmask", [128, 4 * 512], BF16)])
    misc = setup_misc(em)
    em.close_batch("cst")
    kpair = [em.sb("kpair0", [128, L3_TOTAL * 128], BF16)] * 2
    _kb0 = Buf("kpair0")
    kpair_b = [_kb0, _kb0]
    vab = [em.sb(f"vab{i}", [128, L3_TOTAL * 65], BF16) for i in range(2)]
    vab_b = [Buf("vab0"), Buf("vab1")]
    expS = em.sb("expS", [128, 2048], BF16); expS_b = Buf("expS")
    tscr = em.sb("tscr", [128, 2048], F32); tscr_b = Buf("tscr")
    o_un = em.sb("o_un", [64, 512], BF16); o_un_b = Buf("o_un")
    r_row = em.sb("r_row", [1, 512], F32); r_row_b = Buf("r_row")
    xout = em.sb("xout", [128, 512], F32); xout_b = Buf("xout")
    wb1 = em.sb("wbufA", [128, 2048], BF16)
    wb2 = em.sb("wbufB", [128, 2048], BF16)
    wbufs = [(wb1, Buf("wbufA")), (wb2, Buf("wbufB"))]
    pp = PsumPool(em, n=2, width=512)
    ps_s = em.ps("ps_s", [128, 2048]); ps_s_b = Buf("ps_s")
    ps_o = em.ps("ps_o", [128, 512]); ps_o_b = Buf("ps_o")
    out_toks = []
    for ci in range(NCH):
        em.emit_dma("pool", "qin", (lambda ci=ci: lambda e: e.dma_start(
            out=q.h[ci][:, :], in_=qsl[ci*128:(ci+1)*128, :]))(), writes=[q.b[ci]])
    em.close_batch("qin")
    for h in range(16):
        hp, base = h // 2, (h % 2) * 64
        if h % 2 == 0:
            kb, kb_i = kpair[(h // 2) % 2], kpair_b[(h // 2) % 2]
            em.emit_dma("pool", f"kin{(h//2) % 2}", (lambda hp=hp, kb=kb: lambda e: e.dma_start(
                out=kb[:, :], in_=ksl[hp*128:(hp+1)*128, :]))(), writes=[kb_i])
            em.close_batch(f"kin{(h//2) % 2}")
        else:
            kb, kb_i = kpair[(h // 2) % 2], kpair_b[(h // 2) % 2]
        va, va_i = vab[h % 2], vab_b[h % 2]
        em.emit_dma("pool", f"vin{h % 2}", (lambda h=h, va=va: lambda e: e.dma_start(
            out=va[:, :], in_=vaugsl[h*128:(h+1)*128, :]))(), writes=[va_i])
        em.close_batch(f"vin{h % 2}")
        for s in range(4):
            nkc = L3_BOUNDS[s]
            off = L3_OFFS[s]
            q0 = s * 512
            for g in range(nkc // 4):
                is_diag = (g == nkc // 4 - 1)
                def mk_s(g=g, off=off, q0=q0, base=base, hp=hp, kb=kb):
                    def f(e):
                        ins = None
                        for j in range(4):
                            kc = off + g * 4 + j
                            ins = e.matmul(ps_s[:, j*512:(j+1)*512],
                                           kb[base:base+64, kc*128:(kc+1)*128],
                                           q.h[hp][base:base+64, q0:q0+512],
                                           start=True, stop=True)
                        return ins
                    return f
                em.emit("pe", mk_s(), reads=[kb_i, q.b[hp]], writes=[ps_s_b])
                em.emit("act", (lambda: lambda e: e.activation(
                    tscr[:, 0:2048], ps_s[:, 0:2048], AF.Tanh, scale=1.0/400.0))(),
                    reads=[ps_s_b], writes=[tscr_b])
                em.emit("act", (lambda: lambda e: e.activation(
                    expS[:, 0:2048], tscr[:, 0:2048], AF.Exp, scale=50.0))(),
                    reads=[tscr_b], writes=[expS_b])
                if is_diag:
                    em.emit("dve", (lambda: lambda e: e.tensor_mul(
                        expS[:, 0:2048], expS[:, 0:2048], consts["dmask"][0][:, 0:2048]))(),
                        reads=[expS_b, consts["dmask"][1]], writes=[expS_b])
                def mk_av(g=g, off=off, nkc=nkc, s=s, va=va):
                    def f(e):
                        ins = None
                        for j in range(4):
                            kc = off + g * 4 + j
                            first = (g == 0 and j == 0)
                            last = (g == nkc // 4 - 1 and j == 3)
                            ins = e.matmul(ps_o[0:65, 0:512],
                                           va[:, kc*65:kc*65+65],
                                           expS[:, j*512:(j+1)*512],
                                           start=first, stop=last)
                        return ins
                    return f
                em.emit("pe", mk_av(), reads=[va_i, expS_b], writes=[ps_o_b])
            em.emit("dve", (lambda: lambda e: e.reciprocal(r_row[0:1, 0:512], ps_o[64:65, 0:512]))(),
                reads=[ps_o_b], writes=[r_row_b])
            em.emit("dve", (lambda: lambda e: e.tensor_copy(o_un[0:64, 0:512], ps_o[0:64, 0:512]))(),
                reads=[ps_o_b], writes=[o_un_b])
            ps_bc, ps_bc_b = pp.take1()
            em.emit("pe", (lambda ps_bc=ps_bc: lambda e: e.matmul(
                ps_bc[0:64, 0:512], misc["ones_row"][0][0:1, 0:64], r_row[0:1, 0:512],
                start=True, stop=True))(),
                reads=[misc["ones_row"][1], r_row_b], writes=[ps_bc_b])
            em.emit("dve", (lambda hp=hp, base=base, q0=q0, ps_bc=ps_bc: lambda e: e.tensor_mul(
                of.h[hp][base:base+64, q0:q0+512], o_un[0:64, 0:512], ps_bc[0:64, 0:512]))(),
                reads=[o_un_b, ps_bc_b], writes=[of.b[hp]])
    def ep_o(mi, ps, ps_b, t0, tn):
        em.emit_dma("sp", "xres", (lambda mi=mi, t0=t0, tn=tn: lambda e: e.dma_start(
            out=x.h[mi][:, 0:tn], in_=x6s[mi*128:(mi+1)*128, t0:t0+tn]))(),
            writes=[x.b[mi]])
        em.close_batch("xres")
        em.emit("dve", (lambda mi=mi, ps=ps, t0=t0, tn=tn: lambda e: e.tensor_add(
            xout[:, 0:tn], ps[:, 0:tn], x.h[mi][:, 0:tn]))(),
            reads=[ps_b, x.b[mi]], writes=[xout_b])
        out_toks.append(em.emit_dma("sp", "out", (lambda mi=mi, t0=t0, tn=tn: lambda e: e.dma_start(
            out=xo[mi*128:(mi+1)*128, t0:t0+tn], in_=xout[:, 0:tn]))(), reads=[xout_b]))
        em.close_batch("out")
    build_project(em, of, WoT, 1024, 1024, n, pp, wbufs, ep_o, mgroup_chunks=2)
    em.finalize(out_toks)
    return nc

def l3_qts(core_mod4):
    return [4 * s + core_mod4 for s in range(4)]

def l3attn_inputs(x6T, qT, KT, VT, core, ins):
    cm = core % 4
    b0 = (core // 4) * T
    qts = l3_qts(cm)
    qs = np.concatenate([qT[:, b0 + qt*512 : b0 + (qt+1)*512] for qt in qts], axis=1)
    x6 = np.concatenate([x6T[:, b0 + qt*512 : b0 + (qt+1)*512] for qt in qts], axis=1)
    ksl = np.zeros((1024, L3_TOTAL * 128), BF16NP)
    vau = np.zeros((16 * 128, L3_TOTAL * 65), BF16NP)
    Vf = np.asarray(VT)
    for s in range(4):
        qt = qts[s]
        valid = 4 * (qt + 1)
        bound = L3_BOUNDS[s]
        off = L3_OFFS[s]
        order = list(range(valid - 4)) + [-1] * (bound - valid) + list(range(valid - 4, valid))
        for ii, kc in enumerate(order):
            col = (off + ii) * 128
            if kc < 0:
                continue
            ksl[:, col:col+128] = KT[:, kc*128:(kc+1)*128]
            for h in range(16):
                blk = Vf[h*64:(h+1)*64, kc*128:(kc+1)*128]
                vau[h*128:(h+1)*128, (off+ii)*65:(off+ii)*65+64] = blk.T
                vau[h*128:(h+1)*128, (off+ii)*65+64] = 1.0
    dm = np.zeros((128, 4, 512), np.float32)
    for j in range(4):
        ki = np.arange(128)
        for qq in range(512):
            dm[:, j, qq] = (j * 128 + ki <= (qq // 1) + 0) * 0
    for j in range(4):
        ki = np.arange(128)[:, None]
        qq = np.arange(512)[None, :]
        dm[:, j, :] = (j * 128 + ki <= qq).astype(np.float32)
    return {
        "x6s": np.ascontiguousarray(x6, np.float32),
        "qsl": np.ascontiguousarray(qs),
        "ksl": np.ascontiguousarray(ksl), "vaugsl": np.ascontiguousarray(vau),
        "WoT": np.ascontiguousarray(np.asarray(ins["L3_attn_Wo"], np.float32).T),
        "dmask": np.ascontiguousarray(dm.reshape(128, 2048)).astype(BF16NP),
    }


def _host_ext(stream_T, c, halo):
    g0 = c * PT
    b0 = (g0 // T) * T
    lo = g0 - halo
    pad = max(0, b0 - lo)
    seg = stream_T[:, max(lo, b0):g0 + PT]
    if pad:
        seg = np.concatenate([np.zeros((D, pad), np.float32), seg], axis=1)
    return np.ascontiguousarray(seg)

def _gather(res, name, dtype=np.float32):
    out = np.zeros((res.results[0][name].shape[0], NTOK), dtype)
    for c in range(NCORE):
        out[:, c*PT:(c+1)*PT] = res.results[c][name]
    return out

def kernel(**inputs):
    ins = inputs
    x0 = np.asarray(ins["L0_attn_x"], np.float32).reshape(NTOK, D)
    xT = np.ascontiguousarray(x0.T)
    cores = list(range(NCORE))
    def run(nc, maps):
        return run_bass_kernel_spmd(nc, maps, core_ids=cores)

    nc = build_l0attn()
    maps = []
    for c in cores:
        pos = (np.arange(c*PT, (c+1)*PT) % T)
        maps.append(l0attn_inputs(xT[:, c*PT:(c+1)*PT], pos, ins))
    res = run(nc, maps)
    x1 = _gather(res, "xo")
    nc = build_l0ffn()
    maps = [l0ffn_inputs(x1[:, c*PT:(c+1)*PT], ins) for c in cores]
    res = run(nc, maps)
    x2 = _gather(res, "xo")
    nc = build_qkv3()
    wcat = prep_wT(ins["L1_attn_qkv_w"])
    maps = [qkv3_inputs(x2[:, c*PT:(c+1)*PT], wcat) for c in cores]
    res = run(nc, maps)
    qT = _gather(res, "qo", BF16NP)
    kT = _gather(res, "ko", BF16NP)
    vT = _gather(res, "vo", BF16NP)
    nc = build_l1attn()
    maps = []
    for c in cores:
        b0 = (c // 4) * T
        maps.append(l1attn_inputs(x2[:, c*PT:(c+1)*PT], qT[:, c*PT:(c+1)*PT],
                                  kT[:, b0:b0+T], vT[:, b0:b0+T], ins))
    res = run(nc, maps)
    x3 = _gather(res, "xo")
    stream = x3
    for layer in range(2):
        wqkv = ins["L1_ffn_wqkv"][layer]; qs = ins["L1_ffn_q_scale"][layer]
        ks = ins["L1_ffn_k_scale"][layer]; wo = ins["L1_ffn_wo"][layer]
        nc = build_f1attn(layer == 0)
        maps = []
        for c in cores:
            g0 = c * PT; b0 = (g0 // T) * T
            posq = np.arange(g0 - 1024, g0 + PT) - b0
            maps.append(f1attn_inputs(_host_ext(stream, c, 1024), posq, wqkv, qs, ks, wo,
                                      batch_start=(c % 4 == 0)))
        res = run(nc, maps)
        xi = _gather(res, "xo")
        nc = build_f1ffn(layer == 1)
        maps = []
        for c in cores:
            maps.append(f1ffn_inputs(_host_ext(xi, c, 1), ins["L1_ffn_ln_g"][layer],
                                     ins["L1_ffn_ln_b"][layer], ins["L1_ffn_w1"][layer],
                                     ins["L1_ffn_b1"][layer], ins["L1_ffn_w2"][layer],
                                     ins["L1_ffn_b2"][layer],
                                     x3[:, c*PT:(c+1)*PT] if layer == 1 else None))
        res = run(nc, maps)
        stream = _gather(res, "xo")
    x4 = stream
    nc = build_l2a()
    maps = []
    for c in cores:
        g0 = c * PT; b0 = (g0 // T) * T
        pos_q = np.arange(g0, g0 + PT) - b0
        pos_k = np.arange(g0 - 256, g0 + PT) - b0
        maps.append(l2a_inputs(_host_ext(x4, c, 256), pos_q, pos_k, ins))
    res = run(nc, maps)
    x5 = _gather(res, "xo")
    nc = build_l2b1()
    maps = [l2b1_inputs(x5[:, c*PT:(c+1)*PT], ins) for c in cores]
    res = run(nc, maps)
    r1 = res.results
    nc = build_l2b2()
    maps = []
    for c in cores:
        prev = []
        for j in range(4):
            cc = (c // 4) * 4 + j
            prev.append(r1[cc]["sso"] if cc < c else np.zeros((1024, 2), np.float32))
        g0 = c * PT; b0 = (g0 // T) * T
        gpos = np.arange(g0, g0 + PT) - b0
        maps.append(l2b2_inputs_v2(x5[:, c*PT:(c+1)*PT], r1[c]["qo"], r1[c]["ko"],
                                   r1[c]["iqo"], r1[c]["iko"], prev, gpos, ins))
    res = run(nc, maps)
    x6 = _gather(res, "xo")
    nc = build_qkv3()
    wcat3 = np.concatenate([prep_wT(ins["L3_attn_Wq"]), prep_wT(ins["L3_attn_Wk"]),
                            prep_wT(ins["L3_attn_Wv"])], axis=1)
    maps = [qkv3_inputs(x6[:, c*PT:(c+1)*PT], np.ascontiguousarray(wcat3)) for c in cores]
    res = run(nc, maps)
    q3 = _gather(res, "qo", BF16NP)
    k3 = _gather(res, "ko", BF16NP)
    v3 = _gather(res, "vo", BF16NP)
    nc = build_l3attn()
    maps = []
    for c in cores:
        b0 = (c // 4) * T
        maps.append(l3attn_inputs(x6, q3, k3[:, b0:b0+T], v3[:, b0:b0+T], c, ins))
    res = run(nc, maps)
    x7 = np.zeros((D, NTOK), np.float32)
    for c in cores:
        qts = l3_qts(c % 4)
        b0 = (c // 4) * T
        o = res.results[c]["xo"]
        for s, qt in enumerate(qts):
            x7[:, b0 + qt*512 : b0 + (qt+1)*512] = o[:, s*512:(s+1)*512]
    nc = build_l3f()
    maps = [l3f_inputs(x7[:, c*PT:(c+1)*PT], ins) for c in cores]
    res = run(nc, maps)
    x8 = _gather(res, "xo")
    return np.ascontiguousarray(x8.T).reshape(B, T, D).astype(np.float32)


# version 05
# speedup vs baseline: 1.1874x; 1.1874x over previous
import contextlib
import numpy as np
import ml_dtypes
import concourse.bass as bass
import concourse.mybir as mybir
from concourse.bass_utils import run_bass_kernel_spmd

F32 = mybir.dt.float32
BF16 = mybir.dt.bfloat16
I32 = mybir.dt.int32
U32 = mybir.dt.uint32
AF = mybir.ActivationFunctionType
ALU = mybir.AluOpType
AX = mybir.AxisListType
BF16NP = ml_dtypes.bfloat16

NCORE = 8
B, T, D = 2, 8192, 1024
NTOK = B * T
PT = NTOK // NCORE
NCH = D // 128


class Tok:
    __slots__ = ("sem", "val")
    def __init__(self, sem, val):
        self.sem, self.val = sem, val

class Buf:
    __slots__ = ("name", "w", "readers")
    def __init__(self, name=""):
        self.name = name
        self.w = None
        self.readers = []

ENGS = ("pe", "dve", "act", "pool", "sp")

class Em:
    def __init__(self, nc):
        self.nc = nc
        self.es = contextlib.ExitStack()
        self.sems = {}
        self.cnt = {}
        for s in ("s_pe", "s_dve", "s_act", "s_pool", "s_sp"):
            self.sems[s] = self.es.enter_context(nc.semaphore(s))
            self.cnt[s] = 0
        self._batch = {}
        self.q = {e: [] for e in ENGS}
        self.waited = {e: {} for e in ENGS}
        self._sb_bytes = 0

    def sb(self, name, shape, dt):
        h = self.es.enter_context(self.nc.sbuf_tensor(name, list(shape), dt))
        self._sb_bytes += int(np.prod(shape[1:])) * mybir.dt.size(dt) * 0 + \
            int(np.prod(shape)) // shape[0] * mybir.dt.size(dt)
        return h

    def ps(self, name, shape, dt=F32):
        return self.es.enter_context(self.nc.psum_tensor(name, list(shape), dt))

    def sem(self, name):
        self.sems[name] = self.es.enter_context(self.nc.semaphore(name))
        self.cnt[name] = 0
        return self.sems[name]

    def _engine_sem(self, eng):
        return {"pe": "s_pe", "dve": "s_dve", "act": "s_act",
                "pool": "s_pool", "sp": "s_sp"}[eng]

    def dma_slot(self, slot):
        s = "d_" + slot
        if s not in self.sems:
            self.sems[s] = self.es.enter_context(self.nc.semaphore(s))
            self.cnt[s] = 0
            self._batch[s] = []
        return s

    def emit_dma(self, eng, slot, fn, reads=(), writes=(), deps=()):
        s = self.dma_slot(slot)
        tok = self._emit(eng, fn, reads, writes, deps, dma_sem=s)
        self._batch[s].extend(writes)
        self._batch[s].extend(("R", b) for b in reads)
        return tok

    def close_batch(self, slot):
        s = self.dma_slot(slot)
        fence = Tok(s, self.cnt[s])
        for item in self._batch[s]:
            if isinstance(item, tuple):
                b = item[1]
                for i, t in enumerate(b.readers):
                    if t.sem == s and t.val < fence.val:
                        b.readers[i] = fence
            else:
                item.w = fence
        self._batch[s] = []
        return fence

    def emit(self, eng, fn, reads=(), writes=(), deps=()):
        return self._emit(eng, fn, reads, writes, deps, None)

    def _emit(self, eng, fn, reads=(), writes=(), deps=(), dma_sem=None):
        want = []
        for b in reads:
            if b.w is not None:
                want.append(b.w)
        for b in writes:
            if b.w is not None:
                want.append(b.w)
            want.extend(b.readers)
        want.extend(deps)
        need = {}
        for t in want:
            if t is None:
                continue
            if need.get(t.sem, -1) < t.val:
                need[t.sem] = t.val
        if dma_sem is None:
            semn = self._engine_sem(eng)
            self.cnt[semn] += 1
            tok = Tok(semn, self.cnt[semn])
            inc = (self.sems[semn], 1)
        else:
            semn = dma_sem
            self.cnt[semn] += 16
            tok = Tok(semn, self.cnt[semn])
            inc = (self.sems[semn], 16)
        waits = []
        wtab = self.waited[eng]
        for s, v in sorted(need.items()):
            if wtab.get(s, 0) < v:
                waits.append((self.sems[s], v))
                wtab[s] = v
        self.q[eng].append((fn, waits, inc))
        for b in reads:
            b.readers.append(tok)
        for b in writes:
            b.w = tok
            b.readers = []
        return tok

    def barrier_tok(self, eng):
        semn = self._engine_sem(eng)
        return Tok(semn, self.cnt[semn])

    def finalize(self, final_deps):
        nc = self.nc
        need = {}
        for t in final_deps:
            if need.get(t.sem, -1) < t.val:
                need[t.sem] = t.val
        qmap = {"pe": "tensor", "dve": "vector", "act": "scalar",
                "pool": "gpsimd", "sp": "sync"}
        with nc.Block() as block:
            for eng in ENGS:
                items = self.q[eng]
                if not items and eng != "sp":
                    continue
                def mk(eng_items, is_sp):
                    def emit_all(engobj):
                        for fn, waits, inc in eng_items:
                            for sem, v in waits:
                                engobj.wait_ge(sem, v)
                            ins = fn(engobj)
                            assert ins is not None, "emit fn must return last instruction"
                            ins.then_inc(*inc)
                        if is_sp:
                            for s, v in sorted(need.items()):
                                engobj.wait_ge(self.sems[s], v)
                    return emit_all
                getattr(block, qmap[eng])(mk(items, eng == "sp"))
        self.es.close()


def ceil_div(a, b):
    return (a + b - 1) // b

def mm_tiles(n, step=512):
    return [(j, min(step, n - j)) for j in range(0, n, step)]

class Chunked:
    def __init__(self, em, name, nch, ntok, dt):
        self.h = [em.sb(f"{name}{i}", [128, ntok], dt) for i in range(nch)]
        self.b = [Buf(f"{name}{i}") for i in range(nch)]
        self.nch = nch
        self.ntok = ntok


def prep_wT(w):
    return np.ascontiguousarray(np.asarray(w).T.astype(np.float32))

def prep_w(w):
    return np.ascontiguousarray(np.asarray(w).astype(np.float32))


def build_layernorm(em, x, out, psum_pool, misc, ntok, eps=1e-5, inv_n=1.0/1024.0, out_col0=0,
                    xoff=None, gain=None, lbias=None, x_col0=0):
    nc = em.nc
    if xoff is None:
        xoff = [0] * x.nch
    onesc, onesc_b = misc["ones_col"]
    onesr, onesr_b = misc["ones_row"]
    sq, sq_b = misc["sq"]
    r_s1, r_s1_b = misc["r_s1"]
    r_s2, r_s2_b = misc["r_s2"]
    r_a, r_a_b = misc["r_a"]
    r_b, r_b_b = misc["r_b"]
    ps_s1, ps_s1_b = psum_pool.take1()
    ps_s2, ps_s2_b = psum_pool.take1()
    for t0, tn in mm_tiles(ntok):
        def mk_s1(t0=t0, tn=tn):
            def f(e):
                ins = None
                for ci in range(x.nch):
                    o = x_col0 + xoff[ci] + t0
                    ins = e.matmul(ps_s1[0:1, t0:t0+tn], onesc[:, :], x.h[ci][:, o:o+tn],
                                   start=(ci == 0), stop=(ci == x.nch - 1))
                return ins
            return f
        tmm = em.emit("pe", mk_s1(), reads=[onesc_b] + x.b, writes=[ps_s1_b])
        em.emit("act", (lambda t0=t0, tn=tn: lambda e: e.activation(
            r_s1[0:1, t0:t0+tn], ps_s1[0:1, t0:t0+tn], AF.Identity, scale=inv_n))(),
            reads=[ps_s1_b], writes=[r_s1_b])
        for ci in range(x.nch):
            em.emit("act", (lambda ci=ci, t0=t0, tn=tn: lambda e: e.square(
                sq[:, 0:tn], x.h[ci][:, x_col0+xoff[ci]+t0:x_col0+xoff[ci]+t0+tn]))(),
                reads=[x.b[ci]], writes=[sq_b])
            em.emit("pe", (lambda ci=ci, t0=t0, tn=tn: lambda e: e.matmul(
                ps_s2[0:1, t0:t0+tn], onesc[:, :], sq[:, 0:tn],
                start=(ci == 0), stop=(ci == x.nch - 1)))(),
                reads=[onesc_b, sq_b], writes=[ps_s2_b])
        em.emit("act", (lambda t0=t0, tn=tn: lambda e: e.activation(
            r_s2[0:1, t0:t0+tn], ps_s2[0:1, t0:t0+tn], AF.Identity, scale=inv_n))(),
            reads=[ps_s2_b], writes=[r_s2_b])
    em.emit("act", lambda e: e.square(r_a[0:1, 0:ntok], r_s1[0:1, 0:ntok]),
            reads=[r_s1_b], writes=[r_a_b])
    em.emit("dve", lambda e: e.tensor_sub(r_s2[0:1, 0:ntok], r_s2[0:1, 0:ntok], r_a[0:1, 0:ntok]),
            reads=[r_a_b, r_s2_b], writes=[r_s2_b])
    epst, epst_b = misc["eps"]
    em.emit("act", lambda e: e.activation(r_s2[0:1, 0:ntok], r_s2[0:1, 0:ntok], AF.Ln, bias=epst[0:1, 0:1]),
            reads=[r_s2_b, epst_b], writes=[r_s2_b])
    em.emit("act", lambda e: e.activation(r_a[0:1, 0:ntok], r_s2[0:1, 0:ntok], AF.Exp, scale=-0.5),
            reads=[r_s2_b], writes=[r_a_b])
    em.emit("dve", lambda e: e.tensor_mul(r_b[0:1, 0:ntok], r_s1[0:1, 0:ntok], r_a[0:1, 0:ntok]),
            reads=[r_s1_b, r_a_b], writes=[r_b_b])
    em.emit("dve", lambda e: e.tensor_scalar_mul(r_b[0:1, 0:ntok], r_b[0:1, 0:ntok], -1.0),
            reads=[r_b_b], writes=[r_b_b])
    for t0, tn in mm_tiles(ntok):
        ps_a, ps_a_b = psum_pool.take1()
        ps_b, ps_b_b = psum_pool.take1()
        em.emit("pe", (lambda t0=t0, tn=tn, ps_a=ps_a: lambda e: e.matmul(
            ps_a[:, 0:tn], onesr[:, :], r_a[0:1, t0:t0+tn], start=True, stop=True))(),
            reads=[onesr_b, r_a_b], writes=[ps_a_b])
        em.emit("pe", (lambda t0=t0, tn=tn, ps_b=ps_b: lambda e: e.matmul(
            ps_b[:, 0:tn], onesr[:, :], r_b[0:1, t0:t0+tn], start=True, stop=True))(),
            reads=[onesr_b, r_b_b], writes=[ps_b_b])
        for ci in range(x.nch):
            em.emit("dve", (lambda ci=ci, t0=t0, tn=tn, ps_a=ps_a: lambda e: e.tensor_mul(
                sq[:, 0:tn], x.h[ci][:, x_col0+xoff[ci]+t0:x_col0+xoff[ci]+t0+tn], ps_a[:, 0:tn]))(),
                reads=[x.b[ci], ps_a_b], writes=[sq_b])
            if gain is None:
                em.emit("dve", (lambda ci=ci, t0=t0, tn=tn, ps_b=ps_b: lambda e: e.tensor_add(
                    out.h[ci][:, out_col0+t0:out_col0+t0+tn], sq[:, 0:tn], ps_b[:, 0:tn]))(),
                    reads=[sq_b, ps_b_b], writes=[out.b[ci]])
            else:
                gt, gt_b = gain
                lt, lt_b = lbias
                em.emit("dve", (lambda ci=ci, t0=t0, tn=tn, ps_b=ps_b: lambda e: e.tensor_add(
                    sq[:, 0:tn], sq[:, 0:tn], ps_b[:, 0:tn]))(),
                    reads=[sq_b, ps_b_b], writes=[sq_b])
                em.emit("dve", (lambda ci=ci, t0=t0, tn=tn: lambda e: e.scalar_tensor_tensor(
                    out.h[ci][:, out_col0+t0:out_col0+t0+tn], sq[:, 0:tn], gt[:, ci:ci+1],
                    lt[:, ci:ci+1].broadcast_to([128, tn]), ALU.mult, ALU.add))(),
                    reads=[sq_b, gt_b, lt_b], writes=[out.b[ci]])


class PsumPool:
    def __init__(self, em, n=8, width=512, name="pp"):
        self.t = [em.ps(f"{name}{i}", [128, width]) for i in range(n)]
        self.b = [Buf(f"{name}{i}") for i in range(n)]
        self.i = 0
        self.n = n
    def take1(self):
        j = self.i % self.n
        self.i += 1
        return self.t[j], self.b[j]


def build_project(em, xin, w_hbm, K, M, ntok, psum_pool, wbufs, epilogue,
                  mgroup_chunks=4, max_j_per_dma=8, xoff=None):
    NJ = K // 128
    nmch = M // 128
    if xoff is None:
        xoff = [0] * NJ
    for g0 in range(0, nmch, mgroup_chunks):
        gch = min(mgroup_chunks, nmch - g0)
        mg_w = gch * 128
        wslot = "wA" if (g0 // mgroup_chunks) % 2 == 0 else "wB"
        wb, wb_b = wbufs[(g0 // mgroup_chunks) % 2]
        m0 = g0 * 128
        for j0 in range(0, NJ, max_j_per_dma):
            jn = min(max_j_per_dma, NJ - j0)
            def mkload(j0=j0, jn=jn, m0=m0, mg_w=mg_w, wb=wb):
                def f(e):
                    src = w_hbm[j0*128:(j0+jn)*128, m0:m0+mg_w].rearrange(
                        "(j p) m -> p j m", p=128)
                    dst = wb[:, j0*mg_w:(j0+jn)*mg_w].rearrange(
                        "p (j m) -> p j m", m=mg_w)
                    return e.dma_start(out=dst, in_=src)
                return f
            em.emit_dma("pool", wslot, mkload(), writes=[wb_b])
        em.close_batch(wslot)
        for i in range(gch):
            mi = g0 + i
            for t0, tn in mm_tiles(ntok):
                ps, ps_b = psum_pool.take1()
                def mkmm(i=i, t0=t0, tn=tn, mg_w=mg_w, wb=wb, ps=ps):
                    def f(e):
                        ins = None
                        for j in range(NJ):
                            o = xoff[j] + t0
                            ins = e.matmul(ps[:, 0:tn],
                                           wb[:, j*mg_w + i*128: j*mg_w + (i+1)*128],
                                           xin.h[j][:, o:o+tn],
                                           start=(j == 0), stop=(j == NJ - 1))
                        return ins
                    return f
                em.emit("pe", mkmm(), reads=[wb_b] + xin.b, writes=[ps_b])
                epilogue(mi, ps, ps_b, t0, tn)


def build_l0ffn(ntok_core=PT, tt=512):
    nc = bass.Bass("TRN2", target_bir_lowering=False)
    xs = nc.dram_tensor("xs", [D, ntok_core], F32, kind="ExternalInput").ap()
    w1T = nc.dram_tensor("w1T", [1024, 4096], F32, kind="ExternalInput").ap()
    w2T = nc.dram_tensor("w2T", [1024, 4096], F32, kind="ExternalInput").ap()
    h1T = nc.dram_tensor("h1T", [4096, 4096], F32, kind="ExternalInput").ap()
    h2T = nc.dram_tensor("h2T", [4096, 4096], F32, kind="ExternalInput").ap()
    w3T = nc.dram_tensor("w3T", [4096, 1024], F32, kind="ExternalInput").ap()
    s1h = nc.dram_tensor("s1", [128, 32], F32, kind="ExternalInput").ap()
    s2h = nc.dram_tensor("s2", [128, 32], F32, kind="ExternalInput").ap()
    sh1h = nc.dram_tensor("sh1", [128, 32], F32, kind="ExternalInput").ap()
    sh2h = nc.dram_tensor("sh2", [128, 32], F32, kind="ExternalInput").ap()
    s3h = nc.dram_tensor("s3", [128, 8], F32, kind="ExternalInput").ap()
    xo = nc.dram_tensor("xo", [D, ntok_core], F32, kind="ExternalOutput").ap()

    em = Em(nc)
    x = Chunked(em, "x", NCH, tt, F32)
    xn = Chunked(em, "xn", NCH, tt, BF16)
    hA = Chunked(em, "hA", 32, tt, BF16)
    hB = Chunked(em, "hB", 32, tt, BF16)
    xout = em.sb("xout", [128, tt], F32)
    xout_b = Buf("xout")
    scs = {}
    for nm, hh, w in (("s1", s1h, 32), ("s2", s2h, 32), ("sh1", sh1h, 32),
                      ("sh2", sh2h, 32), ("s3", s3h, 8)):
        t = em.sb("sb_" + nm, [128, w], F32)
        b = Buf(nm)
        em.emit_dma("sp", "cst", (lambda t=t, hh=hh: lambda e: e.dma_start(out=t[:, :], in_=hh))(),
                writes=[b])
        scs[nm] = (t, b)
    em.close_batch("cst")
    misc = {}
    for nm, shape, val in (("ones_col", [128, 1], 1.0), ("ones_row", [1, 128], 1.0),
                           ("eps", [1, 1], 1e-5)):
        t = em.sb(nm, shape, F32)
        b = Buf(nm)
        em.emit("dve", (lambda t=t, val=val: lambda e: e.memset(t[:, :], val))(), writes=[b])
        misc[nm] = (t, b)
    for nm, shape in (("sq", [128, 512]), ("r_s1", [1, tt]), ("r_s2", [1, tt]),
                      ("r_a", [1, tt]), ("r_b", [1, tt])):
        t = em.sb(nm, shape, F32)
        misc[nm] = (t, Buf(nm))
    wb1 = em.sb("wbufA", [128, 8192], BF16)
    wb2 = em.sb("wbufB", [128, 8192], BF16)
    wbufs = [(wb1, Buf("wbufA")), (wb2, Buf("wbufB"))]
    pp = PsumPool(em, n=8, width=512)
    out_toks = []

    for t0c in range(0, ntok_core, tt):
        for ci in range(NCH):
            em.emit_dma("sp", "xin", (lambda ci=ci, t0c=t0c: lambda e: e.dma_start(
                out=x.h[ci][:, :], in_=xs[ci*128:(ci+1)*128, t0c:t0c+tt]))(),
                writes=[x.b[ci]])
        em.close_batch("xin")
        build_layernorm(em, x, xn, pp, misc, tt)
        def ep_a(mi, ps, ps_b, t0, tn):
            em.emit("act", (lambda mi=mi, ps=ps, t0=t0, tn=tn: lambda e: e.activation(
                hA.h[mi][:, t0:t0+tn], ps[:, 0:tn], AF.Silu, scale=scs["s1"][0][:, mi:mi+1]))(),
                reads=[ps_b, scs["s1"][1]], writes=[hA.b[mi]])
        build_project(em, xn, w1T, 1024, 4096, tt, pp, wbufs, ep_a, mgroup_chunks=4)
        def ep_b(mi, ps, ps_b, t0, tn):
            em.emit("dve", (lambda mi=mi, ps=ps, t0=t0, tn=tn: lambda e: e.scalar_tensor_tensor(
                hA.h[mi][:, t0:t0+tn], ps[:, 0:tn], scs["s2"][0][:, mi:mi+1],
                hA.h[mi][:, t0:t0+tn], ALU.mult, ALU.mult))(),
                reads=[ps_b, scs["s2"][1], hA.b[mi]], writes=[hA.b[mi]])
        build_project(em, xn, w2T, 1024, 4096, tt, pp, wbufs, ep_b, mgroup_chunks=4)
        def ep_h1(mi, ps, ps_b, t0, tn):
            em.emit("act", (lambda mi=mi, ps=ps, t0=t0, tn=tn: lambda e: e.activation(
                hB.h[mi][:, t0:t0+tn], ps[:, 0:tn], AF.Gelu, scale=scs["sh1"][0][:, mi:mi+1]))(),
                reads=[ps_b, scs["sh1"][1]], writes=[hB.b[mi]])
        build_project(em, hA, h1T, 4096, 4096, tt, pp, wbufs, ep_h1, mgroup_chunks=2)
        def ep_h2(mi, ps, ps_b, t0, tn):
            em.emit("act", (lambda mi=mi, ps=ps, t0=t0, tn=tn: lambda e: e.activation(
                hA.h[mi][:, t0:t0+tn], ps[:, 0:tn], AF.Gelu, scale=scs["sh2"][0][:, mi:mi+1]))(),
                reads=[ps_b, scs["sh2"][1]], writes=[hA.b[mi]])
        build_project(em, hB, h2T, 4096, 4096, tt, pp, wbufs, ep_h2, mgroup_chunks=2)
        def ep_o(mi, ps, ps_b, t0, tn, t0c=t0c):
            em.emit("dve", (lambda mi=mi, ps=ps, t0=t0, tn=tn: lambda e: e.scalar_tensor_tensor(
                xout[:, t0:t0+tn], ps[:, 0:tn], scs["s3"][0][:, mi:mi+1],
                x.h[mi][:, t0:t0+tn], ALU.mult, ALU.add))(),
                reads=[ps_b, scs["s3"][1], x.b[mi]], writes=[xout_b])
            out_toks.append(em.emit_dma("sp", "out", (lambda mi=mi, t0=t0, tn=tn, t0c=t0c: lambda e: e.dma_start(
                out=xo[mi*128:(mi+1)*128, t0c+t0:t0c+t0+tn], in_=xout[:, t0:t0+tn]))(),
                reads=[xout_b]))
            em.close_batch("out")
        build_project(em, hA, w3T, 4096, 1024, tt, pp, wbufs, ep_o, mgroup_chunks=2)
    em.finalize(out_toks)
    return nc

def l0ffn_inputs(xT_shard, ins):
    d = {
        "xs": np.ascontiguousarray(xT_shard, np.float32),
        "w1T": prep_wT(ins["L0_ffn_w1_w"]), "w2T": prep_wT(ins["L0_ffn_w2_w"]),
        "h1T": prep_wT(ins["L0_ffn_h1_w"]), "h2T": prep_wT(ins["L0_ffn_h2_w"]),
        "w3T": prep_wT(ins["L0_ffn_w3_w"]),
    }
    for nm, key, w in (("s1", "L0_ffn_w1_s", 32), ("s2", "L0_ffn_w2_s", 32),
                       ("sh1", "L0_ffn_h1_s", 32), ("sh2", "L0_ffn_h2_s", 32),
                       ("s3", "L0_ffn_w3_s", 8)):
        v = np.asarray(ins[key], np.float32).reshape(w, 128).T
        d[nm] = np.ascontiguousarray(v)
    return d


def rope_apply(em, chunks_h, chunks_b, cosT, cosT_b, sinT, sinT_b, perm, perm_b,
               pp, ntok, strip=None, col0=0):
    p0, pn = (0, 128) if strip is None else strip
    for ci, (h, b) in enumerate(zip(chunks_h, chunks_b)):
        for t0, tn in mm_tiles(ntok):
            c0 = col0 + t0
            ps, ps_b = pp.take1()
            em.emit("pe", (lambda h=h, c0=c0, tn=tn, ps=ps: lambda e: e.matmul(
                ps[:, 0:tn], perm[:, :], h[:, c0:c0+tn], start=True, stop=True))(),
                reads=[perm_b, b], writes=[ps_b])
            sc, sc_b = em.scr
            em.emit("dve", (lambda t0=t0, tn=tn, ps=ps, sc=sc, p0=p0, pn=pn: lambda e: e.tensor_mul(
                sc[p0:pn, 0:tn], ps[p0:pn, 0:tn], sinT[p0:pn, t0:t0+tn]))(),
                reads=[ps_b, sinT_b], writes=[sc_b])
            em.emit("dve", (lambda h=h, c0=c0, t0=t0, tn=tn, sc=sc, p0=p0, pn=pn: lambda e: e.tensor_mul(
                h[p0:pn, c0:c0+tn], h[p0:pn, c0:c0+tn], cosT[p0:pn, t0:t0+tn]))(),
                reads=[b, cosT_b], writes=[b])
            em.emit("dve", (lambda h=h, c0=c0, tn=tn, sc=sc, p0=p0, pn=pn: lambda e: e.tensor_add(
                h[p0:pn, c0:c0+tn], h[p0:pn, c0:c0+tn], sc[p0:pn, 0:tn]))(),
                reads=[b, sc_b], writes=[b])

def pe_transpose(em, dst, dst_b, dst_col, src_ap_fn, src_bufs, ident, ident_b, pt, pt_b):
    em.emit("pe", (lambda: lambda e: e.transpose(pt[:, 0:128], src_ap_fn(), ident[:, :]))(),
            reads=list(src_bufs) + [ident_b], writes=[pt_b])
    em.emit("dve", (lambda dst=dst, dst_col=dst_col: lambda e: e.tensor_copy(
        dst[:, dst_col:dst_col+128], pt[:, 0:128]))(),
        reads=[pt_b], writes=[dst_b])


def build_l0attn(ntok_core=PT):
    n = ntok_core
    nc = bass.Bass("TRN2", target_bir_lowering=False)
    xs = nc.dram_tensor("xs", [D, n], F32, kind="ExternalInput").ap()
    WqT = nc.dram_tensor("WqT", [1024, 1024], F32, kind="ExternalInput").ap()
    WdT = nc.dram_tensor("WdT", [1024, 256], F32, kind="ExternalInput").ap()
    WuT = nc.dram_tensor("WuT", [256, 2048], F32, kind="ExternalInput").ap()
    WoT = nc.dram_tensor("WoT", [1024, 1024], F32, kind="ExternalInput").ap()
    bq = nc.dram_tensor("bq", [128, 8], F32, kind="ExternalInput").ap()
    bd = nc.dram_tensor("bd", [128, 2], F32, kind="ExternalInput").ap()
    bu = nc.dram_tensor("bu", [128, 16], F32, kind="ExternalInput").ap()
    bo = nc.dram_tensor("bo", [128, 8], F32, kind="ExternalInput").ap()
    cosq = nc.dram_tensor("cosq", [128, n], BF16, kind="ExternalInput").ap()
    sinq = nc.dram_tensor("sinq", [128, n], BF16, kind="ExternalInput").ap()
    permh = nc.dram_tensor("perm", [128, 128], BF16, kind="ExternalInput").ap()
    identh = nc.dram_tensor("ident", [128, 128], BF16, kind="ExternalInput").ap()
    xo = nc.dram_tensor("xo", [D, n], F32, kind="ExternalOutput").ap()

    em = Em(nc)
    x = Chunked(em, "x", NCH, 512, F32)
    xn = Chunked(em, "xn", NCH, n, BF16)
    q = Chunked(em, "q", NCH, n, BF16)
    kvl = Chunked(em, "kvl", 2, n, BF16)
    kv = Chunked(em, "kv", 16, n, BF16)
    of = xn
    q_tm = em.sb("q_tm", [128, 1024], BF16); q_tm_b = Buf("q_tm")
    kv_tm = em.sb("kv_tm", [128, 2048], BF16); kv_tm_b = Buf("kv_tm")
    p_scr = em.sb("p_scr", [128, 4096], BF16); p_scr_b = Buf("p_scr")
    s_red = em.sb("s_red", [128, 256], F32); s_red_b = Buf("s_red")
    e_tm = em.sb("e_tm", [128, 256], BF16); e_tm_b = Buf("e_tm")
    z_tm = em.sb("z_tm", [128, 16], F32); z_tm_b = Buf("z_tm")
    o_red = em.sb("o_red", [128, 1024], F32); o_red_b = Buf("o_red")
    o_tm = em.sb("o_tm", [128, 1024], BF16); o_tm_b = Buf("o_tm")
    xout = em.sb("xout", [128, 512], F32); xout_b = Buf("xout")
    scr = em.sb("scr", [128, 512], F32)
    em.scr = (scr, Buf("scr"))
    consts = {}
    for nm, hh, shape, dt in (("bq", bq, [128, 8], F32), ("bd", bd, [128, 2], F32),
                              ("bu", bu, [128, 16], F32), ("bo", bo, [128, 8], F32),
                              ("cosq", cosq, [128, n], BF16), ("sinq", sinq, [128, n], BF16),
                              ("perm", permh, [128, 128], BF16), ("ident", identh, [128, 128], BF16)):
        t = em.sb("c_" + nm, shape, dt)
        b = Buf(nm)
        em.emit_dma("sp", "cst", (lambda t=t, hh=hh: lambda e: e.dma_start(out=t[:, :], in_=hh))(),
                    writes=[b])
        consts[nm] = (t, b)
    misc = {}
    for nm, shape, val in (("ones_col", [128, 1], 1.0), ("ones_row", [1, 128], 1.0),
                           ("eps", [1, 1], 1e-5)):
        t = em.sb(nm, shape, F32)
        b = Buf(nm)
        em.emit("dve", (lambda t=t, val=val: lambda e: e.memset(t[:, :], val))(), writes=[b])
        misc[nm] = (t, b)
    for nm, shape in (("sq", [128, 512]), ("r_s1", [1, 512]), ("r_s2", [1, 512]),
                      ("r_a", [1, 512]), ("r_b", [1, 512])):
        t = em.sb(nm, shape, F32)
        misc[nm] = (t, Buf(nm))
    em.close_batch("cst")
    wb1 = em.sb("wbufA", [128, 2048], BF16)
    wb2 = em.sb("wbufB", [128, 2048], BF16)
    wbufs = [(wb1, Buf("wbufA")), (wb2, Buf("wbufB"))]
    pp = PsumPool(em, n=6, width=512)
    pt1 = em.ps("pt1", [128, 512], BF16)
    pt1_b = Buf("pt1")
    out_toks = []

    for t0c in range(0, n, 512):
        for ci in range(NCH):
            em.emit_dma("sp", "xin", (lambda ci=ci, t0c=t0c: lambda e: e.dma_start(
                out=x.h[ci][:, :], in_=xs[ci*128:(ci+1)*128, t0c:t0c+512]))(), writes=[x.b[ci]])
        em.close_batch("xin")
        build_layernorm(em, x, xn, pp, misc, 512, out_col0=t0c)
    def mk_ep(dst, bias_nm):
        def ep(mi, ps, ps_b, t0, tn):
            em.emit("act", (lambda mi=mi, ps=ps, t0=t0, tn=tn: lambda e: e.activation(
                dst.h[mi][:, t0:t0+tn], ps[:, 0:tn], AF.Identity,
                bias=consts[bias_nm][0][:, mi:mi+1]))(),
                reads=[ps_b, consts[bias_nm][1]], writes=[dst.b[mi]])
        return ep
    build_project(em, xn, WqT, 1024, 1024, n, pp, wbufs, mk_ep(q, "bq"), mgroup_chunks=2)
    build_project(em, xn, WdT, 1024, 256, n, pp, wbufs, mk_ep(kvl, "bd"), mgroup_chunks=2)
    build_project(em, kvl, WuT, 256, 2048, n, pp, wbufs, mk_ep(kv, "bu"), mgroup_chunks=2)
    rope_apply(em, q.h, q.b, consts["cosq"][0], consts["cosq"][1],
               consts["sinq"][0], consts["sinq"][1], consts["perm"][0], consts["perm"][1],
               pp, n)
    rope_apply(em, kv.h, kv.b, consts["cosq"][0], consts["cosq"][1],
               consts["sinq"][0], consts["sinq"][1], consts["perm"][0], consts["perm"][1],
               pp, n, strip=(0, 64))
    ident, ident_b = consts["ident"]
    for tt in range(n // 128):
        c0 = tt * 128
        for ci in range(8):
            pe_transpose(em, q_tm, q_tm_b, ci*128,
                         (lambda ci=ci, c0=c0: q.h[ci][:, c0:c0+128])(ci, c0) if False else
                         (lambda ci=ci, c0=c0: lambda: q.h[ci][:, c0:c0+128])()(),
                         [q.b[ci]], ident, ident_b, pt1, pt1_b) if False else None
        for ci in range(8):
            em.emit("pe", (lambda ci=ci, c0=c0: lambda e: e.transpose(
                pt1[:, 0:128], q.h[ci][:, c0:c0+128], ident[:, :]))(),
                reads=[q.b[ci], ident_b], writes=[pt1_b])
            em.emit("dve", (lambda ci=ci: lambda e: e.tensor_copy(
                q_tm[:, ci*128:(ci+1)*128], pt1[:, 0:128]))(),
                reads=[pt1_b], writes=[q_tm_b])
        for ci in range(16):
            em.emit("pe", (lambda ci=ci, c0=c0: lambda e: e.transpose(
                pt1[:, 0:128], kv.h[ci][:, c0:c0+128], ident[:, :]))(),
                reads=[kv.b[ci], ident_b], writes=[pt1_b])
            em.emit("dve", (lambda ci=ci: lambda e: e.tensor_copy(
                kv_tm[:, ci*128:(ci+1)*128], pt1[:, 0:128]))(),
                reads=[pt1_b], writes=[kv_tm_b])
        for hh in range(4):
            em.emit("dve", (lambda hh=hh: lambda e: e.tensor_mul(
                bass.AP(p_scr, 0, [[4096, 128], [1024, 4], [64, 16], [1, 64]]),
                bass.AP(q_tm, hh*256, [[1024, 128], [64, 4], [0, 16], [1, 64]]),
                bass.AP(kv_tm, 0, [[2048, 128], [0, 4], [128, 16], [1, 64]])))(),
                reads=[q_tm_b, kv_tm_b], writes=[p_scr_b])
            em.emit("dve", (lambda hh=hh: lambda e: e.tensor_reduce(
                s_red[:, hh*64:(hh+1)*64], bass.AP(p_scr, 0, [[4096, 128], [64, 64], [1, 64]]),
                axis=AX.X, op=ALU.add))(),
                reads=[p_scr_b], writes=[s_red_b])
        em.emit("act", (lambda: lambda e: e.activation(
            e_tm[:, :], s_red[:, :], AF.Exp, scale=0.125))(),
            reads=[s_red_b], writes=[e_tm_b])
        em.emit("dve", (lambda: lambda e: e.tensor_reduce(
            z_tm[:, :], bass.AP(e_tm, 0, [[256, 128], [16, 16], [1, 16]]),
            axis=AX.X, op=ALU.add))(),
            reads=[e_tm_b], writes=[z_tm_b])
        em.emit("dve", (lambda: lambda e: e.reciprocal(z_tm[:, :], z_tm[:, :]))(),
            reads=[z_tm_b], writes=[z_tm_b])
        for hh in range(4):
            em.emit("dve", (lambda hh=hh: lambda e: e.tensor_mul(
                bass.AP(p_scr, 0, [[4096, 128], [1024, 4], [16, 64], [1, 16]]),
                bass.AP(e_tm, hh*64, [[256, 128], [16, 4], [0, 64], [1, 16]]),
                bass.AP(kv_tm, 64, [[2048, 128], [0, 4], [1, 64], [128, 16]])))(),
                reads=[e_tm_b, kv_tm_b], writes=[p_scr_b])
            em.emit("dve", (lambda hh=hh: lambda e: e.tensor_reduce(
                o_red[:, hh*256:(hh+1)*256], bass.AP(p_scr, 0, [[4096, 128], [16, 256], [1, 16]]),
                axis=AX.X, op=ALU.add))(),
                reads=[p_scr_b], writes=[o_red_b])
        em.emit("dve", (lambda: lambda e: e.tensor_mul(
            o_tm[:, :], o_red[:, :],
            bass.AP(z_tm, 0, [[16, 128], [1, 16], [0, 64]])))(),
            reads=[o_red_b, z_tm_b], writes=[o_tm_b])
        for ci in range(8):
            em.emit("pe", (lambda ci=ci: lambda e: e.transpose(
                pt1[:, 0:128], o_tm[:, ci*128:(ci+1)*128], ident[:, :]))(),
                reads=[o_tm_b, ident_b], writes=[pt1_b])
            em.emit("dve", (lambda ci=ci, c0=c0: lambda e: e.tensor_copy(
                of.h[ci][:, c0:c0+128], pt1[:, 0:128]))(),
                reads=[pt1_b], writes=[of.b[ci]])
    def ep_o(mi, ps, ps_b, t0, tn):
        em.emit_dma("sp", "xin", (lambda mi=mi, t0=t0, tn=tn: lambda e: e.dma_start(
            out=x.h[mi][:, 0:tn], in_=xs[mi*128:(mi+1)*128, t0:t0+tn]))(),
            writes=[x.b[mi]])
        em.close_batch("xin")
        em.emit("dve", (lambda mi=mi, ps=ps, t0=t0, tn=tn: lambda e: e.scalar_tensor_tensor(
            xout[:, 0:tn], ps[:, 0:tn], consts["bo"][0][:, mi:mi+1],
            x.h[mi][:, 0:tn], ALU.add, ALU.add))(),
            reads=[ps_b, consts["bo"][1], x.b[mi]], writes=[xout_b])
        out_toks.append(em.emit_dma("sp", "out", (lambda mi=mi, t0=t0, tn=tn: lambda e: e.dma_start(
            out=xo[mi*128:(mi+1)*128, t0:t0+tn], in_=xout[:, 0:tn]))(),
            reads=[xout_b]))
        em.close_batch("out")
    build_project(em, of, WoT, 1024, 1024, n, pp, wbufs, ep_o, mgroup_chunks=2)
    em.finalize(out_toks)
    return nc

def rope_tables_l0(pos):
    n = len(pos)
    inv = 1.0 / 10000.0 ** (np.arange(0, 64, 2, dtype=np.float64) / 64.0)
    f = pos[None, :].astype(np.float64) * inv[:, None]
    cos, sin = np.cos(f), np.sin(f)
    cosT = np.zeros((128, n), np.float32)
    sinT = np.zeros((128, n), np.float32)
    for base in (0, 64):
        cosT[base:base+32] = cos; cosT[base+32:base+64] = cos
        sinT[base:base+32] = -sin; sinT[base+32:base+64] = sin
    return cosT, sinT

def swap_perm():
    P = np.zeros((128, 128), np.float32)
    for base in (0, 64):
        for i in range(32):
            P[base + 32 + i, base + i] = 1.0
            P[base + i, base + 32 + i] = 1.0
    return P.astype(BF16NP)

def pack_bias(b, nch):
    return np.ascontiguousarray(np.asarray(b, np.float32).reshape(nch, 128).T)

def l0attn_inputs(xT_shard, pos, ins):
    cosT, sinT = rope_tables_l0(pos)
    return {
        "xs": np.ascontiguousarray(xT_shard, np.float32),
        "WqT": prep_w(ins["L0_attn_Wq"]), "WdT": prep_w(ins["L0_attn_Wdown"]),
        "WuT": prep_w(ins["L0_attn_Wup"]), "WoT": prep_w(ins["L0_attn_Wout"]),
        "bq": pack_bias(ins["L0_attn_bq"], 8), "bd": pack_bias(ins["L0_attn_bdown"], 2),
        "bu": pack_bias(ins["L0_attn_bup"], 16), "bo": pack_bias(ins["L0_attn_bout"], 8),
        "cosq": cosT.astype(BF16NP), "sinq": sinT.astype(BF16NP), "perm": swap_perm(),
        "ident": np.eye(128, dtype=BF16NP),
    }


def setup_misc(em, row_w=512):
    misc = {}
    for nm, shape, val in (("ones_col", [128, 1], 1.0), ("ones_row", [1, 128], 1.0),
                           ("eps", [1, 1], 1e-5)):
        t = em.sb(nm, shape, F32)
        b = Buf(nm)
        em.emit("dve", (lambda t=t, val=val: lambda e: e.memset(t[:, :], val))(), writes=[b])
        misc[nm] = (t, b)
    for nm, shape in (("sq", [128, 512]), ("r_s1", [1, row_w]), ("r_s2", [1, row_w]),
                      ("r_a", [1, row_w]), ("r_b", [1, row_w])):
        t = em.sb(nm, shape, F32)
        misc[nm] = (t, Buf(nm))
    return misc

def load_consts(em, nc, specs):
    out = {}
    for nm, shape, dt in specs:
        hh = nc.dram_tensor(nm, list(shape), dt, kind="ExternalInput").ap()
        t = em.sb("c_" + nm, list(shape), dt)
        b = Buf(nm)
        em.emit_dma("sp", "cst", (lambda t=t, hh=hh: lambda e: e.dma_start(out=t[:, :], in_=hh))(),
                    writes=[b])
        out[nm] = (t, b)
    return out

def load_x_tiled(em, x, xs_ap, n, width=512, slot="xin"):
    for ci in range(x.nch):
        em.emit_dma("sp", slot, (lambda ci=ci: lambda e: e.dma_start(
            out=x.h[ci][:, 0:n], in_=xs_ap[ci*128:(ci+1)*128, 0:n]))(), writes=[x.b[ci]])
    em.close_batch(slot)


def build_qkv3(ntok_core=PT):
    n = ntok_core
    nc = bass.Bass("TRN2", target_bir_lowering=False)
    xs = nc.dram_tensor("xs", [D, n], F32, kind="ExternalInput").ap()
    wcatT = nc.dram_tensor("wcatT", [1024, 3072], F32, kind="ExternalInput").ap()
    qo = nc.dram_tensor("qo", [D, n], BF16, kind="ExternalOutput").ap()
    ko = nc.dram_tensor("ko", [D, n], BF16, kind="ExternalOutput").ap()
    vo = nc.dram_tensor("vo", [D, n], BF16, kind="ExternalOutput").ap()
    em = Em(nc)
    x = Chunked(em, "x", NCH, 512, F32)
    xn = Chunked(em, "xn", NCH, n, BF16)
    ot = em.sb("ot", [128, 512], BF16)
    ot_b = Buf("ot")
    misc = setup_misc(em)
    wb1 = em.sb("wbufA", [128, 4096], BF16)
    wb2 = em.sb("wbufB", [128, 4096], BF16)
    wbufs = [(wb1, Buf("wbufA")), (wb2, Buf("wbufB"))]
    pp = PsumPool(em, n=8, width=512)
    out_toks = []
    for t0c in range(0, n, 512):
        for ci in range(NCH):
            em.emit_dma("sp", "xin", (lambda ci=ci, t0c=t0c: lambda e: e.dma_start(
                out=x.h[ci][:, :], in_=xs[ci*128:(ci+1)*128, t0c:t0c+512]))(), writes=[x.b[ci]])
        em.close_batch("xin")
        build_layernorm(em, x, xn, pp, misc, 512, out_col0=t0c)
    outs = [qo, ko, vo]
    def ep(mi, ps, ps_b, t0, tn):
        dst = outs[mi // 8]
        r0 = (mi % 8) * 128
        em.emit("act", (lambda ps=ps, tn=tn: lambda e: e.activation(
            ot[:, 0:tn], ps[:, 0:tn], AF.Identity))(),
            reads=[ps_b], writes=[ot_b])
        out_toks.append(em.emit_dma("sp", "out", (lambda dst=dst, r0=r0, t0=t0, tn=tn: lambda e: e.dma_start(
            out=dst[r0:r0+128, t0:t0+tn], in_=ot[:, 0:tn]))(), reads=[ot_b]))
        em.close_batch("out")
    build_project(em, xn, wcatT, 1024, 3072, n, pp, wbufs, ep, mgroup_chunks=4)
    em.finalize(out_toks)
    return nc

def qkv3_inputs(xT_shard, wcat):
    return {"xs": np.ascontiguousarray(xT_shard, np.float32),
            "wcatT": np.ascontiguousarray(wcat, np.float32)}


def build_f1attn(do_ln, ntok_core=PT, HALO=1024, debug=False, upto=None):
    n = ntok_core
    W = HALO + n
    nw = n // 512
    nc = bass.Bass("TRN2", target_bir_lowering=False)
    xs = nc.dram_tensor("xe", [D, W], F32, kind="ExternalInput").ap()
    wqkvT = nc.dram_tensor("wqkvT", [1024, 1536], F32, kind="ExternalInput").ap()
    woT = nc.dram_tensor("woT", [512, 1024], F32, kind="ExternalInput").ap()
    xo = nc.dram_tensor("xo", [D, n], F32, kind="ExternalOutput").ap()
    em = Em(nc)
    xe = Chunked(em, "xe", NCH, 512, F32)
    xn = Chunked(em, "xn", NCH, W + 64, BF16)
    qkv = Chunked(em, "qkv", 12, W, BF16)
    import types as _t
    of = _t.SimpleNamespace(h=qkv.h[8:12], b=qkv.b[8:12], nch=4, ntok=W)
    vaug2 = em.sb("vaug2", [128, 1300], BF16); vaug2_b = Buf("vaug2")
    consts = load_consts(em, nc, [
        ("qs", [128, 1], F32), ("ks", [128, 1], F32),
        ("cq", [128, W - 512], BF16), ("sq_", [128, W - 512], BF16),
        ("ck", [128, W - 512], BF16), ("sk", [128, W - 512], BF16),
        ("perm", [128, 128], BF16), ("ident", [128, 128], BF16),
        ("e2", [128, 2], F32), ("e2t", [2, 128], F32),
        ("mA", [128, 4096], BF16), ("mB", [128, 2048], BF16)])
    misc = setup_misc(em)
    em.close_batch("cst")
    rr = em.sb("rr", [2, 512], F32); rr_b = Buf("rr")
    vs = em.sb("vs", [64, 128], BF16); vs_b = Buf("vs")
    NKC = (W - 512) // 128
    vaug = em.sb("vaug", [128, NKC * 65], BF16); vaug_b = Buf("vaug")
    expSH = [(em.sb(f"expS{i}", [128, 1024], BF16), Buf(f"expS{i}")) for i in range(4)]
    o_un = em.sb("o_un", [64, 512], BF16); o_un_b = Buf("o_un")
    r_row = em.sb("r_row", [1, 512], F32); r_row_b = Buf("r_row")
    xout = em.sb("xout", [128, 512], F32); xout_b = Buf("xout")
    scr = em.sb("scr", [128, 512], BF16)
    em.scr = (scr, Buf("scr"))
    wb1 = em.sb("wbufA", [128, 1024], BF16)
    wb2 = em.sb("wbufB", [128, 1024], BF16)
    wbufs = [(wb1, Buf("wbufA")), (wb2, Buf("wbufB"))]
    pp = PsumPool(em, n=2, width=512)
    ps_tb = em.ps("ps_tb", [128, 256], BF16); ps_tb_b = Buf("ps_tb")
    ps_sH = [(em.ps(f"ps_s{i}", [128, 1024]), Buf(f"ps_s{i}")) for i in range(2)]
    ps_o = em.ps("ps_o", [128, 512]); ps_o_b = Buf("ps_o")
    out_toks = []
    for ci in range(NCH):
        em.emit("dve", (lambda ci=ci: lambda e: e.memset(xn.h[ci][:, 0:1], 0.0))(),
                writes=[xn.b[ci]])
    if do_ln:
        for t0c in range(0, W, 512):
            for ci in range(NCH):
                em.emit_dma("sp", "xin", (lambda ci=ci, t0c=t0c: lambda e: e.dma_start(
                    out=xe.h[ci][:, 0:512], in_=xs[ci*128:(ci+1)*128, t0c:t0c+512]))(),
                    writes=[xe.b[ci]])
            em.close_batch("xin")
            build_layernorm(em, xe, xn, pp, misc, 512, out_col0=t0c + 1, x_col0=0)
    else:
        for ci in range(NCH):
            em.emit_dma("pool", "xin", (lambda ci=ci: lambda e: e.dma_start(
                out=xn.h[ci][:, 1:1+W], in_=xs[ci*128:(ci+1)*128, :]))(),
                writes=[xn.b[ci]])
        em.close_batch("xin")
    if upto == "ln":
        dbg2 = nc.dram_tensor("dbg2", [2 * 128, W], BF16, kind="ExternalOutput").ap()
        toks = []
        for i, ci in enumerate((0, 4)):
            toks.append(em.emit_dma("sp", "dbg", (lambda i=i, ci=ci: lambda e: e.dma_start(
                out=dbg2[i*128:(i+1)*128, :], in_=xn.h[ci][:, 1:1+W]))(), reads=[xn.b[ci]]))
        em.close_batch("dbg")
        for s in list(em.cnt):
            if s.startswith("d_"):
                toks.append(Tok(s, em.cnt[s]))
        em.finalize(toks)
        return nc
    def ep_qkv(mi, ps, ps_b, t0, tn):
        em.emit("act", (lambda mi=mi, ps=ps, t0=t0, tn=tn: lambda e: e.activation(
            qkv.h[mi][:, t0:t0+tn], ps[:, 0:tn], AF.Identity))(),
            reads=[ps_b], writes=[qkv.b[mi]])
    build_project(em, xn, wqkvT, 1024, 1536, W, pp, wbufs, ep_qkv,
                  mgroup_chunks=1, xoff=[1, 1, 1, 1, 0, 0, 0, 0])
    e2, e2_b = consts["e2"]
    e2t, e2t_b = consts["e2t"]
    for ci in range(8):
        scale_nm = "qs" if ci < 4 else "ks"
        for t0_, tn in mm_tiles(W - 512):
            t0 = t0_ + 512
            ps_n, ps_n_b = pp.take1()
            em.emit("act", (lambda ci=ci, t0=t0, tn=tn: lambda e: e.square(
                misc["sq"][0][:, 0:tn], qkv.h[ci][:, t0:t0+tn]))(),
                reads=[qkv.b[ci]], writes=[misc["sq"][1]])
            em.emit("pe", (lambda t0=t0, tn=tn, ps_n=ps_n: lambda e: e.matmul(
                ps_n[0:2, 0:tn], e2[:, :], misc["sq"][0][:, 0:tn], start=True, stop=True))(),
                reads=[e2_b, misc["sq"][1]], writes=[ps_n_b])
            em.emit("dve", (lambda tn=tn, ps_n=ps_n: lambda e: e.tensor_scalar_max(
                rr[0:2, 0:tn], ps_n[0:2, 0:tn], 1e-24))(),
                reads=[ps_n_b], writes=[rr_b])
            em.emit("act", (lambda tn=tn: lambda e: e.activation(
                rr[0:2, 0:tn], rr[0:2, 0:tn], AF.Ln))(),
                reads=[rr_b], writes=[rr_b])
            em.emit("act", (lambda tn=tn: lambda e: e.activation(
                rr[0:2, 0:tn], rr[0:2, 0:tn], AF.Exp, scale=-0.5))(),
                reads=[rr_b], writes=[rr_b])
            ps_r, ps_r_b = pp.take1()
            em.emit("pe", (lambda tn=tn, ps_r=ps_r: lambda e: e.matmul(
                ps_r[:, 0:tn], e2t[:, :], rr[0:2, 0:tn], start=True, stop=True))(),
                reads=[e2t_b, rr_b], writes=[ps_r_b])
            em.emit("dve", (lambda ci=ci, t0=t0, tn=tn, ps_r=ps_r, scale_nm=scale_nm: lambda e: e.scalar_tensor_tensor(
                qkv.h[ci][:, t0:t0+tn], qkv.h[ci][:, t0:t0+tn], consts[scale_nm][0][:, 0:1],
                ps_r[:, 0:tn], ALU.mult, ALU.mult))(),
                reads=[qkv.b[ci], consts[scale_nm][1], ps_r_b], writes=[qkv.b[ci]])
    rope_apply(em, qkv.h[0:4], qkv.b[0:4], consts["cq"][0], consts["cq"][1],
               consts["sq_"][0], consts["sq_"][1], consts["perm"][0], consts["perm"][1],
               pp, W - 512, col0=512)
    rope_apply(em, qkv.h[4:8], qkv.b[4:8], consts["ck"][0], consts["ck"][1],
               consts["sk"][0], consts["sk"][1], consts["perm"][0], consts["perm"][1],
               pp, W - 512, col0=512)
    if debug:
        dbg = nc.dram_tensor("dbg", [3 * 128, W], BF16, kind="ExternalOutput").ap()
        for i, ci in enumerate((0, 4, 8)):
            out_toks.append(em.emit_dma("sp", "dbg", (lambda i=i, ci=ci: lambda e: e.dma_start(
                out=dbg[i*128:(i+1)*128, :], in_=qkv.h[ci][:, 0:W]))(), reads=[qkv.b[ci]]))
        dbg2 = nc.dram_tensor("dbg2", [2 * 128, W], BF16, kind="ExternalOutput").ap()
        for i, ci in enumerate((0, 4)):
            out_toks.append(em.emit_dma("sp", "dbg", (lambda i=i, ci=ci: lambda e: e.dma_start(
                out=dbg2[i*128:(i+1)*128, :], in_=xn.h[ci][:, 1:1+W]))(), reads=[xn.b[ci]]))
        em.close_batch("dbg")
    ident, ident_b = consts["ident"]
    def build_vaug(vdst, vdst_b, vch, base):
        em.emit("dve", (lambda vdst=vdst: lambda e: e.memset(vdst[:, :], 1.0))(), writes=[vdst_b])
        for kc in range(NKC):
            em.emit("dve", (lambda vch=vch, base=base, kc=kc: lambda e: e.tensor_copy(
                vs[0:64, 0:128], qkv.h[vch][base:base+64, 512+kc*128:512+(kc+1)*128]))(),
                reads=[qkv.b[vch]], writes=[vs_b])
            em.emit("pe", (lambda kc=kc: lambda e: e.transpose(
                ps_tb[:, 0:64], vs[0:64, 0:128], ident[0:64, 0:64]))(),
                reads=[vs_b, ident_b], writes=[ps_tb_b])
            em.emit("dve", (lambda vdst=vdst, kc=kc: lambda e: e.tensor_copy(
                vdst[:, kc*65:kc*65+64], ps_tb[:, 0:64]))(),
                reads=[ps_tb_b], writes=[vdst_b])
    for hp in range(4):
        vch = 8 + hp
        build_vaug(vaug, vaug_b, vch, 0)
        build_vaug(vaug2, vaug2_b, vch, 64)
        for h in (2*hp, 2*hp + 1):
            base = (h % 2) * 64
            va, va_b = (vaug, vaug_b) if h % 2 == 0 else (vaug2, vaug2_b)
            qch = h // 2
            kch = 4 + h // 2
            for w in range(nw):
                q0 = HALO + w * 512
                k0 = 512 + w * 512
                avq = []
                for g in range(2):
                    mk, mk_b = consts["mB"] if (w == 0 and g == 0) else consts["mA"]
                    for hf in range(2):
                        psS, psS_b = ps_sH[hf]
                        exS, exS_b = expSH[g * 2 + hf]
                        def mk_s(g=g, base=base, qch=qch, kch=kch, q0=q0, k0=k0,
                                 hf=hf, psS=psS):
                            def f(e):
                                ins = None
                                for j in range(2):
                                    kc = g * 4 + hf * 2 + j
                                    ins = e.matmul(psS[:, j*512:(j+1)*512],
                                                   qkv.h[kch][base:base+64, k0+kc*128:k0+(kc+1)*128],
                                                   qkv.h[qch][base:base+64, q0:q0+512],
                                                   start=True, stop=True)
                                return ins
                            return f
                        em.emit("pe", mk_s(), reads=[qkv.b[kch], qkv.b[qch]], writes=[psS_b])
                        em.emit("act", (lambda psS=psS, exS=exS: lambda e: e.activation(
                            exS[:, 0:1024], psS[:, 0:1024], AF.Exp, scale=8.0))(),
                            reads=[psS_b], writes=[exS_b])
                        em.emit("dve", (lambda mk=mk, g=g, w=w, hf=hf, exS=exS: lambda e: e.tensor_mul(
                            exS[:, 0:1024], exS[:, 0:1024],
                            mk[:, hf*1024:(hf+1)*1024] if (w == 0 and g == 0)
                            else mk[:, g*2048+hf*1024:g*2048+(hf+1)*1024]))(),
                            reads=[exS_b, mk_b], writes=[exS_b])
                        def mk_av(w=w, g=g, va=va, hf=hf, exS=exS):
                            def f(e):
                                ins = None
                                for j in range(2):
                                    kc = g * 4 + hf * 2 + j
                                    blk = w * 4 + kc
                                    ins = e.matmul(ps_o[0:65, 0:512],
                                                   va[:, blk*65:blk*65+65],
                                                   exS[:, j*512:(j+1)*512],
                                                   start=(kc == 0), stop=(kc == 7))
                                return ins
                            return f
                        avq.append((mk_av(), [va_b, exS_b]))
                for fn, rds in avq:
                    em.emit("pe", fn, reads=rds, writes=[ps_o_b])
                em.emit("dve", (lambda: lambda e: e.reciprocal(r_row[0:1, 0:512], ps_o[64:65, 0:512]))(),
                    reads=[ps_o_b], writes=[r_row_b])
                em.emit("dve", (lambda: lambda e: e.tensor_copy(o_un[0:64, 0:512], ps_o[0:64, 0:512]))(),
                    reads=[ps_o_b], writes=[o_un_b])
                ps_bc, ps_bc_b = pp.take1()
                em.emit("pe", (lambda ps_bc=ps_bc: lambda e: e.matmul(
                    ps_bc[0:64, 0:512], misc["ones_row"][0][0:1, 0:64], r_row[0:1, 0:512],
                    start=True, stop=True))(),
                    reads=[misc["ones_row"][1], r_row_b], writes=[ps_bc_b])
                em.emit("dve", (lambda hp=hp, w=w, base=base, ps_bc=ps_bc: lambda e: e.tensor_mul(
                    of.h[hp][base:base+64, w*512:(w+1)*512], o_un[0:64, 0:512], ps_bc[0:64, 0:512]))(),
                    reads=[o_un_b, ps_bc_b], writes=[of.b[hp]])
    if upto == "attn":
        dbg3 = nc.dram_tensor("dbg3", [4 * 128, n], BF16, kind="ExternalOutput").ap()
        toks = []
        for ci in range(4):
            toks.append(em.emit_dma("sp", "dbg", (lambda ci=ci: lambda e: e.dma_start(
                out=dbg3[ci*128:(ci+1)*128, :], in_=of.h[ci][:, 0:n]))(), reads=[of.b[ci]]))
        em.close_batch("dbg")
        for s in list(em.cnt):
            if s.startswith("d_"):
                toks.append(Tok(s, em.cnt[s]))
        em.finalize(toks)
        return nc
    def ep_o(mi, ps, ps_b, t0, tn):
        if do_ln:
            em.emit("dve", (lambda mi=mi, ps=ps, t0=t0, tn=tn: lambda e: e.tensor_add(
                xout[:, 0:tn], ps[:, 0:tn], xn.h[mi][:, HALO+1+t0:HALO+1+t0+tn]))(),
                reads=[ps_b, xn.b[mi]], writes=[xout_b])
        else:
            em.emit_dma("sp", "xres", (lambda mi=mi, t0=t0, tn=tn: lambda e: e.dma_start(
                out=xe.h[mi][:, 0:tn], in_=xs[mi*128:(mi+1)*128, HALO+t0:HALO+t0+tn]))(),
                writes=[xe.b[mi]])
            em.close_batch("xres")
            em.emit("dve", (lambda mi=mi, ps=ps, t0=t0, tn=tn: lambda e: e.tensor_add(
                xout[:, 0:tn], ps[:, 0:tn], xe.h[mi][:, 0:tn]))(),
                reads=[ps_b, xe.b[mi]], writes=[xout_b])
        out_toks.append(em.emit_dma("sp", "out", (lambda mi=mi, t0=t0, tn=tn: lambda e: e.dma_start(
            out=xo[mi*128:(mi+1)*128, t0:t0+tn], in_=xout[:, 0:tn]))(), reads=[xout_b]))
    build_project(em, of, woT, 512, 1024, n, pp, wbufs, ep_o, mgroup_chunks=2)
    em.finalize(out_toks)
    return nc


def build_f1ffn(final_add, ntok_core=PT):
    n = ntok_core
    nc = bass.Bass("TRN2", target_bir_lowering=False)
    xs = nc.dram_tensor("xe", [D, n + 1], F32, kind="ExternalInput").ap()
    w1gT = nc.dram_tensor("w1gT", [1024, 2816], F32, kind="ExternalInput").ap()
    w1aT = nc.dram_tensor("w1aT", [1024, 2816], F32, kind="ExternalInput").ap()
    w2T = nc.dram_tensor("w2T", [2816, 1024], F32, kind="ExternalInput").ap()
    x3s = nc.dram_tensor("x3s", [D, n], F32, kind="ExternalInput").ap() if final_add else None
    xo = nc.dram_tensor("xo", [D, n], F32, kind="ExternalOutput").ap()
    em = Em(nc)
    xe = Chunked(em, "xe", NCH, 513, F32)
    xn = Chunked(em, "xn", NCH, n, BF16)
    hg = Chunked(em, "hg", 22, 1024, BF16)
    hh = Chunked(em, "hh", 22, 1024, BF16)
    consts = load_consts(em, nc, [
        ("lg", [128, 8], F32), ("lb", [128, 8], F32),
        ("b1a", [128, 22], F32), ("b1g", [128, 22], F32), ("b2", [128, 8], F32)])
    misc = setup_misc(em)
    em.close_batch("cst")
    xout = em.sb("xout", [128, 512], F32); xout_b = Buf("xout")
    x3t = em.sb("x3t", [128, 512], F32); x3t_b = Buf("x3t")
    wb1 = em.sb("wbufA", [128, 5632], BF16)
    wb2 = em.sb("wbufB", [128, 5632], BF16)
    wbufs = [(wb1, Buf("wbufA")), (wb2, Buf("wbufB"))]
    pp = PsumPool(em, n=8, width=512)
    out_toks = []
    for t0c in range(0, n, 512):
        for ci in range(NCH):
            em.emit_dma("sp", "xin", (lambda ci=ci, t0c=t0c: lambda e: e.dma_start(
                out=xe.h[ci][:, 0:513], in_=xs[ci*128:(ci+1)*128, t0c:t0c+513]))(),
                writes=[xe.b[ci]])
        em.close_batch("xin")
        build_layernorm(em, xe, xn, pp, misc, 512, out_col0=t0c, x_col0=0,
                        xoff=[1, 1, 1, 1, 0, 0, 0, 0],
                        gain=consts["lg"], lbias=consts["lb"])
    for tt in range(0, n, 1024):
        def ep_g(mi, ps, ps_b, t0, tn):
            em.emit("act", (lambda mi=mi, ps=ps, t0=t0, tn=tn: lambda e: e.activation(
                hg.h[mi][:, t0:t0+tn], ps[:, 0:tn], AF.Gelu,
                bias=consts["b1g"][0][:, mi:mi+1]))(),
                reads=[ps_b, consts["b1g"][1]], writes=[hg.b[mi]])
        build_project(em, xn, w1gT, 1024, 2816, 1024, pp, wbufs, ep_g,
                      mgroup_chunks=4, xoff=[tt] * 8)
        def ep_a(mi, ps, ps_b, t0, tn):
            em.emit("dve", (lambda mi=mi, ps=ps, t0=t0, tn=tn: lambda e: e.scalar_tensor_tensor(
                hh.h[mi][:, t0:t0+tn], ps[:, 0:tn], consts["b1a"][0][:, mi:mi+1],
                hg.h[mi][:, t0:t0+tn], ALU.add, ALU.mult))(),
                reads=[ps_b, consts["b1a"][1], hg.b[mi]], writes=[hh.b[mi]])
        build_project(em, xn, w1aT, 1024, 2816, 1024, pp, wbufs, ep_a,
                      mgroup_chunks=4, xoff=[tt] * 8)
        def ep_o(mi, ps, ps_b, t0, tn, tt=tt):
            em.emit_dma("sp", "xres", (lambda mi=mi, t0=t0, tn=tn, tt=tt: lambda e: e.dma_start(
                out=xe.h[mi][:, 0:tn], in_=xs[mi*128:(mi+1)*128, 1+tt+t0:1+tt+t0+tn]))(),
                writes=[xe.b[mi]])
            em.close_batch("xres")
            em.emit("dve", (lambda mi=mi, ps=ps, t0=t0, tn=tn: lambda e: e.scalar_tensor_tensor(
                xout[:, 0:tn], ps[:, 0:tn], consts["b2"][0][:, mi:mi+1],
                xe.h[mi][:, 0:tn], ALU.add, ALU.add))(),
                reads=[ps_b, consts["b2"][1], xe.b[mi]], writes=[xout_b])
            if final_add:
                em.emit_dma("sp", "x3in", (lambda mi=mi, t0=t0, tn=tn, tt=tt: lambda e: e.dma_start(
                    out=x3t[:, 0:tn], in_=x3s[mi*128:(mi+1)*128, tt+t0:tt+t0+tn]))(),
                    writes=[x3t_b])
                em.close_batch("x3in")
                em.emit("dve", (lambda tn=tn: lambda e: e.tensor_add(
                    xout[:, 0:tn], xout[:, 0:tn], x3t[:, 0:tn]))(),
                    reads=[xout_b, x3t_b], writes=[xout_b])
            out_toks.append(em.emit_dma("sp", "out", (lambda mi=mi, t0=t0, tn=tn, tt=tt: lambda e: e.dma_start(
                out=xo[mi*128:(mi+1)*128, tt+t0:tt+t0+tn], in_=xout[:, 0:tn]))(),
                reads=[xout_b]))
            em.close_batch("out")
        build_project(em, hh, w2T, 2816, 1024, 1024, pp, wbufs, ep_o, mgroup_chunks=2)
    em.finalize(out_toks)
    return nc


def xpos_tables(pos):
    DH, WIN = 64, 512
    half = np.arange(0, DH, 2, dtype=np.float64)
    inv = 1.0 / 10000.0 ** (half / DH)
    t = pos.astype(np.float64)
    f = t[None, :] * inv[:, None]
    base = (half + 0.4 * DH) / (1.4 * DH)
    power = (t - 8192 // 2) / float(WIN // 2)
    scale = base[:, None] ** power[None, :]
    cos, sin = np.cos(f), np.sin(f)
    def pack(cv, sv):
        cT = np.zeros((128, len(pos)), np.float32)
        sT = np.zeros((128, len(pos)), np.float32)
        for b0 in (0, 64):
            cT[b0:b0+32] = cv; cT[b0+32:b0+64] = cv
            sT[b0:b0+32] = -sv; sT[b0+32:b0+64] = sv
        return cT.astype(BF16NP), sT.astype(BF16NP)
    cq, sq = pack(cos * scale, sin * scale)
    ck, sk = pack(cos / scale, sin / scale)
    return cq, sq, ck, sk

def f1_masks(batch_start):
    m = np.zeros((128, 8, 512), np.float32)
    for kc in range(8):
        ki = np.arange(128)
        j = kc * 128 + ki
        for qi in range(512):
            d = qi + 512 - j
            m[:, kc, qi] = ((d >= 0) & (d <= 512)).astype(np.float32)
    mA = m.reshape(128, 4096)
    mB = mA.copy()
    if batch_start:
        mB[:, 0:2048] = 0.0
    return mA.astype(BF16NP), mB.astype(BF16NP)

def f1attn_inputs(xe_shard, pos_qkv, wqkv, qs, ks, wo, batch_start):
    cq, sq, ck, sk = xpos_tables(pos_qkv[512:])
    mA, mB = f1_masks(batch_start)
    mB = mB[:, 0:2048].copy()
    return {
        "xe": np.ascontiguousarray(xe_shard, np.float32),
        "wqkvT": prep_w(wqkv), "woT": prep_w(wo),
        "qs": np.ascontiguousarray(np.tile(np.asarray(qs, np.float32), 2)[:, None]),
        "ks": np.ascontiguousarray(np.tile(np.asarray(ks, np.float32), 2)[:, None]),
        "cq": cq, "sq_": sq, "ck": ck, "sk": sk,
        "perm": swap_perm(), "ident": np.eye(128, dtype=BF16NP),
        "e2": np.ascontiguousarray(np.kron(np.eye(2), np.ones((64, 1))).astype(np.float32)),
        "e2t": np.ascontiguousarray(np.kron(np.eye(2), np.ones((1, 64))).astype(np.float32)),
        "mA": mA, "mB": mB,
    }

def f1ffn_inputs(xe_shard, lg, lb, w1, b1, w2, b2, x3_shard=None):
    HID = 2730
    w1 = np.asarray(w1, np.float32)
    w1a = np.zeros((1024, 2816), np.float32); w1a[:, :HID] = w1[:, :HID]
    w1g = np.zeros((1024, 2816), np.float32); w1g[:, :HID] = w1[:, HID:]
    b1 = np.asarray(b1, np.float32)
    b1a = np.zeros(2816, np.float32); b1a[:HID] = b1[:HID]
    b1g = np.zeros(2816, np.float32); b1g[:HID] = b1[HID:]
    w2p = np.zeros((2816, 1024), np.float32); w2p[:HID] = np.asarray(w2, np.float32)
    d = {
        "xe": np.ascontiguousarray(xe_shard, np.float32),
        "w1gT": w1g, "w1aT": w1a, "w2T": w2p,
        "lg": pack_bias(lg, 8), "lb": pack_bias(lb, 8),
        "b1a": pack_bias(b1a, 22), "b1g": pack_bias(b1g, 22),
        "b2": pack_bias(b2, 8),
    }
    if x3_shard is not None:
        d["x3s"] = np.ascontiguousarray(x3_shard, np.float32)
    return d


def build_l2a(ntok_core=PT, HALO=256):
    n = ntok_core
    W = HALO + n
    NKC = W // 128
    nc = bass.Bass("TRN2", target_bir_lowering=False)
    xs = nc.dram_tensor("xe", [D, W], F32, kind="ExternalInput").ap()
    WqT = nc.dram_tensor("WqT", [1024, 1024], F32, kind="ExternalInput").ap()
    WkT = nc.dram_tensor("WkT", [1024, 256], F32, kind="ExternalInput").ap()
    WvT = nc.dram_tensor("WvT", [1024, 256], F32, kind="ExternalInput").ap()
    WoT = nc.dram_tensor("WoT", [1024, 1024], F32, kind="ExternalInput").ap()
    xo = nc.dram_tensor("xo", [D, n], F32, kind="ExternalOutput").ap()
    em = Em(nc)
    xe = Chunked(em, "xe", NCH, 512, F32)
    xn = Chunked(em, "xn", NCH, W, BF16)
    q = Chunked(em, "q", NCH, n, BF16)
    kk = Chunked(em, "kk", 2, W, BF16)
    vv = Chunked(em, "vv", 2, W, BF16)
    of = xn
    of2 = Chunked(em, "of2", NCH, n, BF16)
    consts = load_consts(em, nc, [
        ("cq", [128, n], BF16), ("sq_", [128, n], BF16),
        ("ck", [128, W], BF16), ("sk", [128, W], BF16),
        ("perm", [128, 128], BF16), ("ident", [128, 128], BF16),
        ("e2", [128, 2], F32), ("e2t", [2, 128], F32),
        ("qg", [128, 8], F32), ("pm", [128, 16], F32),
        ("msk", [128, 16 * 384], BF16), ("eps6", [128, 1], F32)])
    misc = setup_misc(em)
    em.close_batch("cst")
    rr = em.sb("rr", [2, 512], F32); rr_b = Buf("rr")
    vs = em.sb("vs", [64, 128], BF16); vs_b = Buf("vs")
    krep = em.sb("krep", [128, W], BF16); krep_b = Buf("krep")
    vaug = em.sb("vaug", [128, NKC * 65], BF16); vaug_b = Buf("vaug")
    expS = em.sb("expS", [128, 384], BF16); expS_b = Buf("expS")
    o_un = em.sb("o_un", [64, 128], BF16); o_un_b = Buf("o_un")
    r_row = em.sb("r_row", [1, 128], F32); r_row_b = Buf("r_row")
    mixt = em.sb("mixt", [128, 512], F32); mixt_b = Buf("mixt")
    mixt2 = em.sb("mixt2", [128, 512], F32); mixt2_b = Buf("mixt2")
    xout = em.sb("xout", [128, 512], F32); xout_b = Buf("xout")
    scr = em.sb("scr", [128, 512], BF16)
    em.scr = (scr, Buf("scr"))
    wb1 = em.sb("wbufA", [128, 2048], BF16)
    wb2 = em.sb("wbufB", [128, 2048], BF16)
    wbufs = [(wb1, Buf("wbufA")), (wb2, Buf("wbufB"))]
    pp = PsumPool(em, n=2, width=512)
    ps_tb = em.ps("ps_tb", [128, 256], BF16); ps_tb_b = Buf("ps_tb")
    ps_s = em.ps("ps_s", [128, 512]); ps_s_b = Buf("ps_s")
    ps_o = em.ps("ps_o", [128, 128]); ps_o_b = Buf("ps_o")
    ps_bc = em.ps("ps_bc", [128, 128]); ps_bc_b = Buf("ps_bc")
    out_toks = []
    for t0c in range(0, W, 512):
        tw = min(512, W - t0c)
        for ci in range(NCH):
            em.emit_dma("sp", "xin", (lambda ci=ci, t0c=t0c, tw=tw: lambda e: e.dma_start(
                out=xe.h[ci][:, 0:tw], in_=xs[ci*128:(ci+1)*128, t0c:t0c+tw]))(),
                writes=[xe.b[ci]])
        em.close_batch("xin")
        build_layernorm(em, xe, xn, pp, misc, tw, out_col0=t0c, x_col0=0)
    def mk_ep(dst, col0=0):
        def ep(mi, ps, ps_b, t0, tn):
            em.emit("act", (lambda mi=mi, ps=ps, t0=t0, tn=tn: lambda e: e.activation(
                dst.h[mi][:, col0+t0:col0+t0+tn], ps[:, 0:tn], AF.Identity))(),
                reads=[ps_b], writes=[dst.b[mi]])
        return ep
    build_project(em, xn, WqT, 1024, 1024, n, pp, wbufs, mk_ep(q), mgroup_chunks=2,
                  xoff=[HALO] * 8)
    build_project(em, xn, WkT, 1024, 256, W, pp, wbufs, mk_ep(kk), mgroup_chunks=2)
    build_project(em, xn, WvT, 1024, 256, W, pp, wbufs, mk_ep(vv), mgroup_chunks=2)
    e2, e2_b = consts["e2"]
    e2t, e2t_b = consts["e2t"]
    def rmsn(ch, b, width):
        for t0, tn in mm_tiles(width):
            ps_n, ps_n_b = pp.take1()
            em.emit("act", (lambda ch=ch, t0=t0, tn=tn: lambda e: e.square(
                misc["sq"][0][:, 0:tn], ch[:, t0:t0+tn]))(),
                reads=[b], writes=[misc["sq"][1]])
            em.emit("pe", (lambda tn=tn, ps_n=ps_n: lambda e: e.matmul(
                ps_n[0:2, 0:tn], e2[:, :], misc["sq"][0][:, 0:tn], start=True, stop=True))(),
                reads=[e2_b, misc["sq"][1]], writes=[ps_n_b])
            em.emit("act", (lambda tn=tn, ps_n=ps_n: lambda e: e.activation(
                rr[0:2, 0:tn], ps_n[0:2, 0:tn], AF.Ln, scale=1.0/64.0,
                bias=consts["eps6"][0][0:2, 0:1]))(),
                reads=[ps_n_b, consts["eps6"][1]], writes=[rr_b])
            em.emit("act", (lambda tn=tn: lambda e: e.activation(
                rr[0:2, 0:tn], rr[0:2, 0:tn], AF.Exp, scale=-0.5))(),
                reads=[rr_b], writes=[rr_b])
            ps_r, ps_r_b = pp.take1()
            em.emit("pe", (lambda tn=tn, ps_r=ps_r: lambda e: e.matmul(
                ps_r[:, 0:tn], e2t[:, :], rr[0:2, 0:tn], start=True, stop=True))(),
                reads=[e2t_b, rr_b], writes=[ps_r_b])
            em.emit("dve", (lambda ch=ch, t0=t0, tn=tn, ps_r=ps_r: lambda e: e.tensor_mul(
                ch[:, t0:t0+tn], ch[:, t0:t0+tn], ps_r[:, 0:tn]))(),
                reads=[b, ps_r_b], writes=[b])
    for ci in range(8):
        rmsn(q.h[ci], q.b[ci], n)
    for ci in range(2):
        rmsn(kk.h[ci], kk.b[ci], W)
    rope_apply(em, q.h, q.b, consts["cq"][0], consts["cq"][1], consts["sq_"][0],
               consts["sq_"][1], consts["perm"][0], consts["perm"][1], pp, n)
    rope_apply(em, kk.h, kk.b, consts["ck"][0], consts["ck"][1], consts["sk"][0],
               consts["sk"][1], consts["perm"][0], consts["perm"][1], pp, W)
    for ci in range(8):
        em.emit("dve", (lambda ci=ci: lambda e: e.tensor_scalar_mul(
            q.h[ci][:, :], q.h[ci][:, :], consts["qg"][0][:, ci:ci+1]))(),
            reads=[q.b[ci], consts["qg"][1]], writes=[q.b[ci]])
    ident, ident_b = consts["ident"]
    for g in range(4):
        kch, kbase = g // 2, (g % 2) * 64
        em.emit("dve", (lambda kch=kch, kbase=kbase: lambda e: e.tensor_copy(
            krep[0:64, 0:W], kk.h[kch][kbase:kbase+64, 0:W]))(),
            reads=[kk.b[kch]], writes=[krep_b])
        em.emit("dve", (lambda kch=kch, kbase=kbase: lambda e: e.tensor_copy(
            krep[64:128, 0:W], kk.h[kch][kbase:kbase+64, 0:W]))(),
            reads=[kk.b[kch]], writes=[krep_b])
        em.emit("dve", (lambda: lambda e: e.memset(vaug[:, :], 1.0))(), writes=[vaug_b])
        for kc in range(NKC):
            em.emit("dve", (lambda kch=kch, kbase=kbase, kc=kc: lambda e: e.tensor_copy(
                vs[0:64, 0:128], vv.h[kch][kbase:kbase+64, kc*128:(kc+1)*128]))(),
                reads=[vv.b[kch]], writes=[vs_b])
            em.emit("pe", (lambda kc=kc: lambda e: e.transpose(
                ps_tb[:, 0:64], vs[0:64, 0:128], ident[0:64, 0:64]))(),
                reads=[vs_b, ident_b], writes=[ps_tb_b])
            em.emit("dve", (lambda kc=kc: lambda e: e.tensor_copy(
                vaug[:, kc*65:kc*65+64], ps_tb[:, 0:64]))(),
                reads=[ps_tb_b], writes=[vaug_b])
        for h in range(4 * g, 4 * g + 4):
            qch, qbase = h // 2, (h % 2) * 64
            for qt in range(n // 128):
                def mk_s(qt=qt, qch=qch, qbase=qbase):
                    def f(e):
                        ins = None
                        for j in range(3):
                            ins = e.matmul(ps_s[:, j*128:(j+1)*128],
                                           krep[qbase:qbase+64, (qt+j)*128:(qt+j+1)*128],
                                           q.h[qch][qbase:qbase+64, qt*128:(qt+1)*128],
                                           start=True, stop=True)
                        return ins
                    return f
                em.emit("pe", mk_s(), reads=[krep_b, q.b[qch]], writes=[ps_s_b])
                em.emit("act", (lambda: lambda e: e.activation(
                    expS[:, 0:384], ps_s[:, 0:384], AF.Exp, scale=0.125))(),
                    reads=[ps_s_b], writes=[expS_b])
                em.emit("dve", (lambda qt=qt: lambda e: e.tensor_mul(
                    expS[:, 0:384], expS[:, 0:384],
                    consts["msk"][0][:, qt*384:(qt+1)*384]))(),
                    reads=[expS_b, consts["msk"][1]], writes=[expS_b])
                def mk_av(qt=qt):
                    def f(e):
                        ins = None
                        for j in range(3):
                            ins = e.matmul(ps_o[0:65, 0:128],
                                           vaug[:, (qt+j)*65:(qt+j)*65+65],
                                           expS[:, j*128:(j+1)*128],
                                           start=(j == 0), stop=(j == 2))
                        return ins
                    return f
                em.emit("pe", mk_av(), reads=[vaug_b, expS_b], writes=[ps_o_b])
                em.emit("dve", (lambda: lambda e: e.reciprocal(
                    r_row[0:1, 0:128], ps_o[64:65, 0:128]))(),
                    reads=[ps_o_b], writes=[r_row_b])
                em.emit("dve", (lambda: lambda e: e.tensor_copy(
                    o_un[0:64, 0:128], ps_o[0:64, 0:128]))(),
                    reads=[ps_o_b], writes=[o_un_b])
                em.emit("pe", (lambda: lambda e: e.matmul(
                    ps_bc[0:64, 0:128], misc["ones_row"][0][0:1, 0:64], r_row[0:1, 0:128],
                    start=True, stop=True))(),
                    reads=[misc["ones_row"][1], r_row_b], writes=[ps_bc_b])
                em.emit("dve", (lambda qch=qch, qbase=qbase, qt=qt: lambda e: e.tensor_mul(
                    of.h[qch][qbase:qbase+64, qt*128:(qt+1)*128], o_un[0:64, 0:128],
                    ps_bc[0:64, 0:128]))(),
                    reads=[o_un_b, ps_bc_b], writes=[of.b[qch]])
    for p in range(8):
        for t0, tn in mm_tiles(n):
            em.emit("dve", (lambda p=p, t0=t0, tn=tn: lambda e: e.tensor_copy(
                mixt[0:64, 0:tn], of.h[p][64:128, t0:t0+tn]))(),
                reads=[of.b[p]], writes=[mixt_b])
            em.emit("dve", (lambda p=p, t0=t0, tn=tn: lambda e: e.tensor_copy(
                mixt[64:128, 0:tn], of.h[p][0:64, t0:t0+tn]))(),
                reads=[of.b[p]], writes=[mixt_b])
            em.emit("dve", (lambda p=p, t0=t0, tn=tn: lambda e: e.tensor_scalar_mul(
                mixt2[0:64, 0:tn], mixt[0:64, 0:tn], consts["pm"][0][0:64, 2*p:2*p+1]))(),
                reads=[mixt_b, consts["pm"][1]], writes=[mixt2_b])
            em.emit("dve", (lambda p=p, t0=t0, tn=tn: lambda e: e.tensor_scalar_mul(
                mixt2[64:128, 0:tn], mixt[64:128, 0:tn], consts["pm"][0][64:128, 2*p:2*p+1]))(),
                reads=[mixt_b, consts["pm"][1]], writes=[mixt2_b])
            em.emit("dve", (lambda p=p, t0=t0, tn=tn: lambda e: e.scalar_tensor_tensor(
                of2.h[p][0:64, t0:t0+tn], of.h[p][0:64, t0:t0+tn],
                consts["pm"][0][0:64, 2*p+1:2*p+2],
                mixt2[0:64, 0:tn], ALU.mult, ALU.add))(),
                reads=[of.b[p], consts["pm"][1], mixt2_b], writes=[of2.b[p]])
            em.emit("dve", (lambda p=p, t0=t0, tn=tn: lambda e: e.scalar_tensor_tensor(
                of2.h[p][64:128, t0:t0+tn], of.h[p][64:128, t0:t0+tn],
                consts["pm"][0][64:128, 2*p+1:2*p+2],
                mixt2[64:128, 0:tn], ALU.mult, ALU.add))(),
                reads=[of.b[p], consts["pm"][1], mixt2_b], writes=[of2.b[p]])
    def ep_o(mi, ps, ps_b, t0, tn):
        em.emit_dma("sp", "xres", (lambda mi=mi, t0=t0, tn=tn: lambda e: e.dma_start(
            out=xe.h[mi][:, 0:tn], in_=xs[mi*128:(mi+1)*128, HALO+t0:HALO+t0+tn]))(),
            writes=[xe.b[mi]])
        em.close_batch("xres")
        em.emit("dve", (lambda mi=mi, ps=ps, t0=t0, tn=tn: lambda e: e.tensor_add(
            xout[:, 0:tn], ps[:, 0:tn], xe.h[mi][:, 0:tn]))(),
            reads=[ps_b, xe.b[mi]], writes=[xout_b])
        out_toks.append(em.emit_dma("sp", "out", (lambda mi=mi, t0=t0, tn=tn: lambda e: e.dma_start(
            out=xo[mi*128:(mi+1)*128, t0:t0+tn], in_=xout[:, 0:tn]))(), reads=[xout_b]))
        em.close_batch("out")
    build_project(em, of2, WoT, 1024, 1024, n, pp, wbufs, ep_o, mgroup_chunks=2)
    em.finalize(out_toks)
    return nc

def rope_tables_l2(pos, width):
    HD = 64
    inv = 1.0 / 10000.0 ** (np.arange(0, HD, 2, dtype=np.float64) / HD)
    f = pos[None, :].astype(np.float64) * inv[:, None]
    cos, sin = np.cos(f), np.sin(f)
    cT = np.zeros((128, width), np.float32)
    sT = np.zeros((128, width), np.float32)
    for b0 in (0, 64):
        cT[b0:b0+32] = cos; cT[b0+32:b0+64] = cos
        sT[b0:b0+32] = sin; sT[b0+32:b0+64] = -sin
    return cT.astype(BF16NP), sT.astype(BF16NP)

def l2a_masks(pos_q, pos_k):
    n = len(pos_q)
    m = np.zeros((128, 16, 3, 128), np.float32)
    for qt in range(16):
        for j in range(3):
            kidx = (qt + j) * 128 + np.arange(128)
            kp = pos_k[kidx]
            qp = pos_q[qt*128:(qt+1)*128]
            dd = qp[None, :] - kp[:, None]
            keep = (dd >= 0) & (dd < 256) & (kp[:, None] >= 0)
            m[:, qt, j, :] = 0.0
            m[:, qt, j, :] = keep.astype(np.float32)[:, 0:128].T if False else keep.astype(np.float32)
    return np.ascontiguousarray(m.reshape(128, 16*384)).astype(BF16NP)

def l2a_inputs(xe_shard, pos_q, pos_k, ins):
    cq, sq = rope_tables_l2(pos_q, len(pos_q))
    ck, sk = rope_tables_l2(np.maximum(pos_k, 0), len(pos_k))
    qg = np.zeros((128, 8), np.float32)
    g = np.asarray(ins["L2_attn_q_gain"], np.float32)
    for ci in range(8):
        qg[0:64, ci] = g[2*ci]
        qg[64:128, ci] = g[2*ci+1]
    pm = np.zeros((128, 16), np.float32)
    pmx = np.asarray(ins["L2_attn_pair_mix"], np.float32)
    for p in range(8):
        pm[0:64, 2*p] = pmx[p, 0, 1]
        pm[0:64, 2*p+1] = pmx[p, 0, 0]
        pm[64:128, 2*p] = pmx[p, 1, 0]
        pm[64:128, 2*p+1] = pmx[p, 1, 1]
    return {
        "xe": np.ascontiguousarray(xe_shard, np.float32),
        "WqT": prep_wT(ins["L2_attn_Wq"]), "WkT": prep_wT(ins["L2_attn_Wk"]),
        "WvT": prep_wT(ins["L2_attn_Wv"]), "WoT": prep_wT(ins["L2_attn_Wo"]),
        "cq": cq, "sq_": sq, "ck": ck, "sk": sk,
        "perm": swap_perm(), "ident": np.eye(128, dtype=BF16NP),
        "e2": np.ascontiguousarray(np.kron(np.eye(2), np.ones((64, 1))).astype(np.float32)),
        "e2t": np.ascontiguousarray(np.kron(np.eye(2), np.ones((1, 64))).astype(np.float32)),
        "qg": qg, "pm": pm,
        "msk": l2a_masks(pos_q, pos_k),
        "eps6": np.full((128, 1), 1e-6, np.float32),
    }

def _mk64(em, name, k, n):
    import types as _t
    h = [em.sb(f"{name}{i}", [64, n], BF16) for i in range(k)]
    b = [Buf(f"{name}{i}") for i in range(k)]
    return _t.SimpleNamespace(h=h, b=b, nch=k, ntok=n)


def build_l2b1(ntok_core=PT):
    n = ntok_core
    nc = bass.Bass("TRN2", target_bir_lowering=False)
    xs = nc.dram_tensor("xs", [D, n], F32, kind="ExternalInput").ap()
    WQT = nc.dram_tensor("WQT", [1024, 1024], F32, kind="ExternalInput").ap()
    WKT = nc.dram_tensor("WKT", [1024, 1024], F32, kind="ExternalInput").ap()
    WiT = nc.dram_tensor("WiT", [1024, 1024], F32, kind="ExternalInput").ap()
    qo = nc.dram_tensor("qo", [D, n], BF16, kind="ExternalOutput").ap()
    ko = nc.dram_tensor("ko", [D, n], BF16, kind="ExternalOutput").ap()
    iqo = nc.dram_tensor("iqo", [D, n], BF16, kind="ExternalOutput").ap()
    iko = nc.dram_tensor("iko", [D, n], BF16, kind="ExternalOutput").ap()
    sso = nc.dram_tensor("sso", [D, 2], F32, kind="ExternalOutput").ap()
    em = Em(nc)
    x = Chunked(em, "x", NCH, 512, F32)
    xn = Chunked(em, "xn", NCH, n, BF16)
    Q = Chunked(em, "Q", NCH, n, BF16)
    K = Chunked(em, "K", NCH, n, BF16)
    iQ = Chunked(em, "iQ", NCH, n, BF16)
    ss = em.sb("ss", [128, 16], F32); ss_b = Buf("ss")
    misc = setup_misc(em)
    wb1 = em.sb("wbufA", [128, 4096], BF16)
    wb2 = em.sb("wbufB", [128, 4096], BF16)
    wbufs = [(wb1, Buf("wbufA")), (wb2, Buf("wbufB"))]
    pp = PsumPool(em, n=8, width=512)
    out_toks = []
    for t0c in range(0, n, 512):
        for ci in range(NCH):
            em.emit_dma("sp", "xin", (lambda ci=ci, t0c=t0c: lambda e: e.dma_start(
                out=x.h[ci][:, :], in_=xs[ci*128:(ci+1)*128, t0c:t0c+512]))(), writes=[x.b[ci]])
        em.close_batch("xin")
        build_layernorm(em, x, xn, pp, misc, 512, out_col0=t0c)
    def mk_ep(dst, out_h, scol):
        def ep(mi, ps, ps_b, t0, tn):
            em.emit("act", (lambda mi=mi, ps=ps, t0=t0, tn=tn: lambda e: e.activation(
                dst.h[mi][:, t0:t0+tn], ps[:, 0:tn], AF.Identity))(),
                reads=[ps_b], writes=[dst.b[mi]])
            out_toks.append(em.emit_dma("sp", "out", (lambda mi=mi, t0=t0, tn=tn: lambda e: e.dma_start(
                out=out_h[mi*128:(mi+1)*128, t0:t0+tn], in_=dst.h[mi][:, t0:t0+tn]))(),
                reads=[dst.b[mi]]))
            em.close_batch("out")
        return ep
    build_project(em, xn, WQT, 1024, 1024, n, pp, wbufs, mk_ep(Q, qo, 0), mgroup_chunks=4)
    build_project(em, xn, WKT, 1024, 1024, n, pp, wbufs, mk_ep(K, ko, 1), mgroup_chunks=4)
    build_project(em, Q, WiT, 1024, 1024, n, pp, wbufs, mk_ep(iQ, iqo, 2), mgroup_chunks=4)
    iK = Q
    build_project(em, K, WiT, 1024, 1024, n, pp, wbufs, mk_ep(iK, iko, 3), mgroup_chunks=4)
    for ci in range(NCH):
        em.emit("dve", (lambda ci=ci: lambda e: e.reduce_sum(
            ss[:, 0:1], K.h[ci][:, 0:n], axis=AX.X))(),
            reads=[K.b[ci]], writes=[ss_b])
        em.emit("dve", (lambda ci=ci: lambda e: e.reduce_sum(
            ss[:, 1:2], iK.h[ci][:, 0:n], axis=AX.X))(),
            reads=[iK.b[ci]], writes=[ss_b])
        out_toks.append(em.emit_dma("sp", "out", (lambda ci=ci: lambda e: e.dma_start(
            out=sso[ci*128:(ci+1)*128, 0:2], in_=ss[:, 0:2]))(), reads=[ss_b]))
        em.close_batch("out")
    em.finalize(out_toks)
    return nc

def build_l2b2(ntok_core=PT):
    n = ntok_core
    nc = bass.Bass("TRN2", target_bir_lowering=False)
    xs = nc.dram_tensor("xs", [D, n], F32, kind="ExternalInput").ap()
    Qh = nc.dram_tensor("Qh", [D, n], BF16, kind="ExternalInput").ap()
    Kh = nc.dram_tensor("Kh", [D, n], BF16, kind="ExternalInput").ap()
    iQh = nc.dram_tensor("iQh", [D, n], BF16, kind="ExternalInput").ap()
    iKh = nc.dram_tensor("iKh", [D, n], BF16, kind="ExternalInput").ap()
    pksh = nc.dram_tensor("pksh", [128, 64], F32, kind="ExternalInput").ap()
    vf = nc.dram_tensor("vf", [1024, 512], F32, kind="ExternalInput").ap()
    wf = nc.dram_tensor("wf", [1024, 512], F32, kind="ExternalInput").ap()
    vi = nc.dram_tensor("vi", [1024, 512], F32, kind="ExternalInput").ap()
    wi = nc.dram_tensor("wi", [1024, 512], F32, kind="ExternalInput").ap()
    uf = nc.dram_tensor("uf", [512, 1024], F32, kind="ExternalInput").ap()
    ui = nc.dram_tensor("ui", [512, 1024], F32, kind="ExternalInput").ap()
    rw1T = nc.dram_tensor("rw1T", [1024, 1024], F32, kind="ExternalInput").ap()
    rw2T = nc.dram_tensor("rw2T", [1024, 128], F32, kind="ExternalInput").ap()
    WOT = nc.dram_tensor("WOT", [1024, 1024], F32, kind="ExternalInput").ap()
    xo = nc.dram_tensor("xo", [D, n], F32, kind="ExternalOutput").ap()
    em = Em(nc)
    x = Chunked(em, "x", NCH, 512, F32)
    QQ = Chunked(em, "QQ", NCH, n, BF16)
    KK = Chunked(em, "KK", NCH, n, BF16)
    prodF = _mk64(em, "prodF", 8, n)
    prodI = _mk64(em, "prodI", 8, n)
    import types as _t
    hR = _t.SimpleNamespace(h=KK.h, b=KK.b, nch=8, ntok=n)
    outp = _t.SimpleNamespace(h=QQ.h, b=QQ.b, nch=8, ntok=n)
    consts = load_consts(em, nc, [
        ("rcol", [128, 1], F32), ("alpha", [128, 1], F32),
        ("rnv", [1, PT], BF16),
        ("bfp", [8, 1024], BF16), ("bip", [8, 1024], BF16),
        ("esel", [8, 512], BF16), ("ones8b", [8, 1], BF16),
    ])
    misc = setup_misc(em)
    em.close_batch("cst")
    yw = em.sb("yw", [64, PT], BF16); yw_b = Buf("yw")
    zz = em.sb("zz", [64, PT], BF16); zz_b = Buf("zz")
    cum = em.sb("cum", [64, PT], F32); cum_b = Buf("cum")
    carry = em.sb("carry", [64, 16], F32); carry_b = Buf("carry")
    scr64 = em.sb("scr64", [64, 512], BF16); scr64_b = Buf("scr64")
    wrow = em.sb("wrow", [128, 2 * PT], BF16); wrow_b = Buf("wrow")
    pksb = em.sb("pksb", [128, 64], BF16); pksb_b = Buf("pksb")
    xout = em.sb("xout", [128, 512], F32); xout_b = Buf("xout")
    wb1 = em.sb("wbufA", [128, 4096], BF16)
    wb2 = em.sb("wbufB", [128, 4096], BF16)
    wbufs = [(wb1, Buf("wbufA")), (wb2, Buf("wbufB"))]
    pp = PsumPool(em, n=6, width=512)
    out_toks = []
    em.emit_dma("pool", "cst2", (lambda: lambda e: e.dma_start(out=pksb[:, :], in_=pksh))(),
                writes=[pksb_b])
    em.close_batch("cst2")
    em.emit("dve", (lambda: lambda e: e.memset(zz[:, :], 0.0))(), writes=[zz_b])

    def load_chunked(dst, src_h, slot):
        for ci in range(NCH):
            em.emit_dma("pool", slot, (lambda ci=ci: lambda e: e.dma_start(
                out=dst.h[ci][:, :], in_=src_h[ci*128:(ci+1)*128, :]))(), writes=[dst.b[ci]])
        em.close_batch(slot)

    def router(srcH, off):
        load_chunked(QQ, srcH, "qin")
        def ep_h(mi, ps, ps_b, t0, tn):
            em.emit("act", (lambda mi=mi, ps=ps, t0=t0, tn=tn: lambda e: e.activation(
                hR.h[mi][:, t0:t0+tn], ps[:, 0:tn], AF.Gelu))(),
                reads=[ps_b], writes=[hR.b[mi]])
        build_project(em, QQ, rw1T, 1024, 1024, n, pp, wbufs, ep_h, mgroup_chunks=4)
        def ep_l(mi, ps, ps_b, t0, tn):
            em.emit("act", (lambda ps=ps, t0=t0, tn=tn: lambda e: e.activation(
                wrow[0:8, off+t0:off+t0+tn], ps[0:8, 0:tn], AF.Exp,
                bias=consts["rcol"][0][0:8, 0:1]))(),
                reads=[ps_b, consts["rcol"][1]], writes=[wrow_b])
        build_project(em, hR, rw2T, 1024, 128, n, pp, wbufs, ep_l, mgroup_chunks=1)
        for t0, tn in mm_tiles(n):
            ps_z, ps_z_b = pp.take1()
            em.emit("pe", (lambda t0=t0, tn=tn, ps_z=ps_z: lambda e: e.matmul(
                ps_z[0:1, 0:tn], consts["ones8b"][0][0:8, 0:1], wrow[0:8, off+t0:off+t0+tn],
                start=True, stop=True))(),
                reads=[consts["ones8b"][1], wrow_b], writes=[ps_z_b])
            em.emit("dve", (lambda tn=tn, ps_z=ps_z: lambda e: e.reciprocal(
                misc["r_a"][0][0:1, 0:tn], ps_z[0:1, 0:tn]))(),
                reads=[ps_z_b], writes=[misc["r_a"][1]])
            em.emit("dve", (lambda t0=t0, tn=tn: lambda e: e.tensor_mul(
                misc["r_b"][0][0:1, 0:tn], misc["r_a"][0][0:1, 0:tn],
                consts["rnv"][0][0:1, t0:t0+tn]))(),
                reads=[misc["r_a"][1], consts["rnv"][1]], writes=[misc["r_b"][1]])
            ps_bc, ps_bc_b = pp.take1()
            em.emit("pe", (lambda tn=tn, ps_bc=ps_bc: lambda e: e.matmul(
                ps_bc[0:8, 0:tn], misc["ones_row"][0][0:1, 0:8], misc["r_b"][0][0:1, 0:tn],
                start=True, stop=True))(),
                reads=[misc["ones_row"][1], misc["r_b"][1]], writes=[ps_bc_b])
            em.emit("dve", (lambda t0=t0, tn=tn, ps_bc=ps_bc: lambda e: e.tensor_mul(
                wrow[0:8, off+t0:off+t0+tn], wrow[0:8, off+t0:off+t0+tn], ps_bc[0:8, 0:tn]))(),
                reads=[wrow_b, ps_bc_b], writes=[wrow_b])
    router(Qh, 0)
    router(iQh, n)
    em.emit("dve", (lambda: lambda e: e.tensor_scalar_mul(
        wrow[0:8, n:2*n], wrow[0:8, n:2*n], consts["alpha"][0][0:8, 0:1]))(),
        reads=[wrow_b, consts["alpha"][1]], writes=[wrow_b])

    def load_slab(dst_wb, dst_b, src, ncols, krows=1024, slot="wslabA"):
        nj = krows // 128
        def f(e):
            sap = src[0:krows, 0:ncols].rearrange("(j p) m -> p j m", p=128)
            dap = dst_wb[:, 0:nj*ncols].rearrange("p (j m) -> p j m", m=ncols)
            return e.dma_start(out=dap, in_=sap)
        em.emit_dma("pool", slot, f, writes=[dst_b])
        em.close_batch(slot)

    def branch(QsrcH, KsrcH, vT, wT, prod, off, pcol):
        load_slab(wb2, wbufs[1][1], wT, 512, slot="wslabB")
        load_slab(wb1, wbufs[0][1], vT, 512, slot="wslabA")
        load_chunked(QQ, QsrcH, "qin")
        load_chunked(KK, KsrcH, "kin")
        for ex in range(8):
            pbase = (ex % 2) * 64
            for t0, tn in mm_tiles(n):
                ps_y, ps_y_b = pp.take1()
                def mk_y(ex=ex, t0=t0, tn=tn, ps_y=ps_y):
                    def f(e):
                        ins = None
                        for j in range(8):
                            ins = e.matmul(ps_y[0:64, 0:tn],
                                           wb2[:, j*512 + ex*64: j*512 + ex*64 + 64],
                                           KK.h[j][:, t0:t0+tn], start=(j == 0), stop=(j == 7))
                        return ins
                    return f
                em.emit("pe", mk_y(), reads=[wbufs[1][1]] + KK.b, writes=[ps_y_b])
                em.emit("act", (lambda t0=t0, tn=tn, ps_y=ps_y: lambda e: e.activation(
                    yw[0:64, t0:t0+tn], ps_y[0:64, 0:tn], AF.Identity))(),
                    reads=[ps_y_b], writes=[yw_b])
            ps_c, ps_c_b = pp.take1()
            def mk_c(ex=ex, ps_c=ps_c, pcol=pcol):
                def f(e):
                    ins = None
                    for j in range(32):
                        ins = e.matmul(ps_c[0:64, 0:1],
                                       wb2[:, (j % 8)*512 + ex*64: (j % 8)*512 + ex*64 + 64],
                                       pksb[:, j*2+pcol:j*2+pcol+1], start=(j == 0), stop=(j == 31))
                    return ins
                return f
            em.emit("pe", mk_c(), reads=[wbufs[1][1], pksb_b], writes=[ps_c_b])
            em.emit("dve", (lambda ex=ex, ps_c=ps_c: lambda e: e.tensor_copy(
                carry[0:64, ex+8*(0 if pcol == 0 else 1):ex+8*(0 if pcol == 0 else 1)+1],
                ps_c[0:64, 0:1]))(),
                reads=[ps_c_b], writes=[carry_b])
            ccol = ex + 8 * (0 if pcol == 0 else 1)
            em.emit("dve", (lambda ccol=ccol: lambda e: e.tensor_tensor_scan(
                cum[0:64, 0:n], yw[0:64, 0:n], zz[0:64, 0:n],
                carry[0:64, ccol:ccol+1], ALU.add, ALU.add))(),
                reads=[yw_b, carry_b, zz_b], writes=[cum_b])
            for t0, tn in mm_tiles(n):
                ps_x, ps_x_b = pp.take1()
                def mk_x(ex=ex, t0=t0, tn=tn, ps_x=ps_x):
                    def f(e):
                        ins = None
                        for j in range(8):
                            ins = e.matmul(ps_x[0:64, 0:tn],
                                           wb1[:, j*512 + ex*64: j*512 + ex*64 + 64],
                                           QQ.h[j][:, t0:t0+tn], start=(j == 0), stop=(j == 7))
                        return ins
                    return f
                em.emit("pe", mk_x(), reads=[wbufs[0][1]] + QQ.b, writes=[ps_x_b])
                em.emit("dve", (lambda t0=t0, tn=tn, ps_x=ps_x: lambda e: e.tensor_mul(
                    scr64[0:64, 0:tn], ps_x[0:64, 0:tn], cum[0:64, t0:t0+tn]))(),
                    reads=[ps_x_b, cum_b], writes=[scr64_b])
                ps_w, ps_w_b = pp.take1()
                em.emit("pe", (lambda ex=ex, t0=t0, tn=tn, ps_w=ps_w, off=off: lambda e: e.matmul(
                    ps_w[0:64, 0:tn], consts["esel"][0][0:8, ex*64:(ex+1)*64],
                    wrow[0:8, off+t0:off+t0+tn], start=True, stop=True))(),
                    reads=[consts["esel"][1], wrow_b], writes=[ps_w_b])
                em.emit("dve", (lambda ex=ex, t0=t0, tn=tn, ps_w=ps_w, prod=prod: lambda e: e.tensor_mul(
                    prod.h[ex][0:64, t0:t0+tn], scr64[0:64, 0:tn], ps_w[0:64, 0:tn]))(),
                    reads=[scr64_b, ps_w_b], writes=[prod.b[ex]])
    import os as _os
    _stop = _os.environ.get("L2STOP", "")
    if _stop == "r":
        toks = [Tok(s, em.cnt[s]) for s in list(em.cnt) if s.startswith("d_")]
        em.finalize(toks + [em.barrier_tok("dve"), em.barrier_tok("pe"), em.barrier_tok("act")])
        return nc
    branch(Qh, Kh, vf, wf, prodF, 0, 0)
    if _stop == "b1":
        toks = [Tok(s, em.cnt[s]) for s in list(em.cnt) if s.startswith("d_")]
        em.finalize(toks + [em.barrier_tok("dve"), em.barrier_tok("pe"), em.barrier_tok("act")])
        return nc
    branch(iQh, iKh, vi, wi, prodI, n, 1)
    if _stop == "b2":
        toks = [Tok(s, em.cnt[s]) for s in list(em.cnt) if s.startswith("d_")]
        em.finalize(toks + [em.barrier_tok("dve"), em.barrier_tok("pe"), em.barrier_tok("act")])
        return nc
    for mi in range(8):
        def mk_lu(mi=mi):
            def f(e):
                sap = uf[0:512, mi*128:(mi+1)*128].rearrange("(a p) m -> p a m", p=64)
                dap = wb1[0:64, 0:1024].rearrange("p (a m) -> p a m", m=128)
                return e.dma_start(out=dap, in_=sap)
            return f
        em.emit_dma("pool", "wslabA", mk_lu(), writes=[wbufs[0][1]])
        em.close_batch("wslabA")
        def mk_li(mi=mi):
            def f(e):
                sap = ui[0:512, mi*128:(mi+1)*128].rearrange("(a p) m -> p a m", p=64)
                dap = wb1[0:64, 1024:2048].rearrange("p (a m) -> p a m", m=128)
                return e.dma_start(out=dap, in_=sap)
            return f
        em.emit_dma("pool", "wslabA", mk_li(), writes=[wbufs[0][1]])
        em.close_batch("wslabA")
        for t0, tn in mm_tiles(n):
            ps, ps_b = pp.take1()
            def mk_u(mi=mi, t0=t0, tn=tn, ps=ps):
                def f(e):
                    ins = None
                    for br in range(2):
                        prod = prodF if br == 0 else prodI
                        for ex in range(8):
                            ins = e.matmul(ps[:, 0:tn],
                                           wb1[0:64, br*1024 + ex*128: br*1024 + (ex+1)*128],
                                           prod.h[ex][0:64, t0:t0+tn],
                                           start=(br == 0 and ex == 0),
                                           stop=(br == 1 and ex == 7))
                    return ins
                return f
            em.emit("pe", mk_u(), reads=[wbufs[0][1], wrow_b]
                    + prodF.b + prodI.b, writes=[ps_b])
            ps2, ps2_b = pp.take1()
            def mk_b(mi=mi, t0=t0, tn=tn, ps2=ps2):
                def f(e):
                    bcol = consts["bfp"][0][0:8, mi*128:(mi+1)*128]
                    e.matmul(ps2[:, 0:tn], bcol, wrow[0:8, t0:t0+tn], start=True, stop=False)
                    bcol2 = consts["bip"][0][0:8, mi*128:(mi+1)*128]
                    return e.matmul(ps2[:, 0:tn], bcol2, wrow[0:8, n+t0:n+t0+tn],
                                    start=False, stop=True)
                return f
            em.emit("pe", mk_b(), reads=[consts["bfp"][1], consts["bip"][1], wrow_b],
                    writes=[ps2_b])
            em.emit("act", (lambda mi=mi, ps=ps, t0=t0, tn=tn: lambda e: e.activation(
                outp.h[mi][:, t0:t0+tn], ps[:, 0:tn], AF.Identity))(),
                reads=[ps_b], writes=[outp.b[mi]])
            em.emit("dve", (lambda mi=mi, ps2=ps2, t0=t0, tn=tn: lambda e: e.tensor_add(
                outp.h[mi][:, t0:t0+tn], outp.h[mi][:, t0:t0+tn], ps2[:, 0:tn]))(),
                reads=[ps2_b, outp.b[mi]], writes=[outp.b[mi]])
    if _stop == "u":
        toks = [Tok(s, em.cnt[s]) for s in list(em.cnt) if s.startswith("d_")]
        em.finalize(toks + [em.barrier_tok("dve"), em.barrier_tok("pe"), em.barrier_tok("act")])
        return nc
    def ep_o(mi, ps, ps_b, t0, tn):
        em.emit_dma("sp", "xres", (lambda mi=mi, t0=t0, tn=tn: lambda e: e.dma_start(
            out=x.h[mi][:, 0:tn], in_=xs[mi*128:(mi+1)*128, t0:t0+tn]))(),
            writes=[x.b[mi]])
        em.close_batch("xres")
        em.emit("dve", (lambda mi=mi, ps=ps, t0=t0, tn=tn: lambda e: e.tensor_add(
            xout[:, 0:tn], ps[:, 0:tn], x.h[mi][:, 0:tn]))(),
            reads=[ps_b, x.b[mi]], writes=[xout_b])
        out_toks.append(em.emit_dma("sp", "out", (lambda mi=mi, t0=t0, tn=tn: lambda e: e.dma_start(
            out=xo[mi*128:(mi+1)*128, t0:t0+tn], in_=xout[:, 0:tn]))(), reads=[xout_b]))
        em.close_batch("out")
    build_project(em, outp, WOT, 1024, 1024, n, pp, wbufs, ep_o, mgroup_chunks=2)
    em.finalize(out_toks)
    return nc

def l2b1_inputs(x5T_shard, ins):
    return {"xs": np.ascontiguousarray(x5T_shard, np.float32),
            "WQT": prep_wT(ins["L2_ffn_W_Q"]), "WKT": prep_wT(ins["L2_ffn_W_K"]),
            "WiT": prep_wT(ins["L2_ffn_W_inv"])}

def l2b2_inputs(x5T_shard, Q, K, iQ, iK, pks_sel, gpos, ins):
    raise NotImplementedError("use l2b2_inputs_v2")

def l2b2_inputs_v2(x5T_shard, Q, K, iQ, iK, prev_sums, gpos, ins):
    pk = np.concatenate([np.asarray(p, np.float32) for p in prev_sums], axis=0)
    pksh = np.zeros((128, 64), np.float32)
    for j in range(32):
        pksh[:, j*2:(j+1)*2] = pk[j*128:(j+1)*128, :]
    al = float(np.asarray(ins["L2_ffn_alpha_bi"]))
    rcol = np.zeros((128, 1), np.float32)
    rcol[0:8, 0] = np.asarray(ins["L2_ffn_router_b2"], np.float32) + np.asarray(ins["L2_ffn_expert_bias"], np.float32)
    esel = np.zeros((8, 512), np.float32)
    for ex in range(8):
        esel[ex, ex*64:(ex+1)*64] = 1.0
    def vpack(a):
        a = np.asarray(a, np.float32)
        return np.ascontiguousarray(a.transpose(1, 0, 2).reshape(1024, 512))
    def upack(a):
        a = np.asarray(a, np.float32)
        return np.ascontiguousarray(a.transpose(0, 2, 1).reshape(512, 1024))
    rw2 = np.zeros((1024, 128), np.float32)
    rw2[:, 0:8] = prep_wT(ins["L2_ffn_router_w2"])
    return {
        "xs": np.ascontiguousarray(x5T_shard, np.float32),
        "Qh": Q, "Kh": K, "iQh": iQ, "iKh": iK,
        "pksh": pksh,
        "vf": vpack(ins["L2_ffn_V_fwd"]), "wf": vpack(ins["L2_ffn_W_fwd"]),
        "vi": vpack(ins["L2_ffn_W_inv_exp"]), "wi": vpack(ins["L2_ffn_V_inv"]),
        "uf": upack(ins["L2_ffn_U_fwd"]), "ui": upack(ins["L2_ffn_U_inv"]),
        "rw1T": prep_wT(ins["L2_ffn_router_w1"]), "rw2T": rw2,
        "WOT": prep_wT(ins["L2_ffn_W_O"]),
        "rcol": rcol, "alpha": np.full((128, 1), al, np.float32),
        "rnv": (1.0 / (np.asarray(gpos, np.float64) + 1.0)).astype(np.float32)[None, :].astype(BF16NP),
        "bfp": np.ascontiguousarray(np.asarray(ins["L2_ffn_b_fwd"], np.float32)).astype(BF16NP),
        "bip": np.ascontiguousarray(np.asarray(ins["L2_ffn_b_inv"], np.float32)).astype(BF16NP),
        "esel": esel.astype(BF16NP), "ones8b": np.ones((8, 1), BF16NP),
    }


def build_l3f(ntok_core=PT):
    n = ntok_core
    TT = 1024
    nc = bass.Bass("TRN2", target_bir_lowering=False)
    xs = nc.dram_tensor("xs", [D, n], F32, kind="ExternalInput").ap()
    rwT = nc.dram_tensor("rwT", [1024, 128], F32, kind="ExternalInput").ap()
    sw1c = nc.dram_tensor("sw1c", [1024, 4096], F32, kind="ExternalInput").ap()
    sw2c = nc.dram_tensor("sw2c", [4096, 1024], F32, kind="ExternalInput").ap()
    w1c = nc.dram_tensor("w1c", [1024, 16384], F32, kind="ExternalInput").ap()
    w2c = nc.dram_tensor("w2c", [16384, 1024], F32, kind="ExternalInput").ap()
    xo = nc.dram_tensor("xo", [D, n], F32, kind="ExternalOutput").ap()
    em = Em(nc)
    x = Chunked(em, "x", NCH, 512, F32)
    xn = Chunked(em, "xn", NCH, n, BF16)
    shc = Chunked(em, "shc", 32, TT, BF16)
    import types as _t
    he = _t.SimpleNamespace(h=shc.h[0:16], b=shc.b[0:16], nch=16, ntok=TT)
    outacc = Chunked(em, "outacc", NCH, TT, F32)
    consts = load_consts(em, nc, [
        ("rcol", [128, 1], F32),
        ("sb1c", [128, 32], F32), ("sb2a", [128, 8], F32), ("sb2b", [128, 8], F32),
        ("b1c", [128, 128], F32),
        ("b2p", [8, 1024], F32),
        ("esel", [8, 1024], F32),
        ("iota8", [128, 8], F32), ("idf", [128, 128], F32)])
    misc = setup_misc(em)
    em.close_batch("cst")
    halfb = em.sb("halfb", [128, 8], F32); halfb_b = Buf("halfb")
    lgt = em.sb("lgt", [8, TT], F32); lgt_b = Buf("lgt")
    lg_tm = em.sb("lg_tm", [128, 8], F32); lg_tm_b = Buf("lg_tm")
    m8 = em.sb("m8", [128, 8], F32); m8_b = Buf("m8")
    idx8 = em.sb("idx8", [128, 8], U32); idx8_b = Buf("idx8")
    idxf = em.sb("idxf", [128, 2], F32); idxf_b = Buf("idxf")
    pp12 = em.sb("pp12", [128, 4], F32); pp12_b = Buf("pp12")
    ctm = em.sb("ctm", [128, 8], F32); ctm_b = Buf("ctm")
    cscr = em.sb("cscr", [128, 8], F32); cscr_b = Buf("cscr")
    comb = em.sb("comb", [8, TT], F32); comb_b = Buf("comb")
    xout = em.sb("xout", [128, 512], F32); xout_b = Buf("xout")
    wb1 = em.sb("wbufA", [128, 4096], BF16)
    wb2 = em.sb("wbufB", [128, 4096], BF16)
    wbufs = [(wb1, Buf("wbufA")), (wb2, Buf("wbufB"))]
    pp = PsumPool(em, n=8, width=512)
    out_toks = []
    em.emit("dve", (lambda: lambda e: e.tensor_add(
        halfb[:, :], consts["sb2a"][0][:, :], consts["sb2b"][0][:, :]))(),
        reads=[consts["sb2a"][1], consts["sb2b"][1]], writes=[halfb_b])
    em.emit("dve", (lambda: lambda e: e.tensor_scalar_mul(halfb[:, :], halfb[:, :], 0.5))(),
        reads=[halfb_b], writes=[halfb_b])
    for t0c in range(0, n, 512):
        for ci in range(NCH):
            em.emit_dma("sp", "xin", (lambda ci=ci, t0c=t0c: lambda e: e.dma_start(
                out=x.h[ci][:, :], in_=xs[ci*128:(ci+1)*128, t0c:t0c+512]))(), writes=[x.b[ci]])
        em.close_batch("xin")
        build_layernorm(em, x, xn, pp, misc, 512, out_col0=t0c)
    for tt in range(0, n, TT):
        xoffs = [tt] * 8
        def ep_sh(mi, ps, ps_b, t0, tn):
            em.emit("act", (lambda mi=mi, ps=ps, t0=t0, tn=tn: lambda e: e.activation(
                shc.h[mi][:, t0:t0+tn], ps[:, 0:tn], AF.Gelu,
                bias=consts["sb1c"][0][:, mi:mi+1]))(),
                reads=[ps_b, consts["sb1c"][1]], writes=[shc.b[mi]])
        build_project(em, xn, sw1c, 1024, 4096, TT, pp, wbufs, ep_sh,
                      mgroup_chunks=4, xoff=xoffs)
        def ep_sh2(mi, ps, ps_b, t0, tn):
            em.emit("act", (lambda mi=mi, ps=ps, t0=t0, tn=tn: lambda e: e.activation(
                outacc.h[mi][:, t0:t0+tn], ps[:, 0:tn], AF.Identity,
                bias=halfb[:, mi:mi+1], scale=0.5))(),
                reads=[ps_b, halfb_b], writes=[outacc.b[mi]])
        build_project(em, shc, sw2c, 4096, 1024, TT, pp, wbufs, ep_sh2, mgroup_chunks=1)
        def ep_r(mi, ps, ps_b, t0, tn):
            em.emit("act", (lambda ps=ps, t0=t0, tn=tn: lambda e: e.activation(
                lgt[0:8, t0:t0+tn], ps[0:8, 0:tn], AF.Identity,
                bias=consts["rcol"][0][0:8, 0:1]))(),
                reads=[ps_b, consts["rcol"][1]], writes=[lgt_b])
        build_project(em, xn, rwT, 1024, 128, TT, pp, wbufs, ep_r,
                      mgroup_chunks=1, xoff=xoffs)
        for q in range(TT // 128):
            ps_t, ps_t_b = pp.take1()
            em.emit("pe", (lambda q=q, ps_t=ps_t: lambda e: e.transpose(
                ps_t[:, 0:8], lgt[0:8, q*128:(q+1)*128], consts["idf"][0][0:8, 0:8]))(),
                reads=[lgt_b, consts["idf"][1]], writes=[ps_t_b])
            em.emit("dve", (lambda ps_t=ps_t: lambda e: e.tensor_copy(
                lg_tm[:, 0:8], ps_t[:, 0:8]))(),
                reads=[ps_t_b], writes=[lg_tm_b])
            em.emit("dve", (lambda: lambda e: e.max(m8[:, 0:8], lg_tm[:, 0:8]))(),
                reads=[lg_tm_b], writes=[m8_b])
            em.emit("dve", (lambda: lambda e: e.max_index(idx8[:, 0:8], m8[:, 0:8], lg_tm[:, 0:8]))(),
                reads=[m8_b, lg_tm_b], writes=[idx8_b])
            em.emit("dve", (lambda: lambda e: e.tensor_copy(idxf[:, 0:2], idx8[:, 0:2]))(),
                reads=[idx8_b], writes=[idxf_b])
            em.emit("dve", (lambda: lambda e: e.tensor_sub(pp12[:, 0:1], m8[:, 1:2], m8[:, 0:1]))(),
                reads=[m8_b], writes=[pp12_b])
            em.emit("act", (lambda: lambda e: e.activation(pp12[:, 1:2], pp12[:, 0:1], AF.Exp))(),
                reads=[pp12_b], writes=[pp12_b])
            em.emit("dve", (lambda: lambda e: e.tensor_scalar_add(pp12[:, 2:3], pp12[:, 1:2], 1.0))(),
                reads=[pp12_b], writes=[pp12_b])
            em.emit("dve", (lambda: lambda e: e.reciprocal(pp12[:, 2:3], pp12[:, 2:3]))(),
                reads=[pp12_b], writes=[pp12_b])
            em.emit("dve", (lambda: lambda e: e.tensor_mul(pp12[:, 3:4], pp12[:, 1:2], pp12[:, 2:3]))(),
                reads=[pp12_b], writes=[pp12_b])
            em.emit("dve", (lambda: lambda e: e.tensor_tensor(
                ctm[:, 0:8], idxf[:, 0:1].broadcast_to([128, 8]), consts["iota8"][0][:, 0:8],
                ALU.is_equal))(),
                reads=[idxf_b, consts["iota8"][1]], writes=[ctm_b])
            em.emit("dve", (lambda: lambda e: e.tensor_scalar_mul(
                ctm[:, 0:8], ctm[:, 0:8], pp12[:, 2:3]))(),
                reads=[ctm_b, pp12_b], writes=[ctm_b])
            em.emit("dve", (lambda: lambda e: e.tensor_tensor(
                cscr[:, 0:8], idxf[:, 1:2].broadcast_to([128, 8]), consts["iota8"][0][:, 0:8],
                ALU.is_equal))(),
                reads=[idxf_b, consts["iota8"][1]], writes=[cscr_b])
            em.emit("dve", (lambda: lambda e: e.scalar_tensor_tensor(
                ctm[:, 0:8], cscr[:, 0:8], pp12[:, 3:4], ctm[:, 0:8], ALU.mult, ALU.add))(),
                reads=[cscr_b, pp12_b, ctm_b], writes=[ctm_b])
            ps_t2, ps_t2_b = pp.take1()
            em.emit("pe", (lambda q=q, ps_t2=ps_t2: lambda e: e.transpose(
                ps_t2[0:8, 0:128], ctm[:, 0:8], consts["idf"][0][:, :]))(),
                reads=[ctm_b, consts["idf"][1]], writes=[ps_t2_b])
            em.emit("dve", (lambda q=q, ps_t2=ps_t2: lambda e: e.tensor_copy(
                comb[0:8, q*128:(q+1)*128], ps_t2[0:8, 0:128]))(),
                reads=[ps_t2_b], writes=[comb_b])
        for ex in range(8):
            def ep_e(mi, ps, ps_b, t0, tn, ex=ex):
                em.emit("act", (lambda mi=mi, ps=ps, t0=t0, tn=tn, ex=ex: lambda e: e.activation(
                    he.h[mi][:, t0:t0+tn], ps[:, 0:tn], AF.Gelu,
                    bias=consts["b1c"][0][:, ex*16+mi:ex*16+mi+1]))(),
                    reads=[ps_b, consts["b1c"][1]], writes=[he.b[mi]])
            build_project(em, xn, w1c[0:1024, ex*2048:(ex+1)*2048], 1024, 2048, TT,
                          pp, wbufs, ep_e, mgroup_chunks=4, xoff=xoffs)
            for t0, tn in mm_tiles(TT):
                ps_cb, ps_cb_b = pp.take1()
                em.emit("pe", (lambda ex=ex, t0=t0, tn=tn, ps_cb=ps_cb: lambda e: e.matmul(
                    ps_cb[:, 0:tn], consts["esel"][0][0:8, ex*128:(ex+1)*128],
                    comb[0:8, t0:t0+tn], start=True, stop=True))(),
                    reads=[consts["esel"][1], comb_b], writes=[ps_cb_b])
                for mi in range(16):
                    em.emit("dve", (lambda mi=mi, t0=t0, tn=tn, ps_cb=ps_cb: lambda e: e.tensor_mul(
                        he.h[mi][:, t0:t0+tn], he.h[mi][:, t0:t0+tn], ps_cb[:, 0:tn]))(),
                        reads=[he.b[mi], ps_cb_b], writes=[he.b[mi]])
            def ep_w2(mi, ps, ps_b, t0, tn, ex=ex):
                em.emit("dve", (lambda mi=mi, ps=ps, t0=t0, tn=tn: lambda e: e.tensor_add(
                    outacc.h[mi][:, t0:t0+tn], outacc.h[mi][:, t0:t0+tn], ps[:, 0:tn]))(),
                    reads=[ps_b, outacc.b[mi]], writes=[outacc.b[mi]])
            build_project(em, he, w2c[ex*2048:(ex+1)*2048, 0:1024], 2048, 1024, TT,
                          pp, wbufs, ep_w2, mgroup_chunks=2)
        for t0, tn in mm_tiles(TT):
            for mi in range(NCH):
                ps_b2, ps_b2_b = pp.take1()
                em.emit("pe", (lambda mi=mi, t0=t0, tn=tn, ps_b2=ps_b2: lambda e: e.matmul(
                    ps_b2[:, 0:tn], consts["b2p"][0][0:8, mi*128:(mi+1)*128],
                    comb[0:8, t0:t0+tn], start=True, stop=True))(),
                    reads=[consts["b2p"][1], comb_b], writes=[ps_b2_b])
                em.emit_dma("sp", "xres", (lambda mi=mi, t0=t0, tn=tn, tt=tt: lambda e: e.dma_start(
                    out=x.h[mi][:, 0:tn], in_=xs[mi*128:(mi+1)*128, tt+t0:tt+t0+tn]))(),
                    writes=[x.b[mi]])
                em.close_batch("xres")
                em.emit("dve", (lambda mi=mi, t0=t0, tn=tn, ps_b2=ps_b2: lambda e: e.tensor_add(
                    xout[:, 0:tn], ps_b2[:, 0:tn], x.h[mi][:, 0:tn]))(),
                    reads=[ps_b2_b, x.b[mi]], writes=[xout_b])
                em.emit("dve", (lambda mi=mi, t0=t0, tn=tn: lambda e: e.tensor_add(
                    xout[:, 0:tn], xout[:, 0:tn], outacc.h[mi][:, t0:t0+tn]))(),
                    reads=[xout_b, outacc.b[mi]], writes=[xout_b])
                out_toks.append(em.emit_dma("sp", "out", (lambda mi=mi, t0=t0, tn=tn, tt=tt: lambda e: e.dma_start(
                    out=xo[mi*128:(mi+1)*128, tt+t0:tt+t0+tn], in_=xout[:, 0:tn]))(), reads=[xout_b]))
                em.close_batch("out")
    em.finalize(out_toks)
    return nc

def l3f_inputs(xT_shard, ins):
    rw = np.zeros((1024, 128), np.float32)
    rw[:, 0:8] = np.asarray(ins["L3_ffn_router_w"], np.float32)
    rcol = np.zeros((128, 1), np.float32)
    rcol[0:8, 0] = np.asarray(ins["L3_ffn_router_b"], np.float32)
    sw1 = np.asarray(ins["L3_ffn_sw1"], np.float32)
    sw1c = np.concatenate([sw1[0], sw1[1]], axis=1)
    sb1 = np.asarray(ins["L3_ffn_sb1"], np.float32)
    sb1c = np.concatenate([sb1[0], sb1[1]])
    sw2 = np.asarray(ins["L3_ffn_sw2"], np.float32)
    sw2c = np.concatenate([sw2[0], sw2[1]], axis=0)
    w1 = np.asarray(ins["L3_ffn_w1"], np.float32)
    w1c = np.concatenate([w1[e] for e in range(8)], axis=1)
    b1 = np.asarray(ins["L3_ffn_b1"], np.float32)
    b1c = np.zeros((128, 128), np.float32)
    for ex in range(8):
        b1c[:, ex*16:(ex+1)*16] = b1[ex].reshape(16, 128).T
    w2 = np.asarray(ins["L3_ffn_w2"], np.float32)
    w2c = np.concatenate([w2[e] for e in range(8)], axis=0)
    esel = np.zeros((8, 1024), np.float32)
    for ex in range(8):
        esel[ex, ex*128:(ex+1)*128] = 1.0
    iota8 = np.tile(np.arange(8, dtype=np.float32)[None, :], (128, 1))
    return {
        "xs": np.ascontiguousarray(xT_shard, np.float32),
        "rwT": rw, "sw1c": sw1c, "sw2c": sw2c, "w1c": np.ascontiguousarray(w1c),
        "w2c": np.ascontiguousarray(w2c),
        "rcol": rcol, "sb1c": pack_bias(sb1c, 32),
        "sb2a": pack_bias(np.asarray(ins["L3_ffn_sb2"], np.float32)[0], 8),
        "sb2b": pack_bias(np.asarray(ins["L3_ffn_sb2"], np.float32)[1], 8),
        "b1c": b1c, "b2p": np.ascontiguousarray(np.asarray(ins["L3_ffn_b2"], np.float32)),
        "esel": esel, "iota8": np.ascontiguousarray(iota8),
        "idf": np.eye(128, dtype=np.float32),
    }


def build_l1attn(ntok_core=PT):
    n = ntok_core
    NQB = n // 128
    nc = bass.Bass("TRN2", target_bir_lowering=False)
    x2s = nc.dram_tensor("x2s", [D, n], F32, kind="ExternalInput").ap()
    qh = nc.dram_tensor("qh", [D, n], BF16, kind="ExternalInput").ap()
    kf = nc.dram_tensor("kf", [D, T], BF16, kind="ExternalInput").ap()
    vaugh = nc.dram_tensor("vaugh", [16 * 128, 64 * 65], BF16, kind="ExternalInput").ap()
    projT = nc.dram_tensor("projT", [1024, 1024], F32, kind="ExternalInput").ap()
    xo = nc.dram_tensor("xo", [D, n], F32, kind="ExternalOutput").ap()
    em = Em(nc)
    x = Chunked(em, "x", NCH, 512, F32)
    q = Chunked(em, "q", NCH, n, BF16)
    of = q
    consts = load_consts(em, nc, [("b16", [16, n], BF16), ("projb", [128, 8], F32)])
    misc = setup_misc(em)
    em.close_batch("cst")
    qaug = em.sb("qaug", [128, n], BF16); qaug_b = Buf("qaug")
    kaug = [em.sb(f"kaug{i}", [128, T], BF16) for i in range(2)]
    kaug_b = [Buf("kaug0"), Buf("kaug1")]
    vab = [em.sb(f"vab{i}", [128, 64 * 65], BF16) for i in range(2)]
    vab_b = [Buf("vab0"), Buf("vab1")]
    qm = em.sb("qm", [128, 16], F32); qm_b = Buf("qm")
    km = em.sb("km", [128, 64], F32); km_b = Buf("km")
    sel = em.sb("sel", [16, 64], F32); sel_b = Buf("sel")
    selm8 = em.sb("selm8", [16, 8], F32); selm8_b = Buf("selm8")
    expSH = [(em.sb(f"expS{i}", [128, 1024], BF16), Buf(f"expS{i}")) for i in range(4)]
    o_un = em.sb("o_un", [64, 512], BF16); o_un_b = Buf("o_un")
    r_row = em.sb("r_row", [1, 512], F32); r_row_b = Buf("r_row")
    xout = em.sb("xout", [128, 512], F32); xout_b = Buf("xout")
    wb1 = em.sb("wbufA", [128, 2048], BF16)
    wb2 = em.sb("wbufB", [128, 2048], BF16)
    wbufs = [(wb1, Buf("wbufA")), (wb2, Buf("wbufB"))]
    pp = PsumPool(em, n=2, width=512)
    ps_sH = [(em.ps(f"ps_s{i}", [128, 1024]), Buf(f"ps_s{i}")) for i in range(2)]
    ps_o = em.ps("ps_o", [128, 512]); ps_o_b = Buf("ps_o")
    ps_m = em.ps("ps_m", [128, 64]); ps_m_b = Buf("ps_m")
    out_toks = []
    for ci in range(NCH):
        em.emit_dma("pool", "qin", (lambda ci=ci: lambda e: e.dma_start(
            out=q.h[ci][:, :], in_=qh[ci*128:(ci+1)*128, :]))(), writes=[q.b[ci]])
    em.close_batch("qin")
    for ci in range(NCH):
        em.emit("dve", (lambda ci=ci: lambda e: e.tensor_reduce(
            qm[:, 0:NQB], q.h[ci][:, :].rearrange("p (a b) -> p a b", a=NQB, b=128),
            axis=AX.X, op=ALU.add))(),
            reads=[q.b[ci]], writes=[qm_b]) if ci == 0 else None
    for h in range(16):
        hp, base = h // 2, (h % 2) * 64
        kb, kb_i = kaug[h % 2], kaug_b[h % 2]
        va, va_i = vab[h % 2], vab_b[h % 2]
        em.emit_dma("pool", f"kin{h % 2}", (lambda h=h, kb=kb: lambda e: e.dma_start(
            out=kb[0:64, :], in_=kf[h*64:(h+1)*64, :]))(), writes=[kb_i])
        em.close_batch(f"kin{h % 2}")
        em.emit_dma("pool", f"vin{h % 2}", (lambda h=h, va=va: lambda e: e.dma_start(
            out=va[:, :], in_=vaugh[h*128:(h+1)*128, :]))(), writes=[va_i])
        em.close_batch(f"vin{h % 2}")
        em.emit("dve", (lambda hp=hp, base=base: lambda e: e.tensor_reduce(
            qm[0:64, 0:NQB],
            q.h[hp][base:base+64, :].rearrange("p (a b) -> p a b", a=NQB, b=128),
            axis=AX.X, op=ALU.add))(),
            reads=[q.b[hp]], writes=[qm_b])
        em.emit("dve", (lambda kb=kb: lambda e: e.tensor_reduce(
            km[0:64, 0:64], kb[0:64, :].rearrange("p (a b) -> p a b", a=64, b=128),
            axis=AX.X, op=ALU.add))(),
            reads=[kb_i], writes=[km_b])
        em.emit("pe", (lambda: lambda e: e.matmul(
            ps_m[0:NQB, 0:64], qm[0:64, 0:NQB], km[0:64, 0:64], start=True, stop=True))(),
            reads=[qm_b, km_b], writes=[ps_m_b])
        em.emit("dve", (lambda: lambda e: e.tensor_copy(sel[0:NQB, 0:64], ps_m[0:NQB, 0:64]))(),
            reads=[ps_m_b], writes=[sel_b])
        for rnd in range(4):
            em.emit("dve", (lambda: lambda e: e.max(selm8[0:NQB, 0:8], sel[0:NQB, 0:64]))(),
                reads=[sel_b], writes=[selm8_b])
            em.emit("dve", (lambda: lambda e: e.match_replace(
                sel[0:NQB, 0:64], selm8[0:NQB, 0:8], sel[0:NQB, 0:64], -1e30))(),
                reads=[selm8_b, sel_b], writes=[sel_b])
        em.emit("dve", (lambda: lambda e: e.tensor_single_scalar(
            sel[0:NQB, 0:64], sel[0:NQB, 0:64], -1e30, ALU.is_equal))(),
            reads=[sel_b], writes=[sel_b])
        em.emit("dve", (lambda kb=kb: lambda e: e.tensor_scalar(
            kb[64:64+NQB, 0:T].rearrange("p (a b) -> p a b", a=64, b=128),
            sel[0:NQB, 0:64].rearrange("p (a o) -> p a o", o=1).broadcast_to([NQB, 64, 128]),
            1.0, 1e9, ALU.subtract, ALU.mult))(),
            reads=[sel_b], writes=[kb_i])
        em.emit("dve", (lambda hp=hp, base=base: lambda e: e.tensor_copy(
            qaug[0:64, 0:n], q.h[hp][base:base+64, :]))(),
            reads=[q.b[hp]], writes=[qaug_b])
        em.emit("dve", (lambda: lambda e: e.tensor_copy(
            qaug[64:80, 0:n], consts["b16"][0][0:16, :]))(),
            reads=[consts["b16"][1]], writes=[qaug_b])
        for qt in range(n // 512):
            prev_av = []
            for g in range(16):
                cur_av = []
                for hf in range(2):
                    psS, psS_b = ps_sH[hf]
                    exS, exS_b = expSH[(g % 2) * 2 + hf]
                    def mk_s(g=g, qt=qt, kb=kb, hf=hf, psS=psS):
                        def f(e):
                            ins = None
                            for j in range(2):
                                kc = g * 4 + hf * 2 + j
                                ins = e.matmul(psS[:, j*512:(j+1)*512],
                                               kb[0:80, kc*128:(kc+1)*128],
                                               qaug[0:80, qt*512:(qt+1)*512],
                                               start=True, stop=True)
                            return ins
                        return f
                    em.emit("pe", mk_s(), reads=[kb_i, qaug_b], writes=[psS_b])
                    em.emit("act", (lambda psS=psS, exS=exS: lambda e: e.activation(
                        exS[:, 0:1024], psS[:, 0:1024], AF.Exp, scale=0.125))(),
                        reads=[psS_b], writes=[exS_b])
                    def mk_av(g=g, hf=hf, va=va, exS=exS):
                        def f(e):
                            ins = None
                            for j in range(2):
                                kc = g * 4 + hf * 2 + j
                                ins = e.matmul(ps_o[0:65, 0:512],
                                               va[:, kc*65:kc*65+65],
                                               exS[:, j*512:(j+1)*512],
                                               start=(kc == 0), stop=(kc == 63))
                            return ins
                        return f
                    cur_av.append((mk_av(), [va_i, exS_b]))
                for fn, rds in prev_av:
                    em.emit("pe", fn, reads=rds, writes=[ps_o_b])
                prev_av = cur_av
            for fn, rds in prev_av:
                em.emit("pe", fn, reads=rds, writes=[ps_o_b])
            em.emit("dve", (lambda: lambda e: e.reciprocal(r_row[0:1, 0:512], ps_o[64:65, 0:512]))(),
                reads=[ps_o_b], writes=[r_row_b])
            em.emit("dve", (lambda: lambda e: e.tensor_copy(o_un[0:64, 0:512], ps_o[0:64, 0:512]))(),
                reads=[ps_o_b], writes=[o_un_b])
            ps_bc, ps_bc_b = pp.take1()
            em.emit("pe", (lambda ps_bc=ps_bc: lambda e: e.matmul(
                ps_bc[0:64, 0:512], misc["ones_row"][0][0:1, 0:64], r_row[0:1, 0:512],
                start=True, stop=True))(),
                reads=[misc["ones_row"][1], r_row_b], writes=[ps_bc_b])
            em.emit("dve", (lambda hp=hp, base=base, qt=qt, ps_bc=ps_bc: lambda e: e.tensor_mul(
                of.h[hp][base:base+64, qt*512:(qt+1)*512], o_un[0:64, 0:512],
                ps_bc[0:64, 0:512]))(),
                reads=[o_un_b, ps_bc_b], writes=[of.b[hp]])
    import os as _os
    if _os.environ.get("L1DBG"):
        dbg = nc.dram_tensor("dbg", [1024, n], BF16, kind="ExternalOutput").ap()
        toks = []
        for ci in range(8):
            toks.append(em.emit_dma("sp", "dbgs", (lambda ci=ci: lambda e: e.dma_start(
                out=dbg[ci*128:(ci+1)*128, :], in_=of.h[ci][:, 0:n]))(), reads=[of.b[ci]]))
        em.close_batch("dbgs")
        for s in list(em.cnt):
            if s.startswith("d_"):
                toks.append(Tok(s, em.cnt[s]))
        em.finalize(toks)
        return nc
    def ep_o(mi, ps, ps_b, t0, tn):
        em.emit_dma("sp", "xres", (lambda mi=mi, t0=t0, tn=tn: lambda e: e.dma_start(
            out=x.h[mi][:, 0:tn], in_=x2s[mi*128:(mi+1)*128, t0:t0+tn]))(),
            writes=[x.b[mi]])
        em.close_batch("xres")
        em.emit("dve", (lambda mi=mi, ps=ps, t0=t0, tn=tn: lambda e: e.scalar_tensor_tensor(
            xout[:, 0:tn], ps[:, 0:tn], consts["projb"][0][:, mi:mi+1],
            x.h[mi][:, 0:tn], ALU.add, ALU.add))(),
            reads=[ps_b, consts["projb"][1], x.b[mi]], writes=[xout_b])
        out_toks.append(em.emit_dma("sp", "out", (lambda mi=mi, t0=t0, tn=tn: lambda e: e.dma_start(
            out=xo[mi*128:(mi+1)*128, t0:t0+tn], in_=xout[:, 0:tn]))(), reads=[xout_b]))
        em.close_batch("out")
    build_project(em, of, projT, 1024, 1024, n, pp, wbufs, ep_o, mgroup_chunks=2)
    em.finalize(out_toks)
    return nc

def l1attn_inputs(x2T_shard, q_shard, K_full, V_full, ins):
    vaug = np.ones((16 * 128, 64 * 65), BF16NP)
    Vf = np.asarray(V_full)
    for h in range(16):
        for kc in range(64):
            blk = Vf[h*64:(h+1)*64, kc*128:(kc+1)*128]
            vaug[h*128:(h+1)*128, kc*65:kc*65+64] = blk.T
    b16 = np.zeros((16, PT), np.float32)
    for qb in range(16):
        b16[qb, qb*128:(qb+1)*128] = 1.0
    return {
        "x2s": np.ascontiguousarray(x2T_shard, np.float32),
        "qh": np.ascontiguousarray(q_shard),
        "kf": np.ascontiguousarray(K_full),
        "vaugh": vaug, "b16": b16.astype(BF16NP),
        "projT": prep_wT(ins["L1_attn_proj_w"]),
        "projb": pack_bias(ins["L1_attn_proj_b"], 8),
    }


L3_BOUNDS = [16, 32, 48, 64]
L3_OFFS = [0, 16, 48, 96]
L3_TOTAL = 160

def build_l3attn(ntok_core=PT):
    n = ntok_core
    nc = bass.Bass("TRN2", target_bir_lowering=False)
    x6s = nc.dram_tensor("x6s", [D, n], F32, kind="ExternalInput").ap()
    qsl = nc.dram_tensor("qsl", [D, n], BF16, kind="ExternalInput").ap()
    ksl = nc.dram_tensor("ksl", [D, L3_TOTAL * 128], BF16, kind="ExternalInput").ap()
    vaugsl = nc.dram_tensor("vaugsl", [16 * 128, L3_TOTAL * 65], BF16, kind="ExternalInput").ap()
    WoT = nc.dram_tensor("WoT", [1024, 1024], F32, kind="ExternalInput").ap()
    xo = nc.dram_tensor("xo", [D, n], F32, kind="ExternalOutput").ap()
    em = Em(nc)
    x = Chunked(em, "x", NCH, 512, F32)
    q = Chunked(em, "q", NCH, n, BF16)
    of = q
    consts = load_consts(em, nc, [("dmask", [128, 4 * 512], BF16)])
    misc = setup_misc(em)
    em.close_batch("cst")
    kpair = [em.sb("kpair0", [128, L3_TOTAL * 128], BF16)] * 2
    _kb0 = Buf("kpair0")
    kpair_b = [_kb0, _kb0]
    vab = [em.sb(f"vab{i}", [128, L3_TOTAL * 65], BF16) for i in range(2)]
    vab_b = [Buf("vab0"), Buf("vab1")]
    expSH = [(em.sb(f"expS{i}", [128, 1024], BF16), Buf(f"expS{i}")) for i in range(4)]
    tscr = em.sb("tscr", [128, 1024], F32); tscr_b = Buf("tscr")
    o_un = em.sb("o_un", [64, 512], BF16); o_un_b = Buf("o_un")
    r_row = em.sb("r_row", [1, 512], F32); r_row_b = Buf("r_row")
    xout = em.sb("xout", [128, 512], F32); xout_b = Buf("xout")
    wb1 = em.sb("wbufA", [128, 2048], BF16)
    wb2 = em.sb("wbufB", [128, 2048], BF16)
    wbufs = [(wb1, Buf("wbufA")), (wb2, Buf("wbufB"))]
    pp = PsumPool(em, n=2, width=512)
    ps_sH = [(em.ps(f"ps_s{i}", [128, 1024]), Buf(f"ps_s{i}")) for i in range(2)]
    ps_o = em.ps("ps_o", [128, 512]); ps_o_b = Buf("ps_o")
    out_toks = []
    for ci in range(NCH):
        em.emit_dma("pool", "qin", (lambda ci=ci: lambda e: e.dma_start(
            out=q.h[ci][:, :], in_=qsl[ci*128:(ci+1)*128, :]))(), writes=[q.b[ci]])
    em.close_batch("qin")
    for h in range(16):
        hp, base = h // 2, (h % 2) * 64
        if h % 2 == 0:
            kb, kb_i = kpair[(h // 2) % 2], kpair_b[(h // 2) % 2]
            em.emit_dma("pool", f"kin{(h//2) % 2}", (lambda hp=hp, kb=kb: lambda e: e.dma_start(
                out=kb[:, :], in_=ksl[hp*128:(hp+1)*128, :]))(), writes=[kb_i])
            em.close_batch(f"kin{(h//2) % 2}")
        else:
            kb, kb_i = kpair[(h // 2) % 2], kpair_b[(h // 2) % 2]
        va, va_i = vab[h % 2], vab_b[h % 2]
        em.emit_dma("pool", f"vin{h % 2}", (lambda h=h, va=va: lambda e: e.dma_start(
            out=va[:, :], in_=vaugsl[h*128:(h+1)*128, :]))(), writes=[va_i])
        em.close_batch(f"vin{h % 2}")
        for s in range(4):
            nkc = L3_BOUNDS[s]
            off = L3_OFFS[s]
            q0 = s * 512
            prev_av = []
            for g in range(nkc // 4):
                is_diag = (g == nkc // 4 - 1)
                cur_av = []
                for hf in range(2):
                    psS, psS_b = ps_sH[hf]
                    exS, exS_b = expSH[(g % 2) * 2 + hf]
                    def mk_s(g=g, off=off, q0=q0, base=base, hp=hp, kb=kb, hf=hf, psS=psS):
                        def f(e):
                            ins = None
                            for j in range(2):
                                kc = off + g * 4 + hf * 2 + j
                                ins = e.matmul(psS[:, j*512:(j+1)*512],
                                               kb[base:base+64, kc*128:(kc+1)*128],
                                               q.h[hp][base:base+64, q0:q0+512],
                                               start=True, stop=True)
                            return ins
                        return f
                    em.emit("pe", mk_s(), reads=[kb_i, q.b[hp]], writes=[psS_b])
                    em.emit("act", (lambda psS=psS: lambda e: e.activation(
                        tscr[:, 0:1024], psS[:, 0:1024], AF.Tanh, scale=1.0/400.0))(),
                        reads=[psS_b], writes=[tscr_b])
                    em.emit("act", (lambda exS=exS: lambda e: e.activation(
                        exS[:, 0:1024], tscr[:, 0:1024], AF.Exp, scale=50.0))(),
                        reads=[tscr_b], writes=[exS_b])
                    if is_diag:
                        em.emit("dve", (lambda exS=exS, hf=hf: lambda e: e.tensor_mul(
                            exS[:, 0:1024], exS[:, 0:1024],
                            consts["dmask"][0][:, hf*1024:(hf+1)*1024]))(),
                            reads=[exS_b, consts["dmask"][1]], writes=[exS_b])
                    def mk_av(g=g, off=off, nkc=nkc, va=va, hf=hf, exS=exS):
                        def f(e):
                            ins = None
                            for j in range(2):
                                kc = off + g * 4 + hf * 2 + j
                                first = (g == 0 and hf == 0 and j == 0)
                                last = (g == nkc // 4 - 1 and hf == 1 and j == 1)
                                ins = e.matmul(ps_o[0:65, 0:512],
                                               va[:, kc*65:kc*65+65],
                                               exS[:, j*512:(j+1)*512],
                                               start=first, stop=last)
                            return ins
                        return f
                    cur_av.append((mk_av(), [va_i, exS_b]))
                for fn, rds in prev_av:
                    em.emit("pe", fn, reads=rds, writes=[ps_o_b])
                prev_av = cur_av
            for fn, rds in prev_av:
                em.emit("pe", fn, reads=rds, writes=[ps_o_b])
            em.emit("dve", (lambda: lambda e: e.reciprocal(r_row[0:1, 0:512], ps_o[64:65, 0:512]))(),
                reads=[ps_o_b], writes=[r_row_b])
            em.emit("dve", (lambda: lambda e: e.tensor_copy(o_un[0:64, 0:512], ps_o[0:64, 0:512]))(),
                reads=[ps_o_b], writes=[o_un_b])
            ps_bc, ps_bc_b = pp.take1()
            em.emit("pe", (lambda ps_bc=ps_bc: lambda e: e.matmul(
                ps_bc[0:64, 0:512], misc["ones_row"][0][0:1, 0:64], r_row[0:1, 0:512],
                start=True, stop=True))(),
                reads=[misc["ones_row"][1], r_row_b], writes=[ps_bc_b])
            em.emit("dve", (lambda hp=hp, base=base, q0=q0, ps_bc=ps_bc: lambda e: e.tensor_mul(
                of.h[hp][base:base+64, q0:q0+512], o_un[0:64, 0:512], ps_bc[0:64, 0:512]))(),
                reads=[o_un_b, ps_bc_b], writes=[of.b[hp]])
    def ep_o(mi, ps, ps_b, t0, tn):
        em.emit_dma("sp", "xres", (lambda mi=mi, t0=t0, tn=tn: lambda e: e.dma_start(
            out=x.h[mi][:, 0:tn], in_=x6s[mi*128:(mi+1)*128, t0:t0+tn]))(),
            writes=[x.b[mi]])
        em.close_batch("xres")
        em.emit("dve", (lambda mi=mi, ps=ps, t0=t0, tn=tn: lambda e: e.tensor_add(
            xout[:, 0:tn], ps[:, 0:tn], x.h[mi][:, 0:tn]))(),
            reads=[ps_b, x.b[mi]], writes=[xout_b])
        out_toks.append(em.emit_dma("sp", "out", (lambda mi=mi, t0=t0, tn=tn: lambda e: e.dma_start(
            out=xo[mi*128:(mi+1)*128, t0:t0+tn], in_=xout[:, 0:tn]))(), reads=[xout_b]))
        em.close_batch("out")
    build_project(em, of, WoT, 1024, 1024, n, pp, wbufs, ep_o, mgroup_chunks=2)
    em.finalize(out_toks)
    return nc

def l3_qts(core_mod4):
    return [4 * s + core_mod4 for s in range(4)]

def l3attn_inputs(x6T, qT, KT, VT, core, ins):
    cm = core % 4
    b0 = (core // 4) * T
    qts = l3_qts(cm)
    qs = np.concatenate([qT[:, b0 + qt*512 : b0 + (qt+1)*512] for qt in qts], axis=1)
    x6 = np.concatenate([x6T[:, b0 + qt*512 : b0 + (qt+1)*512] for qt in qts], axis=1)
    ksl = np.zeros((1024, L3_TOTAL * 128), BF16NP)
    vau = np.zeros((16 * 128, L3_TOTAL * 65), BF16NP)
    Vf = np.asarray(VT)
    for s in range(4):
        qt = qts[s]
        valid = 4 * (qt + 1)
        bound = L3_BOUNDS[s]
        off = L3_OFFS[s]
        order = list(range(valid - 4)) + [-1] * (bound - valid) + list(range(valid - 4, valid))
        for ii, kc in enumerate(order):
            col = (off + ii) * 128
            if kc < 0:
                continue
            ksl[:, col:col+128] = KT[:, kc*128:(kc+1)*128]
            for h in range(16):
                blk = Vf[h*64:(h+1)*64, kc*128:(kc+1)*128]
                vau[h*128:(h+1)*128, (off+ii)*65:(off+ii)*65+64] = blk.T
                vau[h*128:(h+1)*128, (off+ii)*65+64] = 1.0
    dm = np.zeros((128, 4, 512), np.float32)
    for j in range(4):
        ki = np.arange(128)
        for qq in range(512):
            dm[:, j, qq] = (j * 128 + ki <= (qq // 1) + 0) * 0
    for j in range(4):
        ki = np.arange(128)[:, None]
        qq = np.arange(512)[None, :]
        dm[:, j, :] = (j * 128 + ki <= qq).astype(np.float32)
    return {
        "x6s": np.ascontiguousarray(x6, np.float32),
        "qsl": np.ascontiguousarray(qs),
        "ksl": np.ascontiguousarray(ksl), "vaugsl": np.ascontiguousarray(vau),
        "WoT": np.ascontiguousarray(np.asarray(ins["L3_attn_Wo"], np.float32).T),
        "dmask": np.ascontiguousarray(dm.reshape(128, 2048)).astype(BF16NP),
    }


def _host_ext(stream_T, c, halo):
    g0 = c * PT
    b0 = (g0 // T) * T
    lo = g0 - halo
    pad = max(0, b0 - lo)
    seg = stream_T[:, max(lo, b0):g0 + PT]
    if pad:
        seg = np.concatenate([np.zeros((D, pad), np.float32), seg], axis=1)
    return np.ascontiguousarray(seg)

def _gather(res, name, dtype=np.float32):
    out = np.zeros((res.results[0][name].shape[0], NTOK), dtype)
    for c in range(NCORE):
        out[:, c*PT:(c+1)*PT] = res.results[c][name]
    return out

def kernel(**inputs):
    ins = inputs
    x0 = np.asarray(ins["L0_attn_x"], np.float32).reshape(NTOK, D)
    xT = np.ascontiguousarray(x0.T)
    cores = list(range(NCORE))
    def run(nc, maps):
        return run_bass_kernel_spmd(nc, maps, core_ids=cores)

    nc = build_l0attn()
    maps = []
    for c in cores:
        pos = (np.arange(c*PT, (c+1)*PT) % T)
        maps.append(l0attn_inputs(xT[:, c*PT:(c+1)*PT], pos, ins))
    res = run(nc, maps)
    x1 = _gather(res, "xo")
    nc = build_l0ffn()
    maps = [l0ffn_inputs(x1[:, c*PT:(c+1)*PT], ins) for c in cores]
    res = run(nc, maps)
    x2 = _gather(res, "xo")
    nc = build_qkv3()
    wcat = prep_wT(ins["L1_attn_qkv_w"])
    maps = [qkv3_inputs(x2[:, c*PT:(c+1)*PT], wcat) for c in cores]
    res = run(nc, maps)
    qT = _gather(res, "qo", BF16NP)
    kT = _gather(res, "ko", BF16NP)
    vT = _gather(res, "vo", BF16NP)
    nc = build_l1attn()
    maps = []
    for c in cores:
        b0 = (c // 4) * T
        maps.append(l1attn_inputs(x2[:, c*PT:(c+1)*PT], qT[:, c*PT:(c+1)*PT],
                                  kT[:, b0:b0+T], vT[:, b0:b0+T], ins))
    res = run(nc, maps)
    x3 = _gather(res, "xo")
    stream = x3
    for layer in range(2):
        wqkv = ins["L1_ffn_wqkv"][layer]; qs = ins["L1_ffn_q_scale"][layer]
        ks = ins["L1_ffn_k_scale"][layer]; wo = ins["L1_ffn_wo"][layer]
        nc = build_f1attn(layer == 0)
        maps = []
        for c in cores:
            g0 = c * PT; b0 = (g0 // T) * T
            posq = np.arange(g0 - 1024, g0 + PT) - b0
            maps.append(f1attn_inputs(_host_ext(stream, c, 1024), posq, wqkv, qs, ks, wo,
                                      batch_start=(c % 4 == 0)))
        res = run(nc, maps)
        xi = _gather(res, "xo")
        nc = build_f1ffn(layer == 1)
        maps = []
        for c in cores:
            maps.append(f1ffn_inputs(_host_ext(xi, c, 1), ins["L1_ffn_ln_g"][layer],
                                     ins["L1_ffn_ln_b"][layer], ins["L1_ffn_w1"][layer],
                                     ins["L1_ffn_b1"][layer], ins["L1_ffn_w2"][layer],
                                     ins["L1_ffn_b2"][layer],
                                     x3[:, c*PT:(c+1)*PT] if layer == 1 else None))
        res = run(nc, maps)
        stream = _gather(res, "xo")
    x4 = stream
    nc = build_l2a()
    maps = []
    for c in cores:
        g0 = c * PT; b0 = (g0 // T) * T
        pos_q = np.arange(g0, g0 + PT) - b0
        pos_k = np.arange(g0 - 256, g0 + PT) - b0
        maps.append(l2a_inputs(_host_ext(x4, c, 256), pos_q, pos_k, ins))
    res = run(nc, maps)
    x5 = _gather(res, "xo")
    nc = build_l2b1()
    maps = [l2b1_inputs(x5[:, c*PT:(c+1)*PT], ins) for c in cores]
    res = run(nc, maps)
    r1 = res.results
    nc = build_l2b2()
    maps = []
    for c in cores:
        prev = []
        for j in range(4):
            cc = (c // 4) * 4 + j
            prev.append(r1[cc]["sso"] if cc < c else np.zeros((1024, 2), np.float32))
        g0 = c * PT; b0 = (g0 // T) * T
        gpos = np.arange(g0, g0 + PT) - b0
        maps.append(l2b2_inputs_v2(x5[:, c*PT:(c+1)*PT], r1[c]["qo"], r1[c]["ko"],
                                   r1[c]["iqo"], r1[c]["iko"], prev, gpos, ins))
    res = run(nc, maps)
    x6 = _gather(res, "xo")
    nc = build_qkv3()
    wcat3 = np.concatenate([prep_wT(ins["L3_attn_Wq"]), prep_wT(ins["L3_attn_Wk"]),
                            prep_wT(ins["L3_attn_Wv"])], axis=1)
    maps = [qkv3_inputs(x6[:, c*PT:(c+1)*PT], np.ascontiguousarray(wcat3)) for c in cores]
    res = run(nc, maps)
    q3 = _gather(res, "qo", BF16NP)
    k3 = _gather(res, "ko", BF16NP)
    v3 = _gather(res, "vo", BF16NP)
    nc = build_l3attn()
    maps = []
    for c in cores:
        b0 = (c // 4) * T
        maps.append(l3attn_inputs(x6, q3, k3[:, b0:b0+T], v3[:, b0:b0+T], c, ins))
    res = run(nc, maps)
    x7 = np.zeros((D, NTOK), np.float32)
    for c in cores:
        qts = l3_qts(c % 4)
        b0 = (c // 4) * T
        o = res.results[c]["xo"]
        for s, qt in enumerate(qts):
            x7[:, b0 + qt*512 : b0 + (qt+1)*512] = o[:, s*512:(s+1)*512]
    nc = build_l3f()
    maps = [l3f_inputs(x7[:, c*PT:(c+1)*PT], ins) for c in cores]
    res = run(nc, maps)
    x8 = _gather(res, "xo")
    return np.ascontiguousarray(x8.T).reshape(B, T, D).astype(np.float32)


# version 06 (final)
# speedup vs baseline: 1.2487x; 1.0516x over previous
import contextlib
import numpy as np
import ml_dtypes
import concourse.bass as bass
import concourse.mybir as mybir
from concourse.bass_utils import run_bass_kernel_spmd

F32 = mybir.dt.float32
BF16 = mybir.dt.bfloat16
I32 = mybir.dt.int32
U32 = mybir.dt.uint32
AF = mybir.ActivationFunctionType
ALU = mybir.AluOpType
AX = mybir.AxisListType
BF16NP = ml_dtypes.bfloat16

NCORE = 8
B, T, D = 2, 8192, 1024
NTOK = B * T
PT = NTOK // NCORE
NCH = D // 128


class Tok:
    __slots__ = ("sem", "val")
    def __init__(self, sem, val):
        self.sem, self.val = sem, val

class Buf:
    __slots__ = ("name", "w", "readers")
    def __init__(self, name=""):
        self.name = name
        self.w = None
        self.readers = []

ENGS = ("pe", "dve", "act", "pool", "sp")

class Em:
    def __init__(self, nc):
        self.nc = nc
        self.es = contextlib.ExitStack()
        self.sems = {}
        self.cnt = {}
        for s in ("s_pe", "s_dve", "s_act", "s_pool", "s_sp"):
            self.sems[s] = self.es.enter_context(nc.semaphore(s))
            self.cnt[s] = 0
        self._batch = {}
        self.q = {e: [] for e in ENGS}
        self.waited = {e: {} for e in ENGS}
        self._sb_bytes = 0

    def sb(self, name, shape, dt):
        h = self.es.enter_context(self.nc.sbuf_tensor(name, list(shape), dt))
        self._sb_bytes += int(np.prod(shape[1:])) * mybir.dt.size(dt) * 0 + \
            int(np.prod(shape)) // shape[0] * mybir.dt.size(dt)
        return h

    def ps(self, name, shape, dt=F32):
        return self.es.enter_context(self.nc.psum_tensor(name, list(shape), dt))

    def sem(self, name):
        self.sems[name] = self.es.enter_context(self.nc.semaphore(name))
        self.cnt[name] = 0
        return self.sems[name]

    def _engine_sem(self, eng):
        return {"pe": "s_pe", "dve": "s_dve", "act": "s_act",
                "pool": "s_pool", "sp": "s_sp"}[eng]

    def dma_slot(self, slot):
        s = "d_" + slot
        if s not in self.sems:
            self.sems[s] = self.es.enter_context(self.nc.semaphore(s))
            self.cnt[s] = 0
            self._batch[s] = []
        return s

    def emit_dma(self, eng, slot, fn, reads=(), writes=(), deps=()):
        s = self.dma_slot(slot)
        tok = self._emit(eng, fn, reads, writes, deps, dma_sem=s)
        self._batch[s].extend(writes)
        self._batch[s].extend(("R", b) for b in reads)
        return tok

    def close_batch(self, slot):
        s = self.dma_slot(slot)
        fence = Tok(s, self.cnt[s])
        for item in self._batch[s]:
            if isinstance(item, tuple):
                b = item[1]
                for i, t in enumerate(b.readers):
                    if t.sem == s and t.val < fence.val:
                        b.readers[i] = fence
            else:
                item.w = fence
        self._batch[s] = []
        return fence

    def emit(self, eng, fn, reads=(), writes=(), deps=()):
        return self._emit(eng, fn, reads, writes, deps, None)

    def _emit(self, eng, fn, reads=(), writes=(), deps=(), dma_sem=None):
        want = []
        for b in reads:
            if b.w is not None:
                want.append(b.w)
        for b in writes:
            if b.w is not None:
                want.append(b.w)
            want.extend(b.readers)
        want.extend(deps)
        need = {}
        for t in want:
            if t is None:
                continue
            if need.get(t.sem, -1) < t.val:
                need[t.sem] = t.val
        if dma_sem is None:
            semn = self._engine_sem(eng)
            self.cnt[semn] += 1
            tok = Tok(semn, self.cnt[semn])
            inc = (self.sems[semn], 1)
        else:
            semn = dma_sem
            self.cnt[semn] += 16
            tok = Tok(semn, self.cnt[semn])
            inc = (self.sems[semn], 16)
        waits = []
        wtab = self.waited[eng]
        for s, v in sorted(need.items()):
            if wtab.get(s, 0) < v:
                waits.append((self.sems[s], v))
                wtab[s] = v
        self.q[eng].append((fn, waits, inc))
        for b in reads:
            b.readers.append(tok)
        for b in writes:
            b.w = tok
            b.readers = []
        return tok

    def barrier_tok(self, eng):
        semn = self._engine_sem(eng)
        return Tok(semn, self.cnt[semn])

    def finalize(self, final_deps):
        nc = self.nc
        need = {}
        for t in final_deps:
            if need.get(t.sem, -1) < t.val:
                need[t.sem] = t.val
        qmap = {"pe": "tensor", "dve": "vector", "act": "scalar",
                "pool": "gpsimd", "sp": "sync"}
        with nc.Block() as block:
            for eng in ENGS:
                items = self.q[eng]
                if not items and eng != "sp":
                    continue
                def mk(eng_items, is_sp):
                    def emit_all(engobj):
                        for fn, waits, inc in eng_items:
                            for sem, v in waits:
                                engobj.wait_ge(sem, v)
                            ins = fn(engobj)
                            assert ins is not None, "emit fn must return last instruction"
                            ins.then_inc(*inc)
                        if is_sp:
                            for s, v in sorted(need.items()):
                                engobj.wait_ge(self.sems[s], v)
                    return emit_all
                getattr(block, qmap[eng])(mk(items, eng == "sp"))
        self.es.close()


def ceil_div(a, b):
    return (a + b - 1) // b

def mm_tiles(n, step=512):
    return [(j, min(step, n - j)) for j in range(0, n, step)]

class Chunked:
    def __init__(self, em, name, nch, ntok, dt):
        self.h = [em.sb(f"{name}{i}", [128, ntok], dt) for i in range(nch)]
        self.b = [Buf(f"{name}{i}") for i in range(nch)]
        self.nch = nch
        self.ntok = ntok


def prep_wT(w):
    return np.ascontiguousarray(np.asarray(w).T.astype(np.float32))

def prep_w(w):
    return np.ascontiguousarray(np.asarray(w).astype(np.float32))


def build_layernorm(em, x, out, psum_pool, misc, ntok, eps=1e-5, inv_n=1.0/1024.0, out_col0=0,
                    xoff=None, gain=None, lbias=None, x_col0=0):
    nc = em.nc
    if xoff is None:
        xoff = [0] * x.nch
    onesc, onesc_b = misc["ones_col"]
    onesr, onesr_b = misc["ones_row"]
    sq, sq_b = misc["sq"]
    r_s1, r_s1_b = misc["r_s1"]
    r_s2, r_s2_b = misc["r_s2"]
    r_a, r_a_b = misc["r_a"]
    r_b, r_b_b = misc["r_b"]
    ps_s1, ps_s1_b = psum_pool.take1()
    ps_s2, ps_s2_b = psum_pool.take1()
    for t0, tn in mm_tiles(ntok):
        def mk_s1(t0=t0, tn=tn):
            def f(e):
                ins = None
                for ci in range(x.nch):
                    o = x_col0 + xoff[ci] + t0
                    ins = e.matmul(ps_s1[0:1, t0:t0+tn], onesc[:, :], x.h[ci][:, o:o+tn],
                                   start=(ci == 0), stop=(ci == x.nch - 1))
                return ins
            return f
        tmm = em.emit("pe", mk_s1(), reads=[onesc_b] + x.b, writes=[ps_s1_b])
        em.emit("act", (lambda t0=t0, tn=tn: lambda e: e.activation(
            r_s1[0:1, t0:t0+tn], ps_s1[0:1, t0:t0+tn], AF.Identity, scale=inv_n))(),
            reads=[ps_s1_b], writes=[r_s1_b])
        for ci in range(x.nch):
            em.emit("act", (lambda ci=ci, t0=t0, tn=tn: lambda e: e.square(
                sq[:, 0:tn], x.h[ci][:, x_col0+xoff[ci]+t0:x_col0+xoff[ci]+t0+tn]))(),
                reads=[x.b[ci]], writes=[sq_b])
            em.emit("pe", (lambda ci=ci, t0=t0, tn=tn: lambda e: e.matmul(
                ps_s2[0:1, t0:t0+tn], onesc[:, :], sq[:, 0:tn],
                start=(ci == 0), stop=(ci == x.nch - 1)))(),
                reads=[onesc_b, sq_b], writes=[ps_s2_b])
        em.emit("act", (lambda t0=t0, tn=tn: lambda e: e.activation(
            r_s2[0:1, t0:t0+tn], ps_s2[0:1, t0:t0+tn], AF.Identity, scale=inv_n))(),
            reads=[ps_s2_b], writes=[r_s2_b])
    em.emit("act", lambda e: e.square(r_a[0:1, 0:ntok], r_s1[0:1, 0:ntok]),
            reads=[r_s1_b], writes=[r_a_b])
    em.emit("dve", lambda e: e.tensor_sub(r_s2[0:1, 0:ntok], r_s2[0:1, 0:ntok], r_a[0:1, 0:ntok]),
            reads=[r_a_b, r_s2_b], writes=[r_s2_b])
    epst, epst_b = misc["eps"]
    em.emit("act", lambda e: e.activation(r_s2[0:1, 0:ntok], r_s2[0:1, 0:ntok], AF.Ln, bias=epst[0:1, 0:1]),
            reads=[r_s2_b, epst_b], writes=[r_s2_b])
    em.emit("act", lambda e: e.activation(r_a[0:1, 0:ntok], r_s2[0:1, 0:ntok], AF.Exp, scale=-0.5),
            reads=[r_s2_b], writes=[r_a_b])
    em.emit("dve", lambda e: e.tensor_mul(r_b[0:1, 0:ntok], r_s1[0:1, 0:ntok], r_a[0:1, 0:ntok]),
            reads=[r_s1_b, r_a_b], writes=[r_b_b])
    em.emit("dve", lambda e: e.tensor_scalar_mul(r_b[0:1, 0:ntok], r_b[0:1, 0:ntok], -1.0),
            reads=[r_b_b], writes=[r_b_b])
    for t0, tn in mm_tiles(ntok):
        ps_a, ps_a_b = psum_pool.take1()
        ps_b, ps_b_b = psum_pool.take1()
        em.emit("pe", (lambda t0=t0, tn=tn, ps_a=ps_a: lambda e: e.matmul(
            ps_a[:, 0:tn], onesr[:, :], r_a[0:1, t0:t0+tn], start=True, stop=True))(),
            reads=[onesr_b, r_a_b], writes=[ps_a_b])
        em.emit("pe", (lambda t0=t0, tn=tn, ps_b=ps_b: lambda e: e.matmul(
            ps_b[:, 0:tn], onesr[:, :], r_b[0:1, t0:t0+tn], start=True, stop=True))(),
            reads=[onesr_b, r_b_b], writes=[ps_b_b])
        for ci in range(x.nch):
            em.emit("dve", (lambda ci=ci, t0=t0, tn=tn, ps_a=ps_a: lambda e: e.tensor_mul(
                sq[:, 0:tn], x.h[ci][:, x_col0+xoff[ci]+t0:x_col0+xoff[ci]+t0+tn], ps_a[:, 0:tn]))(),
                reads=[x.b[ci], ps_a_b], writes=[sq_b])
            if gain is None:
                em.emit("dve", (lambda ci=ci, t0=t0, tn=tn, ps_b=ps_b: lambda e: e.tensor_add(
                    out.h[ci][:, out_col0+t0:out_col0+t0+tn], sq[:, 0:tn], ps_b[:, 0:tn]))(),
                    reads=[sq_b, ps_b_b], writes=[out.b[ci]])
            else:
                gt, gt_b = gain
                lt, lt_b = lbias
                em.emit("dve", (lambda ci=ci, t0=t0, tn=tn, ps_b=ps_b: lambda e: e.tensor_add(
                    sq[:, 0:tn], sq[:, 0:tn], ps_b[:, 0:tn]))(),
                    reads=[sq_b, ps_b_b], writes=[sq_b])
                em.emit("dve", (lambda ci=ci, t0=t0, tn=tn: lambda e: e.scalar_tensor_tensor(
                    out.h[ci][:, out_col0+t0:out_col0+t0+tn], sq[:, 0:tn], gt[:, ci:ci+1],
                    lt[:, ci:ci+1].broadcast_to([128, tn]), ALU.mult, ALU.add))(),
                    reads=[sq_b, gt_b, lt_b], writes=[out.b[ci]])


class PsumPool:
    def __init__(self, em, n=8, width=512, name="pp"):
        self.t = [em.ps(f"{name}{i}", [128, width]) for i in range(n)]
        self.b = [Buf(f"{name}{i}") for i in range(n)]
        self.i = 0
        self.n = n
    def take1(self):
        j = self.i % self.n
        self.i += 1
        return self.t[j], self.b[j]


def build_project(em, xin, w_hbm, K, M, ntok, psum_pool, wbufs, epilogue,
                  mgroup_chunks=4, max_j_per_dma=8, xoff=None):
    NJ = K // 128
    nmch = M // 128
    if xoff is None:
        xoff = [0] * NJ
    for g0 in range(0, nmch, mgroup_chunks):
        gch = min(mgroup_chunks, nmch - g0)
        mg_w = gch * 128
        wslot = "wA" if (g0 // mgroup_chunks) % 2 == 0 else "wB"
        wb, wb_b = wbufs[(g0 // mgroup_chunks) % 2]
        m0 = g0 * 128
        for j0 in range(0, NJ, max_j_per_dma):
            jn = min(max_j_per_dma, NJ - j0)
            def mkload(j0=j0, jn=jn, m0=m0, mg_w=mg_w, wb=wb):
                def f(e):
                    src = w_hbm[j0*128:(j0+jn)*128, m0:m0+mg_w].rearrange(
                        "(j p) m -> p j m", p=128)
                    dst = wb[:, j0*mg_w:(j0+jn)*mg_w].rearrange(
                        "p (j m) -> p j m", m=mg_w)
                    return e.dma_start(out=dst, in_=src)
                return f
            em.emit_dma("pool", wslot, mkload(), writes=[wb_b])
        em.close_batch(wslot)
        for i in range(gch):
            mi = g0 + i
            for t0, tn in mm_tiles(ntok):
                ps, ps_b = psum_pool.take1()
                def mkmm(i=i, t0=t0, tn=tn, mg_w=mg_w, wb=wb, ps=ps):
                    def f(e):
                        ins = None
                        for j in range(NJ):
                            o = xoff[j] + t0
                            ins = e.matmul(ps[:, 0:tn],
                                           wb[:, j*mg_w + i*128: j*mg_w + (i+1)*128],
                                           xin.h[j][:, o:o+tn],
                                           start=(j == 0), stop=(j == NJ - 1))
                        return ins
                    return f
                em.emit("pe", mkmm(), reads=[wb_b] + xin.b, writes=[ps_b])
                epilogue(mi, ps, ps_b, t0, tn)


def build_l0ffn(ntok_core=PT, tt=1024):
    nc = bass.Bass("TRN2", target_bir_lowering=False)
    xs = nc.dram_tensor("xs", [D, ntok_core], F32, kind="ExternalInput").ap()
    w1T = nc.dram_tensor("w1T", [1024, 4096], F32, kind="ExternalInput").ap()
    w2T = nc.dram_tensor("w2T", [1024, 4096], F32, kind="ExternalInput").ap()
    h1T = nc.dram_tensor("h1T", [4096, 4096], F32, kind="ExternalInput").ap()
    h2T = nc.dram_tensor("h2T", [4096, 4096], F32, kind="ExternalInput").ap()
    w3T = nc.dram_tensor("w3T", [4096, 1024], F32, kind="ExternalInput").ap()
    s1h = nc.dram_tensor("s1", [128, 32], F32, kind="ExternalInput").ap()
    s2h = nc.dram_tensor("s2", [128, 32], F32, kind="ExternalInput").ap()
    sh1h = nc.dram_tensor("sh1", [128, 32], F32, kind="ExternalInput").ap()
    sh2h = nc.dram_tensor("sh2", [128, 32], F32, kind="ExternalInput").ap()
    s3h = nc.dram_tensor("s3", [128, 8], F32, kind="ExternalInput").ap()
    xo = nc.dram_tensor("xo", [D, ntok_core], F32, kind="ExternalOutput").ap()

    em = Em(nc)
    x = Chunked(em, "x", NCH, tt, F32)
    xn = Chunked(em, "xn", NCH, tt, BF16)
    hA = Chunked(em, "hA", 32, tt, BF16)
    hB = Chunked(em, "hB", 32, tt, BF16)
    xout = em.sb("xout", [128, 512], F32)
    xout_b = Buf("xout")
    scs = {}
    for nm, hh, w in (("s1", s1h, 32), ("s2", s2h, 32), ("sh1", sh1h, 32),
                      ("sh2", sh2h, 32), ("s3", s3h, 8)):
        t = em.sb("sb_" + nm, [128, w], F32)
        b = Buf(nm)
        em.emit_dma("sp", "cst", (lambda t=t, hh=hh: lambda e: e.dma_start(out=t[:, :], in_=hh))(),
                writes=[b])
        scs[nm] = (t, b)
    em.close_batch("cst")
    misc = {}
    for nm, shape, val in (("ones_col", [128, 1], 1.0), ("ones_row", [1, 128], 1.0),
                           ("eps", [1, 1], 1e-5)):
        t = em.sb(nm, shape, F32)
        b = Buf(nm)
        em.emit("dve", (lambda t=t, val=val: lambda e: e.memset(t[:, :], val))(), writes=[b])
        misc[nm] = (t, b)
    for nm, shape in (("sq", [128, 512]), ("r_s1", [1, 512]), ("r_s2", [1, 512]),
                      ("r_a", [1, 512]), ("r_b", [1, 512])):
        t = em.sb(nm, shape, F32)
        misc[nm] = (t, Buf(nm))
    wb1 = em.sb("wbufA", [128, 4096], BF16)
    wb2 = em.sb("wbufB", [128, 4096], BF16)
    wbufs = [(wb1, Buf("wbufA")), (wb2, Buf("wbufB"))]
    pp = PsumPool(em, n=8, width=512)
    out_toks = []

    for t0c in range(0, ntok_core, tt):
        for ci in range(NCH):
            em.emit_dma("sp", "xin", (lambda ci=ci, t0c=t0c: lambda e: e.dma_start(
                out=x.h[ci][:, :], in_=xs[ci*128:(ci+1)*128, t0c:t0c+tt]))(),
                writes=[x.b[ci]])
        em.close_batch("xin")
        for sub in range(0, tt, 512):
            build_layernorm(em, x, xn, pp, misc, 512, out_col0=sub, x_col0=sub)
        def ep_a(mi, ps, ps_b, t0, tn):
            em.emit("act", (lambda mi=mi, ps=ps, t0=t0, tn=tn: lambda e: e.activation(
                hA.h[mi][:, t0:t0+tn], ps[:, 0:tn], AF.Silu, scale=scs["s1"][0][:, mi:mi+1]))(),
                reads=[ps_b, scs["s1"][1]], writes=[hA.b[mi]])
        build_project(em, xn, w1T, 1024, 4096, tt, pp, wbufs, ep_a, mgroup_chunks=4)
        def ep_b(mi, ps, ps_b, t0, tn):
            em.emit("dve", (lambda mi=mi, ps=ps, t0=t0, tn=tn: lambda e: e.scalar_tensor_tensor(
                hA.h[mi][:, t0:t0+tn], ps[:, 0:tn], scs["s2"][0][:, mi:mi+1],
                hA.h[mi][:, t0:t0+tn], ALU.mult, ALU.mult))(),
                reads=[ps_b, scs["s2"][1], hA.b[mi]], writes=[hA.b[mi]])
        build_project(em, xn, w2T, 1024, 4096, tt, pp, wbufs, ep_b, mgroup_chunks=4)
        def ep_h1(mi, ps, ps_b, t0, tn):
            em.emit("act", (lambda mi=mi, ps=ps, t0=t0, tn=tn: lambda e: e.activation(
                hB.h[mi][:, t0:t0+tn], ps[:, 0:tn], AF.Gelu, scale=scs["sh1"][0][:, mi:mi+1]))(),
                reads=[ps_b, scs["sh1"][1]], writes=[hB.b[mi]])
        build_project(em, hA, h1T, 4096, 4096, tt, pp, wbufs, ep_h1, mgroup_chunks=1)
        def ep_h2(mi, ps, ps_b, t0, tn):
            em.emit("act", (lambda mi=mi, ps=ps, t0=t0, tn=tn: lambda e: e.activation(
                hA.h[mi][:, t0:t0+tn], ps[:, 0:tn], AF.Gelu, scale=scs["sh2"][0][:, mi:mi+1]))(),
                reads=[ps_b, scs["sh2"][1]], writes=[hA.b[mi]])
        build_project(em, hB, h2T, 4096, 4096, tt, pp, wbufs, ep_h2, mgroup_chunks=1)
        def ep_o(mi, ps, ps_b, t0, tn, t0c=t0c):
            em.emit("dve", (lambda mi=mi, ps=ps, t0=t0, tn=tn: lambda e: e.scalar_tensor_tensor(
                xout[:, 0:tn], ps[:, 0:tn], scs["s3"][0][:, mi:mi+1],
                x.h[mi][:, t0:t0+tn], ALU.mult, ALU.add))(),
                reads=[ps_b, scs["s3"][1], x.b[mi]], writes=[xout_b])
            out_toks.append(em.emit_dma("sp", "out", (lambda mi=mi, t0=t0, tn=tn, t0c=t0c: lambda e: e.dma_start(
                out=xo[mi*128:(mi+1)*128, t0c+t0:t0c+t0+tn], in_=xout[:, 0:tn]))(),
                reads=[xout_b]))
            em.close_batch("out")
        build_project(em, hA, w3T, 4096, 1024, tt, pp, wbufs, ep_o, mgroup_chunks=1)
    em.finalize(out_toks)
    return nc

def l0ffn_inputs(xT_shard, ins):
    d = {
        "xs": np.ascontiguousarray(xT_shard, np.float32),
        "w1T": prep_wT(ins["L0_ffn_w1_w"]), "w2T": prep_wT(ins["L0_ffn_w2_w"]),
        "h1T": prep_wT(ins["L0_ffn_h1_w"]), "h2T": prep_wT(ins["L0_ffn_h2_w"]),
        "w3T": prep_wT(ins["L0_ffn_w3_w"]),
    }
    for nm, key, w in (("s1", "L0_ffn_w1_s", 32), ("s2", "L0_ffn_w2_s", 32),
                       ("sh1", "L0_ffn_h1_s", 32), ("sh2", "L0_ffn_h2_s", 32),
                       ("s3", "L0_ffn_w3_s", 8)):
        v = np.asarray(ins[key], np.float32).reshape(w, 128).T
        d[nm] = np.ascontiguousarray(v)
    return d


def rope_apply(em, chunks_h, chunks_b, cosT, cosT_b, sinT, sinT_b, perm, perm_b,
               pp, ntok, strip=None, col0=0):
    p0, pn = (0, 128) if strip is None else strip
    for ci, (h, b) in enumerate(zip(chunks_h, chunks_b)):
        for t0, tn in mm_tiles(ntok):
            c0 = col0 + t0
            ps, ps_b = pp.take1()
            em.emit("pe", (lambda h=h, c0=c0, tn=tn, ps=ps: lambda e: e.matmul(
                ps[:, 0:tn], perm[:, :], h[:, c0:c0+tn], start=True, stop=True))(),
                reads=[perm_b, b], writes=[ps_b])
            sc, sc_b = em.scr
            em.emit("dve", (lambda t0=t0, tn=tn, ps=ps, sc=sc, p0=p0, pn=pn: lambda e: e.tensor_mul(
                sc[p0:pn, 0:tn], ps[p0:pn, 0:tn], sinT[p0:pn, t0:t0+tn]))(),
                reads=[ps_b, sinT_b], writes=[sc_b])
            em.emit("dve", (lambda h=h, c0=c0, t0=t0, tn=tn, sc=sc, p0=p0, pn=pn: lambda e: e.tensor_mul(
                h[p0:pn, c0:c0+tn], h[p0:pn, c0:c0+tn], cosT[p0:pn, t0:t0+tn]))(),
                reads=[b, cosT_b], writes=[b])
            em.emit("dve", (lambda h=h, c0=c0, tn=tn, sc=sc, p0=p0, pn=pn: lambda e: e.tensor_add(
                h[p0:pn, c0:c0+tn], h[p0:pn, c0:c0+tn], sc[p0:pn, 0:tn]))(),
                reads=[b, sc_b], writes=[b])

def pe_transpose(em, dst, dst_b, dst_col, src_ap_fn, src_bufs, ident, ident_b, pt, pt_b):
    em.emit("pe", (lambda: lambda e: e.transpose(pt[:, 0:128], src_ap_fn(), ident[:, :]))(),
            reads=list(src_bufs) + [ident_b], writes=[pt_b])
    em.emit("dve", (lambda dst=dst, dst_col=dst_col: lambda e: e.tensor_copy(
        dst[:, dst_col:dst_col+128], pt[:, 0:128]))(),
        reads=[pt_b], writes=[dst_b])


def build_l0attn(ntok_core=PT):
    n = ntok_core
    nc = bass.Bass("TRN2", target_bir_lowering=False)
    xs = nc.dram_tensor("xs", [D, n], F32, kind="ExternalInput").ap()
    WqT = nc.dram_tensor("WqT", [1024, 1024], F32, kind="ExternalInput").ap()
    WdT = nc.dram_tensor("WdT", [1024, 256], F32, kind="ExternalInput").ap()
    WuT = nc.dram_tensor("WuT", [256, 2048], F32, kind="ExternalInput").ap()
    WoT = nc.dram_tensor("WoT", [1024, 1024], F32, kind="ExternalInput").ap()
    bq = nc.dram_tensor("bq", [128, 8], F32, kind="ExternalInput").ap()
    bd = nc.dram_tensor("bd", [128, 2], F32, kind="ExternalInput").ap()
    bu = nc.dram_tensor("bu", [128, 16], F32, kind="ExternalInput").ap()
    bo = nc.dram_tensor("bo", [128, 8], F32, kind="ExternalInput").ap()
    cosq = nc.dram_tensor("cosq", [128, n], BF16, kind="ExternalInput").ap()
    sinq = nc.dram_tensor("sinq", [128, n], BF16, kind="ExternalInput").ap()
    permh = nc.dram_tensor("perm", [128, 128], BF16, kind="ExternalInput").ap()
    identh = nc.dram_tensor("ident", [128, 128], BF16, kind="ExternalInput").ap()
    xo = nc.dram_tensor("xo", [D, n], F32, kind="ExternalOutput").ap()

    em = Em(nc)
    x = Chunked(em, "x", NCH, 512, F32)
    xn = Chunked(em, "xn", NCH, n, BF16)
    q = Chunked(em, "q", NCH, n, BF16)
    kvl = Chunked(em, "kvl", 2, n, BF16)
    kv = Chunked(em, "kv", 16, n, BF16)
    of = xn
    q_tm = em.sb("q_tm", [128, 1024], BF16); q_tm_b = Buf("q_tm")
    kv_tm = em.sb("kv_tm", [128, 2048], BF16); kv_tm_b = Buf("kv_tm")
    p_scr = em.sb("p_scr", [128, 4096], BF16); p_scr_b = Buf("p_scr")
    s_red = em.sb("s_red", [128, 256], F32); s_red_b = Buf("s_red")
    e_tm = em.sb("e_tm", [128, 256], BF16); e_tm_b = Buf("e_tm")
    z_tm = em.sb("z_tm", [128, 16], F32); z_tm_b = Buf("z_tm")
    o_red = em.sb("o_red", [128, 1024], F32); o_red_b = Buf("o_red")
    o_tm = em.sb("o_tm", [128, 1024], BF16); o_tm_b = Buf("o_tm")
    xout = em.sb("xout", [128, 512], F32); xout_b = Buf("xout")
    scr = em.sb("scr", [128, 512], F32)
    em.scr = (scr, Buf("scr"))
    consts = {}
    for nm, hh, shape, dt in (("bq", bq, [128, 8], F32), ("bd", bd, [128, 2], F32),
                              ("bu", bu, [128, 16], F32), ("bo", bo, [128, 8], F32),
                              ("cosq", cosq, [128, n], BF16), ("sinq", sinq, [128, n], BF16),
                              ("perm", permh, [128, 128], BF16), ("ident", identh, [128, 128], BF16)):
        t = em.sb("c_" + nm, shape, dt)
        b = Buf(nm)
        em.emit_dma("sp", "cst", (lambda t=t, hh=hh: lambda e: e.dma_start(out=t[:, :], in_=hh))(),
                    writes=[b])
        consts[nm] = (t, b)
    misc = {}
    for nm, shape, val in (("ones_col", [128, 1], 1.0), ("ones_row", [1, 128], 1.0),
                           ("eps", [1, 1], 1e-5)):
        t = em.sb(nm, shape, F32)
        b = Buf(nm)
        em.emit("dve", (lambda t=t, val=val: lambda e: e.memset(t[:, :], val))(), writes=[b])
        misc[nm] = (t, b)
    for nm, shape in (("sq", [128, 512]), ("r_s1", [1, 512]), ("r_s2", [1, 512]),
                      ("r_a", [1, 512]), ("r_b", [1, 512])):
        t = em.sb(nm, shape, F32)
        misc[nm] = (t, Buf(nm))
    em.close_batch("cst")
    wb1 = em.sb("wbufA", [128, 2048], BF16)
    wb2 = em.sb("wbufB", [128, 2048], BF16)
    wbufs = [(wb1, Buf("wbufA")), (wb2, Buf("wbufB"))]
    pp = PsumPool(em, n=6, width=512)
    pt1 = em.ps("pt1", [128, 512], BF16)
    pt1_b = Buf("pt1")
    out_toks = []

    for t0c in range(0, n, 512):
        for ci in range(NCH):
            em.emit_dma("sp", "xin", (lambda ci=ci, t0c=t0c: lambda e: e.dma_start(
                out=x.h[ci][:, :], in_=xs[ci*128:(ci+1)*128, t0c:t0c+512]))(), writes=[x.b[ci]])
        em.close_batch("xin")
        build_layernorm(em, x, xn, pp, misc, 512, out_col0=t0c)
    def mk_ep(dst, bias_nm):
        def ep(mi, ps, ps_b, t0, tn):
            em.emit("act", (lambda mi=mi, ps=ps, t0=t0, tn=tn: lambda e: e.activation(
                dst.h[mi][:, t0:t0+tn], ps[:, 0:tn], AF.Identity,
                bias=consts[bias_nm][0][:, mi:mi+1]))(),
                reads=[ps_b, consts[bias_nm][1]], writes=[dst.b[mi]])
        return ep
    build_project(em, xn, WqT, 1024, 1024, n, pp, wbufs, mk_ep(q, "bq"), mgroup_chunks=2)
    build_project(em, xn, WdT, 1024, 256, n, pp, wbufs, mk_ep(kvl, "bd"), mgroup_chunks=2)
    build_project(em, kvl, WuT, 256, 2048, n, pp, wbufs, mk_ep(kv, "bu"), mgroup_chunks=2)
    rope_apply(em, q.h, q.b, consts["cosq"][0], consts["cosq"][1],
               consts["sinq"][0], consts["sinq"][1], consts["perm"][0], consts["perm"][1],
               pp, n)
    rope_apply(em, kv.h, kv.b, consts["cosq"][0], consts["cosq"][1],
               consts["sinq"][0], consts["sinq"][1], consts["perm"][0], consts["perm"][1],
               pp, n, strip=(0, 64))
    ident, ident_b = consts["ident"]
    for tt in range(n // 128):
        c0 = tt * 128
        for ci in range(8):
            pe_transpose(em, q_tm, q_tm_b, ci*128,
                         (lambda ci=ci, c0=c0: q.h[ci][:, c0:c0+128])(ci, c0) if False else
                         (lambda ci=ci, c0=c0: lambda: q.h[ci][:, c0:c0+128])()(),
                         [q.b[ci]], ident, ident_b, pt1, pt1_b) if False else None
        for ci in range(8):
            em.emit("pe", (lambda ci=ci, c0=c0: lambda e: e.transpose(
                pt1[:, 0:128], q.h[ci][:, c0:c0+128], ident[:, :]))(),
                reads=[q.b[ci], ident_b], writes=[pt1_b])
            em.emit("dve", (lambda ci=ci: lambda e: e.tensor_copy(
                q_tm[:, ci*128:(ci+1)*128], pt1[:, 0:128]))(),
                reads=[pt1_b], writes=[q_tm_b])
        for ci in range(16):
            em.emit("pe", (lambda ci=ci, c0=c0: lambda e: e.transpose(
                pt1[:, 0:128], kv.h[ci][:, c0:c0+128], ident[:, :]))(),
                reads=[kv.b[ci], ident_b], writes=[pt1_b])
            em.emit("dve", (lambda ci=ci: lambda e: e.tensor_copy(
                kv_tm[:, ci*128:(ci+1)*128], pt1[:, 0:128]))(),
                reads=[pt1_b], writes=[kv_tm_b])
        for hh in range(4):
            em.emit("dve", (lambda hh=hh: lambda e: e.tensor_mul(
                bass.AP(p_scr, 0, [[4096, 128], [1024, 4], [64, 16], [1, 64]]),
                bass.AP(q_tm, hh*256, [[1024, 128], [64, 4], [0, 16], [1, 64]]),
                bass.AP(kv_tm, 0, [[2048, 128], [0, 4], [128, 16], [1, 64]])))(),
                reads=[q_tm_b, kv_tm_b], writes=[p_scr_b])
            em.emit("dve", (lambda hh=hh: lambda e: e.tensor_reduce(
                s_red[:, hh*64:(hh+1)*64], bass.AP(p_scr, 0, [[4096, 128], [64, 64], [1, 64]]),
                axis=AX.X, op=ALU.add))(),
                reads=[p_scr_b], writes=[s_red_b])
        em.emit("act", (lambda: lambda e: e.activation(
            e_tm[:, :], s_red[:, :], AF.Exp, scale=0.125))(),
            reads=[s_red_b], writes=[e_tm_b])
        em.emit("dve", (lambda: lambda e: e.tensor_reduce(
            z_tm[:, :], bass.AP(e_tm, 0, [[256, 128], [16, 16], [1, 16]]),
            axis=AX.X, op=ALU.add))(),
            reads=[e_tm_b], writes=[z_tm_b])
        em.emit("dve", (lambda: lambda e: e.reciprocal(z_tm[:, :], z_tm[:, :]))(),
            reads=[z_tm_b], writes=[z_tm_b])
        for hh in range(4):
            em.emit("dve", (lambda hh=hh: lambda e: e.tensor_mul(
                bass.AP(p_scr, 0, [[4096, 128], [1024, 4], [16, 64], [1, 16]]),
                bass.AP(e_tm, hh*64, [[256, 128], [16, 4], [0, 64], [1, 16]]),
                bass.AP(kv_tm, 64, [[2048, 128], [0, 4], [1, 64], [128, 16]])))(),
                reads=[e_tm_b, kv_tm_b], writes=[p_scr_b])
            em.emit("dve", (lambda hh=hh: lambda e: e.tensor_reduce(
                o_red[:, hh*256:(hh+1)*256], bass.AP(p_scr, 0, [[4096, 128], [16, 256], [1, 16]]),
                axis=AX.X, op=ALU.add))(),
                reads=[p_scr_b], writes=[o_red_b])
        em.emit("dve", (lambda: lambda e: e.tensor_mul(
            o_tm[:, :], o_red[:, :],
            bass.AP(z_tm, 0, [[16, 128], [1, 16], [0, 64]])))(),
            reads=[o_red_b, z_tm_b], writes=[o_tm_b])
        for ci in range(8):
            em.emit("pe", (lambda ci=ci: lambda e: e.transpose(
                pt1[:, 0:128], o_tm[:, ci*128:(ci+1)*128], ident[:, :]))(),
                reads=[o_tm_b, ident_b], writes=[pt1_b])
            em.emit("dve", (lambda ci=ci, c0=c0: lambda e: e.tensor_copy(
                of.h[ci][:, c0:c0+128], pt1[:, 0:128]))(),
                reads=[pt1_b], writes=[of.b[ci]])
    def ep_o(mi, ps, ps_b, t0, tn):
        em.emit_dma("sp", "xin", (lambda mi=mi, t0=t0, tn=tn: lambda e: e.dma_start(
            out=x.h[mi][:, 0:tn], in_=xs[mi*128:(mi+1)*128, t0:t0+tn]))(),
            writes=[x.b[mi]])
        em.close_batch("xin")
        em.emit("dve", (lambda mi=mi, ps=ps, t0=t0, tn=tn: lambda e: e.scalar_tensor_tensor(
            xout[:, 0:tn], ps[:, 0:tn], consts["bo"][0][:, mi:mi+1],
            x.h[mi][:, 0:tn], ALU.add, ALU.add))(),
            reads=[ps_b, consts["bo"][1], x.b[mi]], writes=[xout_b])
        out_toks.append(em.emit_dma("sp", "out", (lambda mi=mi, t0=t0, tn=tn: lambda e: e.dma_start(
            out=xo[mi*128:(mi+1)*128, t0:t0+tn], in_=xout[:, 0:tn]))(),
            reads=[xout_b]))
        em.close_batch("out")
    build_project(em, of, WoT, 1024, 1024, n, pp, wbufs, ep_o, mgroup_chunks=2)
    em.finalize(out_toks)
    return nc

def rope_tables_l0(pos):
    n = len(pos)
    inv = 1.0 / 10000.0 ** (np.arange(0, 64, 2, dtype=np.float64) / 64.0)
    f = pos[None, :].astype(np.float64) * inv[:, None]
    cos, sin = np.cos(f), np.sin(f)
    cosT = np.zeros((128, n), np.float32)
    sinT = np.zeros((128, n), np.float32)
    for base in (0, 64):
        cosT[base:base+32] = cos; cosT[base+32:base+64] = cos
        sinT[base:base+32] = -sin; sinT[base+32:base+64] = sin
    return cosT, sinT

def swap_perm():
    P = np.zeros((128, 128), np.float32)
    for base in (0, 64):
        for i in range(32):
            P[base + 32 + i, base + i] = 1.0
            P[base + i, base + 32 + i] = 1.0
    return P.astype(BF16NP)

def pack_bias(b, nch):
    return np.ascontiguousarray(np.asarray(b, np.float32).reshape(nch, 128).T)

def l0attn_inputs(xT_shard, pos, ins):
    cosT, sinT = rope_tables_l0(pos)
    return {
        "xs": np.ascontiguousarray(xT_shard, np.float32),
        "WqT": prep_w(ins["L0_attn_Wq"]), "WdT": prep_w(ins["L0_attn_Wdown"]),
        "WuT": prep_w(ins["L0_attn_Wup"]), "WoT": prep_w(ins["L0_attn_Wout"]),
        "bq": pack_bias(ins["L0_attn_bq"], 8), "bd": pack_bias(ins["L0_attn_bdown"], 2),
        "bu": pack_bias(ins["L0_attn_bup"], 16), "bo": pack_bias(ins["L0_attn_bout"], 8),
        "cosq": cosT.astype(BF16NP), "sinq": sinT.astype(BF16NP), "perm": swap_perm(),
        "ident": np.eye(128, dtype=BF16NP),
    }


def setup_misc(em, row_w=512):
    misc = {}
    for nm, shape, val in (("ones_col", [128, 1], 1.0), ("ones_row", [1, 128], 1.0),
                           ("eps", [1, 1], 1e-5)):
        t = em.sb(nm, shape, F32)
        b = Buf(nm)
        em.emit("dve", (lambda t=t, val=val: lambda e: e.memset(t[:, :], val))(), writes=[b])
        misc[nm] = (t, b)
    for nm, shape in (("sq", [128, 512]), ("r_s1", [1, row_w]), ("r_s2", [1, row_w]),
                      ("r_a", [1, row_w]), ("r_b", [1, row_w])):
        t = em.sb(nm, shape, F32)
        misc[nm] = (t, Buf(nm))
    return misc

def load_consts(em, nc, specs):
    out = {}
    for nm, shape, dt in specs:
        hh = nc.dram_tensor(nm, list(shape), dt, kind="ExternalInput").ap()
        t = em.sb("c_" + nm, list(shape), dt)
        b = Buf(nm)
        em.emit_dma("sp", "cst", (lambda t=t, hh=hh: lambda e: e.dma_start(out=t[:, :], in_=hh))(),
                    writes=[b])
        out[nm] = (t, b)
    return out

def load_x_tiled(em, x, xs_ap, n, width=512, slot="xin"):
    for ci in range(x.nch):
        em.emit_dma("sp", slot, (lambda ci=ci: lambda e: e.dma_start(
            out=x.h[ci][:, 0:n], in_=xs_ap[ci*128:(ci+1)*128, 0:n]))(), writes=[x.b[ci]])
    em.close_batch(slot)


def build_qkv3(ntok_core=PT):
    n = ntok_core
    nc = bass.Bass("TRN2", target_bir_lowering=False)
    xs = nc.dram_tensor("xs", [D, n], F32, kind="ExternalInput").ap()
    wcatT = nc.dram_tensor("wcatT", [1024, 3072], F32, kind="ExternalInput").ap()
    qo = nc.dram_tensor("qo", [D, n], BF16, kind="ExternalOutput").ap()
    ko = nc.dram_tensor("ko", [D, n], BF16, kind="ExternalOutput").ap()
    vo = nc.dram_tensor("vo", [D, n], BF16, kind="ExternalOutput").ap()
    em = Em(nc)
    x = Chunked(em, "x", NCH, 512, F32)
    xn = Chunked(em, "xn", NCH, n, BF16)
    ot = em.sb("ot", [128, 512], BF16)
    ot_b = Buf("ot")
    misc = setup_misc(em)
    wb1 = em.sb("wbufA", [128, 4096], BF16)
    wb2 = em.sb("wbufB", [128, 4096], BF16)
    wbufs = [(wb1, Buf("wbufA")), (wb2, Buf("wbufB"))]
    pp = PsumPool(em, n=8, width=512)
    out_toks = []
    for t0c in range(0, n, 512):
        for ci in range(NCH):
            em.emit_dma("sp", "xin", (lambda ci=ci, t0c=t0c: lambda e: e.dma_start(
                out=x.h[ci][:, :], in_=xs[ci*128:(ci+1)*128, t0c:t0c+512]))(), writes=[x.b[ci]])
        em.close_batch("xin")
        build_layernorm(em, x, xn, pp, misc, 512, out_col0=t0c)
    outs = [qo, ko, vo]
    def ep(mi, ps, ps_b, t0, tn):
        dst = outs[mi // 8]
        r0 = (mi % 8) * 128
        em.emit("act", (lambda ps=ps, tn=tn: lambda e: e.activation(
            ot[:, 0:tn], ps[:, 0:tn], AF.Identity))(),
            reads=[ps_b], writes=[ot_b])
        out_toks.append(em.emit_dma("sp", "out", (lambda dst=dst, r0=r0, t0=t0, tn=tn: lambda e: e.dma_start(
            out=dst[r0:r0+128, t0:t0+tn], in_=ot[:, 0:tn]))(), reads=[ot_b]))
        em.close_batch("out")
    build_project(em, xn, wcatT, 1024, 3072, n, pp, wbufs, ep, mgroup_chunks=4)
    em.finalize(out_toks)
    return nc

def qkv3_inputs(xT_shard, wcat):
    return {"xs": np.ascontiguousarray(xT_shard, np.float32),
            "wcatT": np.ascontiguousarray(wcat, np.float32)}


def build_f1attn(do_ln, ntok_core=PT, HALO=1024, debug=False, upto=None):
    n = ntok_core
    W = HALO + n
    nw = n // 512
    nc = bass.Bass("TRN2", target_bir_lowering=False)
    xs = nc.dram_tensor("xe", [D, W], F32, kind="ExternalInput").ap()
    wqkvT = nc.dram_tensor("wqkvT", [1024, 1536], F32, kind="ExternalInput").ap()
    woT = nc.dram_tensor("woT", [512, 1024], F32, kind="ExternalInput").ap()
    xo = nc.dram_tensor("xo", [D, n], F32, kind="ExternalOutput").ap()
    em = Em(nc)
    xe = Chunked(em, "xe", NCH, 512, F32)
    xn = Chunked(em, "xn", NCH, W + 64, BF16)
    qkv = Chunked(em, "qkv", 12, W, BF16)
    import types as _t
    of = _t.SimpleNamespace(h=qkv.h[8:12], b=qkv.b[8:12], nch=4, ntok=W)
    vaug2 = em.sb("vaug2", [128, 1300], BF16); vaug2_b = Buf("vaug2")
    consts = load_consts(em, nc, [
        ("qs", [128, 1], F32), ("ks", [128, 1], F32),
        ("cq", [128, W - 512], BF16), ("sq_", [128, W - 512], BF16),
        ("ck", [128, W - 512], BF16), ("sk", [128, W - 512], BF16),
        ("perm", [128, 128], BF16), ("ident", [128, 128], BF16),
        ("e2", [128, 2], F32), ("e2t", [2, 128], F32),
        ("mA", [128, 4096], BF16), ("mB", [128, 2048], BF16)])
    misc = setup_misc(em)
    em.close_batch("cst")
    rr = em.sb("rr", [2, 512], F32); rr_b = Buf("rr")
    vs = em.sb("vs", [64, 128], BF16); vs_b = Buf("vs")
    NKC = (W - 512) // 128
    vaug = em.sb("vaug", [128, NKC * 65], BF16); vaug_b = Buf("vaug")
    expSH = [(em.sb(f"expS{i}", [128, 1024], BF16), Buf(f"expS{i}")) for i in range(4)]
    o_un = em.sb("o_un", [64, 512], BF16); o_un_b = Buf("o_un")
    r_row = em.sb("r_row", [1, 512], F32); r_row_b = Buf("r_row")
    xout = em.sb("xout", [128, 512], F32); xout_b = Buf("xout")
    scr = em.sb("scr", [128, 512], BF16)
    em.scr = (scr, Buf("scr"))
    wb1 = em.sb("wbufA", [128, 1024], BF16)
    wb2 = em.sb("wbufB", [128, 1024], BF16)
    wbufs = [(wb1, Buf("wbufA")), (wb2, Buf("wbufB"))]
    pp = PsumPool(em, n=2, width=512)
    ps_tb = em.ps("ps_tb", [128, 256], BF16); ps_tb_b = Buf("ps_tb")
    ps_sH = [(em.ps(f"ps_s{i}", [128, 1024]), Buf(f"ps_s{i}")) for i in range(2)]
    ps_o = em.ps("ps_o", [128, 512]); ps_o_b = Buf("ps_o")
    out_toks = []
    for ci in range(NCH):
        em.emit("dve", (lambda ci=ci: lambda e: e.memset(xn.h[ci][:, 0:1], 0.0))(),
                writes=[xn.b[ci]])
    if do_ln:
        for t0c in range(0, W, 512):
            for ci in range(NCH):
                em.emit_dma("sp", "xin", (lambda ci=ci, t0c=t0c: lambda e: e.dma_start(
                    out=xe.h[ci][:, 0:512], in_=xs[ci*128:(ci+1)*128, t0c:t0c+512]))(),
                    writes=[xe.b[ci]])
            em.close_batch("xin")
            build_layernorm(em, xe, xn, pp, misc, 512, out_col0=t0c + 1, x_col0=0)
    else:
        for ci in range(NCH):
            em.emit_dma("pool", "xin", (lambda ci=ci: lambda e: e.dma_start(
                out=xn.h[ci][:, 1:1+W], in_=xs[ci*128:(ci+1)*128, :]))(),
                writes=[xn.b[ci]])
        em.close_batch("xin")
    if upto == "ln":
        dbg2 = nc.dram_tensor("dbg2", [2 * 128, W], BF16, kind="ExternalOutput").ap()
        toks = []
        for i, ci in enumerate((0, 4)):
            toks.append(em.emit_dma("sp", "dbg", (lambda i=i, ci=ci: lambda e: e.dma_start(
                out=dbg2[i*128:(i+1)*128, :], in_=xn.h[ci][:, 1:1+W]))(), reads=[xn.b[ci]]))
        em.close_batch("dbg")
        for s in list(em.cnt):
            if s.startswith("d_"):
                toks.append(Tok(s, em.cnt[s]))
        em.finalize(toks)
        return nc
    def ep_qkv(mi, ps, ps_b, t0, tn):
        em.emit("act", (lambda mi=mi, ps=ps, t0=t0, tn=tn: lambda e: e.activation(
            qkv.h[mi][:, t0:t0+tn], ps[:, 0:tn], AF.Identity))(),
            reads=[ps_b], writes=[qkv.b[mi]])
    build_project(em, xn, wqkvT, 1024, 1536, W, pp, wbufs, ep_qkv,
                  mgroup_chunks=1, xoff=[1, 1, 1, 1, 0, 0, 0, 0])
    e2, e2_b = consts["e2"]
    e2t, e2t_b = consts["e2t"]
    for ci in range(8):
        scale_nm = "qs" if ci < 4 else "ks"
        for t0_, tn in mm_tiles(W - 512):
            t0 = t0_ + 512
            ps_n, ps_n_b = pp.take1()
            em.emit("act", (lambda ci=ci, t0=t0, tn=tn: lambda e: e.square(
                misc["sq"][0][:, 0:tn], qkv.h[ci][:, t0:t0+tn]))(),
                reads=[qkv.b[ci]], writes=[misc["sq"][1]])
            em.emit("pe", (lambda t0=t0, tn=tn, ps_n=ps_n: lambda e: e.matmul(
                ps_n[0:2, 0:tn], e2[:, :], misc["sq"][0][:, 0:tn], start=True, stop=True))(),
                reads=[e2_b, misc["sq"][1]], writes=[ps_n_b])
            em.emit("dve", (lambda tn=tn, ps_n=ps_n: lambda e: e.tensor_scalar_max(
                rr[0:2, 0:tn], ps_n[0:2, 0:tn], 1e-24))(),
                reads=[ps_n_b], writes=[rr_b])
            em.emit("act", (lambda tn=tn: lambda e: e.activation(
                rr[0:2, 0:tn], rr[0:2, 0:tn], AF.Ln))(),
                reads=[rr_b], writes=[rr_b])
            em.emit("act", (lambda tn=tn: lambda e: e.activation(
                rr[0:2, 0:tn], rr[0:2, 0:tn], AF.Exp, scale=-0.5))(),
                reads=[rr_b], writes=[rr_b])
            ps_r, ps_r_b = pp.take1()
            em.emit("pe", (lambda tn=tn, ps_r=ps_r: lambda e: e.matmul(
                ps_r[:, 0:tn], e2t[:, :], rr[0:2, 0:tn], start=True, stop=True))(),
                reads=[e2t_b, rr_b], writes=[ps_r_b])
            em.emit("dve", (lambda ci=ci, t0=t0, tn=tn, ps_r=ps_r, scale_nm=scale_nm: lambda e: e.scalar_tensor_tensor(
                qkv.h[ci][:, t0:t0+tn], qkv.h[ci][:, t0:t0+tn], consts[scale_nm][0][:, 0:1],
                ps_r[:, 0:tn], ALU.mult, ALU.mult))(),
                reads=[qkv.b[ci], consts[scale_nm][1], ps_r_b], writes=[qkv.b[ci]])
    rope_apply(em, qkv.h[0:4], qkv.b[0:4], consts["cq"][0], consts["cq"][1],
               consts["sq_"][0], consts["sq_"][1], consts["perm"][0], consts["perm"][1],
               pp, W - 512, col0=512)
    rope_apply(em, qkv.h[4:8], qkv.b[4:8], consts["ck"][0], consts["ck"][1],
               consts["sk"][0], consts["sk"][1], consts["perm"][0], consts["perm"][1],
               pp, W - 512, col0=512)
    if debug:
        dbg = nc.dram_tensor("dbg", [3 * 128, W], BF16, kind="ExternalOutput").ap()
        for i, ci in enumerate((0, 4, 8)):
            out_toks.append(em.emit_dma("sp", "dbg", (lambda i=i, ci=ci: lambda e: e.dma_start(
                out=dbg[i*128:(i+1)*128, :], in_=qkv.h[ci][:, 0:W]))(), reads=[qkv.b[ci]]))
        dbg2 = nc.dram_tensor("dbg2", [2 * 128, W], BF16, kind="ExternalOutput").ap()
        for i, ci in enumerate((0, 4)):
            out_toks.append(em.emit_dma("sp", "dbg", (lambda i=i, ci=ci: lambda e: e.dma_start(
                out=dbg2[i*128:(i+1)*128, :], in_=xn.h[ci][:, 1:1+W]))(), reads=[xn.b[ci]]))
        em.close_batch("dbg")
    ident, ident_b = consts["ident"]
    def build_vaug(vdst, vdst_b, vch, base):
        em.emit("dve", (lambda vdst=vdst: lambda e: e.memset(vdst[:, :], 1.0))(), writes=[vdst_b])
        for kc in range(NKC):
            em.emit("dve", (lambda vch=vch, base=base, kc=kc: lambda e: e.tensor_copy(
                vs[0:64, 0:128], qkv.h[vch][base:base+64, 512+kc*128:512+(kc+1)*128]))(),
                reads=[qkv.b[vch]], writes=[vs_b])
            em.emit("pe", (lambda kc=kc: lambda e: e.transpose(
                ps_tb[:, 0:64], vs[0:64, 0:128], ident[0:64, 0:64]))(),
                reads=[vs_b, ident_b], writes=[ps_tb_b])
            em.emit("dve", (lambda vdst=vdst, kc=kc: lambda e: e.tensor_copy(
                vdst[:, kc*65:kc*65+64], ps_tb[:, 0:64]))(),
                reads=[ps_tb_b], writes=[vdst_b])
    for hp in range(4):
        vch = 8 + hp
        build_vaug(vaug, vaug_b, vch, 0)
        build_vaug(vaug2, vaug2_b, vch, 64)
        for h in (2*hp, 2*hp + 1):
            base = (h % 2) * 64
            va, va_b = (vaug, vaug_b) if h % 2 == 0 else (vaug2, vaug2_b)
            qch = h // 2
            kch = 4 + h // 2
            for w in range(nw):
                q0 = HALO + w * 512
                k0 = 512 + w * 512
                avq = []
                for g in range(2):
                    mk, mk_b = consts["mB"] if (w == 0 and g == 0) else consts["mA"]
                    for hf in range(2):
                        psS, psS_b = ps_sH[hf]
                        exS, exS_b = expSH[g * 2 + hf]
                        def mk_s(g=g, base=base, qch=qch, kch=kch, q0=q0, k0=k0,
                                 hf=hf, psS=psS):
                            def f(e):
                                ins = None
                                for j in range(2):
                                    kc = g * 4 + hf * 2 + j
                                    ins = e.matmul(psS[:, j*512:(j+1)*512],
                                                   qkv.h[kch][base:base+64, k0+kc*128:k0+(kc+1)*128],
                                                   qkv.h[qch][base:base+64, q0:q0+512],
                                                   start=True, stop=True)
                                return ins
                            return f
                        em.emit("pe", mk_s(), reads=[qkv.b[kch], qkv.b[qch]], writes=[psS_b])
                        em.emit("act", (lambda psS=psS, exS=exS: lambda e: e.activation(
                            exS[:, 0:1024], psS[:, 0:1024], AF.Exp, scale=8.0))(),
                            reads=[psS_b], writes=[exS_b])
                        em.emit("dve", (lambda mk=mk, g=g, w=w, hf=hf, exS=exS: lambda e: e.tensor_mul(
                            exS[:, 0:1024], exS[:, 0:1024],
                            mk[:, hf*1024:(hf+1)*1024] if (w == 0 and g == 0)
                            else mk[:, g*2048+hf*1024:g*2048+(hf+1)*1024]))(),
                            reads=[exS_b, mk_b], writes=[exS_b])
                        def mk_av(w=w, g=g, va=va, hf=hf, exS=exS):
                            def f(e):
                                ins = None
                                for j in range(2):
                                    kc = g * 4 + hf * 2 + j
                                    blk = w * 4 + kc
                                    ins = e.matmul(ps_o[0:65, 0:512],
                                                   va[:, blk*65:blk*65+65],
                                                   exS[:, j*512:(j+1)*512],
                                                   start=(kc == 0), stop=(kc == 7))
                                return ins
                            return f
                        avq.append((mk_av(), [va_b, exS_b]))
                for fn, rds in avq:
                    em.emit("pe", fn, reads=rds, writes=[ps_o_b])
                em.emit("dve", (lambda: lambda e: e.reciprocal(r_row[0:1, 0:512], ps_o[64:65, 0:512]))(),
                    reads=[ps_o_b], writes=[r_row_b])
                em.emit("dve", (lambda: lambda e: e.tensor_copy(o_un[0:64, 0:512], ps_o[0:64, 0:512]))(),
                    reads=[ps_o_b], writes=[o_un_b])
                ps_bc, ps_bc_b = pp.take1()
                em.emit("pe", (lambda ps_bc=ps_bc: lambda e: e.matmul(
                    ps_bc[0:64, 0:512], misc["ones_row"][0][0:1, 0:64], r_row[0:1, 0:512],
                    start=True, stop=True))(),
                    reads=[misc["ones_row"][1], r_row_b], writes=[ps_bc_b])
                em.emit("dve", (lambda hp=hp, w=w, base=base, ps_bc=ps_bc: lambda e: e.tensor_mul(
                    of.h[hp][base:base+64, w*512:(w+1)*512], o_un[0:64, 0:512], ps_bc[0:64, 0:512]))(),
                    reads=[o_un_b, ps_bc_b], writes=[of.b[hp]])
    if upto == "attn":
        dbg3 = nc.dram_tensor("dbg3", [4 * 128, n], BF16, kind="ExternalOutput").ap()
        toks = []
        for ci in range(4):
            toks.append(em.emit_dma("sp", "dbg", (lambda ci=ci: lambda e: e.dma_start(
                out=dbg3[ci*128:(ci+1)*128, :], in_=of.h[ci][:, 0:n]))(), reads=[of.b[ci]]))
        em.close_batch("dbg")
        for s in list(em.cnt):
            if s.startswith("d_"):
                toks.append(Tok(s, em.cnt[s]))
        em.finalize(toks)
        return nc
    def ep_o(mi, ps, ps_b, t0, tn):
        if do_ln:
            em.emit("dve", (lambda mi=mi, ps=ps, t0=t0, tn=tn: lambda e: e.tensor_add(
                xout[:, 0:tn], ps[:, 0:tn], xn.h[mi][:, HALO+1+t0:HALO+1+t0+tn]))(),
                reads=[ps_b, xn.b[mi]], writes=[xout_b])
        else:
            em.emit_dma("sp", "xres", (lambda mi=mi, t0=t0, tn=tn: lambda e: e.dma_start(
                out=xe.h[mi][:, 0:tn], in_=xs[mi*128:(mi+1)*128, HALO+t0:HALO+t0+tn]))(),
                writes=[xe.b[mi]])
            em.close_batch("xres")
            em.emit("dve", (lambda mi=mi, ps=ps, t0=t0, tn=tn: lambda e: e.tensor_add(
                xout[:, 0:tn], ps[:, 0:tn], xe.h[mi][:, 0:tn]))(),
                reads=[ps_b, xe.b[mi]], writes=[xout_b])
        out_toks.append(em.emit_dma("sp", "out", (lambda mi=mi, t0=t0, tn=tn: lambda e: e.dma_start(
            out=xo[mi*128:(mi+1)*128, t0:t0+tn], in_=xout[:, 0:tn]))(), reads=[xout_b]))
    build_project(em, of, woT, 512, 1024, n, pp, wbufs, ep_o, mgroup_chunks=2)
    em.finalize(out_toks)
    return nc


def build_f1ffn(final_add, ntok_core=PT):
    n = ntok_core
    nc = bass.Bass("TRN2", target_bir_lowering=False)
    xs = nc.dram_tensor("xe", [D, n + 1], F32, kind="ExternalInput").ap()
    w1gT = nc.dram_tensor("w1gT", [1024, 2816], F32, kind="ExternalInput").ap()
    w1aT = nc.dram_tensor("w1aT", [1024, 2816], F32, kind="ExternalInput").ap()
    w2T = nc.dram_tensor("w2T", [2816, 1024], F32, kind="ExternalInput").ap()
    x3s = nc.dram_tensor("x3s", [D, n], F32, kind="ExternalInput").ap() if final_add else None
    xo = nc.dram_tensor("xo", [D, n], F32, kind="ExternalOutput").ap()
    em = Em(nc)
    xe = Chunked(em, "xe", NCH, 513, F32)
    xn = Chunked(em, "xn", NCH, n, BF16)
    hg = Chunked(em, "hg", 22, 1024, BF16)
    hh = Chunked(em, "hh", 22, 1024, BF16)
    consts = load_consts(em, nc, [
        ("lg", [128, 8], F32), ("lb", [128, 8], F32),
        ("b1a", [128, 22], F32), ("b1g", [128, 22], F32), ("b2", [128, 8], F32)])
    misc = setup_misc(em)
    em.close_batch("cst")
    xout = em.sb("xout", [128, 512], F32); xout_b = Buf("xout")
    x3t = em.sb("x3t", [128, 512], F32); x3t_b = Buf("x3t")
    wb1 = em.sb("wbufA", [128, 5632], BF16)
    wb2 = em.sb("wbufB", [128, 5632], BF16)
    wbufs = [(wb1, Buf("wbufA")), (wb2, Buf("wbufB"))]
    pp = PsumPool(em, n=8, width=512)
    out_toks = []
    for t0c in range(0, n, 512):
        for ci in range(NCH):
            em.emit_dma("sp", "xin", (lambda ci=ci, t0c=t0c: lambda e: e.dma_start(
                out=xe.h[ci][:, 0:513], in_=xs[ci*128:(ci+1)*128, t0c:t0c+513]))(),
                writes=[xe.b[ci]])
        em.close_batch("xin")
        build_layernorm(em, xe, xn, pp, misc, 512, out_col0=t0c, x_col0=0,
                        xoff=[1, 1, 1, 1, 0, 0, 0, 0],
                        gain=consts["lg"], lbias=consts["lb"])
    for tt in range(0, n, 1024):
        def ep_g(mi, ps, ps_b, t0, tn):
            em.emit("act", (lambda mi=mi, ps=ps, t0=t0, tn=tn: lambda e: e.activation(
                hg.h[mi][:, t0:t0+tn], ps[:, 0:tn], AF.Gelu,
                bias=consts["b1g"][0][:, mi:mi+1]))(),
                reads=[ps_b, consts["b1g"][1]], writes=[hg.b[mi]])
        build_project(em, xn, w1gT, 1024, 2816, 1024, pp, wbufs, ep_g,
                      mgroup_chunks=4, xoff=[tt] * 8)
        def ep_a(mi, ps, ps_b, t0, tn):
            em.emit("dve", (lambda mi=mi, ps=ps, t0=t0, tn=tn: lambda e: e.scalar_tensor_tensor(
                hh.h[mi][:, t0:t0+tn], ps[:, 0:tn], consts["b1a"][0][:, mi:mi+1],
                hg.h[mi][:, t0:t0+tn], ALU.add, ALU.mult))(),
                reads=[ps_b, consts["b1a"][1], hg.b[mi]], writes=[hh.b[mi]])
        build_project(em, xn, w1aT, 1024, 2816, 1024, pp, wbufs, ep_a,
                      mgroup_chunks=4, xoff=[tt] * 8)
        def ep_o(mi, ps, ps_b, t0, tn, tt=tt):
            em.emit_dma("sp", "xres", (lambda mi=mi, t0=t0, tn=tn, tt=tt: lambda e: e.dma_start(
                out=xe.h[mi][:, 0:tn], in_=xs[mi*128:(mi+1)*128, 1+tt+t0:1+tt+t0+tn]))(),
                writes=[xe.b[mi]])
            em.close_batch("xres")
            em.emit("dve", (lambda mi=mi, ps=ps, t0=t0, tn=tn: lambda e: e.scalar_tensor_tensor(
                xout[:, 0:tn], ps[:, 0:tn], consts["b2"][0][:, mi:mi+1],
                xe.h[mi][:, 0:tn], ALU.add, ALU.add))(),
                reads=[ps_b, consts["b2"][1], xe.b[mi]], writes=[xout_b])
            if final_add:
                em.emit_dma("sp", "x3in", (lambda mi=mi, t0=t0, tn=tn, tt=tt: lambda e: e.dma_start(
                    out=x3t[:, 0:tn], in_=x3s[mi*128:(mi+1)*128, tt+t0:tt+t0+tn]))(),
                    writes=[x3t_b])
                em.close_batch("x3in")
                em.emit("dve", (lambda tn=tn: lambda e: e.tensor_add(
                    xout[:, 0:tn], xout[:, 0:tn], x3t[:, 0:tn]))(),
                    reads=[xout_b, x3t_b], writes=[xout_b])
            out_toks.append(em.emit_dma("sp", "out", (lambda mi=mi, t0=t0, tn=tn, tt=tt: lambda e: e.dma_start(
                out=xo[mi*128:(mi+1)*128, tt+t0:tt+t0+tn], in_=xout[:, 0:tn]))(),
                reads=[xout_b]))
            em.close_batch("out")
        build_project(em, hh, w2T, 2816, 1024, 1024, pp, wbufs, ep_o, mgroup_chunks=2)
    em.finalize(out_toks)
    return nc


def xpos_tables(pos):
    DH, WIN = 64, 512
    half = np.arange(0, DH, 2, dtype=np.float64)
    inv = 1.0 / 10000.0 ** (half / DH)
    t = pos.astype(np.float64)
    f = t[None, :] * inv[:, None]
    base = (half + 0.4 * DH) / (1.4 * DH)
    power = (t - 8192 // 2) / float(WIN // 2)
    scale = base[:, None] ** power[None, :]
    cos, sin = np.cos(f), np.sin(f)
    def pack(cv, sv):
        cT = np.zeros((128, len(pos)), np.float32)
        sT = np.zeros((128, len(pos)), np.float32)
        for b0 in (0, 64):
            cT[b0:b0+32] = cv; cT[b0+32:b0+64] = cv
            sT[b0:b0+32] = -sv; sT[b0+32:b0+64] = sv
        return cT.astype(BF16NP), sT.astype(BF16NP)
    cq, sq = pack(cos * scale, sin * scale)
    ck, sk = pack(cos / scale, sin / scale)
    return cq, sq, ck, sk

def f1_masks(batch_start):
    m = np.zeros((128, 8, 512), np.float32)
    for kc in range(8):
        ki = np.arange(128)
        j = kc * 128 + ki
        for qi in range(512):
            d = qi + 512 - j
            m[:, kc, qi] = ((d >= 0) & (d <= 512)).astype(np.float32)
    mA = m.reshape(128, 4096)
    mB = mA.copy()
    if batch_start:
        mB[:, 0:2048] = 0.0
    return mA.astype(BF16NP), mB.astype(BF16NP)

def f1attn_inputs(xe_shard, pos_qkv, wqkv, qs, ks, wo, batch_start):
    cq, sq, ck, sk = xpos_tables(pos_qkv[512:])
    mA, mB = f1_masks(batch_start)
    mB = mB[:, 0:2048].copy()
    return {
        "xe": np.ascontiguousarray(xe_shard, np.float32),
        "wqkvT": prep_w(wqkv), "woT": prep_w(wo),
        "qs": np.ascontiguousarray(np.tile(np.asarray(qs, np.float32), 2)[:, None]),
        "ks": np.ascontiguousarray(np.tile(np.asarray(ks, np.float32), 2)[:, None]),
        "cq": cq, "sq_": sq, "ck": ck, "sk": sk,
        "perm": swap_perm(), "ident": np.eye(128, dtype=BF16NP),
        "e2": np.ascontiguousarray(np.kron(np.eye(2), np.ones((64, 1))).astype(np.float32)),
        "e2t": np.ascontiguousarray(np.kron(np.eye(2), np.ones((1, 64))).astype(np.float32)),
        "mA": mA, "mB": mB,
    }

def f1ffn_inputs(xe_shard, lg, lb, w1, b1, w2, b2, x3_shard=None):
    HID = 2730
    w1 = np.asarray(w1, np.float32)
    w1a = np.zeros((1024, 2816), np.float32); w1a[:, :HID] = w1[:, :HID]
    w1g = np.zeros((1024, 2816), np.float32); w1g[:, :HID] = w1[:, HID:]
    b1 = np.asarray(b1, np.float32)
    b1a = np.zeros(2816, np.float32); b1a[:HID] = b1[:HID]
    b1g = np.zeros(2816, np.float32); b1g[:HID] = b1[HID:]
    w2p = np.zeros((2816, 1024), np.float32); w2p[:HID] = np.asarray(w2, np.float32)
    d = {
        "xe": np.ascontiguousarray(xe_shard, np.float32),
        "w1gT": w1g, "w1aT": w1a, "w2T": w2p,
        "lg": pack_bias(lg, 8), "lb": pack_bias(lb, 8),
        "b1a": pack_bias(b1a, 22), "b1g": pack_bias(b1g, 22),
        "b2": pack_bias(b2, 8),
    }
    if x3_shard is not None:
        d["x3s"] = np.ascontiguousarray(x3_shard, np.float32)
    return d


def build_l2a(ntok_core=PT, HALO=256):
    n = ntok_core
    W = HALO + n
    NKC = W // 128
    nc = bass.Bass("TRN2", target_bir_lowering=False)
    xs = nc.dram_tensor("xe", [D, W], F32, kind="ExternalInput").ap()
    WqT = nc.dram_tensor("WqT", [1024, 1024], F32, kind="ExternalInput").ap()
    WkT = nc.dram_tensor("WkT", [1024, 256], F32, kind="ExternalInput").ap()
    WvT = nc.dram_tensor("WvT", [1024, 256], F32, kind="ExternalInput").ap()
    WoT = nc.dram_tensor("WoT", [1024, 1024], F32, kind="ExternalInput").ap()
    xo = nc.dram_tensor("xo", [D, n], F32, kind="ExternalOutput").ap()
    em = Em(nc)
    xe = Chunked(em, "xe", NCH, 512, F32)
    xn = Chunked(em, "xn", NCH, W, BF16)
    q = Chunked(em, "q", NCH, n, BF16)
    kk = Chunked(em, "kk", 2, W, BF16)
    vv = Chunked(em, "vv", 2, W, BF16)
    of = xn
    of2 = Chunked(em, "of2", NCH, n, BF16)
    consts = load_consts(em, nc, [
        ("cq", [128, n], BF16), ("sq_", [128, n], BF16),
        ("ck", [128, W], BF16), ("sk", [128, W], BF16),
        ("perm", [128, 128], BF16), ("ident", [128, 128], BF16),
        ("e2", [128, 2], F32), ("e2t", [2, 128], F32),
        ("qg", [128, 8], F32), ("pm", [128, 16], F32),
        ("msk", [128, 16 * 384], BF16), ("eps6", [128, 1], F32)])
    misc = setup_misc(em)
    em.close_batch("cst")
    rr = em.sb("rr", [2, 512], F32); rr_b = Buf("rr")
    vs = em.sb("vs", [64, 128], BF16); vs_b = Buf("vs")
    krep = em.sb("krep", [128, W], BF16); krep_b = Buf("krep")
    vaug = em.sb("vaug", [128, NKC * 65], BF16); vaug_b = Buf("vaug")
    expS = em.sb("expS", [128, 384], BF16); expS_b = Buf("expS")
    o_un = em.sb("o_un", [64, 128], BF16); o_un_b = Buf("o_un")
    r_row = em.sb("r_row", [1, 128], F32); r_row_b = Buf("r_row")
    mixt = em.sb("mixt", [128, 512], F32); mixt_b = Buf("mixt")
    mixt2 = em.sb("mixt2", [128, 512], F32); mixt2_b = Buf("mixt2")
    xout = em.sb("xout", [128, 512], F32); xout_b = Buf("xout")
    scr = em.sb("scr", [128, 512], BF16)
    em.scr = (scr, Buf("scr"))
    wb1 = em.sb("wbufA", [128, 2048], BF16)
    wb2 = em.sb("wbufB", [128, 2048], BF16)
    wbufs = [(wb1, Buf("wbufA")), (wb2, Buf("wbufB"))]
    pp = PsumPool(em, n=2, width=512)
    ps_tb = em.ps("ps_tb", [128, 256], BF16); ps_tb_b = Buf("ps_tb")
    ps_s = em.ps("ps_s", [128, 512]); ps_s_b = Buf("ps_s")
    ps_o = em.ps("ps_o", [128, 128]); ps_o_b = Buf("ps_o")
    ps_bc = em.ps("ps_bc", [128, 128]); ps_bc_b = Buf("ps_bc")
    out_toks = []
    for t0c in range(0, W, 512):
        tw = min(512, W - t0c)
        for ci in range(NCH):
            em.emit_dma("sp", "xin", (lambda ci=ci, t0c=t0c, tw=tw: lambda e: e.dma_start(
                out=xe.h[ci][:, 0:tw], in_=xs[ci*128:(ci+1)*128, t0c:t0c+tw]))(),
                writes=[xe.b[ci]])
        em.close_batch("xin")
        build_layernorm(em, xe, xn, pp, misc, tw, out_col0=t0c, x_col0=0)
    def mk_ep(dst, col0=0):
        def ep(mi, ps, ps_b, t0, tn):
            em.emit("act", (lambda mi=mi, ps=ps, t0=t0, tn=tn: lambda e: e.activation(
                dst.h[mi][:, col0+t0:col0+t0+tn], ps[:, 0:tn], AF.Identity))(),
                reads=[ps_b], writes=[dst.b[mi]])
        return ep
    build_project(em, xn, WqT, 1024, 1024, n, pp, wbufs, mk_ep(q), mgroup_chunks=2,
                  xoff=[HALO] * 8)
    build_project(em, xn, WkT, 1024, 256, W, pp, wbufs, mk_ep(kk), mgroup_chunks=2)
    build_project(em, xn, WvT, 1024, 256, W, pp, wbufs, mk_ep(vv), mgroup_chunks=2)
    e2, e2_b = consts["e2"]
    e2t, e2t_b = consts["e2t"]
    def rmsn(ch, b, width):
        for t0, tn in mm_tiles(width):
            ps_n, ps_n_b = pp.take1()
            em.emit("act", (lambda ch=ch, t0=t0, tn=tn: lambda e: e.square(
                misc["sq"][0][:, 0:tn], ch[:, t0:t0+tn]))(),
                reads=[b], writes=[misc["sq"][1]])
            em.emit("pe", (lambda tn=tn, ps_n=ps_n: lambda e: e.matmul(
                ps_n[0:2, 0:tn], e2[:, :], misc["sq"][0][:, 0:tn], start=True, stop=True))(),
                reads=[e2_b, misc["sq"][1]], writes=[ps_n_b])
            em.emit("act", (lambda tn=tn, ps_n=ps_n: lambda e: e.activation(
                rr[0:2, 0:tn], ps_n[0:2, 0:tn], AF.Ln, scale=1.0/64.0,
                bias=consts["eps6"][0][0:2, 0:1]))(),
                reads=[ps_n_b, consts["eps6"][1]], writes=[rr_b])
            em.emit("act", (lambda tn=tn: lambda e: e.activation(
                rr[0:2, 0:tn], rr[0:2, 0:tn], AF.Exp, scale=-0.5))(),
                reads=[rr_b], writes=[rr_b])
            ps_r, ps_r_b = pp.take1()
            em.emit("pe", (lambda tn=tn, ps_r=ps_r: lambda e: e.matmul(
                ps_r[:, 0:tn], e2t[:, :], rr[0:2, 0:tn], start=True, stop=True))(),
                reads=[e2t_b, rr_b], writes=[ps_r_b])
            em.emit("dve", (lambda ch=ch, t0=t0, tn=tn, ps_r=ps_r: lambda e: e.tensor_mul(
                ch[:, t0:t0+tn], ch[:, t0:t0+tn], ps_r[:, 0:tn]))(),
                reads=[b, ps_r_b], writes=[b])
    for ci in range(8):
        rmsn(q.h[ci], q.b[ci], n)
    for ci in range(2):
        rmsn(kk.h[ci], kk.b[ci], W)
    rope_apply(em, q.h, q.b, consts["cq"][0], consts["cq"][1], consts["sq_"][0],
               consts["sq_"][1], consts["perm"][0], consts["perm"][1], pp, n)
    rope_apply(em, kk.h, kk.b, consts["ck"][0], consts["ck"][1], consts["sk"][0],
               consts["sk"][1], consts["perm"][0], consts["perm"][1], pp, W)
    for ci in range(8):
        em.emit("dve", (lambda ci=ci: lambda e: e.tensor_scalar_mul(
            q.h[ci][:, :], q.h[ci][:, :], consts["qg"][0][:, ci:ci+1]))(),
            reads=[q.b[ci], consts["qg"][1]], writes=[q.b[ci]])
    ident, ident_b = consts["ident"]
    for g in range(4):
        kch, kbase = g // 2, (g % 2) * 64
        em.emit("dve", (lambda kch=kch, kbase=kbase: lambda e: e.tensor_copy(
            krep[0:64, 0:W], kk.h[kch][kbase:kbase+64, 0:W]))(),
            reads=[kk.b[kch]], writes=[krep_b])
        em.emit("dve", (lambda kch=kch, kbase=kbase: lambda e: e.tensor_copy(
            krep[64:128, 0:W], kk.h[kch][kbase:kbase+64, 0:W]))(),
            reads=[kk.b[kch]], writes=[krep_b])
        em.emit("dve", (lambda: lambda e: e.memset(vaug[:, :], 1.0))(), writes=[vaug_b])
        for kc in range(NKC):
            em.emit("dve", (lambda kch=kch, kbase=kbase, kc=kc: lambda e: e.tensor_copy(
                vs[0:64, 0:128], vv.h[kch][kbase:kbase+64, kc*128:(kc+1)*128]))(),
                reads=[vv.b[kch]], writes=[vs_b])
            em.emit("pe", (lambda kc=kc: lambda e: e.transpose(
                ps_tb[:, 0:64], vs[0:64, 0:128], ident[0:64, 0:64]))(),
                reads=[vs_b, ident_b], writes=[ps_tb_b])
            em.emit("dve", (lambda kc=kc: lambda e: e.tensor_copy(
                vaug[:, kc*65:kc*65+64], ps_tb[:, 0:64]))(),
                reads=[ps_tb_b], writes=[vaug_b])
        for h in range(4 * g, 4 * g + 4):
            qch, qbase = h // 2, (h % 2) * 64
            for qt in range(n // 128):
                def mk_s(qt=qt, qch=qch, qbase=qbase):
                    def f(e):
                        ins = None
                        for j in range(3):
                            ins = e.matmul(ps_s[:, j*128:(j+1)*128],
                                           krep[qbase:qbase+64, (qt+j)*128:(qt+j+1)*128],
                                           q.h[qch][qbase:qbase+64, qt*128:(qt+1)*128],
                                           start=True, stop=True)
                        return ins
                    return f
                em.emit("pe", mk_s(), reads=[krep_b, q.b[qch]], writes=[ps_s_b])
                em.emit("act", (lambda: lambda e: e.activation(
                    expS[:, 0:384], ps_s[:, 0:384], AF.Exp, scale=0.125))(),
                    reads=[ps_s_b], writes=[expS_b])
                em.emit("dve", (lambda qt=qt: lambda e: e.tensor_mul(
                    expS[:, 0:384], expS[:, 0:384],
                    consts["msk"][0][:, qt*384:(qt+1)*384]))(),
                    reads=[expS_b, consts["msk"][1]], writes=[expS_b])
                def mk_av(qt=qt):
                    def f(e):
                        ins = None
                        for j in range(3):
                            ins = e.matmul(ps_o[0:65, 0:128],
                                           vaug[:, (qt+j)*65:(qt+j)*65+65],
                                           expS[:, j*128:(j+1)*128],
                                           start=(j == 0), stop=(j == 2))
                        return ins
                    return f
                em.emit("pe", mk_av(), reads=[vaug_b, expS_b], writes=[ps_o_b])
                em.emit("dve", (lambda: lambda e: e.reciprocal(
                    r_row[0:1, 0:128], ps_o[64:65, 0:128]))(),
                    reads=[ps_o_b], writes=[r_row_b])
                em.emit("dve", (lambda: lambda e: e.tensor_copy(
                    o_un[0:64, 0:128], ps_o[0:64, 0:128]))(),
                    reads=[ps_o_b], writes=[o_un_b])
                em.emit("pe", (lambda: lambda e: e.matmul(
                    ps_bc[0:64, 0:128], misc["ones_row"][0][0:1, 0:64], r_row[0:1, 0:128],
                    start=True, stop=True))(),
                    reads=[misc["ones_row"][1], r_row_b], writes=[ps_bc_b])
                em.emit("dve", (lambda qch=qch, qbase=qbase, qt=qt: lambda e: e.tensor_mul(
                    of.h[qch][qbase:qbase+64, qt*128:(qt+1)*128], o_un[0:64, 0:128],
                    ps_bc[0:64, 0:128]))(),
                    reads=[o_un_b, ps_bc_b], writes=[of.b[qch]])
    for p in range(8):
        for t0, tn in mm_tiles(n):
            em.emit("dve", (lambda p=p, t0=t0, tn=tn: lambda e: e.tensor_copy(
                mixt[0:64, 0:tn], of.h[p][64:128, t0:t0+tn]))(),
                reads=[of.b[p]], writes=[mixt_b])
            em.emit("dve", (lambda p=p, t0=t0, tn=tn: lambda e: e.tensor_copy(
                mixt[64:128, 0:tn], of.h[p][0:64, t0:t0+tn]))(),
                reads=[of.b[p]], writes=[mixt_b])
            em.emit("dve", (lambda p=p, t0=t0, tn=tn: lambda e: e.tensor_scalar_mul(
                mixt2[0:64, 0:tn], mixt[0:64, 0:tn], consts["pm"][0][0:64, 2*p:2*p+1]))(),
                reads=[mixt_b, consts["pm"][1]], writes=[mixt2_b])
            em.emit("dve", (lambda p=p, t0=t0, tn=tn: lambda e: e.tensor_scalar_mul(
                mixt2[64:128, 0:tn], mixt[64:128, 0:tn], consts["pm"][0][64:128, 2*p:2*p+1]))(),
                reads=[mixt_b, consts["pm"][1]], writes=[mixt2_b])
            em.emit("dve", (lambda p=p, t0=t0, tn=tn: lambda e: e.scalar_tensor_tensor(
                of2.h[p][0:64, t0:t0+tn], of.h[p][0:64, t0:t0+tn],
                consts["pm"][0][0:64, 2*p+1:2*p+2],
                mixt2[0:64, 0:tn], ALU.mult, ALU.add))(),
                reads=[of.b[p], consts["pm"][1], mixt2_b], writes=[of2.b[p]])
            em.emit("dve", (lambda p=p, t0=t0, tn=tn: lambda e: e.scalar_tensor_tensor(
                of2.h[p][64:128, t0:t0+tn], of.h[p][64:128, t0:t0+tn],
                consts["pm"][0][64:128, 2*p+1:2*p+2],
                mixt2[64:128, 0:tn], ALU.mult, ALU.add))(),
                reads=[of.b[p], consts["pm"][1], mixt2_b], writes=[of2.b[p]])
    def ep_o(mi, ps, ps_b, t0, tn):
        em.emit_dma("sp", "xres", (lambda mi=mi, t0=t0, tn=tn: lambda e: e.dma_start(
            out=xe.h[mi][:, 0:tn], in_=xs[mi*128:(mi+1)*128, HALO+t0:HALO+t0+tn]))(),
            writes=[xe.b[mi]])
        em.close_batch("xres")
        em.emit("dve", (lambda mi=mi, ps=ps, t0=t0, tn=tn: lambda e: e.tensor_add(
            xout[:, 0:tn], ps[:, 0:tn], xe.h[mi][:, 0:tn]))(),
            reads=[ps_b, xe.b[mi]], writes=[xout_b])
        out_toks.append(em.emit_dma("sp", "out", (lambda mi=mi, t0=t0, tn=tn: lambda e: e.dma_start(
            out=xo[mi*128:(mi+1)*128, t0:t0+tn], in_=xout[:, 0:tn]))(), reads=[xout_b]))
        em.close_batch("out")
    build_project(em, of2, WoT, 1024, 1024, n, pp, wbufs, ep_o, mgroup_chunks=2)
    em.finalize(out_toks)
    return nc

def rope_tables_l2(pos, width):
    HD = 64
    inv = 1.0 / 10000.0 ** (np.arange(0, HD, 2, dtype=np.float64) / HD)
    f = pos[None, :].astype(np.float64) * inv[:, None]
    cos, sin = np.cos(f), np.sin(f)
    cT = np.zeros((128, width), np.float32)
    sT = np.zeros((128, width), np.float32)
    for b0 in (0, 64):
        cT[b0:b0+32] = cos; cT[b0+32:b0+64] = cos
        sT[b0:b0+32] = sin; sT[b0+32:b0+64] = -sin
    return cT.astype(BF16NP), sT.astype(BF16NP)

def l2a_masks(pos_q, pos_k):
    n = len(pos_q)
    m = np.zeros((128, 16, 3, 128), np.float32)
    for qt in range(16):
        for j in range(3):
            kidx = (qt + j) * 128 + np.arange(128)
            kp = pos_k[kidx]
            qp = pos_q[qt*128:(qt+1)*128]
            dd = qp[None, :] - kp[:, None]
            keep = (dd >= 0) & (dd < 256) & (kp[:, None] >= 0)
            m[:, qt, j, :] = 0.0
            m[:, qt, j, :] = keep.astype(np.float32)[:, 0:128].T if False else keep.astype(np.float32)
    return np.ascontiguousarray(m.reshape(128, 16*384)).astype(BF16NP)

def l2a_inputs(xe_shard, pos_q, pos_k, ins):
    cq, sq = rope_tables_l2(pos_q, len(pos_q))
    ck, sk = rope_tables_l2(np.maximum(pos_k, 0), len(pos_k))
    qg = np.zeros((128, 8), np.float32)
    g = np.asarray(ins["L2_attn_q_gain"], np.float32)
    for ci in range(8):
        qg[0:64, ci] = g[2*ci]
        qg[64:128, ci] = g[2*ci+1]
    pm = np.zeros((128, 16), np.float32)
    pmx = np.asarray(ins["L2_attn_pair_mix"], np.float32)
    for p in range(8):
        pm[0:64, 2*p] = pmx[p, 0, 1]
        pm[0:64, 2*p+1] = pmx[p, 0, 0]
        pm[64:128, 2*p] = pmx[p, 1, 0]
        pm[64:128, 2*p+1] = pmx[p, 1, 1]
    return {
        "xe": np.ascontiguousarray(xe_shard, np.float32),
        "WqT": prep_wT(ins["L2_attn_Wq"]), "WkT": prep_wT(ins["L2_attn_Wk"]),
        "WvT": prep_wT(ins["L2_attn_Wv"]), "WoT": prep_wT(ins["L2_attn_Wo"]),
        "cq": cq, "sq_": sq, "ck": ck, "sk": sk,
        "perm": swap_perm(), "ident": np.eye(128, dtype=BF16NP),
        "e2": np.ascontiguousarray(np.kron(np.eye(2), np.ones((64, 1))).astype(np.float32)),
        "e2t": np.ascontiguousarray(np.kron(np.eye(2), np.ones((1, 64))).astype(np.float32)),
        "qg": qg, "pm": pm,
        "msk": l2a_masks(pos_q, pos_k),
        "eps6": np.full((128, 1), 1e-6, np.float32),
    }

def _mk64(em, name, k, n):
    import types as _t
    h = [em.sb(f"{name}{i}", [64, n], BF16) for i in range(k)]
    b = [Buf(f"{name}{i}") for i in range(k)]
    return _t.SimpleNamespace(h=h, b=b, nch=k, ntok=n)


def build_l2b1(ntok_core=PT):
    n = ntok_core
    nc = bass.Bass("TRN2", target_bir_lowering=False)
    xs = nc.dram_tensor("xs", [D, n], F32, kind="ExternalInput").ap()
    WQT = nc.dram_tensor("WQT", [1024, 1024], F32, kind="ExternalInput").ap()
    WKT = nc.dram_tensor("WKT", [1024, 1024], F32, kind="ExternalInput").ap()
    WiT = nc.dram_tensor("WiT", [1024, 1024], F32, kind="ExternalInput").ap()
    qo = nc.dram_tensor("qo", [D, n], BF16, kind="ExternalOutput").ap()
    ko = nc.dram_tensor("ko", [D, n], BF16, kind="ExternalOutput").ap()
    iqo = nc.dram_tensor("iqo", [D, n], BF16, kind="ExternalOutput").ap()
    iko = nc.dram_tensor("iko", [D, n], BF16, kind="ExternalOutput").ap()
    sso = nc.dram_tensor("sso", [D, 2], F32, kind="ExternalOutput").ap()
    em = Em(nc)
    x = Chunked(em, "x", NCH, 512, F32)
    xn = Chunked(em, "xn", NCH, n, BF16)
    Q = Chunked(em, "Q", NCH, n, BF16)
    K = Chunked(em, "K", NCH, n, BF16)
    iQ = Chunked(em, "iQ", NCH, n, BF16)
    ss = em.sb("ss", [128, 16], F32); ss_b = Buf("ss")
    misc = setup_misc(em)
    wb1 = em.sb("wbufA", [128, 4096], BF16)
    wb2 = em.sb("wbufB", [128, 4096], BF16)
    wbufs = [(wb1, Buf("wbufA")), (wb2, Buf("wbufB"))]
    pp = PsumPool(em, n=8, width=512)
    out_toks = []
    for t0c in range(0, n, 512):
        for ci in range(NCH):
            em.emit_dma("sp", "xin", (lambda ci=ci, t0c=t0c: lambda e: e.dma_start(
                out=x.h[ci][:, :], in_=xs[ci*128:(ci+1)*128, t0c:t0c+512]))(), writes=[x.b[ci]])
        em.close_batch("xin")
        build_layernorm(em, x, xn, pp, misc, 512, out_col0=t0c)
    def mk_ep(dst, out_h, scol):
        def ep(mi, ps, ps_b, t0, tn):
            em.emit("act", (lambda mi=mi, ps=ps, t0=t0, tn=tn: lambda e: e.activation(
                dst.h[mi][:, t0:t0+tn], ps[:, 0:tn], AF.Identity))(),
                reads=[ps_b], writes=[dst.b[mi]])
            out_toks.append(em.emit_dma("sp", "out", (lambda mi=mi, t0=t0, tn=tn: lambda e: e.dma_start(
                out=out_h[mi*128:(mi+1)*128, t0:t0+tn], in_=dst.h[mi][:, t0:t0+tn]))(),
                reads=[dst.b[mi]]))
            em.close_batch("out")
        return ep
    build_project(em, xn, WQT, 1024, 1024, n, pp, wbufs, mk_ep(Q, qo, 0), mgroup_chunks=4)
    build_project(em, xn, WKT, 1024, 1024, n, pp, wbufs, mk_ep(K, ko, 1), mgroup_chunks=4)
    build_project(em, Q, WiT, 1024, 1024, n, pp, wbufs, mk_ep(iQ, iqo, 2), mgroup_chunks=4)
    iK = Q
    build_project(em, K, WiT, 1024, 1024, n, pp, wbufs, mk_ep(iK, iko, 3), mgroup_chunks=4)
    for ci in range(NCH):
        em.emit("dve", (lambda ci=ci: lambda e: e.reduce_sum(
            ss[:, 0:1], K.h[ci][:, 0:n], axis=AX.X))(),
            reads=[K.b[ci]], writes=[ss_b])
        em.emit("dve", (lambda ci=ci: lambda e: e.reduce_sum(
            ss[:, 1:2], iK.h[ci][:, 0:n], axis=AX.X))(),
            reads=[iK.b[ci]], writes=[ss_b])
        out_toks.append(em.emit_dma("sp", "out", (lambda ci=ci: lambda e: e.dma_start(
            out=sso[ci*128:(ci+1)*128, 0:2], in_=ss[:, 0:2]))(), reads=[ss_b]))
        em.close_batch("out")
    em.finalize(out_toks)
    return nc

def build_l2b2(ntok_core=PT):
    n = ntok_core
    nc = bass.Bass("TRN2", target_bir_lowering=False)
    xs = nc.dram_tensor("xs", [D, n], F32, kind="ExternalInput").ap()
    Qh = nc.dram_tensor("Qh", [D, n], BF16, kind="ExternalInput").ap()
    Kh = nc.dram_tensor("Kh", [D, n], BF16, kind="ExternalInput").ap()
    iQh = nc.dram_tensor("iQh", [D, n], BF16, kind="ExternalInput").ap()
    iKh = nc.dram_tensor("iKh", [D, n], BF16, kind="ExternalInput").ap()
    pksh = nc.dram_tensor("pksh", [128, 64], F32, kind="ExternalInput").ap()
    vf = nc.dram_tensor("vf", [1024, 512], F32, kind="ExternalInput").ap()
    wf = nc.dram_tensor("wf", [1024, 512], F32, kind="ExternalInput").ap()
    vi = nc.dram_tensor("vi", [1024, 512], F32, kind="ExternalInput").ap()
    wi = nc.dram_tensor("wi", [1024, 512], F32, kind="ExternalInput").ap()
    uf = nc.dram_tensor("uf", [512, 1024], F32, kind="ExternalInput").ap()
    ui = nc.dram_tensor("ui", [512, 1024], F32, kind="ExternalInput").ap()
    rw1T = nc.dram_tensor("rw1T", [1024, 1024], F32, kind="ExternalInput").ap()
    rw2T = nc.dram_tensor("rw2T", [1024, 128], F32, kind="ExternalInput").ap()
    WOT = nc.dram_tensor("WOT", [1024, 1024], F32, kind="ExternalInput").ap()
    xo = nc.dram_tensor("xo", [D, n], F32, kind="ExternalOutput").ap()
    em = Em(nc)
    x = Chunked(em, "x", NCH, 512, F32)
    QQ = Chunked(em, "QQ", NCH, n, BF16)
    KK = Chunked(em, "KK", NCH, n, BF16)
    prodF = _mk64(em, "prodF", 8, n)
    prodI = _mk64(em, "prodI", 8, n)
    import types as _t
    hR = _t.SimpleNamespace(h=KK.h, b=KK.b, nch=8, ntok=n)
    outp = _t.SimpleNamespace(h=QQ.h, b=QQ.b, nch=8, ntok=n)
    consts = load_consts(em, nc, [
        ("rcol", [128, 1], F32), ("alpha", [128, 1], F32),
        ("rnv", [1, PT], BF16),
        ("bfp", [8, 1024], BF16), ("bip", [8, 1024], BF16),
        ("esel", [8, 512], BF16), ("ones8b", [8, 1], BF16),
    ])
    misc = setup_misc(em)
    em.close_batch("cst")
    yw = em.sb("yw", [64, PT], BF16); yw_b = Buf("yw")
    zz = em.sb("zz", [64, PT], BF16); zz_b = Buf("zz")
    cum = em.sb("cum", [64, PT], F32); cum_b = Buf("cum")
    carry = em.sb("carry", [64, 16], F32); carry_b = Buf("carry")
    scr64 = em.sb("scr64", [64, 512], BF16); scr64_b = Buf("scr64")
    wrow = em.sb("wrow", [128, 2 * PT], BF16); wrow_b = Buf("wrow")
    pksb = em.sb("pksb", [128, 64], BF16); pksb_b = Buf("pksb")
    xout = em.sb("xout", [128, 512], F32); xout_b = Buf("xout")
    wb1 = em.sb("wbufA", [128, 4096], BF16)
    wb2 = em.sb("wbufB", [128, 4096], BF16)
    wbufs = [(wb1, Buf("wbufA")), (wb2, Buf("wbufB"))]
    pp = PsumPool(em, n=6, width=512)
    out_toks = []
    em.emit_dma("pool", "cst2", (lambda: lambda e: e.dma_start(out=pksb[:, :], in_=pksh))(),
                writes=[pksb_b])
    em.close_batch("cst2")
    em.emit("dve", (lambda: lambda e: e.memset(zz[:, :], 0.0))(), writes=[zz_b])

    def load_chunked(dst, src_h, slot):
        for ci in range(NCH):
            em.emit_dma("pool", slot, (lambda ci=ci: lambda e: e.dma_start(
                out=dst.h[ci][:, :], in_=src_h[ci*128:(ci+1)*128, :]))(), writes=[dst.b[ci]])
        em.close_batch(slot)

    def router(srcH, off):
        load_chunked(QQ, srcH, "qin")
        def ep_h(mi, ps, ps_b, t0, tn):
            em.emit("act", (lambda mi=mi, ps=ps, t0=t0, tn=tn: lambda e: e.activation(
                hR.h[mi][:, t0:t0+tn], ps[:, 0:tn], AF.Gelu))(),
                reads=[ps_b], writes=[hR.b[mi]])
        build_project(em, QQ, rw1T, 1024, 1024, n, pp, wbufs, ep_h, mgroup_chunks=4)
        def ep_l(mi, ps, ps_b, t0, tn):
            em.emit("act", (lambda ps=ps, t0=t0, tn=tn: lambda e: e.activation(
                wrow[0:8, off+t0:off+t0+tn], ps[0:8, 0:tn], AF.Exp,
                bias=consts["rcol"][0][0:8, 0:1]))(),
                reads=[ps_b, consts["rcol"][1]], writes=[wrow_b])
        build_project(em, hR, rw2T, 1024, 128, n, pp, wbufs, ep_l, mgroup_chunks=1)
        for t0, tn in mm_tiles(n):
            ps_z, ps_z_b = pp.take1()
            em.emit("pe", (lambda t0=t0, tn=tn, ps_z=ps_z: lambda e: e.matmul(
                ps_z[0:1, 0:tn], consts["ones8b"][0][0:8, 0:1], wrow[0:8, off+t0:off+t0+tn],
                start=True, stop=True))(),
                reads=[consts["ones8b"][1], wrow_b], writes=[ps_z_b])
            em.emit("dve", (lambda tn=tn, ps_z=ps_z: lambda e: e.reciprocal(
                misc["r_a"][0][0:1, 0:tn], ps_z[0:1, 0:tn]))(),
                reads=[ps_z_b], writes=[misc["r_a"][1]])
            em.emit("dve", (lambda t0=t0, tn=tn: lambda e: e.tensor_mul(
                misc["r_b"][0][0:1, 0:tn], misc["r_a"][0][0:1, 0:tn],
                consts["rnv"][0][0:1, t0:t0+tn]))(),
                reads=[misc["r_a"][1], consts["rnv"][1]], writes=[misc["r_b"][1]])
            ps_bc, ps_bc_b = pp.take1()
            em.emit("pe", (lambda tn=tn, ps_bc=ps_bc: lambda e: e.matmul(
                ps_bc[0:8, 0:tn], misc["ones_row"][0][0:1, 0:8], misc["r_b"][0][0:1, 0:tn],
                start=True, stop=True))(),
                reads=[misc["ones_row"][1], misc["r_b"][1]], writes=[ps_bc_b])
            em.emit("dve", (lambda t0=t0, tn=tn, ps_bc=ps_bc: lambda e: e.tensor_mul(
                wrow[0:8, off+t0:off+t0+tn], wrow[0:8, off+t0:off+t0+tn], ps_bc[0:8, 0:tn]))(),
                reads=[wrow_b, ps_bc_b], writes=[wrow_b])
    router(Qh, 0)
    router(iQh, n)
    em.emit("dve", (lambda: lambda e: e.tensor_scalar_mul(
        wrow[0:8, n:2*n], wrow[0:8, n:2*n], consts["alpha"][0][0:8, 0:1]))(),
        reads=[wrow_b, consts["alpha"][1]], writes=[wrow_b])

    def load_slab(dst_wb, dst_b, src, ncols, krows=1024, slot="wslabA"):
        nj = krows // 128
        def f(e):
            sap = src[0:krows, 0:ncols].rearrange("(j p) m -> p j m", p=128)
            dap = dst_wb[:, 0:nj*ncols].rearrange("p (j m) -> p j m", m=ncols)
            return e.dma_start(out=dap, in_=sap)
        em.emit_dma("pool", slot, f, writes=[dst_b])
        em.close_batch(slot)

    def branch(QsrcH, KsrcH, vT, wT, prod, off, pcol):
        load_slab(wb2, wbufs[1][1], wT, 512, slot="wslabB")
        load_slab(wb1, wbufs[0][1], vT, 512, slot="wslabA")
        load_chunked(QQ, QsrcH, "qin")
        load_chunked(KK, KsrcH, "kin")
        for ex in range(8):
            pbase = (ex % 2) * 64
            for t0, tn in mm_tiles(n):
                ps_y, ps_y_b = pp.take1()
                def mk_y(ex=ex, t0=t0, tn=tn, ps_y=ps_y):
                    def f(e):
                        ins = None
                        for j in range(8):
                            ins = e.matmul(ps_y[0:64, 0:tn],
                                           wb2[:, j*512 + ex*64: j*512 + ex*64 + 64],
                                           KK.h[j][:, t0:t0+tn], start=(j == 0), stop=(j == 7))
                        return ins
                    return f
                em.emit("pe", mk_y(), reads=[wbufs[1][1]] + KK.b, writes=[ps_y_b])
                em.emit("act", (lambda t0=t0, tn=tn, ps_y=ps_y: lambda e: e.activation(
                    yw[0:64, t0:t0+tn], ps_y[0:64, 0:tn], AF.Identity))(),
                    reads=[ps_y_b], writes=[yw_b])
            ps_c, ps_c_b = pp.take1()
            def mk_c(ex=ex, ps_c=ps_c, pcol=pcol):
                def f(e):
                    ins = None
                    for j in range(32):
                        ins = e.matmul(ps_c[0:64, 0:1],
                                       wb2[:, (j % 8)*512 + ex*64: (j % 8)*512 + ex*64 + 64],
                                       pksb[:, j*2+pcol:j*2+pcol+1], start=(j == 0), stop=(j == 31))
                    return ins
                return f
            em.emit("pe", mk_c(), reads=[wbufs[1][1], pksb_b], writes=[ps_c_b])
            em.emit("dve", (lambda ex=ex, ps_c=ps_c: lambda e: e.tensor_copy(
                carry[0:64, ex+8*(0 if pcol == 0 else 1):ex+8*(0 if pcol == 0 else 1)+1],
                ps_c[0:64, 0:1]))(),
                reads=[ps_c_b], writes=[carry_b])
            ccol = ex + 8 * (0 if pcol == 0 else 1)
            em.emit("dve", (lambda ccol=ccol: lambda e: e.tensor_tensor_scan(
                cum[0:64, 0:n], yw[0:64, 0:n], zz[0:64, 0:n],
                carry[0:64, ccol:ccol+1], ALU.add, ALU.add))(),
                reads=[yw_b, carry_b, zz_b], writes=[cum_b])
            for t0, tn in mm_tiles(n):
                ps_x, ps_x_b = pp.take1()
                def mk_x(ex=ex, t0=t0, tn=tn, ps_x=ps_x):
                    def f(e):
                        ins = None
                        for j in range(8):
                            ins = e.matmul(ps_x[0:64, 0:tn],
                                           wb1[:, j*512 + ex*64: j*512 + ex*64 + 64],
                                           QQ.h[j][:, t0:t0+tn], start=(j == 0), stop=(j == 7))
                        return ins
                    return f
                em.emit("pe", mk_x(), reads=[wbufs[0][1]] + QQ.b, writes=[ps_x_b])
                em.emit("dve", (lambda t0=t0, tn=tn, ps_x=ps_x: lambda e: e.tensor_mul(
                    scr64[0:64, 0:tn], ps_x[0:64, 0:tn], cum[0:64, t0:t0+tn]))(),
                    reads=[ps_x_b, cum_b], writes=[scr64_b])
                ps_w, ps_w_b = pp.take1()
                em.emit("pe", (lambda ex=ex, t0=t0, tn=tn, ps_w=ps_w, off=off: lambda e: e.matmul(
                    ps_w[0:64, 0:tn], consts["esel"][0][0:8, ex*64:(ex+1)*64],
                    wrow[0:8, off+t0:off+t0+tn], start=True, stop=True))(),
                    reads=[consts["esel"][1], wrow_b], writes=[ps_w_b])
                em.emit("dve", (lambda ex=ex, t0=t0, tn=tn, ps_w=ps_w, prod=prod: lambda e: e.tensor_mul(
                    prod.h[ex][0:64, t0:t0+tn], scr64[0:64, 0:tn], ps_w[0:64, 0:tn]))(),
                    reads=[scr64_b, ps_w_b], writes=[prod.b[ex]])
    import os as _os
    _stop = _os.environ.get("L2STOP", "")
    if _stop == "r":
        toks = [Tok(s, em.cnt[s]) for s in list(em.cnt) if s.startswith("d_")]
        em.finalize(toks + [em.barrier_tok("dve"), em.barrier_tok("pe"), em.barrier_tok("act")])
        return nc
    branch(Qh, Kh, vf, wf, prodF, 0, 0)
    if _stop == "b1":
        toks = [Tok(s, em.cnt[s]) for s in list(em.cnt) if s.startswith("d_")]
        em.finalize(toks + [em.barrier_tok("dve"), em.barrier_tok("pe"), em.barrier_tok("act")])
        return nc
    branch(iQh, iKh, vi, wi, prodI, n, 1)
    if _stop == "b2":
        toks = [Tok(s, em.cnt[s]) for s in list(em.cnt) if s.startswith("d_")]
        em.finalize(toks + [em.barrier_tok("dve"), em.barrier_tok("pe"), em.barrier_tok("act")])
        return nc
    for mi in range(8):
        def mk_lu(mi=mi):
            def f(e):
                sap = uf[0:512, mi*128:(mi+1)*128].rearrange("(a p) m -> p a m", p=64)
                dap = wb1[0:64, 0:1024].rearrange("p (a m) -> p a m", m=128)
                return e.dma_start(out=dap, in_=sap)
            return f
        em.emit_dma("pool", "wslabA", mk_lu(), writes=[wbufs[0][1]])
        em.close_batch("wslabA")
        def mk_li(mi=mi):
            def f(e):
                sap = ui[0:512, mi*128:(mi+1)*128].rearrange("(a p) m -> p a m", p=64)
                dap = wb1[0:64, 1024:2048].rearrange("p (a m) -> p a m", m=128)
                return e.dma_start(out=dap, in_=sap)
            return f
        em.emit_dma("pool", "wslabA", mk_li(), writes=[wbufs[0][1]])
        em.close_batch("wslabA")
        for t0, tn in mm_tiles(n):
            ps, ps_b = pp.take1()
            def mk_u(mi=mi, t0=t0, tn=tn, ps=ps):
                def f(e):
                    ins = None
                    for br in range(2):
                        prod = prodF if br == 0 else prodI
                        for ex in range(8):
                            ins = e.matmul(ps[:, 0:tn],
                                           wb1[0:64, br*1024 + ex*128: br*1024 + (ex+1)*128],
                                           prod.h[ex][0:64, t0:t0+tn],
                                           start=(br == 0 and ex == 0),
                                           stop=(br == 1 and ex == 7))
                    return ins
                return f
            em.emit("pe", mk_u(), reads=[wbufs[0][1], wrow_b]
                    + prodF.b + prodI.b, writes=[ps_b])
            ps2, ps2_b = pp.take1()
            def mk_b(mi=mi, t0=t0, tn=tn, ps2=ps2):
                def f(e):
                    bcol = consts["bfp"][0][0:8, mi*128:(mi+1)*128]
                    e.matmul(ps2[:, 0:tn], bcol, wrow[0:8, t0:t0+tn], start=True, stop=False)
                    bcol2 = consts["bip"][0][0:8, mi*128:(mi+1)*128]
                    return e.matmul(ps2[:, 0:tn], bcol2, wrow[0:8, n+t0:n+t0+tn],
                                    start=False, stop=True)
                return f
            em.emit("pe", mk_b(), reads=[consts["bfp"][1], consts["bip"][1], wrow_b],
                    writes=[ps2_b])
            em.emit("act", (lambda mi=mi, ps=ps, t0=t0, tn=tn: lambda e: e.activation(
                outp.h[mi][:, t0:t0+tn], ps[:, 0:tn], AF.Identity))(),
                reads=[ps_b], writes=[outp.b[mi]])
            em.emit("dve", (lambda mi=mi, ps2=ps2, t0=t0, tn=tn: lambda e: e.tensor_add(
                outp.h[mi][:, t0:t0+tn], outp.h[mi][:, t0:t0+tn], ps2[:, 0:tn]))(),
                reads=[ps2_b, outp.b[mi]], writes=[outp.b[mi]])
    if _stop == "u":
        toks = [Tok(s, em.cnt[s]) for s in list(em.cnt) if s.startswith("d_")]
        em.finalize(toks + [em.barrier_tok("dve"), em.barrier_tok("pe"), em.barrier_tok("act")])
        return nc
    def ep_o(mi, ps, ps_b, t0, tn):
        em.emit_dma("sp", "xres", (lambda mi=mi, t0=t0, tn=tn: lambda e: e.dma_start(
            out=x.h[mi][:, 0:tn], in_=xs[mi*128:(mi+1)*128, t0:t0+tn]))(),
            writes=[x.b[mi]])
        em.close_batch("xres")
        em.emit("dve", (lambda mi=mi, ps=ps, t0=t0, tn=tn: lambda e: e.tensor_add(
            xout[:, 0:tn], ps[:, 0:tn], x.h[mi][:, 0:tn]))(),
            reads=[ps_b, x.b[mi]], writes=[xout_b])
        out_toks.append(em.emit_dma("sp", "out", (lambda mi=mi, t0=t0, tn=tn: lambda e: e.dma_start(
            out=xo[mi*128:(mi+1)*128, t0:t0+tn], in_=xout[:, 0:tn]))(), reads=[xout_b]))
        em.close_batch("out")
    build_project(em, outp, WOT, 1024, 1024, n, pp, wbufs, ep_o, mgroup_chunks=2)
    em.finalize(out_toks)
    return nc

def l2b1_inputs(x5T_shard, ins):
    return {"xs": np.ascontiguousarray(x5T_shard, np.float32),
            "WQT": prep_wT(ins["L2_ffn_W_Q"]), "WKT": prep_wT(ins["L2_ffn_W_K"]),
            "WiT": prep_wT(ins["L2_ffn_W_inv"])}

def l2b2_inputs(x5T_shard, Q, K, iQ, iK, pks_sel, gpos, ins):
    raise NotImplementedError("use l2b2_inputs_v2")

def l2b2_inputs_v2(x5T_shard, Q, K, iQ, iK, prev_sums, gpos, ins):
    pk = np.concatenate([np.asarray(p, np.float32) for p in prev_sums], axis=0)
    pksh = np.zeros((128, 64), np.float32)
    for j in range(32):
        pksh[:, j*2:(j+1)*2] = pk[j*128:(j+1)*128, :]
    al = float(np.asarray(ins["L2_ffn_alpha_bi"]))
    rcol = np.zeros((128, 1), np.float32)
    rcol[0:8, 0] = np.asarray(ins["L2_ffn_router_b2"], np.float32) + np.asarray(ins["L2_ffn_expert_bias"], np.float32)
    esel = np.zeros((8, 512), np.float32)
    for ex in range(8):
        esel[ex, ex*64:(ex+1)*64] = 1.0
    def vpack(a):
        a = np.asarray(a, np.float32)
        return np.ascontiguousarray(a.transpose(1, 0, 2).reshape(1024, 512))
    def upack(a):
        a = np.asarray(a, np.float32)
        return np.ascontiguousarray(a.transpose(0, 2, 1).reshape(512, 1024))
    rw2 = np.zeros((1024, 128), np.float32)
    rw2[:, 0:8] = prep_wT(ins["L2_ffn_router_w2"])
    return {
        "xs": np.ascontiguousarray(x5T_shard, np.float32),
        "Qh": Q, "Kh": K, "iQh": iQ, "iKh": iK,
        "pksh": pksh,
        "vf": vpack(ins["L2_ffn_V_fwd"]), "wf": vpack(ins["L2_ffn_W_fwd"]),
        "vi": vpack(ins["L2_ffn_W_inv_exp"]), "wi": vpack(ins["L2_ffn_V_inv"]),
        "uf": upack(ins["L2_ffn_U_fwd"]), "ui": upack(ins["L2_ffn_U_inv"]),
        "rw1T": prep_wT(ins["L2_ffn_router_w1"]), "rw2T": rw2,
        "WOT": prep_wT(ins["L2_ffn_W_O"]),
        "rcol": rcol, "alpha": np.full((128, 1), al, np.float32),
        "rnv": (1.0 / (np.asarray(gpos, np.float64) + 1.0)).astype(np.float32)[None, :].astype(BF16NP),
        "bfp": np.ascontiguousarray(np.asarray(ins["L2_ffn_b_fwd"], np.float32)).astype(BF16NP),
        "bip": np.ascontiguousarray(np.asarray(ins["L2_ffn_b_inv"], np.float32)).astype(BF16NP),
        "esel": esel.astype(BF16NP), "ones8b": np.ones((8, 1), BF16NP),
    }


def build_l3f(ntok_core=PT):
    n = ntok_core
    TT = 1024
    nc = bass.Bass("TRN2", target_bir_lowering=False)
    xs = nc.dram_tensor("xs", [D, n], F32, kind="ExternalInput").ap()
    rwT = nc.dram_tensor("rwT", [1024, 128], F32, kind="ExternalInput").ap()
    sw1c = nc.dram_tensor("sw1c", [1024, 4096], F32, kind="ExternalInput").ap()
    sw2c = nc.dram_tensor("sw2c", [4096, 1024], F32, kind="ExternalInput").ap()
    w1c = nc.dram_tensor("w1c", [1024, 16384], F32, kind="ExternalInput").ap()
    w2c = nc.dram_tensor("w2c", [16384, 1024], F32, kind="ExternalInput").ap()
    xo = nc.dram_tensor("xo", [D, n], F32, kind="ExternalOutput").ap()
    em = Em(nc)
    x = Chunked(em, "x", NCH, 512, F32)
    xn = Chunked(em, "xn", NCH, n, BF16)
    shc = Chunked(em, "shc", 32, TT, BF16)
    import types as _t
    he = _t.SimpleNamespace(h=shc.h[0:16], b=shc.b[0:16], nch=16, ntok=TT)
    outacc = Chunked(em, "outacc", NCH, TT, F32)
    consts = load_consts(em, nc, [
        ("rcol", [128, 1], F32),
        ("sb1c", [128, 32], F32), ("sb2a", [128, 8], F32), ("sb2b", [128, 8], F32),
        ("b1c", [128, 128], F32),
        ("b2p", [8, 1024], F32),
        ("esel", [8, 1024], F32),
        ("iota8", [128, 8], F32), ("idf", [128, 128], F32)])
    misc = setup_misc(em)
    em.close_batch("cst")
    halfb = em.sb("halfb", [128, 8], F32); halfb_b = Buf("halfb")
    lgt = em.sb("lgt", [8, TT], F32); lgt_b = Buf("lgt")
    lg_tm = em.sb("lg_tm", [128, 8], F32); lg_tm_b = Buf("lg_tm")
    m8 = em.sb("m8", [128, 8], F32); m8_b = Buf("m8")
    idx8 = em.sb("idx8", [128, 8], U32); idx8_b = Buf("idx8")
    idxf = em.sb("idxf", [128, 2], F32); idxf_b = Buf("idxf")
    pp12 = em.sb("pp12", [128, 4], F32); pp12_b = Buf("pp12")
    ctm = em.sb("ctm", [128, 8], F32); ctm_b = Buf("ctm")
    cscr = em.sb("cscr", [128, 8], F32); cscr_b = Buf("cscr")
    comb = em.sb("comb", [8, TT], F32); comb_b = Buf("comb")
    xout = em.sb("xout", [128, 512], F32); xout_b = Buf("xout")
    wb1 = em.sb("wbufA", [128, 4096], BF16)
    wb2 = em.sb("wbufB", [128, 4096], BF16)
    wbufs = [(wb1, Buf("wbufA")), (wb2, Buf("wbufB"))]
    pp = PsumPool(em, n=8, width=512)
    out_toks = []
    em.emit("dve", (lambda: lambda e: e.tensor_add(
        halfb[:, :], consts["sb2a"][0][:, :], consts["sb2b"][0][:, :]))(),
        reads=[consts["sb2a"][1], consts["sb2b"][1]], writes=[halfb_b])
    em.emit("dve", (lambda: lambda e: e.tensor_scalar_mul(halfb[:, :], halfb[:, :], 0.5))(),
        reads=[halfb_b], writes=[halfb_b])
    for t0c in range(0, n, 512):
        for ci in range(NCH):
            em.emit_dma("sp", "xin", (lambda ci=ci, t0c=t0c: lambda e: e.dma_start(
                out=x.h[ci][:, :], in_=xs[ci*128:(ci+1)*128, t0c:t0c+512]))(), writes=[x.b[ci]])
        em.close_batch("xin")
        build_layernorm(em, x, xn, pp, misc, 512, out_col0=t0c)
    for tt in range(0, n, TT):
        xoffs = [tt] * 8
        def ep_sh(mi, ps, ps_b, t0, tn):
            em.emit("act", (lambda mi=mi, ps=ps, t0=t0, tn=tn: lambda e: e.activation(
                shc.h[mi][:, t0:t0+tn], ps[:, 0:tn], AF.Gelu,
                bias=consts["sb1c"][0][:, mi:mi+1]))(),
                reads=[ps_b, consts["sb1c"][1]], writes=[shc.b[mi]])
        build_project(em, xn, sw1c, 1024, 4096, TT, pp, wbufs, ep_sh,
                      mgroup_chunks=4, xoff=xoffs)
        def ep_sh2(mi, ps, ps_b, t0, tn):
            em.emit("act", (lambda mi=mi, ps=ps, t0=t0, tn=tn: lambda e: e.activation(
                outacc.h[mi][:, t0:t0+tn], ps[:, 0:tn], AF.Identity,
                bias=halfb[:, mi:mi+1], scale=0.5))(),
                reads=[ps_b, halfb_b], writes=[outacc.b[mi]])
        build_project(em, shc, sw2c, 4096, 1024, TT, pp, wbufs, ep_sh2, mgroup_chunks=1)
        def ep_r(mi, ps, ps_b, t0, tn):
            em.emit("act", (lambda ps=ps, t0=t0, tn=tn: lambda e: e.activation(
                lgt[0:8, t0:t0+tn], ps[0:8, 0:tn], AF.Identity,
                bias=consts["rcol"][0][0:8, 0:1]))(),
                reads=[ps_b, consts["rcol"][1]], writes=[lgt_b])
        build_project(em, xn, rwT, 1024, 128, TT, pp, wbufs, ep_r,
                      mgroup_chunks=1, xoff=xoffs)
        for q in range(TT // 128):
            ps_t, ps_t_b = pp.take1()
            em.emit("pe", (lambda q=q, ps_t=ps_t: lambda e: e.transpose(
                ps_t[:, 0:8], lgt[0:8, q*128:(q+1)*128], consts["idf"][0][0:8, 0:8]))(),
                reads=[lgt_b, consts["idf"][1]], writes=[ps_t_b])
            em.emit("dve", (lambda ps_t=ps_t: lambda e: e.tensor_copy(
                lg_tm[:, 0:8], ps_t[:, 0:8]))(),
                reads=[ps_t_b], writes=[lg_tm_b])
            em.emit("dve", (lambda: lambda e: e.max(m8[:, 0:8], lg_tm[:, 0:8]))(),
                reads=[lg_tm_b], writes=[m8_b])
            em.emit("dve", (lambda: lambda e: e.max_index(idx8[:, 0:8], m8[:, 0:8], lg_tm[:, 0:8]))(),
                reads=[m8_b, lg_tm_b], writes=[idx8_b])
            em.emit("dve", (lambda: lambda e: e.tensor_copy(idxf[:, 0:2], idx8[:, 0:2]))(),
                reads=[idx8_b], writes=[idxf_b])
            em.emit("dve", (lambda: lambda e: e.tensor_sub(pp12[:, 0:1], m8[:, 1:2], m8[:, 0:1]))(),
                reads=[m8_b], writes=[pp12_b])
            em.emit("act", (lambda: lambda e: e.activation(pp12[:, 1:2], pp12[:, 0:1], AF.Exp))(),
                reads=[pp12_b], writes=[pp12_b])
            em.emit("dve", (lambda: lambda e: e.tensor_scalar_add(pp12[:, 2:3], pp12[:, 1:2], 1.0))(),
                reads=[pp12_b], writes=[pp12_b])
            em.emit("dve", (lambda: lambda e: e.reciprocal(pp12[:, 2:3], pp12[:, 2:3]))(),
                reads=[pp12_b], writes=[pp12_b])
            em.emit("dve", (lambda: lambda e: e.tensor_mul(pp12[:, 3:4], pp12[:, 1:2], pp12[:, 2:3]))(),
                reads=[pp12_b], writes=[pp12_b])
            em.emit("dve", (lambda: lambda e: e.tensor_tensor(
                ctm[:, 0:8], idxf[:, 0:1].broadcast_to([128, 8]), consts["iota8"][0][:, 0:8],
                ALU.is_equal))(),
                reads=[idxf_b, consts["iota8"][1]], writes=[ctm_b])
            em.emit("dve", (lambda: lambda e: e.tensor_scalar_mul(
                ctm[:, 0:8], ctm[:, 0:8], pp12[:, 2:3]))(),
                reads=[ctm_b, pp12_b], writes=[ctm_b])
            em.emit("dve", (lambda: lambda e: e.tensor_tensor(
                cscr[:, 0:8], idxf[:, 1:2].broadcast_to([128, 8]), consts["iota8"][0][:, 0:8],
                ALU.is_equal))(),
                reads=[idxf_b, consts["iota8"][1]], writes=[cscr_b])
            em.emit("dve", (lambda: lambda e: e.scalar_tensor_tensor(
                ctm[:, 0:8], cscr[:, 0:8], pp12[:, 3:4], ctm[:, 0:8], ALU.mult, ALU.add))(),
                reads=[cscr_b, pp12_b, ctm_b], writes=[ctm_b])
            ps_t2, ps_t2_b = pp.take1()
            em.emit("pe", (lambda q=q, ps_t2=ps_t2: lambda e: e.transpose(
                ps_t2[0:8, 0:128], ctm[:, 0:8], consts["idf"][0][:, :]))(),
                reads=[ctm_b, consts["idf"][1]], writes=[ps_t2_b])
            em.emit("dve", (lambda q=q, ps_t2=ps_t2: lambda e: e.tensor_copy(
                comb[0:8, q*128:(q+1)*128], ps_t2[0:8, 0:128]))(),
                reads=[ps_t2_b], writes=[comb_b])
        for ex in range(8):
            def ep_e(mi, ps, ps_b, t0, tn, ex=ex):
                em.emit("act", (lambda mi=mi, ps=ps, t0=t0, tn=tn, ex=ex: lambda e: e.activation(
                    he.h[mi][:, t0:t0+tn], ps[:, 0:tn], AF.Gelu,
                    bias=consts["b1c"][0][:, ex*16+mi:ex*16+mi+1]))(),
                    reads=[ps_b, consts["b1c"][1]], writes=[he.b[mi]])
            build_project(em, xn, w1c[0:1024, ex*2048:(ex+1)*2048], 1024, 2048, TT,
                          pp, wbufs, ep_e, mgroup_chunks=4, xoff=xoffs)
            for t0, tn in mm_tiles(TT):
                ps_cb, ps_cb_b = pp.take1()
                em.emit("pe", (lambda ex=ex, t0=t0, tn=tn, ps_cb=ps_cb: lambda e: e.matmul(
                    ps_cb[:, 0:tn], consts["esel"][0][0:8, ex*128:(ex+1)*128],
                    comb[0:8, t0:t0+tn], start=True, stop=True))(),
                    reads=[consts["esel"][1], comb_b], writes=[ps_cb_b])
                for mi in range(16):
                    em.emit("dve", (lambda mi=mi, t0=t0, tn=tn, ps_cb=ps_cb: lambda e: e.tensor_mul(
                        he.h[mi][:, t0:t0+tn], he.h[mi][:, t0:t0+tn], ps_cb[:, 0:tn]))(),
                        reads=[he.b[mi], ps_cb_b], writes=[he.b[mi]])
            def ep_w2(mi, ps, ps_b, t0, tn, ex=ex):
                em.emit("dve", (lambda mi=mi, ps=ps, t0=t0, tn=tn: lambda e: e.tensor_add(
                    outacc.h[mi][:, t0:t0+tn], outacc.h[mi][:, t0:t0+tn], ps[:, 0:tn]))(),
                    reads=[ps_b, outacc.b[mi]], writes=[outacc.b[mi]])
            build_project(em, he, w2c[ex*2048:(ex+1)*2048, 0:1024], 2048, 1024, TT,
                          pp, wbufs, ep_w2, mgroup_chunks=2)
        for t0, tn in mm_tiles(TT):
            for mi in range(NCH):
                ps_b2, ps_b2_b = pp.take1()
                em.emit("pe", (lambda mi=mi, t0=t0, tn=tn, ps_b2=ps_b2: lambda e: e.matmul(
                    ps_b2[:, 0:tn], consts["b2p"][0][0:8, mi*128:(mi+1)*128],
                    comb[0:8, t0:t0+tn], start=True, stop=True))(),
                    reads=[consts["b2p"][1], comb_b], writes=[ps_b2_b])
                em.emit_dma("sp", "xres", (lambda mi=mi, t0=t0, tn=tn, tt=tt: lambda e: e.dma_start(
                    out=x.h[mi][:, 0:tn], in_=xs[mi*128:(mi+1)*128, tt+t0:tt+t0+tn]))(),
                    writes=[x.b[mi]])
                em.close_batch("xres")
                em.emit("dve", (lambda mi=mi, t0=t0, tn=tn, ps_b2=ps_b2: lambda e: e.tensor_add(
                    xout[:, 0:tn], ps_b2[:, 0:tn], x.h[mi][:, 0:tn]))(),
                    reads=[ps_b2_b, x.b[mi]], writes=[xout_b])
                em.emit("dve", (lambda mi=mi, t0=t0, tn=tn: lambda e: e.tensor_add(
                    xout[:, 0:tn], xout[:, 0:tn], outacc.h[mi][:, t0:t0+tn]))(),
                    reads=[xout_b, outacc.b[mi]], writes=[xout_b])
                out_toks.append(em.emit_dma("sp", "out", (lambda mi=mi, t0=t0, tn=tn, tt=tt: lambda e: e.dma_start(
                    out=xo[mi*128:(mi+1)*128, tt+t0:tt+t0+tn], in_=xout[:, 0:tn]))(), reads=[xout_b]))
                em.close_batch("out")
    em.finalize(out_toks)
    return nc

def l3f_inputs(xT_shard, ins):
    rw = np.zeros((1024, 128), np.float32)
    rw[:, 0:8] = np.asarray(ins["L3_ffn_router_w"], np.float32)
    rcol = np.zeros((128, 1), np.float32)
    rcol[0:8, 0] = np.asarray(ins["L3_ffn_router_b"], np.float32)
    sw1 = np.asarray(ins["L3_ffn_sw1"], np.float32)
    sw1c = np.concatenate([sw1[0], sw1[1]], axis=1)
    sb1 = np.asarray(ins["L3_ffn_sb1"], np.float32)
    sb1c = np.concatenate([sb1[0], sb1[1]])
    sw2 = np.asarray(ins["L3_ffn_sw2"], np.float32)
    sw2c = np.concatenate([sw2[0], sw2[1]], axis=0)
    w1 = np.asarray(ins["L3_ffn_w1"], np.float32)
    w1c = np.concatenate([w1[e] for e in range(8)], axis=1)
    b1 = np.asarray(ins["L3_ffn_b1"], np.float32)
    b1c = np.zeros((128, 128), np.float32)
    for ex in range(8):
        b1c[:, ex*16:(ex+1)*16] = b1[ex].reshape(16, 128).T
    w2 = np.asarray(ins["L3_ffn_w2"], np.float32)
    w2c = np.concatenate([w2[e] for e in range(8)], axis=0)
    esel = np.zeros((8, 1024), np.float32)
    for ex in range(8):
        esel[ex, ex*128:(ex+1)*128] = 1.0
    iota8 = np.tile(np.arange(8, dtype=np.float32)[None, :], (128, 1))
    return {
        "xs": np.ascontiguousarray(xT_shard, np.float32),
        "rwT": rw, "sw1c": sw1c, "sw2c": sw2c, "w1c": np.ascontiguousarray(w1c),
        "w2c": np.ascontiguousarray(w2c),
        "rcol": rcol, "sb1c": pack_bias(sb1c, 32),
        "sb2a": pack_bias(np.asarray(ins["L3_ffn_sb2"], np.float32)[0], 8),
        "sb2b": pack_bias(np.asarray(ins["L3_ffn_sb2"], np.float32)[1], 8),
        "b1c": b1c, "b2p": np.ascontiguousarray(np.asarray(ins["L3_ffn_b2"], np.float32)),
        "esel": esel, "iota8": np.ascontiguousarray(iota8),
        "idf": np.eye(128, dtype=np.float32),
    }


def build_l1attn(ntok_core=PT):
    n = ntok_core
    NQB = n // 128
    nc = bass.Bass("TRN2", target_bir_lowering=False)
    x2s = nc.dram_tensor("x2s", [D, n], F32, kind="ExternalInput").ap()
    qh = nc.dram_tensor("qh", [D, n], BF16, kind="ExternalInput").ap()
    kf = nc.dram_tensor("kf", [D, T], BF16, kind="ExternalInput").ap()
    vaugh = nc.dram_tensor("vaugh", [16 * 128, 64 * 65], BF16, kind="ExternalInput").ap()
    projT = nc.dram_tensor("projT", [1024, 1024], F32, kind="ExternalInput").ap()
    xo = nc.dram_tensor("xo", [D, n], F32, kind="ExternalOutput").ap()
    em = Em(nc)
    x = Chunked(em, "x", NCH, 512, F32)
    q = Chunked(em, "q", NCH, n, BF16)
    of = q
    consts = load_consts(em, nc, [("b16", [16, n], BF16), ("projb", [128, 8], F32)])
    misc = setup_misc(em)
    em.close_batch("cst")
    qaug = em.sb("qaug", [128, n], BF16); qaug_b = Buf("qaug")
    kaug = [em.sb(f"kaug{i}", [128, T], BF16) for i in range(2)]
    kaug_b = [Buf("kaug0"), Buf("kaug1")]
    vab = [em.sb(f"vab{i}", [128, 64 * 65], BF16) for i in range(2)]
    vab_b = [Buf("vab0"), Buf("vab1")]
    qm = em.sb("qm", [128, 16], F32); qm_b = Buf("qm")
    km = em.sb("km", [128, 64], F32); km_b = Buf("km")
    sel = em.sb("sel", [16, 64], F32); sel_b = Buf("sel")
    selm8 = em.sb("selm8", [16, 8], F32); selm8_b = Buf("selm8")
    expSH = [(em.sb(f"expS{i}", [128, 1024], BF16), Buf(f"expS{i}")) for i in range(4)]
    o_un = em.sb("o_un", [64, 512], BF16); o_un_b = Buf("o_un")
    r_row = em.sb("r_row", [1, 512], F32); r_row_b = Buf("r_row")
    xout = em.sb("xout", [128, 512], F32); xout_b = Buf("xout")
    wb1 = em.sb("wbufA", [128, 2048], BF16)
    wb2 = em.sb("wbufB", [128, 2048], BF16)
    wbufs = [(wb1, Buf("wbufA")), (wb2, Buf("wbufB"))]
    pp = PsumPool(em, n=2, width=512)
    ps_sH = [(em.ps(f"ps_s{i}", [128, 1024]), Buf(f"ps_s{i}")) for i in range(2)]
    ps_o = em.ps("ps_o", [128, 512]); ps_o_b = Buf("ps_o")
    ps_m = em.ps("ps_m", [128, 64]); ps_m_b = Buf("ps_m")
    out_toks = []
    for ci in range(NCH):
        em.emit_dma("pool", "qin", (lambda ci=ci: lambda e: e.dma_start(
            out=q.h[ci][:, :], in_=qh[ci*128:(ci+1)*128, :]))(), writes=[q.b[ci]])
    em.close_batch("qin")
    for ci in range(NCH):
        em.emit("dve", (lambda ci=ci: lambda e: e.tensor_reduce(
            qm[:, 0:NQB], q.h[ci][:, :].rearrange("p (a b) -> p a b", a=NQB, b=128),
            axis=AX.X, op=ALU.add))(),
            reads=[q.b[ci]], writes=[qm_b]) if ci == 0 else None
    for h in range(16):
        hp, base = h // 2, (h % 2) * 64
        kb, kb_i = kaug[h % 2], kaug_b[h % 2]
        va, va_i = vab[h % 2], vab_b[h % 2]
        em.emit_dma("pool", f"kin{h % 2}", (lambda h=h, kb=kb: lambda e: e.dma_start(
            out=kb[0:64, :], in_=kf[h*64:(h+1)*64, :]))(), writes=[kb_i])
        em.close_batch(f"kin{h % 2}")
        em.emit_dma("pool", f"vin{h % 2}", (lambda h=h, va=va: lambda e: e.dma_start(
            out=va[:, :], in_=vaugh[h*128:(h+1)*128, :]))(), writes=[va_i])
        em.close_batch(f"vin{h % 2}")
        em.emit("dve", (lambda hp=hp, base=base: lambda e: e.tensor_reduce(
            qm[0:64, 0:NQB],
            q.h[hp][base:base+64, :].rearrange("p (a b) -> p a b", a=NQB, b=128),
            axis=AX.X, op=ALU.add))(),
            reads=[q.b[hp]], writes=[qm_b])
        em.emit("dve", (lambda kb=kb: lambda e: e.tensor_reduce(
            km[0:64, 0:64], kb[0:64, :].rearrange("p (a b) -> p a b", a=64, b=128),
            axis=AX.X, op=ALU.add))(),
            reads=[kb_i], writes=[km_b])
        em.emit("pe", (lambda: lambda e: e.matmul(
            ps_m[0:NQB, 0:64], qm[0:64, 0:NQB], km[0:64, 0:64], start=True, stop=True))(),
            reads=[qm_b, km_b], writes=[ps_m_b])
        em.emit("dve", (lambda: lambda e: e.tensor_copy(sel[0:NQB, 0:64], ps_m[0:NQB, 0:64]))(),
            reads=[ps_m_b], writes=[sel_b])
        for rnd in range(4):
            em.emit("dve", (lambda: lambda e: e.max(selm8[0:NQB, 0:8], sel[0:NQB, 0:64]))(),
                reads=[sel_b], writes=[selm8_b])
            em.emit("dve", (lambda: lambda e: e.match_replace(
                sel[0:NQB, 0:64], selm8[0:NQB, 0:8], sel[0:NQB, 0:64], -1e30))(),
                reads=[selm8_b, sel_b], writes=[sel_b])
        em.emit("dve", (lambda: lambda e: e.tensor_single_scalar(
            sel[0:NQB, 0:64], sel[0:NQB, 0:64], -1e30, ALU.is_equal))(),
            reads=[sel_b], writes=[sel_b])
        em.emit("dve", (lambda kb=kb: lambda e: e.tensor_scalar(
            kb[64:64+NQB, 0:T].rearrange("p (a b) -> p a b", a=64, b=128),
            sel[0:NQB, 0:64].rearrange("p (a o) -> p a o", o=1).broadcast_to([NQB, 64, 128]),
            1.0, 1e9, ALU.subtract, ALU.mult))(),
            reads=[sel_b], writes=[kb_i])
        em.emit("dve", (lambda hp=hp, base=base: lambda e: e.tensor_copy(
            qaug[0:64, 0:n], q.h[hp][base:base+64, :]))(),
            reads=[q.b[hp]], writes=[qaug_b])
        em.emit("dve", (lambda: lambda e: e.tensor_copy(
            qaug[64:80, 0:n], consts["b16"][0][0:16, :]))(),
            reads=[consts["b16"][1]], writes=[qaug_b])
        for qt in range(n // 512):
            prev_av = []
            for g in range(16):
                cur_av = []
                for hf in range(2):
                    psS, psS_b = ps_sH[hf]
                    exS, exS_b = expSH[(g % 2) * 2 + hf]
                    def mk_s(g=g, qt=qt, kb=kb, hf=hf, psS=psS):
                        def f(e):
                            ins = None
                            for j in range(2):
                                kc = g * 4 + hf * 2 + j
                                ins = e.matmul(psS[:, j*512:(j+1)*512],
                                               kb[0:80, kc*128:(kc+1)*128],
                                               qaug[0:80, qt*512:(qt+1)*512],
                                               start=True, stop=True)
                            return ins
                        return f
                    em.emit("pe", mk_s(), reads=[kb_i, qaug_b], writes=[psS_b])
                    em.emit("act", (lambda psS=psS, exS=exS: lambda e: e.activation(
                        exS[:, 0:1024], psS[:, 0:1024], AF.Exp, scale=0.125))(),
                        reads=[psS_b], writes=[exS_b])
                    def mk_av(g=g, hf=hf, va=va, exS=exS):
                        def f(e):
                            ins = None
                            for j in range(2):
                                kc = g * 4 + hf * 2 + j
                                ins = e.matmul(ps_o[0:65, 0:512],
                                               va[:, kc*65:kc*65+65],
                                               exS[:, j*512:(j+1)*512],
                                               start=(kc == 0), stop=(kc == 63))
                            return ins
                        return f
                    cur_av.append((mk_av(), [va_i, exS_b]))
                for fn, rds in prev_av:
                    em.emit("pe", fn, reads=rds, writes=[ps_o_b])
                prev_av = cur_av
            for fn, rds in prev_av:
                em.emit("pe", fn, reads=rds, writes=[ps_o_b])
            em.emit("dve", (lambda: lambda e: e.reciprocal(r_row[0:1, 0:512], ps_o[64:65, 0:512]))(),
                reads=[ps_o_b], writes=[r_row_b])
            em.emit("dve", (lambda: lambda e: e.tensor_copy(o_un[0:64, 0:512], ps_o[0:64, 0:512]))(),
                reads=[ps_o_b], writes=[o_un_b])
            ps_bc, ps_bc_b = pp.take1()
            em.emit("pe", (lambda ps_bc=ps_bc: lambda e: e.matmul(
                ps_bc[0:64, 0:512], misc["ones_row"][0][0:1, 0:64], r_row[0:1, 0:512],
                start=True, stop=True))(),
                reads=[misc["ones_row"][1], r_row_b], writes=[ps_bc_b])
            em.emit("dve", (lambda hp=hp, base=base, qt=qt, ps_bc=ps_bc: lambda e: e.tensor_mul(
                of.h[hp][base:base+64, qt*512:(qt+1)*512], o_un[0:64, 0:512],
                ps_bc[0:64, 0:512]))(),
                reads=[o_un_b, ps_bc_b], writes=[of.b[hp]])
    import os as _os
    if _os.environ.get("L1DBG"):
        dbg = nc.dram_tensor("dbg", [1024, n], BF16, kind="ExternalOutput").ap()
        toks = []
        for ci in range(8):
            toks.append(em.emit_dma("sp", "dbgs", (lambda ci=ci: lambda e: e.dma_start(
                out=dbg[ci*128:(ci+1)*128, :], in_=of.h[ci][:, 0:n]))(), reads=[of.b[ci]]))
        em.close_batch("dbgs")
        for s in list(em.cnt):
            if s.startswith("d_"):
                toks.append(Tok(s, em.cnt[s]))
        em.finalize(toks)
        return nc
    def ep_o(mi, ps, ps_b, t0, tn):
        em.emit_dma("sp", "xres", (lambda mi=mi, t0=t0, tn=tn: lambda e: e.dma_start(
            out=x.h[mi][:, 0:tn], in_=x2s[mi*128:(mi+1)*128, t0:t0+tn]))(),
            writes=[x.b[mi]])
        em.close_batch("xres")
        em.emit("dve", (lambda mi=mi, ps=ps, t0=t0, tn=tn: lambda e: e.scalar_tensor_tensor(
            xout[:, 0:tn], ps[:, 0:tn], consts["projb"][0][:, mi:mi+1],
            x.h[mi][:, 0:tn], ALU.add, ALU.add))(),
            reads=[ps_b, consts["projb"][1], x.b[mi]], writes=[xout_b])
        out_toks.append(em.emit_dma("sp", "out", (lambda mi=mi, t0=t0, tn=tn: lambda e: e.dma_start(
            out=xo[mi*128:(mi+1)*128, t0:t0+tn], in_=xout[:, 0:tn]))(), reads=[xout_b]))
        em.close_batch("out")
    build_project(em, of, projT, 1024, 1024, n, pp, wbufs, ep_o, mgroup_chunks=2)
    em.finalize(out_toks)
    return nc

def l1attn_inputs(x2T_shard, q_shard, K_full, V_full, ins):
    vaug = np.ones((16 * 128, 64 * 65), BF16NP)
    Vf = np.asarray(V_full)
    for h in range(16):
        for kc in range(64):
            blk = Vf[h*64:(h+1)*64, kc*128:(kc+1)*128]
            vaug[h*128:(h+1)*128, kc*65:kc*65+64] = blk.T
    b16 = np.zeros((16, PT), np.float32)
    for qb in range(16):
        b16[qb, qb*128:(qb+1)*128] = 1.0
    return {
        "x2s": np.ascontiguousarray(x2T_shard, np.float32),
        "qh": np.ascontiguousarray(q_shard),
        "kf": np.ascontiguousarray(K_full),
        "vaugh": vaug, "b16": b16.astype(BF16NP),
        "projT": prep_wT(ins["L1_attn_proj_w"]),
        "projb": pack_bias(ins["L1_attn_proj_b"], 8),
    }


L3_BOUNDS = [16, 32, 48, 64]
L3_OFFS = [0, 16, 48, 96]
L3_TOTAL = 160

def build_l3attn(ntok_core=PT):
    n = ntok_core
    nc = bass.Bass("TRN2", target_bir_lowering=False)
    x6s = nc.dram_tensor("x6s", [D, n], F32, kind="ExternalInput").ap()
    qsl = nc.dram_tensor("qsl", [D, n], BF16, kind="ExternalInput").ap()
    ksl = nc.dram_tensor("ksl", [D, L3_TOTAL * 128], BF16, kind="ExternalInput").ap()
    vaugsl = nc.dram_tensor("vaugsl", [16 * 128, L3_TOTAL * 65], BF16, kind="ExternalInput").ap()
    WoT = nc.dram_tensor("WoT", [1024, 1024], F32, kind="ExternalInput").ap()
    xo = nc.dram_tensor("xo", [D, n], F32, kind="ExternalOutput").ap()
    em = Em(nc)
    x = Chunked(em, "x", NCH, 512, F32)
    q = Chunked(em, "q", NCH, n, BF16)
    of = q
    consts = load_consts(em, nc, [("dmask", [128, 4 * 512], BF16)])
    misc = setup_misc(em)
    em.close_batch("cst")
    kpair = [em.sb("kpair0", [128, L3_TOTAL * 128], BF16)] * 2
    _kb0 = Buf("kpair0")
    kpair_b = [_kb0, _kb0]
    vab = [em.sb(f"vab{i}", [128, L3_TOTAL * 65], BF16) for i in range(2)]
    vab_b = [Buf("vab0"), Buf("vab1")]
    expSH = [(em.sb(f"expS{i}", [128, 1024], BF16), Buf(f"expS{i}")) for i in range(4)]
    tscr = em.sb("tscr", [128, 1024], F32); tscr_b = Buf("tscr")
    o_un = em.sb("o_un", [64, 512], BF16); o_un_b = Buf("o_un")
    r_row = em.sb("r_row", [1, 512], F32); r_row_b = Buf("r_row")
    xout = em.sb("xout", [128, 512], F32); xout_b = Buf("xout")
    wb1 = em.sb("wbufA", [128, 2048], BF16)
    wb2 = em.sb("wbufB", [128, 2048], BF16)
    wbufs = [(wb1, Buf("wbufA")), (wb2, Buf("wbufB"))]
    pp = PsumPool(em, n=2, width=512)
    ps_sH = [(em.ps(f"ps_s{i}", [128, 1024]), Buf(f"ps_s{i}")) for i in range(2)]
    ps_o = em.ps("ps_o", [128, 512]); ps_o_b = Buf("ps_o")
    out_toks = []
    for ci in range(NCH):
        em.emit_dma("pool", "qin", (lambda ci=ci: lambda e: e.dma_start(
            out=q.h[ci][:, :], in_=qsl[ci*128:(ci+1)*128, :]))(), writes=[q.b[ci]])
    em.close_batch("qin")
    for h in range(16):
        hp, base = h // 2, (h % 2) * 64
        if h % 2 == 0:
            kb, kb_i = kpair[(h // 2) % 2], kpair_b[(h // 2) % 2]
            em.emit_dma("pool", f"kin{(h//2) % 2}", (lambda hp=hp, kb=kb: lambda e: e.dma_start(
                out=kb[:, :], in_=ksl[hp*128:(hp+1)*128, :]))(), writes=[kb_i])
            em.close_batch(f"kin{(h//2) % 2}")
        else:
            kb, kb_i = kpair[(h // 2) % 2], kpair_b[(h // 2) % 2]
        va, va_i = vab[h % 2], vab_b[h % 2]
        em.emit_dma("pool", f"vin{h % 2}", (lambda h=h, va=va: lambda e: e.dma_start(
            out=va[:, :], in_=vaugsl[h*128:(h+1)*128, :]))(), writes=[va_i])
        em.close_batch(f"vin{h % 2}")
        for s in range(4):
            nkc = L3_BOUNDS[s]
            off = L3_OFFS[s]
            q0 = s * 512
            prev_av = []
            for g in range(nkc // 4):
                is_diag = (g == nkc // 4 - 1)
                cur_av = []
                for hf in range(2):
                    psS, psS_b = ps_sH[hf]
                    exS, exS_b = expSH[(g % 2) * 2 + hf]
                    def mk_s(g=g, off=off, q0=q0, base=base, hp=hp, kb=kb, hf=hf, psS=psS):
                        def f(e):
                            ins = None
                            for j in range(2):
                                kc = off + g * 4 + hf * 2 + j
                                ins = e.matmul(psS[:, j*512:(j+1)*512],
                                               kb[base:base+64, kc*128:(kc+1)*128],
                                               q.h[hp][base:base+64, q0:q0+512],
                                               start=True, stop=True)
                            return ins
                        return f
                    em.emit("pe", mk_s(), reads=[kb_i, q.b[hp]], writes=[psS_b])
                    em.emit("act", (lambda psS=psS: lambda e: e.activation(
                        tscr[:, 0:1024], psS[:, 0:1024], AF.Tanh, scale=1.0/400.0))(),
                        reads=[psS_b], writes=[tscr_b])
                    em.emit("act", (lambda exS=exS: lambda e: e.activation(
                        exS[:, 0:1024], tscr[:, 0:1024], AF.Exp, scale=50.0))(),
                        reads=[tscr_b], writes=[exS_b])
                    if is_diag:
                        em.emit("dve", (lambda exS=exS, hf=hf: lambda e: e.tensor_mul(
                            exS[:, 0:1024], exS[:, 0:1024],
                            consts["dmask"][0][:, hf*1024:(hf+1)*1024]))(),
                            reads=[exS_b, consts["dmask"][1]], writes=[exS_b])
                    def mk_av(g=g, off=off, nkc=nkc, va=va, hf=hf, exS=exS):
                        def f(e):
                            ins = None
                            for j in range(2):
                                kc = off + g * 4 + hf * 2 + j
                                first = (g == 0 and hf == 0 and j == 0)
                                last = (g == nkc // 4 - 1 and hf == 1 and j == 1)
                                ins = e.matmul(ps_o[0:65, 0:512],
                                               va[:, kc*65:kc*65+65],
                                               exS[:, j*512:(j+1)*512],
                                               start=first, stop=last)
                            return ins
                        return f
                    cur_av.append((mk_av(), [va_i, exS_b]))
                for fn, rds in prev_av:
                    em.emit("pe", fn, reads=rds, writes=[ps_o_b])
                prev_av = cur_av
            for fn, rds in prev_av:
                em.emit("pe", fn, reads=rds, writes=[ps_o_b])
            em.emit("dve", (lambda: lambda e: e.reciprocal(r_row[0:1, 0:512], ps_o[64:65, 0:512]))(),
                reads=[ps_o_b], writes=[r_row_b])
            em.emit("dve", (lambda: lambda e: e.tensor_copy(o_un[0:64, 0:512], ps_o[0:64, 0:512]))(),
                reads=[ps_o_b], writes=[o_un_b])
            ps_bc, ps_bc_b = pp.take1()
            em.emit("pe", (lambda ps_bc=ps_bc: lambda e: e.matmul(
                ps_bc[0:64, 0:512], misc["ones_row"][0][0:1, 0:64], r_row[0:1, 0:512],
                start=True, stop=True))(),
                reads=[misc["ones_row"][1], r_row_b], writes=[ps_bc_b])
            em.emit("dve", (lambda hp=hp, base=base, q0=q0, ps_bc=ps_bc: lambda e: e.tensor_mul(
                of.h[hp][base:base+64, q0:q0+512], o_un[0:64, 0:512], ps_bc[0:64, 0:512]))(),
                reads=[o_un_b, ps_bc_b], writes=[of.b[hp]])
    def ep_o(mi, ps, ps_b, t0, tn):
        em.emit_dma("sp", "xres", (lambda mi=mi, t0=t0, tn=tn: lambda e: e.dma_start(
            out=x.h[mi][:, 0:tn], in_=x6s[mi*128:(mi+1)*128, t0:t0+tn]))(),
            writes=[x.b[mi]])
        em.close_batch("xres")
        em.emit("dve", (lambda mi=mi, ps=ps, t0=t0, tn=tn: lambda e: e.tensor_add(
            xout[:, 0:tn], ps[:, 0:tn], x.h[mi][:, 0:tn]))(),
            reads=[ps_b, x.b[mi]], writes=[xout_b])
        out_toks.append(em.emit_dma("sp", "out", (lambda mi=mi, t0=t0, tn=tn: lambda e: e.dma_start(
            out=xo[mi*128:(mi+1)*128, t0:t0+tn], in_=xout[:, 0:tn]))(), reads=[xout_b]))
        em.close_batch("out")
    build_project(em, of, WoT, 1024, 1024, n, pp, wbufs, ep_o, mgroup_chunks=2)
    em.finalize(out_toks)
    return nc

def l3_qts(core_mod4):
    return [4 * s + core_mod4 for s in range(4)]

def l3attn_inputs(x6T, qT, KT, VT, core, ins):
    cm = core % 4
    b0 = (core // 4) * T
    qts = l3_qts(cm)
    qs = np.concatenate([qT[:, b0 + qt*512 : b0 + (qt+1)*512] for qt in qts], axis=1)
    x6 = np.concatenate([x6T[:, b0 + qt*512 : b0 + (qt+1)*512] for qt in qts], axis=1)
    ksl = np.zeros((1024, L3_TOTAL * 128), BF16NP)
    vau = np.zeros((16 * 128, L3_TOTAL * 65), BF16NP)
    Vf = np.asarray(VT)
    for s in range(4):
        qt = qts[s]
        valid = 4 * (qt + 1)
        bound = L3_BOUNDS[s]
        off = L3_OFFS[s]
        order = list(range(valid - 4)) + [-1] * (bound - valid) + list(range(valid - 4, valid))
        for ii, kc in enumerate(order):
            col = (off + ii) * 128
            if kc < 0:
                continue
            ksl[:, col:col+128] = KT[:, kc*128:(kc+1)*128]
            for h in range(16):
                blk = Vf[h*64:(h+1)*64, kc*128:(kc+1)*128]
                vau[h*128:(h+1)*128, (off+ii)*65:(off+ii)*65+64] = blk.T
                vau[h*128:(h+1)*128, (off+ii)*65+64] = 1.0
    dm = np.zeros((128, 4, 512), np.float32)
    for j in range(4):
        ki = np.arange(128)
        for qq in range(512):
            dm[:, j, qq] = (j * 128 + ki <= (qq // 1) + 0) * 0
    for j in range(4):
        ki = np.arange(128)[:, None]
        qq = np.arange(512)[None, :]
        dm[:, j, :] = (j * 128 + ki <= qq).astype(np.float32)
    return {
        "x6s": np.ascontiguousarray(x6, np.float32),
        "qsl": np.ascontiguousarray(qs),
        "ksl": np.ascontiguousarray(ksl), "vaugsl": np.ascontiguousarray(vau),
        "WoT": np.ascontiguousarray(np.asarray(ins["L3_attn_Wo"], np.float32).T),
        "dmask": np.ascontiguousarray(dm.reshape(128, 2048)).astype(BF16NP),
    }


def _host_ext(stream_T, c, halo):
    g0 = c * PT
    b0 = (g0 // T) * T
    lo = g0 - halo
    pad = max(0, b0 - lo)
    seg = stream_T[:, max(lo, b0):g0 + PT]
    if pad:
        seg = np.concatenate([np.zeros((D, pad), np.float32), seg], axis=1)
    return np.ascontiguousarray(seg)

def _gather(res, name, dtype=np.float32):
    out = np.zeros((res.results[0][name].shape[0], NTOK), dtype)
    for c in range(NCORE):
        out[:, c*PT:(c+1)*PT] = res.results[c][name]
    return out

def kernel(**inputs):
    ins = inputs
    x0 = np.asarray(ins["L0_attn_x"], np.float32).reshape(NTOK, D)
    xT = np.ascontiguousarray(x0.T)
    cores = list(range(NCORE))
    def run(nc, maps):
        return run_bass_kernel_spmd(nc, maps, core_ids=cores)

    nc = build_l0attn()
    maps = []
    for c in cores:
        pos = (np.arange(c*PT, (c+1)*PT) % T)
        maps.append(l0attn_inputs(xT[:, c*PT:(c+1)*PT], pos, ins))
    res = run(nc, maps)
    x1 = _gather(res, "xo")
    nc = build_l0ffn()
    maps = [l0ffn_inputs(x1[:, c*PT:(c+1)*PT], ins) for c in cores]
    res = run(nc, maps)
    x2 = _gather(res, "xo")
    nc = build_qkv3()
    wcat = prep_wT(ins["L1_attn_qkv_w"])
    maps = [qkv3_inputs(x2[:, c*PT:(c+1)*PT], wcat) for c in cores]
    res = run(nc, maps)
    qT = _gather(res, "qo", BF16NP)
    kT = _gather(res, "ko", BF16NP)
    vT = _gather(res, "vo", BF16NP)
    nc = build_l1attn()
    maps = []
    for c in cores:
        b0 = (c // 4) * T
        maps.append(l1attn_inputs(x2[:, c*PT:(c+1)*PT], qT[:, c*PT:(c+1)*PT],
                                  kT[:, b0:b0+T], vT[:, b0:b0+T], ins))
    res = run(nc, maps)
    x3 = _gather(res, "xo")
    stream = x3
    for layer in range(2):
        wqkv = ins["L1_ffn_wqkv"][layer]; qs = ins["L1_ffn_q_scale"][layer]
        ks = ins["L1_ffn_k_scale"][layer]; wo = ins["L1_ffn_wo"][layer]
        nc = build_f1attn(layer == 0)
        maps = []
        for c in cores:
            g0 = c * PT; b0 = (g0 // T) * T
            posq = np.arange(g0 - 1024, g0 + PT) - b0
            maps.append(f1attn_inputs(_host_ext(stream, c, 1024), posq, wqkv, qs, ks, wo,
                                      batch_start=(c % 4 == 0)))
        res = run(nc, maps)
        xi = _gather(res, "xo")
        nc = build_f1ffn(layer == 1)
        maps = []
        for c in cores:
            maps.append(f1ffn_inputs(_host_ext(xi, c, 1), ins["L1_ffn_ln_g"][layer],
                                     ins["L1_ffn_ln_b"][layer], ins["L1_ffn_w1"][layer],
                                     ins["L1_ffn_b1"][layer], ins["L1_ffn_w2"][layer],
                                     ins["L1_ffn_b2"][layer],
                                     x3[:, c*PT:(c+1)*PT] if layer == 1 else None))
        res = run(nc, maps)
        stream = _gather(res, "xo")
    x4 = stream
    nc = build_l2a()
    maps = []
    for c in cores:
        g0 = c * PT; b0 = (g0 // T) * T
        pos_q = np.arange(g0, g0 + PT) - b0
        pos_k = np.arange(g0 - 256, g0 + PT) - b0
        maps.append(l2a_inputs(_host_ext(x4, c, 256), pos_q, pos_k, ins))
    res = run(nc, maps)
    x5 = _gather(res, "xo")
    nc = build_l2b1()
    maps = [l2b1_inputs(x5[:, c*PT:(c+1)*PT], ins) for c in cores]
    res = run(nc, maps)
    r1 = res.results
    nc = build_l2b2()
    maps = []
    for c in cores:
        prev = []
        for j in range(4):
            cc = (c // 4) * 4 + j
            prev.append(r1[cc]["sso"] if cc < c else np.zeros((1024, 2), np.float32))
        g0 = c * PT; b0 = (g0 // T) * T
        gpos = np.arange(g0, g0 + PT) - b0
        maps.append(l2b2_inputs_v2(x5[:, c*PT:(c+1)*PT], r1[c]["qo"], r1[c]["ko"],
                                   r1[c]["iqo"], r1[c]["iko"], prev, gpos, ins))
    res = run(nc, maps)
    x6 = _gather(res, "xo")
    nc = build_qkv3()
    wcat3 = np.concatenate([prep_wT(ins["L3_attn_Wq"]), prep_wT(ins["L3_attn_Wk"]),
                            prep_wT(ins["L3_attn_Wv"])], axis=1)
    maps = [qkv3_inputs(x6[:, c*PT:(c+1)*PT], np.ascontiguousarray(wcat3)) for c in cores]
    res = run(nc, maps)
    q3 = _gather(res, "qo", BF16NP)
    k3 = _gather(res, "ko", BF16NP)
    v3 = _gather(res, "vo", BF16NP)
    nc = build_l3attn()
    maps = []
    for c in cores:
        b0 = (c // 4) * T
        maps.append(l3attn_inputs(x6, q3, k3[:, b0:b0+T], v3[:, b0:b0+T], c, ins))
    res = run(nc, maps)
    x7 = np.zeros((D, NTOK), np.float32)
    for c in cores:
        qts = l3_qts(c % 4)
        b0 = (c // 4) * T
        o = res.results[c]["xo"]
        for s, qt in enumerate(qts):
            x7[:, b0 + qt*512 : b0 + (qt+1)*512] = o[:, s*512:(s+1)*512]
    nc = build_l3f()
    maps = [l3f_inputs(x7[:, c*PT:(c+1)*PT], ins) for c in cores]
    res = run(nc, maps)
    x8 = _gather(res, "xo")
    return np.ascontiguousarray(x8.T).reshape(B, T, D).astype(np.float32)
